# version 06
# baseline (speedup 1.0000x reference)
import math
from contextlib import ExitStack

import numpy as np

import concourse.bass as bass
import concourse.mybir as mybir
from concourse.bass_utils import run_bass_kernel_spmd

F32 = mybir.dt.float32
BF16 = mybir.dt.bfloat16
AF = mybir.ActivationFunctionType
ALU = mybir.AluOpType
AX = mybir.AxisListType.X

TOK = 2048
DM = 2048
NSL = 4
SL = 512
RNE = 12582912.0
NEG = -30000.0


class Sem:
    __slots__ = ("h", "n")

    def __init__(self, h):
        self.h = h
        self.n = 0


class Prog:

    def __init__(self, nc):
        self.nc = nc
        self.th = {k: [] for k in "tvsgy"}
        self.ctx = ExitStack()
        self.cur_ctx = self.ctx
        self.prefix = ""
        self._i = 0
        self.autosem = {}
        self.last = {}

    def begin_phase(self, tag=""):
        assert self.cur_ctx is self.ctx
        self.cur_ctx = ExitStack()
        self.prefix = tag

    def end_phase(self):
        self.cur_ctx.close()
        self.cur_ctx = self.ctx
        self.prefix = ""

    def barrier(self, cm):
        sems = [cm.ld, cm.gld, cm.mm, cm.act, cm.dve, cm.pool, cm.st]
        sems += [s for s in cm.aux if s.n > 0]
        for eng in "tvsgy":
            for s in sems:
                if s.n > 0:
                    self.wait(eng, s, s.n)

    def sem(self, name=None):
        self._i += 1
        return Sem(self.ctx.enter_context(self.nc.semaphore(name or f"sm{self._i}")))

    def sb(self, name, shape, dt=F32):
        return self.cur_ctx.enter_context(
            self.nc.sbuf_tensor(self.prefix + name, shape, dt))

    def ps(self, name, shape, dt=F32):
        return self.cur_ctx.enter_context(
            self.nc.psum_tensor(self.prefix + name, shape, dt))

    def dram(self, name, shape, dt=BF16):
        return self.nc.dram_tensor(self.prefix + name, shape, dt,
                                   kind="Internal").ap()

    def emit(self, eng, fn, inc=None, n=1, dma=False):
        auto = self.autosem.get(eng) if not dma else None
        if auto is not None:
            prev = self.last.get(eng)
            if prev is not None:
                self.th[eng].append(lambda e, h=prev[0], v=prev[1]: e.wait_ge(h, v))
            if inc is None:
                inc = auto
        if inc is not None:
            inc.n += n
            tgt = inc.n
            self.th[eng].append(lambda e, fn=fn, h=inc.h, n=n: fn(e).then_inc(h, n))
            if auto is not None:
                self.last[eng] = (inc.h, tgt)
            return tgt
        self.th[eng].append(fn)
        return None

    def wait(self, eng, sem, val=None):
        v = sem.n if val is None else val
        self.th[eng].append(lambda e, h=sem.h, v=v: e.wait_ge(h, v))

    def waitall(self, eng, sem):
        self.th[eng].append(lambda e, h=sem.h, v=sem.n: e.wait_ge(h, v))

    def finish(self):
        with self.nc.Block() as block:
            block.tensor(lambda e: [f(e) for f in self.th["t"]])
            block.vector(lambda e: [f(e) for f in self.th["v"]])
            block.scalar(lambda e: [f(e) for f in self.th["s"]])
            block.gpsimd(lambda e: [f(e) for f in self.th["g"]])
            block.sync(lambda e: [f(e) for f in self.th["y"]])



class Common:

    def __init__(self, P: Prog, n_dram_out=1):
        self.P = P
        self.ld = P.sem("ld")
        self.gld = P.sem("gld")
        self.mm = P.sem("mm")
        self.act = P.sem("act")
        self.dve = P.sem("dve")
        self.pool = P.sem("pool")
        self.st = P.sem("st")
        self.aux = [P.sem(f"aux{i}") for i in range(88)]
        self._free = list(range(88))
        self._pools = {"y": [], "g": [], "c": []}
        self._cursor = {"y": 0, "g": 0, "c": 0}
        P.autosem = {"v": self.dve, "s": self.act, "g": self.pool}

    def newsem(self, kind="y"):
        cur = self._cursor[kind]
        self._cursor[kind] = cur + 1
        pool = self._pools[kind]
        if cur < len(pool):
            return pool[cur]
        s = self.aux[self._free.pop(0)]
        pool.append(s)
        return s

    def reset_aux(self):
        self._cursor = {"y": 0, "g": 0, "c": 0}

    def build_basics(self, P):
        self.iotaPF = P.sb("iotaPF", [128, SL], F32)
        self.iotaP = P.sb("iotaP", [128, 1], F32)
        self.iotaF = P.sb("iotaF", [1, SL], F32)
        P.emit("g", lambda g: g.iota(self.iotaPF[:, :], pattern=[[-1, SL]], base=0,
                                     channel_multiplier=1,
                                     allow_small_or_imprecise_dtypes=True), inc=self.pool)
        P.emit("g", lambda g: g.iota(self.iotaP[:, :], pattern=[[0, 1]], base=0,
                                     channel_multiplier=1,
                                     allow_small_or_imprecise_dtypes=True), inc=self.pool)
        P.emit("g", lambda g: g.iota(self.iotaF[:, :], pattern=[[1, SL]], base=0,
                                     channel_multiplier=0,
                                     allow_small_or_imprecise_dtypes=True), inc=self.pool)
        self.pool_basics = self.pool.n
        self.ones1_128 = P.sb("ones1_128", [1, 128], F32)
        self.ones128_1b = P.sb("ones128_1b", [128, 1], BF16)
        self.ones1_128b = P.sb("ones1_128b", [1, 128], BF16)
        P.emit("v", lambda v: v.memset(self.ones1_128[:, :], 1.0), inc=self.dve)
        P.emit("v", lambda v: v.memset(self.ones128_1b[:, :], 1.0), inc=self.dve)
        P.emit("v", lambda v: v.memset(self.ones1_128b[:, :], 1.0), inc=self.dve)
        self.cst = P.sb("cst", [128, 32], F32)
        self.n_cst = 0
        self.dve_basics = self.dve.n

    def const_col(self, val):
        P = self.P
        j = self.n_cst
        self.n_cst += 1
        assert j < 32
        ap = self.cst[:, j:j + 1]
        P.emit("v", lambda v, ap=ap, val=float(val): v.memset(ap, val), inc=self.dve)
        return ap


def bcast_row(P, cm, row_ap, out_psum, out_sbuf, width, sdep=()):
    P.wait("t", cm.act, cm.act.n)
    t1 = P.emit("t", lambda t: t.matmul(out_psum[0:128, 0:width], cm.ones1_128[:, :],
                                        row_ap, start=True, stop=True), inc=cm.mm)
    P.wait("s", cm.mm, t1)
    t2 = P.emit("s", lambda s: s.copy(out_sbuf[0:128, 0:width], out_psum[0:128, 0:width]),
                inc=cm.act)
    return t1, t2


def part_reduce_max(P, cm, acc, width, cols_out, psum_t, j0=0, absolute=False):
    nch = width // 128
    P.wait("t", cm.dve, cm.dve.n)
    tlast = None
    for j in range(nch):
        tlast = P.emit("t", lambda t, j=j: t.transpose(
            psum_t[0:128, j0 + 128 * j:j0 + 128 * (j + 1)],
            acc[0:128, 128 * j:128 * (j + 1)], cm.identf[0:128, 0:128]), inc=cm.mm)
    P.wait("v", cm.mm, tlast)
    elast = None
    for j in range(nch):
        elast = P.emit("v", lambda v, j=j: v.reduce_max(
            cols_out[0:128, j:j + 1], psum_t[0:128, j0 + 128 * j:j0 + 128 * (j + 1)],
            axis=AX, apply_absolute_value=absolute), inc=cm.dve)
    return elast


def amax_start(P, cm, mx, mn, src):
    P.emit("v", lambda v: v.tensor_copy(mx, src), inc=cm.dve)
    return P.emit("v", lambda v: v.tensor_copy(mn, src), inc=cm.dve)


def amax_step(P, cm, mx, mn, src):
    P.emit("v", lambda v: v.tensor_tensor(mx, mx, src, ALU.max), inc=cm.dve)
    return P.emit("v", lambda v: v.tensor_tensor(mn, mn, src, ALU.min), inc=cm.dve)


def amax_finish(P, cm, mx, mn):
    P.emit("v", lambda v: v.tensor_scalar(mn, mn, -1.0, None, op0=ALU.mult),
           inc=cm.dve)
    return P.emit("v", lambda v: v.tensor_tensor(mx, mx, mn, ALU.max), inc=cm.dve)


def transpose_col_to_row(P, cm, col_ap, psum_ap, row_out, n=128):
    t1 = P.emit("t", lambda t: t.transpose(psum_ap[0:1, 0:n], col_ap, cm.identf[0:n, 0:n]),
                inc=cm.mm)
    P.wait("s", cm.mm, t1)
    t2 = P.emit("s", lambda s: s.copy(row_out[0:1, 0:n], psum_ap[0:1, 0:n]), inc=cm.act)
    return t2


def build_idents(P, cm):
    cm.identf = P.sb("identf", [128, 128], F32)
    cm.identb = P.sb("identb", [128, 128], BF16)
    cm.ident32f = cm.identf
    P.wait("v", cm.pool, cm.pool_basics)
    e1 = P.emit("v", lambda v: v.tensor_scalar(
        out=cm.identf[:, :], in0=cm.iotaPF[:, 0:128], scalar1=0.0, scalar2=None,
        op0=ALU.is_equal), inc=cm.dve)
    e2 = P.emit("v", lambda v: v.tensor_scalar(
        out=cm.identb[:, :], in0=cm.iotaPF[:, 0:128], scalar1=0.0, scalar2=None,
        op0=ALU.is_equal), inc=cm.dve)
    return e2



def w_to_scratch(P, cm, name, w_ap, O, I, buf=None, prev=None):
    sc = P.dram(f"sc_{name}", [O, I], BF16)
    if buf is None:
        buf = P.sb(f"wsc_{name}", [128, 2 * I], BF16)
    if prev is not None:
        P.wait("g", prev, prev.n)
    ssem = cm.newsem("y")
    lsem = [cm.newsem("g"), cm.newsem("g")]
    CW = min(I, 2048)
    tiles = [(o0, c0) for o0 in range(0, O, 128) for c0 in range(0, I, CW)]
    for j, (o0, c0) in enumerate(tiles):
        half = (j % 2) * CW
        b = buf[:, half:half + CW]
        cw = min(CW, I - c0)
        if j >= 2:
            P.waitall("g", ssem)
        tgt = P.emit("g", lambda g, b=b, cw=cw, s=w_ap[o0:o0 + 128, c0:c0 + CW]:
                     g.dma_start(out=b[:, 0:cw] if cw < CW else b, in_=s),
                     inc=lsem[j % 2], n=16, dma=True)
        P.wait("y", lsem[j % 2], tgt)
        P.emit("y", lambda y, b=b, cw=cw, d=sc[o0:o0 + 128, c0:c0 + CW]:
               y.dma_start(out=d, in_=b[:, 0:cw] if cw < CW else b), inc=ssem, n=16)
    return sc, ssem


def w_to_scratch_ternary(P, cm, name, w_ap, O, I, eps, mode, buf, f32buf, psA, psB,
                         prev_ssem=None, Opad=None, Ipad=None):
    Opad = Opad or O
    Ipad = Ipad or I
    sc = P.dram(f"sc_{name}", [Opad, Ipad], BF16)
    if prev_ssem is not None:
        P.waitall("g", prev_ssem)
        P.waitall("y", prev_ssem)
    CW = min(I, 2048)
    tiles = [(o0, c0) for o0 in range(0, O, 128) for c0 in range(0, I, CW)]
    row = P.sb(f"wqr_{name}", [1, 520], F32)
    gcol = P.sb(f"wqg_{name}", [128, 4], F32)
    lsem = [cm.newsem("g"), cm.newsem("g")]
    first = True
    mm_abs = 0
    P.wait("t", cm.dve, cm.dve.n)
    nslice = CW // 512
    for j, (o0, c0) in enumerate(tiles):
        half = (j % 2) * CW
        cw = min(CW, I - c0)
        ow = min(128, O - o0)
        b = buf[0:ow, half:half + cw]
        if j >= 2:
            P.wait("g", cm.mm, mm_abs)
        tgt = P.emit("g", lambda g, b=b, s=w_ap[o0:o0 + ow, c0:c0 + cw]:
                     g.dma_start(out=b, in_=s), inc=lsem[j % 2], n=16, dma=True)
        P.wait("s", lsem[j % 2], tgt)
        ev = P.emit("s", lambda s, b=b: s.activation(out=b, in_=b, func=AF.Abs),
                    inc=cm.act)
        P.wait("t", cm.act, ev)
        nsl_ = (cw + 511) // 512
        for k in range(nsl_):
            kw_ = min(512, cw - 512 * k)
            stp = (k == nsl_ - 1 and j == len(tiles) - 1)
            mm_abs = P.emit("t", lambda t, b=b, k=k, kw_=kw_, ow=ow, first=first, stp=stp: t.matmul(
                psA[0:1, 0:kw_], cm.ones128_1b[0:ow, 0:1], b[0:ow, 512 * k:512 * k + kw_],
                start=first, stop=stp), inc=cm.mm)
            first = False
    P.wait("s", cm.mm, mm_abs)
    e1 = P.emit("s", lambda s: s.copy(row[0:1, 0:512], psA[0:1, 0:512]), inc=cm.act)
    P.wait("v", cm.act, e1)
    P.emit("v", lambda v: v.tensor_reduce(row[0:1, 512:513], row[0:1, 0:512],
                                          axis=AX, op=ALU.add), inc=cm.dve)
    scale = 1.0 / (O * I)
    if mode == "wq":
        P.emit("v", lambda v: v.tensor_scalar(
            row[0:1, 513:514], row[0:1, 512:513], scale, float(eps),
            op0=ALU.mult, op1=ALU.max), inc=cm.dve)
        e4 = P.emit("v", lambda v: v.tensor_scalar(
            row[0:1, 514:515], row[0:1, 513:514], 0.5, None, op0=ALU.mult), inc=cm.dve)
    else:
        P.emit("v", lambda v: v.tensor_scalar(
            row[0:1, 513:514], row[0:1, 512:513], scale, None, op0=ALU.mult),
            inc=cm.dve)
        e4 = P.emit("v", lambda v: v.tensor_scalar(
            row[0:1, 514:515], row[0:1, 513:514], float(eps), 0.5,
            op0=ALU.add, op1=ALU.mult), inc=cm.dve)
    e5 = P.emit("v", lambda v: v.tensor_scalar(
        row[0:1, 515:516], row[0:1, 513:514], 0.5, None, op0=ALU.mult), inc=cm.dve)
    P.wait("t", cm.dve, e5)
    P.wait("t", cm.act, cm.act.n)
    t1 = P.emit("t", lambda t: t.matmul(psB[0:128, 0:2], cm.ones1_128[:, :],
                                        row[0:1, 514:516], start=True, stop=True),
                inc=cm.mm)
    P.wait("s", cm.mm, t1)
    P.emit("s", lambda s: s.copy(gcol[0:128, 0:1], psB[0:128, 0:1]), inc=cm.act)
    P.emit("s", lambda s: s.mul(gcol[0:128, 1:2], psB[0:128, 0:1], -1.0), inc=cm.act)
    rdy = P.emit("s", lambda s: s.copy(gcol[0:128, 2:3], psB[0:128, 1:2]), inc=cm.act)
    sgn2 = P.sb(f"wq4_{name}", [128, CW], BF16)
    ssem = cm.newsem("g")
    p2s = [cm.newsem("y"), cm.newsem("y")]
    eb_hist = {}
    ec_prev = 0
    P.wait("s", cm.act, rdy)
    for j, (o0, c0) in enumerate(tiles):
        half = (j % 2) * CW
        cw = min(CW, I - c0)
        ow = min(128, O - o0)
        b = f32buf[0:ow, half:half + cw]
        sg_a = buf[0:ow, half:half + cw]
        if j >= 2:
            P.wait("y", cm.act, eb_hist[j - 2])
        tgt = P.emit("y", lambda y, b=b, s=w_ap[o0:o0 + ow, c0:c0 + cw]:
                     y.dma_start(out=b, in_=s), inc=p2s[j % 2], n=16)
        P.wait("s", p2s[j % 2], tgt)
        if j >= 2:
            P.waitall("s", ssem)
        ea = P.emit("s", lambda s, b=b, o=sg_a, ow=ow: s.activation(
            out=o, in_=b, func=AF.Sign, bias=gcol[0:ow, 1:2], scale=1.0), inc=cm.act)
        if ec_prev:
            P.wait("s", cm.dve, ec_prev)
        eb = P.emit("s", lambda s, b=b, ow=ow, cw=cw: s.activation(
            out=sgn2[0:ow, 0:cw], in_=b, func=AF.Sign, bias=gcol[0:ow, 0:1], scale=1.0),
            inc=cm.act)
        eb_hist[j] = eb
        P.wait("v", cm.act, eb)
        ec_prev = P.emit("v", lambda v, a=sg_a, ow=ow, cw=cw: v.tensor_tensor(
            a, a, sgn2[0:ow, 0:cw], ALU.add), inc=cm.dve)
        P.wait("g", cm.dve, ec_prev)
        P.emit("g", lambda g, a=sg_a, d=sc[o0:o0 + ow, c0:c0 + cw]:
               g.dma_start(out=d, in_=a), inc=ssem, n=16, dma=True)
    if Opad > O or Ipad > I:
        P.waitall("v", ssem)
        ez = P.emit("v", lambda v: v.memset(buf[:, 0:2048], 0.0), inc=cm.dve)
        P.wait("g", cm.dve, ez)
        if Opad > O:
            for c0 in range(0, Ipad, 2048):
                cw = min(2048, Ipad - c0)
                P.emit("g", lambda g, c0=c0, cw=cw, dO=Opad - O: g.dma_start(
                    out=sc[O:Opad, c0:c0 + cw], in_=buf[0:Opad - O, 0:cw]),
                    inc=ssem, n=16, dma=True)
        if Ipad > I:
            for o0 in range(0, O, 128):
                P.emit("g", lambda g, o0=o0: g.dma_start(
                    out=sc[o0:min(o0 + 128, O), I:Ipad],
                    in_=buf[0:min(128, O - o0), 0:Ipad - I]), inc=ssem, n=16, dma=True)
    return sc, ssem, gcol[0:128, 2:3], rdy


def load_wT(P, cm, sc, i0, o0, oW, dst, dep_sem, dep_val, sem=None):
    P.wait("y", dep_sem, dep_val)
    return P.emit("y", lambda y: y.dma_start(
        out=dst, in_=sc[o0:o0 + oW, i0:i0 + 128], transpose=True),
        inc=(sem or cm.ld), n=16)



def build_rope_tables(P, cm, theta, hd, width, c2_scalar_ap, posmul=1.0,
                      posB_ap=None, frq_ap=None):
    half = hd // 2
    iotaW = (posB_ap[0:1, :] if posB_ap is not None
             else P.sb(f"iotaW{theta}", [1, width], F32))
    P.emit("g", lambda g: g.iota(iotaW[:, :], pattern=[[1, width]], base=0,
                                 channel_multiplier=0,
                                 allow_small_or_imprecise_dtypes=True), inc=cm.pool)
    gtgt = cm.pool.n
    posr = (frq_ap[0:1, :] if frq_ap is not None
            else P.sb(f"posr{theta}", [1, width], F32))
    P.wait("v", cm.pool, gtgt)
    if posmul != 1.0:
        e = P.emit("v", lambda v: v.tensor_scalar(
            posr[:, :], iotaW[:, :], c2_scalar_ap, float(posmul),
            op0=ALU.subtract, op1=ALU.mult), inc=cm.dve)
    else:
        e = P.emit("v", lambda v: v.tensor_scalar(
            posr[:, :], iotaW[:, :], c2_scalar_ap, None,
            op0=ALU.subtract), inc=cm.dve)
    invf = P.sb(f"invf{theta}", [128, 1], F32)
    P.wait("s", cm.pool, cm.pool_basics)
    a1 = P.emit("s", lambda s: s.activation(
        out=invf[:, :], in_=cm.iotaP[:, :], func=AF.Exp,
        scale=-2.0 * math.log(theta) / hd), inc=cm.act)
    posB = posB_ap if posB_ap is not None else P.sb(f"posB{theta}", [128, width], F32)
    pb = cm.pb_misc
    P.wait("t", cm.dve, e)
    last = None
    for k in range(width // 512):
        if last is not None:
            P.wait("t", cm.act, last)
        t1 = P.emit("t", lambda t, k=k: t.matmul(
            pb[0:128, 0:512], cm.ones1_128[:, :], posr[0:1, 512 * k:512 * (k + 1)],
            start=True, stop=True), inc=cm.mm)
        P.wait("s", cm.mm, t1)
        last = P.emit("s", lambda s, k=k: s.copy(
            posB[0:128, 512 * k:512 * (k + 1)], pb[0:128, 0:512]), inc=cm.act)
    frq = frq_ap if frq_ap is not None else P.sb(f"frq{theta}", [128, width], F32)
    P.wait("v", cm.act, last)
    e1 = P.emit("v", lambda v: v.tensor_scalar(
        frq[0:half, :], posB[0:half, :], invf[0:half, 0:1], None, op0=ALU.mult),
        inc=cm.dve)
    TWO_PI = 2.0 * math.pi
    m = posB
    e2 = P.emit("v", lambda v: v.tensor_scalar(
        m[0:half, :], frq[0:half, :], 1.0 / TWO_PI, None, op0=ALU.mult), inc=cm.dve)
    e3 = P.emit("v", lambda v: v.tensor_scalar(
        m[0:half, :], m[0:half, :], RNE, RNE, op0=ALU.add, op1=ALU.subtract), inc=cm.dve)
    e4 = P.emit("v", lambda v: v.scalar_tensor_tensor(
        m[0:half, :], m[0:half, :], -TWO_PI, frq[0:half, :], op0=ALU.mult, op1=ALU.add),
        inc=cm.dve)
    e5 = P.emit("v", lambda v: v.tensor_scalar(
        m[0:half, :], m[0:half, :], math.pi, -math.pi, op0=ALU.min, op1=ALU.max),
        inc=cm.dve)
    sin_sb = P.sb(f"sin{theta}", [128, width], BF16)
    cos_sb = P.sb(f"cos{theta}", [128, width], BF16)
    P.wait("s", cm.dve, e5)
    a2 = P.emit("s", lambda s: s.activation(out=sin_sb[0:half, :], in_=m[0:half, :],
                                            func=AF.Sin), inc=cm.act)
    e6 = P.emit("v", lambda v: v.tensor_scalar(
        frq[0:half, :], m[0:half, :], math.pi / 2.0, None, op0=ALU.add), inc=cm.dve)
    P.wait("v", cm.act, a2)
    e7 = P.emit("v", lambda v: v.tensor_scalar(
        m[0:half, :], frq[0:half, :], math.pi, None, op0=ALU.is_gt), inc=cm.dve)
    e8 = P.emit("v", lambda v: v.scalar_tensor_tensor(
        m[0:half, :], m[0:half, :], -TWO_PI, frq[0:half, :], op0=ALU.mult, op1=ALU.add),
        inc=cm.dve)
    e9 = P.emit("v", lambda v: v.tensor_scalar(
        m[0:half, :], m[0:half, :], math.pi, -math.pi, op0=ALU.min, op1=ALU.max),
        inc=cm.dve)
    P.wait("s", cm.dve, e9)
    a3 = P.emit("s", lambda s: s.activation(out=cos_sb[0:half, :], in_=m[0:half, :],
                                            func=AF.Sin), inc=cm.act)
    return cos_sb, sin_sb, a3


def rope_apply(P, cm, src0, src1, cosS, sinS, dst0, dst1, tmp, w, half=128,
               tmp2=None, ps_shift=None):
    h = half
    if tmp2 is not None:
        psB = ps_shift
        P.wait("t", cm.dve, cm.dve.n)
        P.wait("t", cm.act, cm.act.n)
        mm1 = P.emit("t", lambda t, src1=src1: t.matmul(
            psB[0:h, 0:w], cm.identf[64:128, 64:128], src1,
            start=True, stop=True), inc=cm.mm)
        P.wait("s", cm.mm, mm1)
        a0 = P.emit("s", lambda s: s.copy(tmp2[0:h, 0:w], psB[0:h, 0:w]),
                    inc=cm.act)
        P.wait("v", cm.act, a0)
        src1 = tmp2[0:h, 0:w]
    e1 = P.emit("v", lambda v: v.tensor_tensor(tmp[0:h, 0:w], src1, sinS, ALU.mult),
                inc=cm.dve)
    e2 = P.emit("v", lambda v: v.tensor_tensor(dst0, src0, cosS, ALU.mult), inc=cm.dve)
    e3 = P.emit("v", lambda v: v.tensor_tensor(dst0, dst0, tmp[0:h, 0:w], ALU.subtract),
                inc=cm.dve)
    e4 = P.emit("v", lambda v: v.tensor_tensor(tmp[0:h, 0:w], src0, sinS, ALU.mult),
                inc=cm.dve)
    if tmp2 is not None:
        e5 = P.emit("v", lambda v: v.tensor_tensor(tmp2[0:h, 0:w], tmp2[0:h, 0:w],
                                                   cosS, ALU.mult), inc=cm.dve)
        e6 = P.emit("v", lambda v: v.tensor_tensor(tmp2[0:h, 0:w], tmp2[0:h, 0:w],
                                                   tmp[0:h, 0:w], ALU.add), inc=cm.dve)
        P.wait("t", cm.dve, e6)
        mm2 = P.emit("t", lambda t: t.matmul(
            psB[h:2 * h, 0:w], cm.identf[0:h, 0:h], tmp2[0:h, 0:w],
            start=True, stop=True), inc=cm.mm)
        P.wait("s", cm.mm, mm2)
        a1 = P.emit("s", lambda s: s.copy(dst1, psB[h:2 * h, 0:w]), inc=cm.act)
        P.wait("v", cm.act, a1)
        e7 = P.emit("v", lambda v: v.engine_nop(), inc=cm.dve)
        return e7
    e5 = P.emit("v", lambda v: v.tensor_tensor(dst1, src1, cosS, ALU.mult), inc=cm.dve)
    e6 = P.emit("v", lambda v: v.tensor_tensor(dst1, dst1, tmp[0:h, 0:w], ALU.add),
                inc=cm.dve)
    return e6



class PostLN:

    def __init__(self, P, cm, name, eps=1e-5, Rs_ap=None, W=SL):
        self.P, self.cm = P, cm
        self.eps = eps
        self.W = W
        self.Rs = Rs_ap if Rs_ap is not None else P.sb(f"{name}_Rs", [128, 16 * W], F32)
        W = W
        ali = getattr(cm, "pln_alias", None)
        if ali is None:
            self.rb = P.sb(f"{name}_rb", [128, W], BF16)
            self.sq = P.sb(f"{name}_sq", [128, W], BF16)
            self.aB = P.sb(f"{name}_aB", [128, W], F32)
            self.bB = P.sb(f"{name}_bB", [128, W], F32)
        else:
            self.rb, self.sq, self.aB, self.bB = ali
        self.rows = P.sb(f"{name}_rows", [1, 8], F32)
        self.mu2 = P.sb(f"{name}_mu2", [1, W], F32)
        self.arow = self.bB[0:1, 0:W]
        self.brow = self.aB[0:1, 0:W]

    def begin(self):
        self.n_mm = 0
        self.first = True
        self.P.wait("v", self.cm.st, self.cm.st.n)
        self.P.wait("y", self.cm.st, self.cm.st.n)

    def add_chunk(self, oc, psum_ap, res_ap=None):
        P, cm = self.P, self.cm
        R = self.Rs[:, self.W * oc:self.W * (oc + 1)]
        if getattr(self, "scale_col", None) is not None:
            sc_ = self.scale_col
            e1 = P.emit("v", lambda v: v.scalar_tensor_tensor(
                R, psum_ap, sc_, R, op0=ALU.mult, op1=ALU.add), inc=cm.dve)
        else:
            e1 = P.emit("v", lambda v: v.tensor_tensor(R, R, psum_ap, ALU.add),
                        inc=cm.dve)
        P.wait("s", cm.dve, e1)
        if not self.first:
            P.wait("s", cm.mm, self.last_mm)
        a1 = P.emit("s", lambda s: s.copy(self.rb[:, :], R), inc=cm.act)
        a2 = P.emit("s", lambda s: s.activation(out=self.sq[:, :], in_=R,
                                                func=AF.Square), inc=cm.act)
        P.wait("t", cm.act, a2)
        fst = bool(self.first)
        t1 = P.emit("t", lambda t, fst=fst, oc=oc: t.matmul(
            cm.pbR1[0:1, 0:self.W], cm.ones128_1b[:, :], self.rb[:, :],
            start=fst, stop=(oc == 15)), inc=cm.mm)
        t2 = P.emit("t", lambda t, fst=fst, oc=oc: t.matmul(
            cm.pbR2[0:1, 0:self.W], cm.ones128_1b[:, :], self.sq[:, :],
            start=fst, stop=(oc == 15)), inc=cm.mm)
        self.first = False
        self.last_mm = t2
        return e1

    def finalize(self, dst_dram_ap=None, dst_sbuf=None, dst_bf16=None):
        P, cm = self.P, self.cm
        P.wait("s", cm.mm, self.last_mm)
        a1 = P.emit("s", lambda s: s.copy(self.arow[0:1, 0:self.W], cm.pbR1[0:1, 0:self.W]),
                    inc=cm.act)
        a2 = P.emit("s", lambda s: s.copy(self.brow[0:1, 0:self.W], cm.pbR2[0:1, 0:self.W]),
                    inc=cm.act)
        P.wait("v", cm.act, a2)
        e1 = P.emit("v", lambda v: v.tensor_scalar(
            self.arow[0:1, 0:self.W], self.arow[0:1, 0:self.W], 1.0 / DM, None, op0=ALU.mult),
            inc=cm.dve)
        e2 = P.emit("v", lambda v: v.tensor_scalar(
            self.brow[0:1, 0:self.W], self.brow[0:1, 0:self.W], 1.0 / DM, None, op0=ALU.mult),
            inc=cm.dve)
        tmp = self.sq
        e3 = P.emit("v", lambda v: v.tensor_tensor(
            self.mu2[0:1, 0:self.W], self.arow[0:1, 0:self.W], self.arow[0:1, 0:self.W], ALU.mult),
            inc=cm.dve)
        e4 = P.emit("v", lambda v: v.tensor_tensor(
            self.brow[0:1, 0:self.W], self.brow[0:1, 0:self.W], self.mu2[0:1, 0:self.W],
            ALU.subtract), inc=cm.dve)
        e5 = P.emit("v", lambda v: v.tensor_scalar(
            self.brow[0:1, 0:self.W], self.brow[0:1, 0:self.W], float(self.eps), None,
            op0=ALU.add), inc=cm.dve)
        P.wait("s", cm.dve, e5)
        a3 = P.emit("s", lambda s: s.activation(out=self.brow[0:1, 0:self.W],
                                                in_=self.brow[0:1, 0:self.W], func=AF.Sqrt),
                    inc=cm.act)
        P.wait("v", cm.act, a3)
        e6 = P.emit("v", lambda v: v.reciprocal(self.brow[0:1, 0:self.W],
                                                self.brow[0:1, 0:self.W]), inc=cm.dve)
        e7 = P.emit("v", lambda v: v.scalar_tensor_tensor(
            self.arow[0:1, 0:self.W], self.arow[0:1, 0:self.W], -1.0, self.brow[0:1, 0:self.W],
            op0=ALU.mult, op1=ALU.mult), inc=cm.dve)
        P.wait("t", cm.dve, e7)
        P.wait("s", cm.dve, e7)
        bcast_row(P, cm, self.brow[0:1, 0:self.W], cm.pbB, self.aB, self.W)
        _, ab = bcast_row(P, cm, self.arow[0:1, 0:self.W], cm.pbB, self.bB, self.W)
        P.wait("v", cm.act, ab)
        stt = None
        outs = []
        for oc in range(16):
            R = self.Rs[:, self.W * oc:self.W * (oc + 1)]
            e8 = P.emit("v", lambda v, R=R: v.tensor_tensor(
                R, R, self.aB[0:128, 0:self.W], ALU.mult), inc=cm.dve)
            e9 = P.emit("v", lambda v, R=R: v.tensor_tensor(
                R, R, self.bB[0:128, 0:self.W], ALU.add), inc=cm.dve)
            if dst_bf16 is not None:
                P.wait("s", cm.dve, e9)
                ac = P.emit("s", lambda s, R=R, oc=oc: s.copy(
                    dst_bf16[:, self.W * oc:self.W * (oc + 1)], R), inc=cm.act)
                outs.append(("act", ac))
            if dst_dram_ap is not None:
                P.wait("y", cm.dve, e9)
                stt = P.emit("y", lambda y, R=R, oc=oc: y.dma_start(
                    out=dst_dram_ap[128 * oc:128 * (oc + 1), :], in_=R),
                    inc=cm.st, n=16)
                outs.append(("st", stt))
        return outs


def _single(nc, body, io, **kw):
    return _multi(nc, [(body, io, kw)])


def _multi(nc, phases):
    P = Prog(nc)
    cm = Common(P)
    cm.build_basics(P)
    build_idents(P, cm)
    for i, (body, io, kw) in enumerate(phases):
        if i:
            P.barrier(cm)
            cm.reset_aux()
            cm.pln_alias = None
        P.begin_phase(f"ph{i}_")
        body(P, cm, io, **kw)
        P.end_phase()
    P.finish()
    return nc


def _ei(nc, name, shape, dt=F32):
    return nc.dram_tensor(name, shape, dt, kind="ExternalInput").ap()


def _eo(nc, name, shape, dt=F32):
    return nc.dram_tensor(name, shape, dt, kind="ExternalOutput").ap()



def body_l0_attn(P, cm, io, dbg=False):
    xctx, qw_d, kw_d, vw_d, ow_d, qn_d, kn_d, cc_d, out_d = (
        io["xctx"], io["qw"], io["kw"], io["vw"], io["ow"], io["qn"],
        io["kn"], io["cc"], io["out"])
    if dbg:
        d_rope, d_kt, d_vt, d_qs, d_ctx, d_misc = (
            io["d_rope"], io["d_kt"], io["d_vt"], io["d_qs"], io["d_ctx"],
            io["d_misc"])

    pb = [P.ps(f"pb{i}", [128, 512], F32) for i in range(8)]
    cm.pbR1, cm.pbR2, cm.pbB, cm.pbB2 = pb[0], pb[1], pb[2], pb[3]
    cm.pb_misc = pb[2]
    pbS = [pb[4], pb[5]]
    pbO = [pb[6], pb[7]]
    ident_dve = cm.dve.n

    qn = P.sb("qn_r", [1, 256], F32)
    kn = P.sb("kn_r", [1, 256], F32)
    ccr = P.sb("cc_r", [1, 8], F32)
    sml = cm.newsem("y")
    P.emit("y", lambda y: y.dma_start(out=qn[:, :], in_=qn_d[:, :]), inc=sml, n=16)
    P.emit("y", lambda y: y.dma_start(out=kn[:, :], in_=kn_d[:, :]), inc=sml, n=16)
    P.emit("y", lambda y: y.dma_start(out=ccr[:, :], in_=cc_d[:, :]), inc=sml, n=16)
    c2_ap = ccr[0:1, 0:1]

    cc_cols = P.sb("cc_cols", [128, 8], F32)
    P.waitall("t", sml)
    P.wait("t", cm.dve, ident_dve)
    t1 = P.emit("t", lambda t: t.matmul(cm.pbB[0:128, 0:8], cm.ones1_128[:, :],
                                        ccr[0:1, 0:8], start=True, stop=True), inc=cm.mm)
    P.wait("s", cm.mm, t1)
    acc0 = P.emit("s", lambda s: s.copy(cc_cols[:, :], cm.pbB[0:128, 0:8]), inc=cm.act)

    qn_cols = P.sb("qn_cols", [128, 2], F32)
    kn_cols = P.sb("kn_cols", [128, 2], F32)
    P.wait("t", cm.act, acc0)
    for c in range(2):
        P.emit("t", lambda t, c=c: t.transpose(
            cm.pbB[0:128, c:c + 1], qn[0:1, 128 * c:128 * (c + 1)],
            cm.identf[0:1, 0:1]), inc=cm.mm)
    P.wait("s", cm.mm, cm.mm.n)
    aq = P.emit("s", lambda s: s.copy(qn_cols[:, :], cm.pbB[0:128, 0:2]), inc=cm.act)
    P.wait("t", cm.act, aq)
    for c in range(2):
        P.emit("t", lambda t, c=c: t.transpose(
            cm.pbB[0:128, c:c + 1], kn[0:1, 128 * c:128 * (c + 1)],
            cm.identf[0:1, 0:1]), inc=cm.mm)
    P.wait("s", cm.mm, cm.mm.n)
    ak = P.emit("s", lambda s: s.copy(kn_cols[:, :], cm.pbB[0:128, 0:2]), inc=cm.act)

    mrow = P.sb("mrow", [1, 4], F32)
    negB = P.sb("negB", [128, 1], F32)
    P.waitall("v", sml)
    P.emit("v", lambda v: v.reduce_max(mrow[0:1, 0:1], qn[0:1, :], axis=AX,
                                       apply_absolute_value=True), inc=cm.dve)
    P.emit("v", lambda v: v.reduce_max(mrow[0:1, 1:2], kn[0:1, :], axis=AX,
                                       apply_absolute_value=True), inc=cm.dve)
    P.emit("v", lambda v: v.tensor_tensor(mrow[0:1, 2:3], mrow[0:1, 0:1],
                                          mrow[0:1, 1:2], ALU.mult), inc=cm.dve)
    e4 = P.emit("v", lambda v: v.tensor_scalar(mrow[0:1, 2:3], mrow[0:1, 2:3],
                                               -16.0, None, op0=ALU.mult), inc=cm.dve)
    P.wait("t", cm.dve, e4)
    P.wait("t", cm.act, ak)
    t1 = P.emit("t", lambda t: t.matmul(cm.pbB[0:128, 0:1], cm.ones1_128[:, :],
                                        mrow[0:1, 2:3], start=True, stop=True), inc=cm.mm)
    P.wait("s", cm.mm, t1)
    anb = P.emit("s", lambda s: s.copy(negB[:, :], cm.pbB[0:128, 0:1]), inc=cm.act)

    big1 = P.sb("big1", [128, 8192], F32)
    P.wait("t", cm.act, anb)
    cosT, sinT, rope_rdy = build_rope_tables(
        P, cm, 1000000.0, 256, 4096, c2_ap,
        posB_ap=big1[:, 0:4096], frq_ap=big1[:, 4096:8192])

    QsT = P.sb("QsT", [128, 16 * SL], BF16)
    wscb = QsT[:, 0:4096]
    sc_q, ss_q = w_to_scratch(P, cm, "q", qw_d, 2048, 2048, buf=wscb)
    sc_k, ss_k = w_to_scratch(P, cm, "k", kw_d, 256, 2048, buf=wscb, prev=ss_q)
    sc_v, ss_v = w_to_scratch(P, cm, "v", vw_d, 256, 2048, buf=wscb, prev=ss_k)
    sc_o, ss_o = w_to_scratch(P, cm, "o", ow_d, 2048, 2048, buf=wscb, prev=ss_v)

    s_kw = cm.newsem("y")
    s_vw = cm.newsem("y")
    kwT = P.sb("kwT", [128, 16 * 256], BF16)
    for i in range(16):
        load_wT(P, cm, sc_k, 128 * i, 0, 256, kwT[:, 256 * i:256 * (i + 1)],
                ss_k, ss_k.n, sem=s_kw)
    vwT = P.sb("vwT", [128, 16 * 256], BF16)
    for i in range(16):
        P.wait("y", ss_v, ss_v.n)
        P.emit("y", lambda y, i=i: y.dma_start(
            out=vwT[:, 256 * i:256 * (i + 1)], in_=sc_v[0:256, 128 * i:128 * (i + 1)],
            transpose=True), inc=s_vw, n=16)

    Kt = P.sb("Kt", [128, 2 * 4096], BF16)
    Vt = P.sb("Vt", [128, 32 * 256], BF16)
    xr = P.sb("xr", [128, 4 * 512], BF16)
    xr_cons = [0, 0, 0, 0]
    s_xr = [cm.newsem("g") for _ in range(4)]
    s_xv = [cm.newsem("g") for _ in range(4)]
    rstdB = P.sb("rstdB", [128, SL], F32)
    msrow = rstdB[0:1, 0:SL]
    k0n = P.sb("k0n", [128, SL], F32)
    k1n = P.sb("k1n", [128, SL], F32)
    ktmp = P.sb("ktmp", [128, SL], F32)
    sqb = P.sb("sqb", [128, 2 * SL], BF16)
    cm.pln_alias = (sqb[:, 0:SL], sqb[:, SL:2 * SL], k0n[:, 0:SL], k1n[:, 0:SL])

    P.wait("v", cm.act, rope_rdy)
    pbS_free = 0
    for ts in range(8):
        col = slice(512 * ts, 512 * (ts + 1))
        for i in range(16):
            slot = i % 4
            b = xr[:, 512 * slot:512 * (slot + 1)]
            if xr_cons[slot]:
                P.wait("g", cm.mm, xr_cons[slot])
            tgt = P.emit("g", lambda g, b=b, s=xctx[128 * i:128 * (i + 1), col]:
                         g.dma_start(out=b, in_=s), inc=s_xr[slot], n=16, dma=True)
            P.wait("t", s_xr[slot], tgt)
            if i == 0:
                P.waitall("t", s_kw)
                if pbS_free:
                    P.wait("t", cm.dve, pbS_free)
            m1 = P.emit("t", lambda t, b=b, i=i: t.matmul(
                pbS[0][0:128, 0:512], kwT[:, 256 * i:256 * i + 128], b,
                start=(i == 0), stop=(i == 15)), inc=cm.mm)
            m2 = P.emit("t", lambda t, b=b, i=i: t.matmul(
                pbS[1][0:128, 0:512], kwT[:, 256 * i + 128:256 * i + 256], b,
                start=(i == 0), stop=(i == 15)), inc=cm.mm)
            xr_cons[slot] = m2
        P.wait("s", cm.mm, m2)
        a1 = P.emit("s", lambda s: s.activation(out=sqb[:, 0:SL], in_=pbS[0][0:128, 0:512],
                                                func=AF.Square), inc=cm.act)
        a2 = P.emit("s", lambda s: s.activation(out=sqb[:, SL:2 * SL],
                                                in_=pbS[1][0:128, 0:512],
                                                func=AF.Square), inc=cm.act)
        P.wait("t", cm.act, a2)
        s1 = P.emit("t", lambda t: t.matmul(cm.pbR1[0:1, 0:SL], cm.ones128_1b[:, :],
                                            sqb[:, 0:SL], start=True, stop=False),
                    inc=cm.mm)
        s2 = P.emit("t", lambda t: t.matmul(cm.pbR1[0:1, 0:SL], cm.ones128_1b[:, :],
                                            sqb[:, SL:2 * SL], start=False, stop=True),
                    inc=cm.mm)
        P.wait("s", cm.mm, s2)
        if pbS_free:
            P.wait("s", cm.dve, pbS_free)
        a3 = P.emit("s", lambda s: s.copy(msrow[0:1, :], cm.pbR1[0:1, 0:SL]), inc=cm.act)
        P.wait("v", cm.act, a3)
        e1 = P.emit("v", lambda v: v.tensor_scalar(
            msrow[0:1, :], msrow[0:1, :], 1.0 / 256.0, 1e-6, op0=ALU.mult, op1=ALU.add),
            inc=cm.dve)
        P.wait("s", cm.dve, e1)
        a4 = P.emit("s", lambda s: s.activation(out=msrow[0:1, :], in_=msrow[0:1, :],
                                                func=AF.Sqrt), inc=cm.act)
        P.wait("v", cm.act, a4)
        e2 = P.emit("v", lambda v: v.reciprocal(msrow[0:1, :], msrow[0:1, :]), inc=cm.dve)
        P.wait("t", cm.dve, e2)
        _, ab = bcast_row(P, cm, msrow[0:1, :], cm.pbB, rstdB, SL)
        P.wait("v", cm.act, ab)
        e3 = P.emit("v", lambda v: v.scalar_tensor_tensor(
            k0n[:, :], pbS[0][0:128, 0:512], kn_cols[0:128, 0:1], rstdB[:, :],
            op0=ALU.mult, op1=ALU.mult), inc=cm.dve)
        e4k = P.emit("v", lambda v: v.scalar_tensor_tensor(
            k1n[:, :], pbS[1][0:128, 0:512], kn_cols[0:128, 1:2], rstdB[:, :],
            op0=ALU.mult, op1=ALU.mult), inc=cm.dve)
        pbS_free = e4k
        rope_apply(P, cm, k0n[:, :], k1n[:, :], cosT[:, col], sinT[:, col],
                   Kt[:, 512 * ts:512 * (ts + 1)],
                   Kt[:, 4096 + 512 * ts:4096 + 512 * (ts + 1)], ktmp, SL)
    k_done_dve = cm.dve.n
    if dbg:
        P.wait("y", cm.dve, k_done_dve)
        P.emit("y", lambda y: y.dma_start(out=d_kt[:, :], in_=Kt[:, 0:8192]), inc=cm.st, n=16)
        P.wait("y", cm.act, rope_rdy)
        P.emit("y", lambda y: y.dma_start(out=d_rope[:, 0:4096], in_=cosT[:, :]), inc=cm.st, n=16)
        P.emit("y", lambda y: y.dma_start(out=d_rope[:, 4096:8192], in_=sinT[:, :]), inc=cm.st, n=16)

    xv = P.sb("xv", [128, 4 * 128], BF16)
    xv_cons = [0, 0, 0, 0]
    va_prev = {}
    for kt in range(32):
        bank = pbO[kt % 2]
        for i in range(16):
            slot = i % 4
            b = xv[:, 128 * slot:128 * (slot + 1)]
            if xv_cons[slot]:
                P.wait("g", cm.mm, xv_cons[slot])
            tgt = P.emit("g", lambda g, b=b,
                         s=xctx[128 * i:128 * (i + 1), 128 * kt:128 * (kt + 1)]:
                         g.dma_start(out=b, in_=s), inc=s_xv[slot], n=16, dma=True)
            P.wait("t", s_xv[slot], tgt)
            if kt == 0 and i == 0:
                P.waitall("t", s_vw)
            if i == 0 and kt >= 2:
                P.wait("t", cm.act, va_prev[kt % 2])
            m1 = P.emit("t", lambda t, b=b, i=i, bank=bank: t.matmul(
                bank[0:128, 0:256], b, vwT[:, 256 * i:256 * (i + 1)],
                start=(i == 0), stop=(i == 15)), inc=cm.mm)
            xv_cons[slot] = m1
        P.wait("s", cm.mm, m1)
        va = P.emit("s", lambda s, kt=kt, bank=bank: s.copy(
            Vt[:, 256 * kt:256 * (kt + 1)], bank[0:128, 0:256]), inc=cm.act)
        va_prev[kt % 2] = va
    v_done = va
    if dbg:
        P.wait("y", cm.act, v_done)
        P.emit("y", lambda y: y.dma_start(out=d_vt[:, :], in_=Vt[:, 0:8192]), inc=cm.st, n=16)

    Ed = P.sb("Ed", [128, 4 * SL], BF16)
    for d in range(4):
        P.emit("v", lambda v, d=d: v.tensor_scalar(
            Ed[:, SL * d:SL * (d + 1)], cm.iotaPF[:, :], float(128 * d), 0.0,
            op0=ALU.add, op1=ALU.is_le), inc=cm.dve)
    ed_done = cm.dve.n

    qwT = P.sb("qwT", [128, 16 * 512], BF16)
    xq = P.sb("xq", [128, 16 * 512], BF16)
    Qs = QsT
    Ctx = xq
    Pt = P.sb("Pt", [128, 2 * SL], BF16)
    bvs = P.sb("bvs", [128, 36], F32)
    rbS = rstdB
    sumr = rstdB[0:1, 0:SL]
    owT = qwT
    pln = PostLN(P, cm, "l0a", Rs_ap=big1[:, 0:8192])

    q0n, q1n = k0n, k1n
    exp_t = [0, 0]
    pv_t = [0, 0]
    pmask_t = [0, 0]
    sum_evac = 0
    qwT_free_mm = 0
    oproj_last_mm = 0
    for s_ in range(NSL):
        ts4 = 16 + 4 * s_
        T_s = ts4 + 4
        colq = slice(2048 + SL * s_, 2048 + SL * (s_ + 1))
        P.wait("g", cm.mm, cm.mm.n)
        if s_ == 0:
            s_xq = cm.newsem("g")
            s_qw = cm.newsem("y")
            s_res = cm.newsem("y")
            P.wait("y", ss_q, ss_q.n)
            P.wait("v", ss_o, ss_o.n)
            P.wait("s", ss_o, ss_o.n)
        for i in range(16):
            P.emit("g", lambda g, i=i, s=xctx[128 * i:128 * (i + 1), colq]:
                   g.dma_start(out=xq[:, 512 * i:512 * (i + 1)], in_=s), inc=s_xq, n=16, dma=True)
        q_bank_free = k_done_dve if s_ == 0 else 0
        for oblk in range(4):
            if s_ == 0 and oblk == 0:
                P.wait("t", cm.act, v_done)
            if qwT_free_mm:
                P.wait("y", cm.mm, qwT_free_mm)
            for i in range(16):
                P.emit("y", lambda y, i=i, oblk=oblk: y.dma_start(
                    out=qwT[:, 512 * i:512 * (i + 1)],
                    in_=sc_q[512 * oblk:512 * (oblk + 1), 128 * i:128 * (i + 1)],
                    transpose=True), inc=s_qw, n=16)
            P.waitall("t", s_qw)
            if oblk == 0:
                P.waitall("t", s_xq)
            if q_bank_free:
                P.wait("t", cm.dve, q_bank_free)
            for i in range(16):
                for oc in range(4):
                    bank = pb[4 + oc]
                    m1 = P.emit("t", lambda t, i=i, oc=oc, bank=bank: t.matmul(
                        bank[0:128, 0:512], qwT[:, 512 * i + 128 * oc:512 * i + 128 * (oc + 1)],
                        xq[:, 512 * i:512 * (i + 1)],
                        start=(i == 0), stop=(i == 15)), inc=cm.mm)
            qwT_free_mm = m1
            for hh in range(2):
                head = oblk * 2 + hh
                c0 = pb[4 + 2 * hh]
                c1 = pb[4 + 2 * hh + 1]
                P.wait("s", cm.mm, m1)
                a1 = P.emit("s", lambda s, c0=c0: s.activation(
                    out=sqb[:, 0:SL], in_=c0[0:128, 0:512], func=AF.Square), inc=cm.act)
                a2 = P.emit("s", lambda s, c1=c1: s.activation(
                    out=sqb[:, SL:2 * SL], in_=c1[0:128, 0:512], func=AF.Square),
                    inc=cm.act)
                P.wait("t", cm.act, a2)
                s1 = P.emit("t", lambda t: t.matmul(
                    cm.pbR1[0:1, 0:SL], cm.ones128_1b[:, :], sqb[:, 0:SL],
                    start=True, stop=False), inc=cm.mm)
                s2 = P.emit("t", lambda t: t.matmul(
                    cm.pbR1[0:1, 0:SL], cm.ones128_1b[:, :], sqb[:, SL:2 * SL],
                    start=False, stop=True), inc=cm.mm)
                P.wait("s", cm.mm, s2)
                P.wait("s", cm.dve, cm.dve.n)
                a3 = P.emit("s", lambda s: s.copy(msrow[0:1, :], cm.pbR1[0:1, 0:SL]),
                            inc=cm.act)
                P.wait("v", cm.act, a3)
                e1 = P.emit("v", lambda v: v.tensor_scalar(
                    msrow[0:1, :], msrow[0:1, :], 1.0 / 256.0, 1e-6,
                    op0=ALU.mult, op1=ALU.add), inc=cm.dve)
                P.wait("s", cm.dve, e1)
                a4 = P.emit("s", lambda s: s.activation(
                    out=msrow[0:1, :], in_=msrow[0:1, :], func=AF.Sqrt), inc=cm.act)
                P.wait("v", cm.act, a4)
                e2 = P.emit("v", lambda v: v.reciprocal(msrow[0:1, :], msrow[0:1, :]),
                            inc=cm.dve)
                P.wait("t", cm.dve, e2)
                _, ab = bcast_row(P, cm, msrow[0:1, :], cm.pbB, rstdB, SL)
                P.wait("v", cm.act, ab)
                e3 = P.emit("v", lambda v, c0=c0: v.scalar_tensor_tensor(
                    q0n[:, :], c0[0:128, 0:512], qn_cols[0:128, 0:1],
                    rstdB[:, :], op0=ALU.mult, op1=ALU.mult), inc=cm.dve)
                e4q = P.emit("v", lambda v, c1=c1: v.scalar_tensor_tensor(
                    q1n[:, :], c1[0:128, 0:512], qn_cols[0:128, 1:2],
                    rstdB[:, :], op0=ALU.mult, op1=ALU.mult), inc=cm.dve)
                rope_apply(P, cm, q0n[:, :], q1n[:, :], cosT[:, colq], sinT[:, colq],
                           Qs[:, SL * (2 * head):SL * (2 * head + 1)],
                           Qs[:, SL * (2 * head + 1):SL * (2 * head + 2)], ktmp, SL)
                q_bank_free = e4q
        q_rope_done = cm.dve.n

        P.wait("v", cm.act, max(anb, acc0))
        for kt in range(T_s):
            P.emit("v", lambda v, kt=kt: v.tensor_scalar(
                bvs[:, kt:kt + 1], cm.iotaP[:, :], float(128 * kt), cc_cols[0:128, 0:1],
                op0=ALU.add, op1=ALU.is_lt), inc=cm.dve)
            P.emit("v", lambda v, kt=kt: v.scalar_tensor_tensor(
                bvs[:, kt:kt + 1], bvs[:, kt:kt + 1], NEG, negB[:, :],
                op0=ALU.mult, op1=ALU.add), inc=cm.dve)
        bv_done = cm.dve.n

        if dbg and s_ == 0:
            P.wait("y", cm.dve, q_rope_done)
            P.emit("y", lambda y: y.dma_start(out=d_qs[:, :], in_=Qs[:, 0:8192]), inc=cm.st, n=16)
            P.wait("y", cm.dve, bv_done)
            P.emit("y", lambda y: y.dma_start(out=d_misc[:, 0:20], in_=bvs[:, 0:20]), inc=cm.st, n=16)
            P.emit("y", lambda y: y.dma_start(out=d_misc[:, 36:37], in_=negB[:, :], allow_slow_non_contiguous=True), inc=cm.st, n=16)
        P.wait("t", cm.dve, max(k_done_dve, ed_done, q_rope_done))
        P.wait("t", cm.act, v_done)
        for h in range(8):
            if sum_evac:
                P.wait("t", cm.act, sum_evac)
            for kt in range(T_s):
                par = kt % 2
                bankS = pbS[par]
                if exp_t[par]:
                    P.wait("t", cm.act, exp_t[par])
                if pv_t[par]:
                    P.wait("t", cm.mm, pv_t[par])
                m1 = P.emit("t", lambda t, h=h, kt=kt, bankS=bankS: t.matmul(
                    bankS[0:128, 0:512], Kt[:, 128 * kt:128 * (kt + 1)],
                    Qs[:, SL * 2 * h:SL * (2 * h + 1)], start=True, stop=False),
                    inc=cm.mm)
                m2 = P.emit("t", lambda t, h=h, kt=kt, bankS=bankS: t.matmul(
                    bankS[0:128, 0:512], Kt[:, 4096 + 128 * kt:4096 + 128 * (kt + 1)],
                    Qs[:, SL * (2 * h + 1):SL * (2 * h + 2)], start=False, stop=True),
                    inc=cm.mm)
                P.wait("s", cm.mm, m2)
                P.wait("s", cm.dve, bv_done)
                ptile = Pt[:, SL * par:SL * (par + 1)]
                if pmask_t[par]:
                    P.wait("s", cm.dve, pmask_t[par])
                ex = P.emit("s", lambda s, bankS=bankS, ptile=ptile, kt=kt: s.activation(
                    out=ptile, in_=bankS[0:128, 0:512], func=AF.Exp,
                    bias=bvs[0:128, kt:kt + 1], scale=1.0 / 16.0), inc=cm.act)
                exp_t[par] = ex
                dlt = kt - ts4
                if dlt >= 0:
                    P.wait("v", cm.act, ex)
                    pm = P.emit("v", lambda v, ptile=ptile, dlt=dlt: v.tensor_tensor(
                        ptile, ptile, Ed[:, SL * dlt:SL * (dlt + 1)], ALU.mult),
                        inc=cm.dve)
                    pmask_t[par] = pm
                    P.wait("t", cm.dve, pm)
                else:
                    P.wait("t", cm.act, ex)
                st_ = (kt == T_s - 1)
                p1 = P.emit("t", lambda t, kt=kt, ptile=ptile, st_=st_: t.matmul(
                    pbO[0][0:128, 0:512], Vt[:, 256 * kt:256 * kt + 128], ptile,
                    start=(kt == 0), stop=st_), inc=cm.mm)
                p2 = P.emit("t", lambda t, kt=kt, ptile=ptile, st_=st_: t.matmul(
                    pbO[1][0:128, 0:512], Vt[:, 256 * kt + 128:256 * kt + 256], ptile,
                    start=(kt == 0), stop=st_), inc=cm.mm)
                p3 = P.emit("t", lambda t, kt=kt, ptile=ptile, st_=st_: t.matmul(
                    cm.pbB2[0:1, 0:512], cm.ones128_1b[:, :], ptile,
                    start=(kt == 0), stop=st_), inc=cm.mm)
                pv_t[par] = p3
            P.wait("s", cm.mm, p3)
            sum_evac = P.emit("s", lambda s: s.copy(sumr[0:1, :], cm.pbB2[0:1, 0:512]),
                              inc=cm.act)
            P.wait("v", cm.act, sum_evac)
            e1 = P.emit("v", lambda v: v.reciprocal(sumr[0:1, :], sumr[0:1, :]),
                        inc=cm.dve)
            P.wait("t", cm.dve, e1)
            _, ab = bcast_row(P, cm, sumr[0:1, :], cm.pbB, rbS, SL)
            P.wait("v", cm.act, ab)
            ec0 = P.emit("v", lambda v, h=h: v.tensor_tensor(
                Ctx[:, SL * 2 * h:SL * (2 * h + 1)], pbO[0][0:128, 0:512], rbS[:, :],
                ALU.mult), inc=cm.dve)
            ec1 = P.emit("v", lambda v, h=h: v.tensor_tensor(
                Ctx[:, SL * (2 * h + 1):SL * (2 * h + 2)], pbO[1][0:128, 0:512],
                rbS[:, :], ALU.mult), inc=cm.dve)
            P.wait("t", cm.dve, ec1)

        if dbg and s_ == 0:
            P.wait("y", cm.dve, cm.dve.n)
            P.emit("y", lambda y: y.dma_start(out=d_ctx[:, :], in_=Ctx[:, 0:8192]), inc=cm.st, n=16)
        P.wait("v", cm.act, rope_rdy)
        pln.begin()
        res_cons = [0, 0]
        for oblk in range(4):
            if s_ == 0 and oblk == 0:
                P.wait("y", ss_o, ss_o.n)
            P.wait("y", cm.mm, max(oproj_last_mm, qwT_free_mm))
            for i in range(16):
                P.emit("y", lambda y, i=i, oblk=oblk: y.dma_start(
                    out=owT[:, 512 * i:512 * (i + 1)],
                    in_=sc_o[512 * oblk:512 * (oblk + 1), 128 * i:128 * (i + 1)],
                    transpose=True), inc=s_qw, n=16)
            P.waitall("t", s_qw)
            for i in range(16):
                for j in range(4):
                    bank = pb[4 + j]
                    if i == 0:
                        pass
                    m1 = P.emit("t", lambda t, i=i, j=j, oblk=oblk, bank=bank: t.matmul(
                        bank[0:128, 0:512],
                        owT[:, 512 * i + 128 * j:512 * i + 128 * (j + 1)],
                        Ctx[:, 512 * i:512 * (i + 1)],
                        start=(i == 0), stop=(i == 15)), inc=cm.mm)
            oproj_last_mm = m1
            qwT_free_mm = m1
            for j in range(4):
                oc = oblk * 4 + j
                bank = pb[4 + j]
                if res_cons[0]:
                    P.wait("y", cm.dve, res_cons[0])
                rtgt = P.emit("y", lambda y, oc=oc, colq=colq: y.dma_start(
                    out=pln.Rs[:, SL * oc:SL * (oc + 1)],
                    in_=xctx[128 * oc:128 * (oc + 1), colq]), inc=s_res, n=16)
                P.wait("v", cm.mm, oproj_last_mm)
                P.waitall("v", s_res)
                ee = pln.add_chunk(oc, bank[0:128, 0:512])
                res_cons[0] = ee
            P.wait("t", cm.dve, ee)
        pln.finalize(dst_dram_ap=out_d[:, SL * s_:SL * (s_ + 1)])

    P.wait("y", cm.st, cm.st.n)


def build_l0_attn(dbg=False):
    nc = bass.Bass("TRN2", target_bir_lowering=False)
    io = {
        "xctx": _ei(nc, "xctx", [DM, 4096]),
        "qw": _ei(nc, "qw", [2048, 2048]), "kw": _ei(nc, "kw", [256, 2048]),
        "vw": _ei(nc, "vw", [256, 2048]), "ow": _ei(nc, "ow", [2048, 2048]),
        "qn": _ei(nc, "qn", [1, 256]), "kn": _ei(nc, "kn", [1, 256]),
        "cc": _ei(nc, "cc", [1, 8]),
        "out": _eo(nc, "out", [DM, TOK]),
    }
    if dbg:
        for nm in ("d_rope", "d_kt", "d_vt", "d_qs", "d_ctx"):
            io[nm] = _eo(nc, nm, [128, 8192], BF16)
        io["d_misc"] = _eo(nc, "d_misc", [128, 64])
    return _single(nc, body_l0_attn, io, dbg=dbg)


def quant_scales(P, cm, acc, W, sB, invsB, srow, cols, eps=1e-5):
    nch = W // 128
    elast = part_reduce_max(P, cm, acc, W, cols[:, 0:nch], cm.pbB2, j0=0)
    e1 = P.emit("v", lambda v: v.tensor_scalar(
        cols[:, 0:nch], cols[:, 0:nch], float(eps), None, op0=ALU.max), inc=cm.dve)
    e2 = P.emit("v", lambda v: v.tensor_scalar(
        cols[:, nch:2 * nch], cols[:, 0:nch], 1.0 / 127.0, None, op0=ALU.mult),
        inc=cm.dve)
    e3 = P.emit("v", lambda v: v.reciprocal(cols[:, 0:nch], cols[:, 0:nch]), inc=cm.dve)
    e4 = P.emit("v", lambda v: v.tensor_scalar(
        cols[:, 0:nch], cols[:, 0:nch], 127.0, None, op0=ALU.mult), inc=cm.dve)
    P.wait("t", cm.dve, e4)
    P.wait("t", cm.act, cm.act.n)
    tl = None
    for j in range(nch):
        tl = P.emit("t", lambda t, j=j: t.transpose(
            cm.pbB2[0:1, 128 * j:128 * (j + 1)],
            cols[:, j:j + 1], cm.identf[0:128, 0:128]), inc=cm.mm)
        tl = P.emit("t", lambda t, j=j: t.transpose(
            cm.pbB[0:1, 128 * j:128 * (j + 1)],
            cols[:, nch + j:nch + j + 1], cm.identf[0:128, 0:128]), inc=cm.mm)
    P.wait("s", cm.mm, tl)
    a1 = P.emit("s", lambda s: s.copy(srow[0:1, 0:W], cm.pbB2[0:1, 0:W]),
                inc=cm.act)
    a1 = P.emit("s", lambda s: s.copy(srow[0:1, W:2 * W], cm.pbB[0:1, 0:W]),
                inc=cm.act)
    P.wait("t", cm.act, a1)
    bcast_row(P, cm, srow[0:1, 0:W], cm.pbB, sB, W)
    _, ab = bcast_row(P, cm, srow[0:1, W:2 * W], cm.pbB, invsB, W)
    return ab



def body_l0_ffn(P, cm, io):
    xin, w1_d, w2_d, out_d = io["xin"], io["w1"], io["w2"], io["out"]
    pb = [P.ps(f"pb{i}", [128, 512], F32) for i in range(8)]
    cm.pbR1, cm.pbR2, cm.pbB, cm.pbB2 = pb[0], pb[1], pb[2], pb[3]
    cm.pb_misc = pb[2]

    W = 512
    NSLF = TOK // W

    buf = P.sb("tern_b", [128, 2 * 2048], BF16)
    f32buf = P.sb("tern_f", [128, 2 * 2048], F32)
    sc1, ss1, oscale1, rdy1 = w_to_scratch_ternary(
        P, cm, "w1", w1_d, 8192, 2048, 1e-5, "wq", buf, f32buf, cm.pbR1, cm.pbB2)
    sc2, ss2, oscale2, rdy2 = w_to_scratch_ternary(
        P, cm, "w2", w2_d, 2048, 8192, 1e-5, "wq", buf, f32buf, cm.pbR1, cm.pbB2,
        prev_ssem=ss1)

    H = P.sb("Hf", [128, 64 * W], BF16)
    tq = P.sb("tq", [128, W], F32)
    XQ = P.sb("XQ", [128, 16 * W], BF16)
    wring = P.sb("wring", [128, 8 * 512], BF16)
    acc = P.sb("acc", [128, W], F32)
    acc2 = P.sb("acc2", [128, W], F32)
    accH = P.sb("accH", [128, W], F32)
    accH2 = P.sb("accH2", [128, W], F32)
    sB = P.sb("sB", [128, W], F32)
    invsB = P.sb("invsB", [128, W], F32)
    srow = P.sb("srow", [1, 2 * W], F32)
    cols = P.sb("colsq", [128, 8], F32)
    pln = PostLN(P, cm, "l0f", W=W)
    ring_sems = [cm.newsem("y") for _ in range(8)]
    ring_cons = [0] * 8

    P.waitall("y", ss1)
    P.waitall("y", ss2)
    P.wait("v", cm.act, max(rdy1, rdy2))
    P.wait("s", cm.act, max(rdy1, rdy2))
    res_prev = 0
    s_res = cm.newsem("y")
    for s_ in range(NSLF):
        colt = slice(W * s_, W * (s_ + 1))
        pln.begin()
        for i in range(16):
            if res_prev:
                P.wait("y", cm.dve, res_prev)
            rt = P.emit("y", lambda y, i=i, colt=colt: y.dma_start(
                out=pln.Rs[:, W * i:W * (i + 1)], in_=xin[128 * i:128 * (i + 1), colt]),
                inc=s_res, n=16)
            P.waitall("v", s_res)
            if i == 0:
                e = amax_start(P, cm, acc[:, :], acc2[:, :], pln.Rs[:, 0:W])
            else:
                e = amax_step(P, cm, acc[:, :], acc2[:, :],
                              pln.Rs[:, W * i:W * (i + 1)])
            res_prev = e
        amax_finish(P, cm, acc[:, :], acc2[:, :])
        qs = quant_scales(P, cm, acc, W, sB, invsB, srow, cols)
        P.wait("v", cm.act, qs)
        for i in range(16):
            e1 = P.emit("v", lambda v, i=i: v.tensor_tensor(
                XQ[:, W * i:W * (i + 1)], pln.Rs[:, W * i:W * (i + 1)], sB[:, :],
                ALU.mult), inc=cm.dve)
            e2 = P.emit("v", lambda v, i=i: v.tensor_scalar(
                XQ[:, W * i:W * (i + 1)], XQ[:, W * i:W * (i + 1)], RNE, RNE,
                op0=ALU.add, op1=ALU.subtract), inc=cm.dve)
            e3 = P.emit("v", lambda v, i=i: v.tensor_tensor(
                XQ[:, W * i:W * (i + 1)], XQ[:, W * i:W * (i + 1)], invsB[:, :],
                ALU.mult), inc=cm.dve)
        xq_done = e3
        P.wait("t", cm.dve, xq_done)
        hacc_first = True
        bank_free_dve = 0
        bank_free_act = 0
        for hb in range(16):
            for i in range(16):
                slot = (hb * 16 + i) % 8
                if ring_cons[slot]:
                    P.wait("y", cm.mm, ring_cons[slot])
                lt = P.emit("y", lambda y, hb=hb, i=i, slot=slot: y.dma_start(
                    out=wring[:, 512 * slot:512 * (slot + 1)],
                    in_=sc1[512 * hb:512 * (hb + 1), 128 * i:128 * (i + 1)],
                    transpose=True), inc=ring_sems[slot], n=16)
                P.wait("t", ring_sems[slot], lt)
                if i == 0 and bank_free_act:
                    P.wait("t", cm.act, bank_free_act)
                    P.wait("t", cm.dve, bank_free_dve)
                for oc in range(4):
                    m1 = P.emit("t", lambda t, slot=slot, oc=oc, i=i: t.matmul(
                        pb[4 + oc][0:128, 0:W],
                        wring[:, 512 * slot + 128 * oc:512 * slot + 128 * (oc + 1)],
                        XQ[:, W * i:W * (i + 1)],
                        start=(i == 0), stop=(i == 15)), inc=cm.mm)
                ring_cons[slot] = m1
            P.wait("s", cm.mm, m1)
            for oc in range(4):
                ch = hb * 4 + oc
                a1 = P.emit("s", lambda s, oc=oc, ch=ch: s.activation(
                    out=H[:, W * ch:W * (ch + 1)], in_=pb[4 + oc][0:128, 0:W],
                    func=AF.Gelu_apprx_tanh, scale=oscale1), inc=cm.act)
                P.wait("v", cm.act, a1)
                if hacc_first:
                    e = amax_start(P, cm, accH[:, :], accH2[:, :],
                                   H[:, W * ch:W * (ch + 1)])
                    hacc_first = False
                else:
                    e = amax_step(P, cm, accH[:, :], accH2[:, :],
                                  H[:, W * ch:W * (ch + 1)])
            bank_free_act = a1
            bank_free_dve = e
        amax_finish(P, cm, accH[:, :], accH2[:, :])
        qs2 = quant_scales(P, cm, accH, W, sB, invsB, srow, cols)
        P.wait("v", cm.act, qs2)
        for ch in range(64):
            e1 = P.emit("v", lambda v, ch=ch: v.tensor_tensor(
                tq[:, :], H[:, W * ch:W * (ch + 1)], sB[:, :],
                ALU.mult), inc=cm.dve)
            e2 = P.emit("v", lambda v, ch=ch: v.tensor_scalar(
                tq[:, :], tq[:, :], RNE, RNE,
                op0=ALU.add, op1=ALU.subtract), inc=cm.dve)
            e3 = P.emit("v", lambda v, ch=ch: v.tensor_tensor(
                H[:, W * ch:W * (ch + 1)], tq[:, :], invsB[:, :],
                ALU.mult), inc=cm.dve)
        hq_done = e3
        P.wait("t", cm.dve, hq_done)
        pln.scale_col = oscale2
        oproj_mm = 0
        for ob in range(4):
            for i in range(64):
                slot = i % 8
                if ring_cons[slot]:
                    P.wait("y", cm.mm, ring_cons[slot])
                lt = P.emit("y", lambda y, ob=ob, i=i, slot=slot: y.dma_start(
                    out=wring[:, 512 * slot:512 * (slot + 1)],
                    in_=sc2[512 * ob:512 * (ob + 1), 128 * i:128 * (i + 1)],
                    transpose=True), inc=ring_sems[slot], n=16)
                P.wait("t", ring_sems[slot], lt)
                if i == 0 and ob == 0:
                    pass
                for oc in range(4):
                    m1 = P.emit("t", lambda t, slot=slot, oc=oc, i=i: t.matmul(
                        pb[4 + oc][0:128, 0:W],
                        wring[:, 512 * slot + 128 * oc:512 * slot + 128 * (oc + 1)],
                        H[:, W * i:W * (i + 1)],
                        start=(i == 0), stop=(i == 63)), inc=cm.mm)
                ring_cons[slot] = m1
            oproj_mm = m1
            for oc in range(4):
                ch = ob * 4 + oc
                P.wait("v", cm.mm, oproj_mm)
                ee = pln.add_chunk(ch, pb[4 + oc][0:128, 0:W])
            P.wait("t", cm.dve, ee)
        pln.finalize(dst_dram_ap=out_d[:, W * s_:W * (s_ + 1)])
        res_prev = ee
    P.wait("y", cm.st, cm.st.n)


def build_l0_ffn():
    nc = bass.Bass("TRN2", target_bir_lowering=False)
    io = {
        "xin": _ei(nc, "xin", [DM, TOK]),
        "w1": _ei(nc, "w1", [8192, 2048]), "w2": _ei(nc, "w2", [2048, 8192]),
        "out": _eo(nc, "out", [DM, TOK]),
    }
    return _single(nc, body_l0_ffn, io)



def body_l1_attn(P, cm, io):
    xbuf, qw_d, kw_d, vw_d, ow_d, cc_d, out_d = (
        io["xbuf"], io["qw"], io["kw"], io["vw"], io["ow"], io["cc"], io["out"])
    pb = [P.ps(f"pb{i}", [128, 512], F32) for i in range(8)]
    cm.pbR1, cm.pbR2, cm.pbB, cm.pbB2 = pb[0], pb[1], pb[2], pb[3]
    cm.pb_misc = pb[2]
    pbS = [pb[4], pb[5]]
    pbO = [pb[6], pb[7]]
    ident_dve = cm.dve.n

    ccr = P.sb("cc_r", [1, 8], F32)
    sml = cm.newsem("y")
    P.emit("y", lambda y: y.dma_start(out=ccr[:, :], in_=cc_d[:, :]), inc=sml, n=16)
    cc_cols = P.sb("cc_cols", [128, 8], F32)
    P.waitall("t", sml)
    P.wait("t", cm.dve, ident_dve)
    t1 = P.emit("t", lambda t: t.matmul(cm.pbB[0:128, 0:8], cm.ones1_128[:, :],
                                        ccr[0:1, 0:8], start=True, stop=True), inc=cm.mm)
    P.wait("s", cm.mm, t1)
    acc0 = P.emit("s", lambda s: s.copy(cc_cols[:, :], cm.pbB[0:128, 0:8]), inc=cm.act)
    negB = cm.const_col(-math.sqrt(128.0))

    big1 = P.sb("big1", [128, 8192], F32)
    P.wait("t", cm.act, acc0)
    P.waitall("v", sml)
    cosT, sinT, rope_rdy = build_rope_tables(
        P, cm, 10000.0, 128, 3072, ccr[0:1, 1:2],
        posB_ap=big1[:, 0:3072], frq_ap=big1[:, 3072:6144])

    QsT = P.sb("QsT", [128, 16 * SL], BF16)
    wscb = QsT[:, 0:4096]
    sc_q, ss_q = w_to_scratch(P, cm, "q", qw_d, 2048, 2048, buf=wscb)
    sc_k, ss_k = w_to_scratch(P, cm, "k", kw_d, 512, 2048, buf=wscb, prev=ss_q)
    sc_v, ss_v = w_to_scratch(P, cm, "v", vw_d, 512, 2048, buf=wscb, prev=ss_k)
    sc_o, ss_o = w_to_scratch(P, cm, "o", ow_d, 2048, 2048, buf=wscb, prev=ss_v)

    s_kw = cm.newsem("y")
    s_vw = cm.newsem("y")
    kwT = P.sb("kwT", [128, 16 * 512], BF16)
    for i in range(16):
        load_wT(P, cm, sc_k, 128 * i, 0, 512, kwT[:, 512 * i:512 * (i + 1)],
                ss_k, ss_k.n, sem=s_kw)
    vwT = P.sb("vwT", [128, 16 * 512], BF16)
    for i in range(16):
        P.wait("y", ss_v, ss_v.n)
        P.emit("y", lambda y, i=i: y.dma_start(
            out=vwT[:, 512 * i:512 * (i + 1)], in_=sc_v[0:512, 128 * i:128 * (i + 1)],
            transpose=True), inc=s_vw, n=16)

    NKT = 24
    Kt = P.sb("Kt", [128, 4 * 3072], BF16)
    Vt = P.sb("Vt", [128, NKT * 512], BF16)
    xr = P.sb("xr", [128, 4 * 512], BF16)
    xr_cons = [0, 0, 0, 0]
    s_xr = [cm.newsem("g") for _ in range(4)]
    s_xv = [cm.newsem("g") for _ in range(4)]
    rstdB = P.sb("rstdB", [128, SL], F32)
    msrow = rstdB[0:1, 0:SL]
    k0n = P.sb("k0n", [128, SL], F32)
    k1n = P.sb("k1n", [128, SL], F32)
    ktmp = P.sb("ktmp", [128, SL], F32)
    ktmp2 = P.sb("ktmp2", [64, SL], F32)
    sqb = P.sb("sqb", [128, 2 * SL], BF16)
    cm.pln_alias = (sqb[:, 0:SL], sqb[:, SL:2 * SL], k0n[:, 0:SL], k1n[:, 0:SL])

    P.wait("v", cm.act, rope_rdy)
    pbS_free = 0
    for ts in range(6):
        col = slice(512 * ts, 512 * (ts + 1))
        for g in range(2):
            for i in range(16):
                slot = i % 4
                b = xr[:, 512 * slot:512 * (slot + 1)]
                if xr_cons[slot]:
                    P.wait("g", cm.mm, xr_cons[slot])
                tgt = P.emit("g", lambda g_, b=b, s=xbuf[128 * i:128 * (i + 1), col]:
                             g_.dma_start(out=b, in_=s), inc=s_xr[slot], n=16, dma=True)
                P.wait("t", s_xr[slot], tgt)
                if i == 0:
                    P.waitall("t", s_kw)
                    if pbS_free:
                        P.wait("t", cm.dve, pbS_free)
                m1 = P.emit("t", lambda t, b=b, i=i, g=g: t.matmul(
                    pbS[0][0:128, 0:512], kwT[:, 512 * i + 256 * g:512 * i + 256 * g + 128],
                    b, start=(i == 0), stop=(i == 15)), inc=cm.mm)
                m2 = P.emit("t", lambda t, b=b, i=i, g=g: t.matmul(
                    pbS[1][0:128, 0:512],
                    kwT[:, 512 * i + 256 * g + 128:512 * i + 256 * (g + 1)],
                    b, start=(i == 0), stop=(i == 15)), inc=cm.mm)
                xr_cons[slot] = m2
            for hh in range(2):
                kvh = 2 * g + hh
                bank = pbS[hh]
                P.wait("s", cm.mm, m2)
                if pbS_free:
                    P.wait("s", cm.dve, pbS_free)
                a1 = P.emit("s", lambda s, bank=bank: s.activation(
                    out=sqb[:, 0:SL], in_=bank[0:128, 0:512], func=AF.Square),
                    inc=cm.act)
                P.wait("t", cm.act, a1)
                s1 = P.emit("t", lambda t: t.matmul(
                    cm.pbR1[0:1, 0:SL], cm.ones128_1b[:, :], sqb[:, 0:SL],
                    start=True, stop=True), inc=cm.mm)
                P.wait("s", cm.mm, s1)
                a3 = P.emit("s", lambda s: s.copy(msrow[0:1, :], cm.pbR1[0:1, 0:SL]),
                            inc=cm.act)
                P.wait("v", cm.act, a3)
                e1 = P.emit("v", lambda v: v.tensor_scalar(
                    msrow[0:1, :], msrow[0:1, :], 1.0 / 128.0, 1e-6,
                    op0=ALU.mult, op1=ALU.add), inc=cm.dve)
                P.wait("s", cm.dve, e1)
                a4 = P.emit("s", lambda s: s.activation(
                    out=msrow[0:1, :], in_=msrow[0:1, :], func=AF.Sqrt), inc=cm.act)
                P.wait("v", cm.act, a4)
                e2 = P.emit("v", lambda v: v.reciprocal(msrow[0:1, :], msrow[0:1, :]),
                            inc=cm.dve)
                P.wait("t", cm.dve, e2)
                _, ab = bcast_row(P, cm, msrow[0:1, :], cm.pbB, rstdB, SL)
                P.wait("v", cm.act, ab)
                e3 = P.emit("v", lambda v, bank=bank: v.tensor_tensor(
                    k0n[:, :], bank[0:128, 0:512], rstdB[:, :], ALU.mult), inc=cm.dve)
                pbS_free = e3
                rope_apply(P, cm, k0n[0:64, :], k0n[64:128, :],
                           cosT[0:64, col], sinT[0:64, col],
                           Kt[0:64, 3072 * kvh + 512 * ts:3072 * kvh + 512 * (ts + 1)],
                           Kt[64:128, 3072 * kvh + 512 * ts:3072 * kvh + 512 * (ts + 1)],
                           ktmp, SL, half=64, tmp2=ktmp2, ps_shift=cm.pbB)
                pbS_free = cm.dve.n
    k_done_dve = cm.dve.n

    xv = P.sb("xv", [128, 4 * 128], BF16)
    xv_cons = [0, 0, 0, 0]
    va_prev = {}
    for kt in range(NKT):
        bank = pbO[kt % 2]
        for i in range(16):
            slot = i % 4
            b = xv[:, 128 * slot:128 * (slot + 1)]
            if xv_cons[slot]:
                P.wait("g", cm.mm, xv_cons[slot])
            tgt = P.emit("g", lambda g_, b=b,
                         s=xbuf[128 * i:128 * (i + 1), 128 * kt:128 * (kt + 1)]:
                         g_.dma_start(out=b, in_=s), inc=s_xv[slot], n=16, dma=True)
            P.wait("t", s_xv[slot], tgt)
            if kt == 0 and i == 0:
                P.waitall("t", s_vw)
            if i == 0 and kt >= 2:
                P.wait("t", cm.act, va_prev[kt % 2])
            m1 = P.emit("t", lambda t, b=b, i=i, bank=bank: t.matmul(
                bank[0:128, 0:512], b, vwT[:, 512 * i:512 * (i + 1)],
                start=(i == 0), stop=(i == 15)), inc=cm.mm)
            xv_cons[slot] = m1
        P.wait("s", cm.mm, m1)
        va = P.emit("s", lambda s, kt=kt, bank=bank: s.copy(
            Vt[:, 512 * kt:512 * (kt + 1)], bank[0:128, 0:512]), inc=cm.act)
        va_prev[kt % 2] = va
    v_done = va

    Ed = P.sb("Ed", [128, 4 * SL], BF16)
    for d in range(4):
        P.emit("v", lambda v, d=d: v.tensor_scalar(
            Ed[:, SL * d:SL * (d + 1)], cm.iotaPF[:, :], float(128 * d), 0.0,
            op0=ALU.add, op1=ALU.is_le), inc=cm.dve)
    ed_done = cm.dve.n

    qwT = P.sb("qwT", [128, 16 * 512], BF16)
    xq = P.sb("xq", [128, 16 * 512], BF16)
    Qs = QsT
    Ctx = xq
    Pt = P.sb("Pt", [128, 2 * SL], BF16)
    bvs = P.sb("bvs", [128, 16], F32)
    rbS = rstdB
    sumr = rstdB[0:1, 0:SL]
    owT = qwT
    pln = PostLN(P, cm, "l1a", Rs_ap=big1[:, 0:8192])
    q0n, q1n = k0n, k1n

    exp_t = [0, 0]
    pv_t = [0, 0]
    pmask_t = [0, 0]
    sum_evac = 0
    qwT_free_mm = 0
    oproj_last_mm = 0
    for s_ in range(NSL):
        colq = slice(1024 + SL * s_, 1024 + SL * (s_ + 1))
        P.wait("g", cm.mm, cm.mm.n)
        if s_ == 0:
            s_xq = cm.newsem("g")
            s_qw = cm.newsem("y")
            s_res = cm.newsem("y")
            P.wait("y", ss_q, ss_q.n)
            P.wait("v", ss_o, ss_o.n)
            P.wait("s", ss_o, ss_o.n)
        for i in range(16):
            P.emit("g", lambda g_, i=i, s=xbuf[128 * i:128 * (i + 1), colq]:
                   g_.dma_start(out=xq[:, 512 * i:512 * (i + 1)], in_=s),
                   inc=s_xq, n=16, dma=True)
        q_bank_free = k_done_dve if s_ == 0 else 0
        for oblk in range(4):
            if s_ == 0 and oblk == 0:
                P.wait("t", cm.act, v_done)
            P.wait("y", cm.mm, max(qwT_free_mm, oproj_last_mm))
            for i in range(16):
                P.emit("y", lambda y, i=i, oblk=oblk: y.dma_start(
                    out=qwT[:, 512 * i:512 * (i + 1)],
                    in_=sc_q[512 * oblk:512 * (oblk + 1), 128 * i:128 * (i + 1)],
                    transpose=True), inc=s_qw, n=16)
            P.waitall("t", s_qw)
            if oblk == 0:
                P.waitall("t", s_xq)
            if q_bank_free:
                P.wait("t", cm.dve, q_bank_free)
            for i in range(16):
                for oc in range(4):
                    m1 = P.emit("t", lambda t, i=i, oc=oc: t.matmul(
                        pb[4 + oc][0:128, 0:512],
                        qwT[:, 512 * i + 128 * oc:512 * i + 128 * (oc + 1)],
                        xq[:, 512 * i:512 * (i + 1)],
                        start=(i == 0), stop=(i == 15)), inc=cm.mm)
            qwT_free_mm = m1
            for hh in range(4):
                head = oblk * 4 + hh
                bank = pb[4 + hh]
                P.wait("s", cm.mm, m1)
                P.wait("s", cm.dve, cm.dve.n)
                a1 = P.emit("s", lambda s, bank=bank: s.activation(
                    out=sqb[:, 0:SL], in_=bank[0:128, 0:512], func=AF.Square),
                    inc=cm.act)
                P.wait("t", cm.act, a1)
                s1 = P.emit("t", lambda t: t.matmul(
                    cm.pbR1[0:1, 0:SL], cm.ones128_1b[:, :], sqb[:, 0:SL],
                    start=True, stop=True), inc=cm.mm)
                P.wait("s", cm.mm, s1)
                a3 = P.emit("s", lambda s: s.copy(msrow[0:1, :], cm.pbR1[0:1, 0:SL]),
                            inc=cm.act)
                P.wait("v", cm.act, a3)
                e1 = P.emit("v", lambda v: v.tensor_scalar(
                    msrow[0:1, :], msrow[0:1, :], 1.0 / 128.0, 1e-6,
                    op0=ALU.mult, op1=ALU.add), inc=cm.dve)
                P.wait("s", cm.dve, e1)
                a4 = P.emit("s", lambda s: s.activation(
                    out=msrow[0:1, :], in_=msrow[0:1, :], func=AF.Sqrt), inc=cm.act)
                P.wait("v", cm.act, a4)
                e2 = P.emit("v", lambda v: v.reciprocal(msrow[0:1, :], msrow[0:1, :]),
                            inc=cm.dve)
                P.wait("t", cm.dve, e2)
                _, ab = bcast_row(P, cm, msrow[0:1, :], cm.pbB, rstdB, SL)
                P.wait("v", cm.act, ab)
                e3 = P.emit("v", lambda v, bank=bank: v.tensor_tensor(
                    q0n[:, :], bank[0:128, 0:512], rstdB[:, :], ALU.mult), inc=cm.dve)
                rope_apply(P, cm, q0n[0:64, :], q0n[64:128, :],
                           cosT[0:64, colq], sinT[0:64, colq],
                           Qs[0:64, SL * head:SL * (head + 1)],
                           Qs[64:128, SL * head:SL * (head + 1)], ktmp, SL,
                           half=64, tmp2=ktmp2, ps_shift=cm.pbB)
                q_bank_free = cm.dve.n
        q_rope_done = cm.dve.n

        P.wait("v", cm.act, acc0)
        if exp_t[0] or exp_t[1]:
            P.wait("v", cm.act, max(exp_t))
        for j in range(12):
            kt = 4 * s_ + j
            P.emit("v", lambda v, j=j, kt=kt: v.tensor_scalar(
                bvs[:, j:j + 1], cm.iotaP[:, :], float(128 * kt), cc_cols[0:128, 0:1],
                op0=ALU.add, op1=ALU.is_lt), inc=cm.dve)
            P.emit("v", lambda v, j=j, kt=kt, s_=s_: v.tensor_scalar(
                bvs[:, 12:13], cm.iotaP[:, :], float(128 * kt - (512 * s_ + 1)), 0.0,
                op0=ALU.add, op1=ALU.is_lt), inc=cm.dve)
            P.emit("v", lambda v, j=j: v.tensor_tensor(
                bvs[:, j:j + 1], bvs[:, j:j + 1], bvs[:, 12:13], ALU.add), inc=cm.dve)
            P.emit("v", lambda v, j=j: v.scalar_tensor_tensor(
                bvs[:, j:j + 1], bvs[:, j:j + 1], NEG, negB[:, :],
                op0=ALU.mult, op1=ALU.add), inc=cm.dve)
        bv_done = cm.dve.n

        P.wait("t", cm.dve, max(k_done_dve, ed_done, q_rope_done, bv_done))
        P.wait("t", cm.act, v_done)
        for h in range(16):
            kvh = h // 4
            if sum_evac:
                P.wait("t", cm.act, sum_evac)
            for j in range(12):
                kt = 4 * s_ + j
                par = j % 2
                bankS = pbS[par]
                if exp_t[par]:
                    P.wait("t", cm.act, exp_t[par])
                if pv_t[par]:
                    P.wait("t", cm.mm, pv_t[par])
                m1 = P.emit("t", lambda t, h=h, kt=kt, kvh=kvh, bankS=bankS: t.matmul(
                    bankS[0:128, 0:512], Kt[:, 3072 * kvh + 128 * kt:3072 * kvh + 128 * (kt + 1)],
                    Qs[:, SL * h:SL * (h + 1)], start=True, stop=True), inc=cm.mm)
                P.wait("s", cm.mm, m1)
                P.wait("s", cm.dve, bv_done)
                ptile = Pt[:, SL * par:SL * (par + 1)]
                if pmask_t[par]:
                    P.wait("s", cm.dve, pmask_t[par])
                ex = P.emit("s", lambda s, bankS=bankS, ptile=ptile, j=j: s.activation(
                    out=ptile, in_=bankS[0:128, 0:512], func=AF.Exp,
                    bias=bvs[0:128, j:j + 1], scale=1.0 / math.sqrt(128.0)), inc=cm.act)
                exp_t[par] = ex
                dep = ("act", ex)
                if j >= 8:
                    P.wait("v", cm.act, ex)
                    pm = P.emit("v", lambda v, ptile=ptile, j=j: v.tensor_tensor(
                        ptile, ptile, Ed[:, SL * (j - 8):SL * (j - 7)], ALU.mult),
                        inc=cm.dve)
                    pmask_t[par] = pm
                    dep = ("dve", pm)
                if dep[0] == "act":
                    P.wait("t", cm.act, dep[1])
                else:
                    P.wait("t", cm.dve, dep[1])
                p1 = P.emit("t", lambda t, kt=kt, kvh=kvh, ptile=ptile, j=j: t.matmul(
                    pbO[0][0:128, 0:512], Vt[:, 512 * kt + 128 * kvh:512 * kt + 128 * (kvh + 1)],
                    ptile, start=(j == 0), stop=(j == 11)), inc=cm.mm)
                p3 = P.emit("t", lambda t, ptile=ptile, j=j: t.matmul(
                    cm.pbB2[0:1, 0:512], cm.ones128_1b[:, :], ptile,
                    start=(j == 0), stop=(j == 11)), inc=cm.mm)
                pv_t[par] = p3
            P.wait("s", cm.mm, p3)
            sum_evac = P.emit("s", lambda s: s.copy(sumr[0:1, :], cm.pbB2[0:1, 0:512]),
                              inc=cm.act)
            P.wait("v", cm.act, sum_evac)
            e1 = P.emit("v", lambda v: v.reciprocal(sumr[0:1, :], sumr[0:1, :]),
                        inc=cm.dve)
            P.wait("t", cm.dve, e1)
            _, ab = bcast_row(P, cm, sumr[0:1, :], cm.pbB, rbS, SL)
            P.wait("v", cm.act, ab)
            ec0 = P.emit("v", lambda v, h=h: v.tensor_tensor(
                Ctx[:, SL * h:SL * (h + 1)], pbO[0][0:128, 0:512], rbS[:, :],
                ALU.mult), inc=cm.dve)
            P.wait("t", cm.dve, ec0)

        P.wait("v", cm.act, rope_rdy)
        pln.begin()
        res_cons = [0, 0]
        for oblk in range(4):
            if s_ == 0 and oblk == 0:
                P.wait("y", ss_o, ss_o.n)
            P.wait("y", cm.mm, max(oproj_last_mm, qwT_free_mm))
            for i in range(16):
                P.emit("y", lambda y, i=i, oblk=oblk: y.dma_start(
                    out=owT[:, 512 * i:512 * (i + 1)],
                    in_=sc_o[512 * oblk:512 * (oblk + 1), 128 * i:128 * (i + 1)],
                    transpose=True), inc=s_qw, n=16)
            P.waitall("t", s_qw)
            for i in range(16):
                for j in range(4):
                    m1 = P.emit("t", lambda t, i=i, j=j: t.matmul(
                        pb[4 + j][0:128, 0:512],
                        owT[:, 512 * i + 128 * j:512 * i + 128 * (j + 1)],
                        Ctx[:, 512 * i:512 * (i + 1)],
                        start=(i == 0), stop=(i == 15)), inc=cm.mm)
            oproj_last_mm = m1
            qwT_free_mm = m1
            for j in range(4):
                oc = oblk * 4 + j
                if res_cons[0]:
                    P.wait("y", cm.dve, res_cons[0])
                P.emit("y", lambda y, oc=oc, colq=colq: y.dma_start(
                    out=pln.Rs[:, SL * oc:SL * (oc + 1)],
                    in_=xbuf[128 * oc:128 * (oc + 1), colq]), inc=s_res, n=16)
                P.wait("v", cm.mm, oproj_last_mm)
                P.waitall("v", s_res)
                ee = pln.add_chunk(oc, pb[4 + j][0:128, 0:512])
                res_cons[0] = ee
            P.wait("t", cm.dve, ee)
        pln.finalize(dst_dram_ap=out_d[:, SL * s_:SL * (s_ + 1)])
    P.wait("y", cm.st, cm.st.n)


def build_l1_attn():
    nc = bass.Bass("TRN2", target_bir_lowering=False)
    io = {
        "xbuf": _ei(nc, "xbuf", [DM, 3072]),
        "qw": _ei(nc, "qw", [2048, 2048]), "kw": _ei(nc, "kw", [512, 2048]),
        "vw": _ei(nc, "vw", [512, 2048]), "ow": _ei(nc, "ow", [2048, 2048]),
        "cc": _ei(nc, "cc", [1, 8]),
        "out": _eo(nc, "out", [DM, TOK]),
    }
    return _single(nc, body_l1_attn, io)


I32 = mybir.dt.int32


def build_hadamard(P, cm, name, n, scale):
    ha = P.sb(f"had_{name}", [128, n], BF16)
    ai = P.sb(f"hadi_{name}", [128, 2 * n], I32)
    A = ai[0:n, 0:n]
    B = ai[0:n, n:2 * n]
    P.emit("g", lambda g: g.iota(A, pattern=[[0, n]], base=0, channel_multiplier=1),
           inc=cm.pool)
    gp = P.emit("g", lambda g: g.iota(B, pattern=[[1, n]], base=0, channel_multiplier=0),
                inc=cm.pool)
    P.wait("v", cm.pool, gp)
    P.emit("v", lambda v: v.tensor_tensor(A, A, B, ALU.bitwise_and), inc=cm.dve)
    for sh in (1, 2, 4, 8, 16, 32, 64):
        if sh < n:
            P.emit("v", lambda v, sh=sh: v.tensor_scalar(
                B, A, sh, None, op0=ALU.logical_shift_right), inc=cm.dve)
            P.emit("v", lambda v: v.tensor_tensor(A, A, B, ALU.bitwise_xor), inc=cm.dve)
    P.emit("v", lambda v: v.tensor_scalar(A, A, 1, None, op0=ALU.bitwise_and),
           inc=cm.dve)
    e1 = P.emit("v", lambda v: v.tensor_copy(ha[0:n, 0:n], A), inc=cm.dve)
    e2 = P.emit("v", lambda v: v.tensor_scalar(
        ha[0:n, 0:n], ha[0:n, 0:n], -2.0 * scale, float(scale),
        op0=ALU.mult, op1=ALU.add), inc=cm.dve)
    return ha, e2



def body_l1_ffn(P, cm, io):
    xin, w1_d, w2_d, w3_d, out_d = (io["xin"], io["w1"], io["w2"], io["w3"],
                                    io["out"])
    pb = [P.ps(f"pb{i}", [128, 512], F32) for i in range(8)]
    cm.pbR1, cm.pbR2, cm.pbB, cm.pbB2 = pb[0], pb[1], pb[2], pb[3]
    cm.pb_misc = pb[2]

    W = 512
    NSLF = TOK // W
    h128, hrdy1 = build_hadamard(P, cm, "h128", 128, 2.0 ** -7)
    h64, hrdy2 = build_hadamard(P, cm, "h64", 64, 1.0)

    QsT = P.sb("QsT", [128, 64 * W], BF16)
    HG = QsT
    wscb = QsT[:, 0:4096]
    sc1, ss1 = w_to_scratch(P, cm, "w1", w1_d, 8192, 2048, buf=wscb)
    sc3, ss3 = w_to_scratch(P, cm, "w3", w3_d, 8192, 2048, buf=wscb, prev=ss1)
    sc2, ss2 = w_to_scratch(P, cm, "w2", w2_d, 2048, 8192, buf=wscb, prev=ss3)
    zscr = P.dram("zscr", [8192, W], BF16)
    zview = zscr.rearrange("(k p) t -> k (p t)", k=64)
    XB = P.sb("XB", [128, 16 * W], BF16)
    SG = P.sb("SGb", [128, W], BF16)
    wring = P.sb("wring", [128, 8 * 512], BF16)
    zring = P.sb("zring", [64, 4 * 512], BF16)
    pln = PostLN(P, cm, "l1f", W=W)
    ring_sems = [cm.newsem("y") for _ in range(8)]
    ring_cons = [0] * 8
    zr_sems = [cm.newsem("y") for _ in range(4)]
    zr_cons = [0] * 4
    s_xb = cm.newsem("g")
    s_res = cm.newsem("y")
    s_zst = cm.newsem("y")
    s_yst = cm.newsem("y")
    s_hld = cm.newsem("y")

    P.waitall("y", ss2)
    P.waitall("g", ss2)
    P.wait("t", cm.dve, max(hrdy1, hrdy2))
    res_prev = 0
    xb_cons = 0
    hg_evac_prev = 0
    hload_prev = 0
    for s_ in range(NSLF):
        colt = slice(W * s_, W * (s_ + 1))
        pln.begin()
        for i in range(16):
            if res_prev:
                P.wait("y", cm.dve, res_prev)
            P.emit("y", lambda y, i=i, colt=colt: y.dma_start(
                out=pln.Rs[:, W * i:W * (i + 1)], in_=xin[128 * i:128 * (i + 1), colt]),
                inc=s_res, n=16)
            res_prev = 0 if i < 15 else res_prev
        if xb_cons:
            P.wait("g", cm.mm, xb_cons)
        for i in range(16):
            P.emit("g", lambda g, i=i, colt=colt: g.dma_start(
                out=XB[:, W * i:W * (i + 1)], in_=xin[128 * i:128 * (i + 1), colt]),
                inc=s_xb, n=16, dma=True)
        P.waitall("t", s_xb)
        P.wait("t", cm.act, cm.act.n)
        for hb in range(16):
            banks_done = 0
            for src, bs in ((sc1, 4), (sc3, 0)):
                for i in range(16):
                    slot = (hb * 32 + (0 if bs else 16) + i) % 8
                    if ring_cons[slot]:
                        P.wait("y", cm.mm, ring_cons[slot])
                    lt = P.emit("y", lambda y, hb=hb, i=i, slot=slot, src=src: y.dma_start(
                        out=wring[:, 512 * slot:512 * (slot + 1)],
                        in_=src[512 * hb:512 * (hb + 1), 128 * i:128 * (i + 1)],
                        transpose=True), inc=ring_sems[slot], n=16)
                    P.wait("t", ring_sems[slot], lt)
                    if hb == 0 and i == 0 and bs == 4 and s_ == 0:
                        pass
                    if i == 0 and hg_evac_prev and bs == 4:
                        P.wait("t", cm.dve, hg_evac_prev)
                    for oc in range(4):
                        m1 = P.emit("t", lambda t, slot=slot, oc=oc, i=i, bs=bs: t.matmul(
                            pb[bs + oc][0:128, 0:W],
                            wring[:, 512 * slot + 128 * oc:512 * slot + 128 * (oc + 1)],
                            XB[:, W * i:W * (i + 1)],
                            start=(i == 0), stop=(i == 15)), inc=cm.mm)
                    ring_cons[slot] = m1
            P.wait("s", cm.mm, m1)
            for oc in range(4):
                ch = hb * 4 + oc
                if hg_evac_prev:
                    P.wait("s", cm.dve, hg_evac_prev)
                a1 = P.emit("s", lambda s, oc=oc: s.activation(
                    out=SG[:, :], in_=pb[4 + oc][0:128, 0:W], func=AF.Silu), inc=cm.act)
                P.wait("v", cm.act, a1)
                e1 = P.emit("v", lambda v, oc=oc, ch=ch: v.tensor_tensor(
                    HG[:, W * ch:W * (ch + 1)], SG[:, :], pb[oc][0:128, 0:W], ALU.mult),
                    inc=cm.dve)
                hg_evac_prev = e1
        av_par = {0: 0, 1: 0}
        for ch in range(64):
            P.wait("t", cm.dve, hg_evac_prev)
            if av_par[ch % 2]:
                P.wait("t", cm.act, av_par[ch % 2])
            m1 = P.emit("t", lambda t, ch=ch: t.matmul(
                pb[4 + (ch % 2)][0:128, 0:W], h128[0:128, 0:128],
                HG[:, W * ch:W * (ch + 1)], start=True, stop=True), inc=cm.mm)
            P.wait("s", cm.mm, m1)
            av = P.emit("s", lambda s, ch=ch: s.copy(
                HG[:, W * ch:W * (ch + 1)], pb[4 + (ch % 2)][0:128, 0:W]), inc=cm.act)
            av_par[ch % 2] = av
            P.wait("y", cm.act, av)
            P.emit("y", lambda y, ch=ch: y.dma_start(
                out=zscr[128 * ch:128 * (ch + 1), :], in_=HG[:, W * ch:W * (ch + 1)]),
                inc=s_zst, n=16)
        P.waitall("y", s_zst)
        nz = (128 * W) // 512
        for j in range(nz):
            slot = j % 4
            if zr_cons[slot]:
                P.wait("y", cm.mm, zr_cons[slot])
            lt = P.emit("y", lambda y, j=j, slot=slot: y.dma_start(
                out=zring[:, 512 * slot:512 * (slot + 1)],
                in_=zview[:, 512 * j:512 * (j + 1)]), inc=zr_sems[slot], n=16)
            P.wait("t", zr_sems[slot], lt)
            m1 = P.emit("t", lambda t, slot=slot, j=j: t.matmul(
                pb[4 + (j % 2)][0:64, 0:512], h64[0:64, 0:64],
                zring[0:64, 512 * slot:512 * (slot + 1)], start=True, stop=True),
                inc=cm.mm)
            zr_cons[slot] = m1
            P.wait("s", cm.mm, m1)
            av = P.emit("s", lambda s, slot=slot, j=j: s.activation(
                out=zring[0:64, 512 * slot:512 * (slot + 1)],
                in_=pb[4 + (j % 2)][0:64, 0:512], func=AF.Copy,
                scale=float(math.sqrt(2.0))), inc=cm.act)
            P.wait("y", cm.act, av)
            zr_cons[slot] = cm.mm.n
            P.emit("y", lambda y, j=j, slot=slot: y.dma_start(
                out=zview[:, 512 * j:512 * (j + 1)],
                in_=zring[0:64, 512 * slot:512 * (slot + 1)]), inc=s_yst, n=16)
            zr_cons[slot] = 0
            P.waitall("y", s_yst)
        P.waitall("y", s_yst)
        if hload_prev:
            P.wait("y", cm.mm, hload_prev)
        for ch in range(64):
            P.emit("y", lambda y, ch=ch: y.dma_start(
                out=HG[:, W * ch:W * (ch + 1)], in_=zscr[128 * ch:128 * (ch + 1), :]),
                inc=s_hld, n=16)
        P.waitall("t", s_hld)
        oproj_mm = 0
        for ob in range(4):
            for i in range(64):
                slot = i % 8
                if ring_cons[slot]:
                    P.wait("y", cm.mm, ring_cons[slot])
                lt = P.emit("y", lambda y, ob=ob, i=i, slot=slot: y.dma_start(
                    out=wring[:, 512 * slot:512 * (slot + 1)],
                    in_=sc2[512 * ob:512 * (ob + 1), 128 * i:128 * (i + 1)],
                    transpose=True), inc=ring_sems[slot], n=16)
                P.wait("t", ring_sems[slot], lt)
                for oc in range(4):
                    m1 = P.emit("t", lambda t, slot=slot, oc=oc, i=i: t.matmul(
                        pb[4 + oc][0:128, 0:W],
                        wring[:, 512 * slot + 128 * oc:512 * slot + 128 * (oc + 1)],
                        HG[:, W * i:W * (i + 1)],
                        start=(i == 0), stop=(i == 63)), inc=cm.mm)
                ring_cons[slot] = m1
            oproj_mm = m1
            for oc in range(4):
                chn = ob * 4 + oc
                P.wait("v", cm.mm, oproj_mm)
                P.waitall("v", s_res)
                ee = pln.add_chunk(chn, pb[4 + oc][0:128, 0:W])
            P.wait("t", cm.dve, ee)
        pln.finalize(dst_dram_ap=out_d[:, W * s_:W * (s_ + 1)])
        res_prev = ee
        xb_cons = cm.mm.n
        hload_prev = cm.mm.n
    P.wait("y", cm.st, cm.st.n)


def build_l1_ffn():
    nc = bass.Bass("TRN2", target_bir_lowering=False)
    io = {
        "xin": _ei(nc, "xin", [DM, TOK]),
        "w1": _ei(nc, "w1", [8192, 2048]), "w2": _ei(nc, "w2", [2048, 8192]),
        "w3": _ei(nc, "w3", [8192, 2048]),
        "out": _eo(nc, "out", [DM, TOK]),
    }
    return _single(nc, body_l1_ffn, io)



def body_l2a(P, cm, io, dbg=False):
    xin, wq_d, wk_d, wv_d, ld_d, o_out, q_out, s_out = (
        io["xin"], io["wq"], io["wk"], io["wv"], io["ld"], io["o_out"],
        io["q_out"], io["s_out"])
    if dbg:
        d_dbg = io["d_dbg"]
    pb = [P.ps(f"pb{i}", [128, 512], F32) for i in range(7)]
    pbT = P.ps("pbT", [128, 1024], BF16)
    cm.pbB = pb[2]
    cm.pb_misc = pb[2]
    ident_dve = cm.dve.n

    ldr = P.sb("ldr", [1, 16], F32)
    sml = cm.newsem("y")
    P.emit("y", lambda y: y.dma_start(out=ldr[:, :], in_=ld_d[:, :]), inc=sml, n=16)
    ldc = P.sb("ldc", [128, 64], F32)
    P.waitall("t", sml)
    P.wait("t", cm.dve, ident_dve)
    t1 = P.emit("t", lambda t: t.matmul(pb[2][0:128, 0:16], cm.ones1_128[:, :],
                                        ldr[0:1, 0:16], start=True, stop=True), inc=cm.mm)
    P.wait("s", cm.mm, t1)
    a1 = P.emit("s", lambda s: s.copy(ldc[:, 0:16], pb[2][0:128, 0:16]), inc=cm.act)
    P.wait("v", cm.act, a1)
    P.emit("v", lambda v: v.tensor_scalar(ldc[:, 16:32], ldc[:, 0:16], -1.0, None,
                                          op0=ALU.mult), inc=cm.dve)
    el = P.emit("v", lambda v: v.tensor_scalar(ldc[:, 32:48], ldc[:, 0:16], 127.0, None,
                                               op0=ALU.mult), inc=cm.dve)
    Dt = P.sb("Dt", [128, 16 * 128], BF16)
    qdec = P.sb("qdec", [128, 16], F32)
    kdec = P.sb("kdec", [128, 16], F32)
    cdec = P.sb("cdec", [128, 16], F32)
    Ed0 = P.sb("Ed0", [128, 128], BF16)
    P.emit("v", lambda v: v.tensor_scalar(
        Ed0[:, :], cm.iotaPF[:, 0:128], 0.0, None, op0=ALU.is_le), inc=cm.dve)
    dtmp = P.sb("dtmp", [128, 128], F32)
    P.wait("s", cm.dve, el)
    dtmp2 = P.sb("dtmp2", [128, 128], F32)
    for h in range(16):
        ea = P.emit("v", lambda v, h=h: v.tensor_scalar(
            dtmp2[:, :], cm.iotaPF[:, 0:128], ldc[0:128, 16 + h:17 + h], 0.0,
            op0=ALU.mult, op1=ALU.min), inc=cm.dve)
        P.wait("s", cm.dve, ea)
        a2 = P.emit("s", lambda s, h=h: s.activation(
            out=dtmp[:, :], in_=dtmp2[:, :], func=AF.Exp), inc=cm.act)
        P.wait("v", cm.act, a2)
        ed = P.emit("v", lambda v, h=h: v.tensor_tensor(
            Dt[:, 128 * h:128 * (h + 1)], dtmp[:, :], Ed0[:, :], ALU.mult), inc=cm.dve)
        P.wait("s", cm.dve, ed)
        P.emit("s", lambda s, h=h: s.activation(
            out=qdec[:, h:h + 1], in_=cm.iotaP[:, :], func=AF.Exp,
            scale=ldc[0:128, h:h + 1], bias=ldc[0:128, h:h + 1]), inc=cm.act)
        P.emit("s", lambda s, h=h: s.activation(
            out=kdec[:, h:h + 1], in_=cm.iotaP[:, :], func=AF.Exp,
            scale=ldc[0:128, 16 + h:17 + h], bias=ldc[0:128, 32 + h:33 + h]), inc=cm.act)
        P.emit("s", lambda s, h=h: s.activation(
            out=cdec[:, h:h + 1], in_=ldc[0:128, h:h + 1], func=AF.Exp,
            scale=128.0), inc=cm.act)
    dec_done_act = cm.act.n
    dec_done_dve = cm.dve.n

    QG = P.sb("QG", [128, 4 * 2048], BF16)
    KG = P.sb("KG", [128, 16 * 512], BF16)
    VG = P.sb("VG", [128, 16 * 512], BF16)
    OG = P.sb("OG", [128, 16 * 512], BF16)
    wring = P.sb("wring", [128, 8 * 512], BF16)
    wring2 = P.sb("wring2", [128, 8 * 512], BF16)
    xr = P.sb("xr", [128, 4 * 512], BF16)
    xv = P.sb("xv", [128, 4 * 128], BF16)
    ring_sems = [cm.newsem("g") for _ in range(8)]
    ring2_sems = [cm.newsem("g") for _ in range(8)]
    xr_sems = [cm.newsem("g") for _ in range(4)]
    xv_sems = [cm.newsem("g") for _ in range(4)]
    ring_cons = [0] * 8
    ring2_cons = [0] * 8
    xr_cons = [0] * 4
    xv_cons = [0] * 4
    ktb = P.sb("ktb", [128, 128], BF16)
    kd_b = P.sb("kd_b", [128, 128], BF16)
    oi_b = P.sb("oi_b", [128, 128], BF16)
    S32 = P.sb("S32", [128, 2 * 128], F32)
    Sb = P.sb("Sb", [128, 128], BF16)
    s_st = cm.newsem("y")

    P.wait("t", cm.dve, dec_done_dve)
    P.wait("t", cm.act, dec_done_act)
    XG = P.sb("XG2", [128, 16 * 512], BF16)
    kwT2 = P.sb("kwT2", [128, 16 * 512], BF16)
    s_xg = cm.newsem("g")
    s_kw2 = cm.newsem("g")
    xg_cons = 0
    kw_cons = 0
    kv_evac_prev = 0
    for g in range(4):
        for tsl in range(4):
            for i in range(16):
                xslot = i % 4
                if xr_cons[xslot]:
                    P.wait("g", cm.mm, xr_cons[xslot])
                xt = P.emit("g", lambda g_, i=i, tsl=tsl, xslot=xslot: g_.dma_start(
                    out=xr[:, 512 * xslot:512 * (xslot + 1)],
                    in_=xin[128 * i:128 * (i + 1), 512 * tsl:512 * (tsl + 1)]),
                    inc=xr_sems[xslot], n=16, dma=True)
                slot = i % 8
                if ring_cons[slot]:
                    P.wait("g", cm.mm, ring_cons[slot])
                lt = P.emit("g", lambda g_, i=i, g=g, slot=slot: g_.dma_start(
                    out=wring[:, 512 * slot:512 * (slot + 1)],
                    in_=wq_d[128 * i:128 * (i + 1), 512 * g:512 * (g + 1)]),
                    inc=ring_sems[slot], n=16, dma=True)
                P.wait("t", ring_sems[slot], lt)
                P.wait("t", xr_sems[xslot], xt)
                for cc in range(4):
                    m1 = P.emit("t", lambda t, slot=slot, xslot=xslot, cc=cc, i=i: t.matmul(
                        pb[3 + cc][0:128, 0:512],
                        wring[:, 512 * slot + 128 * cc:512 * slot + 128 * (cc + 1)],
                        xr[:, 512 * xslot:512 * (xslot + 1)],
                        start=(i == 0), stop=(i == 15)), inc=cm.mm)
                ring_cons[slot] = m1
                xr_cons[xslot] = m1
            P.wait("s", cm.mm, m1)
            for cc in range(4):
                av = P.emit("s", lambda s, cc=cc, tsl=tsl: s.activation(
                    out=QG[:, 2048 * cc + 512 * tsl:2048 * cc + 512 * (tsl + 1)],
                    in_=pb[3 + cc][0:128, 0:512], func=AF.Silu), inc=cm.act)
            P.wait("t", cm.act, av)
        q_done = cm.act.n
        P.wait("y", cm.act, q_done)
        for cc in range(4):
            P.emit("y", lambda y, g=g, cc=cc: y.dma_start(
                out=q_out[512 * g + 128 * cc:512 * g + 128 * (cc + 1), :],
                in_=QG[:, 2048 * cc:2048 * (cc + 1)]), inc=s_st, n=16)
        for grp in range(4):
            tok0 = 512 * grp
            P.wait("g", cm.mm, max(xg_cons, kw_cons) if (xg_cons or kw_cons)
                   else 0)
            for i in range(16):
                P.emit("g", lambda g_, i=i, tok0=tok0: g_.dma_start(
                    out=XG[:, 512 * i:512 * (i + 1)],
                    in_=xin[128 * i:128 * (i + 1), tok0:tok0 + 512]),
                    inc=s_xg, n=16, dma=True)
            P.waitall("t", s_xg)
            for src, dstG in ((wk_d, KG), (wv_d, VG)):
                if kw_cons:
                    P.wait("g", cm.mm, kw_cons)
                for i in range(16):
                    P.emit("g", lambda g_, i=i, g=g, src=src: g_.dma_start(
                        out=kwT2[:, 512 * i:512 * (i + 1)],
                        in_=src[128 * i:128 * (i + 1), 512 * g:512 * (g + 1)]),
                        inc=s_kw2, n=16, dma=True)
                P.waitall("t", s_kw2)
                if kv_evac_prev:
                    P.wait("t", cm.act, kv_evac_prev)
                for i in range(16):
                    for t4 in range(4):
                        m1 = P.emit("t", lambda t, i=i, t4=t4: t.matmul(
                            pb[3 + t4][0:128, 0:512],
                            XG[:, 512 * i + 128 * t4:512 * i + 128 * (t4 + 1)],
                            kwT2[:, 512 * i:512 * (i + 1)],
                            start=(i == 0), stop=(i == 15)), inc=cm.mm)
                kw_cons = m1
                P.wait("s", cm.mm, m1)
                for t4 in range(4):
                    tt = 4 * grp + t4
                    av = P.emit("s", lambda s, t4=t4, tt=tt, dstG=dstG: s.activation(
                        out=dstG[:, 512 * tt:512 * (tt + 1)],
                        in_=pb[3 + t4][0:128, 0:512], func=AF.Silu), inc=cm.act)
                kv_evac_prev = av
            xg_cons = cm.mm.n
        P.wait("t", cm.act, kv_evac_prev)
        kv_done = cm.act.n
        P.wait("t", cm.act, kv_done)
        for hh in range(4):
            hgl = 4 * g + hh
            P.waitall("v", s_st)
            P.emit("v", lambda v: v.memset(S32[:, 0:128], 0.0), inc=cm.dve)
            e2 = P.emit("v", lambda v: v.memset(Sb[:, :], 0.0), inc=cm.dve)
            P.wait("t", cm.dve, e2)
            spar = 0
            for c in range(16):
                tr = P.emit("t", lambda t, c=c, hh=hh: t.transpose(
                    pbT[0:128, 0:128], KG[:, 512 * c + 128 * hh:512 * c + 128 * (hh + 1)],
                    cm.identb[:, :]), inc=cm.mm)
                P.wait("s", cm.mm, tr)
                ak = P.emit("s", lambda s: s.copy(ktb[:, :], pbT[0:128, 0:128]),
                            inc=cm.act)
                P.wait("t", cm.act, ak)
                m1 = P.emit("t", lambda t, c=c, hh=hh: t.matmul(
                    pb[4][0:128, 0:128], ktb[:, :],
                    QG[:, 2048 * hh + 128 * c:2048 * hh + 128 * (c + 1)],
                    start=True, stop=True), inc=cm.mm)
                P.wait("v", cm.mm, m1)
                P.wait("v", cm.act, ak)
                ep = P.emit("v", lambda v, hgl=hgl: v.tensor_tensor(
                    ktb[:, :], pb[4][0:128, 0:128], Dt[:, 128 * hgl:128 * (hgl + 1)],
                    ALU.mult), inc=cm.dve)
                P.wait("t", cm.dve, ep)
                m2 = P.emit("t", lambda t, c=c, hh=hh: t.matmul(
                    pb[5][0:128, 0:128], ktb[:, :],
                    VG[:, 512 * c + 128 * hh:512 * c + 128 * (hh + 1)],
                    start=True, stop=True), inc=cm.mm)
                P.wait("s", cm.mm, m2)
                ao = P.emit("s", lambda s: s.copy(oi_b[:, :], pb[5][0:128, 0:128]),
                            inc=cm.act)
                m3 = P.emit("t", lambda t, c=c, hh=hh: t.matmul(
                    pb[6][0:128, 0:128],
                    QG[:, 2048 * hh + 128 * c:2048 * hh + 128 * (c + 1)], Sb[:, :],
                    start=True, stop=True), inc=cm.mm)
                P.wait("v", cm.mm, m3)
                P.wait("v", cm.act, ao)
                eo = P.emit("v", lambda v, hgl=hgl, c=c, hh=hh: v.scalar_tensor_tensor(
                    OG[:, 512 * c + 128 * hh:512 * c + 128 * (hh + 1)],
                    pb[6][0:128, 0:128], qdec[0:128, hgl:hgl + 1], oi_b[:, :],
                    op0=ALU.mult, op1=ALU.add), inc=cm.dve)
                ekd = P.emit("v", lambda v, c=c, hh=hh, hgl=hgl: v.tensor_scalar(
                    kd_b[:, :], KG[:, 512 * c + 128 * hh:512 * c + 128 * (hh + 1)],
                    kdec[0:128, hgl:hgl + 1], None, op0=ALU.mult), inc=cm.dve)
                P.wait("t", cm.dve, ekd)
                m4 = P.emit("t", lambda t, c=c, hh=hh: t.matmul(
                    pb[4][0:128, 0:128], kd_b[:, :],
                    VG[:, 512 * c + 128 * hh:512 * c + 128 * (hh + 1)],
                    start=True, stop=True), inc=cm.mm)
                P.wait("v", cm.mm, m4)
                src = S32[:, 128 * spar:128 * (spar + 1)]
                dst_ = S32[:, 128 * (1 - spar):128 * (2 - spar)]
                P.emit("v", lambda v, src=src, dst_=dst_, hgl=hgl: v.scalar_tensor_tensor(
                    dst_, src, cdec[0:128, hgl:hgl + 1], pb[4][0:128, 0:128],
                    op0=ALU.mult, op1=ALU.add), inc=cm.dve)
                ecp = P.emit("v", lambda v, dst_=dst_: v.tensor_copy(Sb[:, :], dst_),
                             inc=cm.dve)
                P.wait("t", cm.dve, ecp)
                if dbg and g == 0 and hh == 0 and c == 0:
                    dd = cm.newsem("y")
                    P.wait("y", cm.dve, cm.dve.n)
                    P.wait("g", cm.dve, cm.dve.n)
                    P.emit("y", lambda y, dst_=dst_: y.dma_start(
                        out=d_dbg[:, 0:128], in_=dst_), inc=dd, n=16)
                    dd2 = cm.newsem("g")
                    P.emit("g", lambda g_: g_.dma_start(
                        out=d_dbg[:, 128:256], in_=kd_b[:, :]), inc=dd2, n=16, dma=True)
                    P.emit("g", lambda g_: g_.dma_start(
                        out=d_dbg[:, 256:384], in_=ktb[:, :]), inc=dd2, n=16, dma=True)
                    P.emit("g", lambda g_: g_.dma_start(
                        out=d_dbg[:, 384:512], in_=Dt[:, 0:128]), inc=dd2, n=16, dma=True)
                    P.emit("y", lambda y: y.dma_start(
                        out=d_dbg[:, 512:528], in_=kdec[:, 0:16],
                        allow_slow_non_contiguous=True), inc=dd, n=16)
                    P.waitall("s", dd)
                    P.waitall("s", dd2)
                    P.waitall("v", dd2)
                spar = 1 - spar
            P.wait("y", cm.dve, cm.dve.n)
            P.emit("y", lambda y, hgl=hgl, spar=spar: y.dma_start(
                out=s_out[128 * hgl:128 * (hgl + 1), :],
                in_=S32[:, 128 * spar:128 * (spar + 1)]), inc=s_st, n=16)
        P.wait("y", cm.dve, cm.dve.n)
        for tt in range(16):
            P.emit("y", lambda y, g=g, tt=tt: y.dma_start(
                out=o_out[128 * tt:128 * (tt + 1), 512 * g:512 * (g + 1)],
                in_=OG[:, 512 * tt:512 * (tt + 1)]), inc=s_st, n=16)
        P.waitall("s", s_st)
        P.waitall("t", s_st)
        P.waitall("g", s_st)
    P.waitall("y", s_st)


def build_l2a(dbg=False):
    nc = bass.Bass("TRN2", target_bir_lowering=False)
    io = {
        "xin": _ei(nc, "xin", [DM, TOK]),
        "wq": _ei(nc, "wq", [2048, 2048]), "wk": _ei(nc, "wk", [2048, 2048]),
        "wv": _ei(nc, "wv", [2048, 2048]), "ld": _ei(nc, "ld", [1, 16]),
        "o_out": _eo(nc, "o_out", [TOK, DM], BF16),
        "q_out": _eo(nc, "q_out", [DM, TOK], BF16),
        "s_out": _eo(nc, "s_out", [16 * 128, 128]),
    }
    if dbg:
        io["d_dbg"] = _eo(nc, "d_dbg", [128, 1024])
    return _single(nc, body_l2a, io, dbg=dbg)



def body_l2b(P, cm, io):
    xin, o_in, q_in, s0_d, ld_d, wg1_d, wg2_d, nw_d, wo_d, out_d = (
        io["xin"], io["o_in"], io["q_in"], io["s0"], io["ld"], io["wg1"],
        io["wg2"], io["nw"], io["wo"], io["out"])
    pb = [P.ps(f"pb{i}", [128, 512], F32) for i in range(7)]
    pbT = P.ps("pbT", [128, 1024], BF16)
    cm.pbR1, cm.pbR2, cm.pbB = pb[0], pb[1], pb[2]
    cm.pb_misc = pb[2]
    ident_dve = cm.dve.n

    ldr = P.sb("ldr", [1, 16], F32)
    nwr = P.sb("nwr", [1, 2048], F32)
    sml = cm.newsem("y")
    P.emit("y", lambda y: y.dma_start(out=ldr[:, :], in_=ld_d[:, :]), inc=sml, n=16)
    P.emit("y", lambda y: y.dma_start(out=nwr[:, :], in_=nw_d[:, :]), inc=sml, n=16)
    ldc = P.sb("ldc", [128, 4], F32)
    P.waitall("t", sml)
    P.wait("t", cm.dve, ident_dve)
    t1 = P.emit("t", lambda t: t.matmul(pb[2][0:128, 0:16], cm.ones1_128[:, :],
                                        ldr[0:1, 0:16], start=True, stop=True), inc=cm.mm)
    P.wait("s", cm.mm, t1)
    ldc16 = P.sb("ldc16", [128, 16], F32)
    a1 = P.emit("s", lambda s: s.copy(ldc16[:, :], pb[2][0:128, 0:16]), inc=cm.act)
    nwB = P.sb("nwB", [128, 2048], F32)
    P.wait("t", cm.act, a1)
    ab = 0
    for k in range(4):
        _, ab = bcast_row(P, cm, nwr[0:1, 512 * k:512 * (k + 1)], cm.pbB,
                          nwB[:, 512 * k:512 * (k + 1)], 512)
    nw_done = ab

    QT = P.sb("QT", [128, 16 * 2048], BF16)
    S0B = P.sb("S0B", [128, 16 * 128], F32)
    S0b = P.sb("S0b", [128, 16 * 128], BF16)
    wg1b = P.sb("wg1b", [128, 16 * 128], BF16)
    wg2b = P.sb("wg2b", [128, 2048], BF16)
    g1T = P.sb("g1T", [128, 2048], BF16)
    s_ld = cm.newsem("y")
    for ch in range(16):
        P.emit("y", lambda y, ch=ch: y.dma_start(
            out=QT[:, 2048 * ch:2048 * (ch + 1)], in_=q_in[128 * ch:128 * (ch + 1), :]),
            inc=s_ld, n=16)
    for h in range(16):
        P.emit("y", lambda y, h=h: y.dma_start(
            out=S0B[:, 128 * h:128 * (h + 1)], in_=s0_d[128 * h:128 * (h + 1), :]),
            inc=s_ld, n=16)
    s_gld2 = cm.newsem("g")
    for c_ in range(16):
        P.emit("g", lambda g, c_=c_: g.dma_start(
            out=wg1b[:, 128 * c_:128 * (c_ + 1)],
            in_=wg1_d[128 * c_:128 * (c_ + 1), :]), inc=s_gld2, n=16, dma=True)
    P.emit("g", lambda g: g.dma_start(out=wg2b[:, :], in_=wg2_d[:, :]),
           inc=s_gld2, n=16, dma=True)
    P.waitall("v", s_ld)
    P.waitall("v", s_gld2)
    for h in range(16):
        ecast = P.emit("v", lambda v, h=h: v.tensor_copy(
            S0b[:, 128 * h:128 * (h + 1)], S0B[:, 128 * h:128 * (h + 1)]), inc=cm.dve)
    s0_done = ecast

    xr = P.sb("xr", [128, 4 * 512], BF16)
    xr_sems = [cm.newsem("g") for _ in range(4)]
    xr_cons = [0] * 4
    P.waitall("t", s_ld)
    P.waitall("t", s_gld2)
    for tsl in range(4):
        for i in range(16):
            xslot = i % 4
            if xr_cons[xslot]:
                P.wait("g", cm.mm, xr_cons[xslot])
            xt = P.emit("g", lambda g_, i=i, tsl=tsl, xslot=xslot: g_.dma_start(
                out=xr[:, 512 * xslot:512 * (xslot + 1)],
                in_=xin[128 * i:128 * (i + 1), 512 * tsl:512 * (tsl + 1)]),
                inc=xr_sems[xslot], n=16, dma=True)
            P.wait("t", xr_sems[xslot], xt)
            m1 = P.emit("t", lambda t, i=i, xslot=xslot: t.matmul(
                pb[3][0:128, 0:512], wg1b[:, 128 * i:128 * (i + 1)],
                xr[:, 512 * xslot:512 * (xslot + 1)],
                start=(i == 0), stop=(i == 15)), inc=cm.mm)
            xr_cons[xslot] = m1
        P.wait("s", cm.mm, m1)
        ag = P.emit("s", lambda s, tsl=tsl: s.activation(
            out=g1T[:, 512 * tsl:512 * (tsl + 1)], in_=pb[3][0:128, 0:512],
            func=AF.Copy), inc=cm.act)
        P.wait("t", cm.act, ag)
    g1_done = cm.act.n

    OB = P.sb("OB", [128, 2048], BF16)
    SGB = P.sb("SGB", [128, 2048], BF16)
    YT = P.sb("YT", [128, 16 * 512], BF16)
    qd_col = P.sb("qd_col", [128, 2], F32)
    rst_col = P.sb("rst_col", [128, 2], F32)
    sq_s = P.sb("sq_s", [128, 128], BF16)
    wring = P.sb("wring", [128, 8 * 512], BF16)
    ring_sems = [cm.newsem("g") for _ in range(8)]
    ring_cons = [0] * 8
    s_ob = cm.newsem("y")
    s_res = cm.newsem("y")
    pln = PostLN(P, cm, "l2b", W=SL)
    tmp32 = P.sb("tmp32", [128, 512], F32)

    P.wait("v", cm.act, max(g1_done, nw_done))
    ob_cons = 0
    yt_cons = 0
    res_prev = 0
    oproj_mm = 0
    for s_ in range(4):
        P.wait("t", cm.dve, cm.dve.n)
        for tt4 in range(4):
            tt = 4 * s_ + tt4
            if ob_cons:
                P.wait("y", cm.mm, ob_cons)
            P.emit("y", lambda y, tt=tt: y.dma_start(
                out=OB[:, :], in_=o_in[128 * tt:128 * (tt + 1), :]), inc=s_ob, n=16)
            P.waitall("v", s_ob)
            P.wait("v", cm.dve, s0_done)
            for h in range(16):
                e1 = P.emit("v", lambda v, h=h, tt=tt: v.tensor_scalar(
                    qd_col[:, 0:1], ldc16[:, h:h + 1], float(128 * tt + 1), None,
                    op0=ALU.mult), inc=cm.dve)
                P.wait("s", cm.dve, e1)
                a2 = P.emit("s", lambda s, h=h: s.activation(
                    out=qd_col[:, 1:2], in_=cm.iotaP[:, :], func=AF.Exp,
                    scale=ldc16[0:128, h:h + 1], bias=qd_col[0:128, 0:1]), inc=cm.act)
                m1 = P.emit("t", lambda t, h=h, tt=tt: t.matmul(
                    pb[3 + (h % 2)][0:128, 0:128],
                    QT[:, 2048 * h + 128 * tt:2048 * h + 128 * (tt + 1)],
                    S0b[:, 128 * h:128 * (h + 1)], start=True, stop=True), inc=cm.mm)
                P.wait("v", cm.mm, m1)
                P.wait("v", cm.act, a2)
                eo = P.emit("v", lambda v, h=h: v.scalar_tensor_tensor(
                    OB[:, 128 * h:128 * (h + 1)], pb[3 + (h % 2)][0:128, 0:128],
                    qd_col[0:128, 1:2], OB[:, 128 * h:128 * (h + 1)],
                    op0=ALU.mult, op1=ALU.add), inc=cm.dve)
                P.wait("t", cm.dve, eo)
            for eb in range(4):
                m1 = P.emit("t", lambda t, tt=tt, eb=eb: t.matmul(
                    pb[5][0:128, 0:512], g1T[:, 128 * tt:128 * (tt + 1)],
                    wg2b[:, 512 * eb:512 * (eb + 1)], start=True, stop=True), inc=cm.mm)
                P.wait("s", cm.mm, m1)
                asg = P.emit("s", lambda s, eb=eb: s.activation(
                    out=SGB[:, 512 * eb:512 * (eb + 1)], in_=pb[5][0:128, 0:512],
                    func=AF.Sigmoid), inc=cm.act)
                P.wait("t", cm.act, asg)
            P.wait("v", cm.act, cm.act.n)
            eog = P.emit("v", lambda v: v.tensor_tensor(OB[:, :], OB[:, :], SGB[:, :],
                                                        ALU.mult), inc=cm.dve)
            P.wait("s", cm.dve, eog)
            for h in range(16):
                a3 = P.emit("s", lambda s, h=h: s.activation(
                    out=sq_s[:, :], in_=OB[:, 128 * h:128 * (h + 1)], func=AF.Square,
                    accum_out=rst_col[:, 0:1]), inc=cm.act)
                P.wait("v", cm.act, a3)
                e2 = P.emit("v", lambda v: v.tensor_scalar(
                    rst_col[:, 0:1], rst_col[:, 0:1], 1.0 / 128.0, 1e-5,
                    op0=ALU.mult, op1=ALU.add), inc=cm.dve)
                P.wait("s", cm.dve, e2)
                a4 = P.emit("s", lambda s: s.activation(
                    out=rst_col[:, 1:2], in_=rst_col[:, 0:1], func=AF.Sqrt), inc=cm.act)
                P.wait("v", cm.act, a4)
                e3 = P.emit("v", lambda v: v.reciprocal(rst_col[:, 1:2],
                                                        rst_col[:, 1:2]), inc=cm.dve)
                e4 = P.emit("v", lambda v, h=h: v.tensor_scalar(
                    OB[:, 128 * h:128 * (h + 1)], OB[:, 128 * h:128 * (h + 1)],
                    rst_col[0:128, 1:2], None, op0=ALU.mult), inc=cm.dve)
                P.wait("s", cm.dve, e4)
            enw = P.emit("v", lambda v: v.tensor_tensor(OB[:, :], OB[:, :], nwB[:, :],
                                                        ALU.mult), inc=cm.dve)
            ob_cons = enw
            P.wait("t", cm.dve, enw)
            if yt_cons:
                P.wait("t", cm.act, 0) if False else None
            for ch in range(16):
                tr = P.emit("t", lambda t, ch=ch: t.transpose(
                    pbT[0:128, 0:128], OB[:, 128 * ch:128 * (ch + 1)], cm.identb[:, :]),
                    inc=cm.mm)
                P.wait("s", cm.mm, tr)
                aty = P.emit("s", lambda s, ch=ch, tt4=tt4: s.copy(
                    YT[:, 512 * ch + 128 * tt4:512 * ch + 128 * (tt4 + 1)],
                    pbT[0:128, 0:128]), inc=cm.act)
                P.wait("t", cm.act, aty)
            ob_cons = cm.mm.n
        yt_done = cm.act.n
        pln.begin()
        P.wait("t", cm.act, yt_done)
        colq = slice(512 * s_, 512 * (s_ + 1))
        for eb in range(4):
            for i in range(16):
                slot = i % 8
                if ring_cons[slot]:
                    P.wait("g", cm.mm, ring_cons[slot])
                lt = P.emit("g", lambda g_, i=i, eb=eb, slot=slot: g_.dma_start(
                    out=wring[:, 512 * slot:512 * (slot + 1)],
                    in_=wo_d[128 * i:128 * (i + 1), 512 * eb:512 * (eb + 1)]),
                    inc=ring_sems[slot], n=16, dma=True)
                P.wait("t", ring_sems[slot], lt)
                for ec in range(4):
                    m1 = P.emit("t", lambda t, slot=slot, ec=ec, i=i: t.matmul(
                        pb[3 + ec][0:128, 0:512],
                        wring[:, 512 * slot + 128 * ec:512 * slot + 128 * (ec + 1)],
                        YT[:, 512 * i:512 * (i + 1)],
                        start=(i == 0), stop=(i == 15)), inc=cm.mm)
                ring_cons[slot] = m1
            oproj_mm = m1
            for ec in range(4):
                oc = eb * 4 + ec
                if res_prev:
                    P.wait("y", cm.dve, res_prev)
                P.emit("y", lambda y, oc=oc, colq=colq: y.dma_start(
                    out=pln.Rs[:, SL * oc:SL * (oc + 1)],
                    in_=xin[128 * oc:128 * (oc + 1), colq]), inc=s_res, n=16)
                P.wait("v", cm.mm, oproj_mm)
                P.waitall("v", s_res)
                ee = pln.add_chunk(oc, pb[3 + ec][0:128, 0:512])
                res_prev = ee
            P.wait("t", cm.dve, ee)
        pln.finalize(dst_dram_ap=out_d[:, SL * s_:SL * (s_ + 1)])
        res_prev = ee
        yt_cons = cm.mm.n
    P.wait("y", cm.st, cm.st.n)


def build_l2b():
    nc = bass.Bass("TRN2", target_bir_lowering=False)
    io = {
        "xin": _ei(nc, "xin", [DM, TOK]),
        "o_in": _ei(nc, "o_in", [TOK, DM], BF16),
        "q_in": _ei(nc, "q_in", [DM, TOK], BF16),
        "s0": _ei(nc, "s0", [16 * 128, 128]), "ld": _ei(nc, "ld", [1, 16]),
        "wg1": _ei(nc, "wg1", [2048, 128]), "wg2": _ei(nc, "wg2", [128, 2048]),
        "nw": _ei(nc, "nw", [1, 2048]), "wo": _ei(nc, "wo", [2048, 2048]),
        "out": _eo(nc, "out", [DM, TOK]),
    }
    return _single(nc, body_l2b, io)



def body_l2_ffn(P, cm, io):
    xin, wg_d, wu_d, wd_d, out_d = (io["xin"], io["wg"], io["wu"], io["wd"],
                                    io["out"])
    pb = [P.ps(f"pb{i}", [128, 512], F32) for i in range(8)]
    cm.pbR1, cm.pbR2, cm.pbB, cm.pbB2 = pb[0], pb[1], pb[2], pb[3]
    cm.pb_misc = pb[2]

    W = 512
    NSLF = TOK // W
    NH = 51
    buf = P.sb("tern_b", [128, 2 * 2048], BF16)
    f32buf = P.sb("tern_f", [128, 2 * 2048], F32)
    scg, ssg, oscg, rdyg = w_to_scratch_ternary(
        P, cm, "wg", wg_d, 6464, 2048, 1e-5, "wq", buf, f32buf, cm.pbR1, cm.pbB2,
        Opad=6528)
    scu, ssu, oscu, rdyu = w_to_scratch_ternary(
        P, cm, "wu", wu_d, 6464, 2048, 1e-5, "wq", buf, f32buf, cm.pbR1, cm.pbB2,
        prev_ssem=ssg, Opad=6528)
    scd, ssd, oscd, rdyd = w_to_scratch_ternary(
        P, cm, "wd", wd_d, 2048, 6464, 1e-5, "wq", buf, f32buf, cm.pbR1, cm.pbB2,
        prev_ssem=ssu, Ipad=6528)

    H = P.sb("Hf", [128, NH * W], BF16)
    tq = P.sb("tq", [128, W], F32)
    XQ = P.sb("XQ", [128, 16 * W], BF16)
    UP = P.sb("UPf", [128, W], F32)
    wring = P.sb("wring", [128, 8 * 512], BF16)
    wring2 = P.sb("wring2", [128, 8 * 512], BF16)
    acc = P.sb("acc", [128, W], F32)
    acc2 = P.sb("acc2", [128, W], F32)
    sB = P.sb("sB", [128, W], F32)
    invsB = P.sb("invsB", [128, W], F32)
    srow = P.sb("srow", [1, 2 * W], F32)
    cols = P.sb("colsq", [128, 8], F32)
    pln = PostLN(P, cm, "l2f", W=W)
    ring_sems = [cm.newsem("y") for _ in range(8)]
    ring2_sems = [cm.newsem("y") for _ in range(8)]
    ring_cons = [0] * 8
    ring2_cons = [0] * 8

    P.waitall("y", ssd)
    P.waitall("g", ssd)
    P.wait("v", cm.act, max(rdyg, rdyu, rdyd))
    P.wait("s", cm.act, max(rdyg, rdyu, rdyd))
    res_prev = 0
    s_res = cm.newsem("y")
    hblocks = [(b_ * 512, 512) for b_ in range(12)] + [(6144, 384)]
    for s_ in range(NSLF):
        colt = slice(W * s_, W * (s_ + 1))
        pln.begin()
        for i in range(16):
            if res_prev:
                P.wait("y", cm.dve, res_prev)
                res_prev = 0
            P.emit("y", lambda y, i=i, colt=colt: y.dma_start(
                out=pln.Rs[:, W * i:W * (i + 1)], in_=xin[128 * i:128 * (i + 1), colt]),
                inc=s_res, n=16)
            P.waitall("v", s_res)
            if i == 0:
                e = amax_start(P, cm, acc[:, :], acc2[:, :], pln.Rs[:, 0:W])
            else:
                e = amax_step(P, cm, acc[:, :], acc2[:, :],
                              pln.Rs[:, W * i:W * (i + 1)])
            res_prev = e
        amax_finish(P, cm, acc[:, :], acc2[:, :])
        qs = quant_scales(P, cm, acc, W, sB, invsB, srow, cols)
        P.wait("v", cm.act, qs)
        for i in range(16):
            P.emit("v", lambda v, i=i: v.tensor_tensor(
                XQ[:, W * i:W * (i + 1)], pln.Rs[:, W * i:W * (i + 1)], sB[:, :],
                ALU.mult), inc=cm.dve)
            P.emit("v", lambda v, i=i: v.tensor_scalar(
                XQ[:, W * i:W * (i + 1)], XQ[:, W * i:W * (i + 1)], RNE, RNE,
                op0=ALU.add, op1=ALU.subtract), inc=cm.dve)
            e3 = P.emit("v", lambda v, i=i: v.tensor_tensor(
                XQ[:, W * i:W * (i + 1)], XQ[:, W * i:W * (i + 1)], invsB[:, :],
                ALU.mult), inc=cm.dve)
        xq_done = e3
        P.wait("t", cm.dve, xq_done)
        P.wait("t", cm.act, cm.act.n)
        hacc_first = True
        bank_free_dve = 0
        for bi, (h0, hw) in enumerate(hblocks):
            ncw = hw // 128 if hw % 128 == 0 else hw // 128 + 1
            for i in range(16):
                slot = i % 8
                if ring_cons[slot]:
                    P.wait("y", cm.mm, ring_cons[slot])
                P.emit("y", lambda y, h0=h0, hw=hw, i=i, slot=slot: y.dma_start(
                    out=wring[:, 512 * slot:512 * slot + hw],
                    in_=scg[h0:h0 + hw, 128 * i:128 * (i + 1)],
                    transpose=True), inc=ring_sems[slot], n=16)
                if ring2_cons[slot]:
                    P.wait("y", cm.mm, ring2_cons[slot])
                lt2 = P.emit("y", lambda y, h0=h0, hw=hw, i=i, slot=slot: y.dma_start(
                    out=wring2[:, 512 * slot:512 * slot + hw],
                    in_=scu[h0:h0 + hw, 128 * i:128 * (i + 1)],
                    transpose=True), inc=ring2_sems[slot], n=16)
                P.waitall("t", ring_sems[slot])
                P.waitall("t", ring2_sems[slot])
                if i == 0 and bank_free_dve:
                    P.wait("t", cm.dve, bank_free_dve)
                for oc in range((hw + 127) // 128):
                    ow = min(128, hw - 128 * oc)
                    m1 = P.emit("t", lambda t, slot=slot, oc=oc, i=i, ow=ow: t.matmul(
                        pb[4 + oc][0:ow, 0:W],
                        wring[:, 512 * slot + 128 * oc:512 * slot + 128 * oc + ow],
                        XQ[:, W * i:W * (i + 1)],
                        start=(i == 0), stop=(i == 15)), inc=cm.mm)
                    m2 = P.emit("t", lambda t, slot=slot, oc=oc, i=i, ow=ow: t.matmul(
                        pb[0 + oc][0:ow, 0:W],
                        wring2[:, 512 * slot + 128 * oc:512 * slot + 128 * oc + ow],
                        XQ[:, W * i:W * (i + 1)],
                        start=(i == 0), stop=(i == 15)), inc=cm.mm)
                ring_cons[slot] = m2
                ring2_cons[slot] = m2
            P.wait("s", cm.mm, m2)
            for oc in range((hw + 127) // 128):
                ch = h0 // 128 + oc
                ow = min(128, hw - 128 * oc)
                if not hacc_first:
                    P.wait("s", cm.dve, bank_free_dve)
                a0 = P.emit("s", lambda s, oc=oc, ow=ow: s.activation(
                    out=UP[0:ow, :], in_=pb[0 + oc][0:ow, 0:W], func=AF.Copy,
                    scale=oscu[0:ow, 0:1]), inc=cm.act)
                a1 = P.emit("s", lambda s, oc=oc, ch=ch, ow=ow: s.activation(
                    out=H[0:ow, W * ch:W * (ch + 1)], in_=pb[4 + oc][0:ow, 0:W],
                    func=AF.Silu, scale=oscg[0:ow, 0:1]), inc=cm.act)
                P.wait("v", cm.act, a1)
                em = P.emit("v", lambda v, ch=ch, ow=ow: v.tensor_tensor(
                    H[0:ow, W * ch:W * (ch + 1)], H[0:ow, W * ch:W * (ch + 1)],
                    UP[0:ow, :], ALU.mult), inc=cm.dve)
                if hacc_first:
                    e = amax_start(P, cm, acc[:, :], acc2[:, :],
                                   H[:, W * ch:W * (ch + 1)])
                    hacc_first = False
                else:
                    e = amax_step(P, cm, acc[0:ow, :], acc2[0:ow, :],
                                  H[0:ow, W * ch:W * (ch + 1)])
                bank_free_dve = e
        amax_finish(P, cm, acc[:, :], acc2[:, :])
        qs2 = quant_scales(P, cm, acc, W, sB, invsB, srow, cols)
        P.wait("v", cm.act, qs2)
        if s_ == 0:
            P.emit("v", lambda v: v.memset(H[64:128, W * 50:W * 51], 0.0), inc=cm.dve)
        for ch in range(NH):
            ow = 64 if ch == 50 else 128
            P.emit("v", lambda v, ch=ch, ow=ow: v.tensor_tensor(
                tq[0:ow, :], H[0:ow, W * ch:W * (ch + 1)], sB[0:ow, :],
                ALU.mult), inc=cm.dve)
            P.emit("v", lambda v, ch=ch, ow=ow: v.tensor_scalar(
                tq[0:ow, :], tq[0:ow, :], RNE, RNE,
                op0=ALU.add, op1=ALU.subtract), inc=cm.dve)
            e3 = P.emit("v", lambda v, ch=ch, ow=ow: v.tensor_tensor(
                H[0:ow, W * ch:W * (ch + 1)], tq[0:ow, :],
                invsB[0:ow, :], ALU.mult), inc=cm.dve)
        hq_done = e3
        P.wait("t", cm.dve, hq_done)
        pln.scale_col = oscd
        for ob in range(4):
            for i in range(NH):
                slot = i % 8
                if ring_cons[slot]:
                    P.wait("y", cm.mm, ring_cons[slot])
                P.emit("y", lambda y, ob=ob, i=i, slot=slot: y.dma_start(
                    out=wring[:, 512 * slot:512 * (slot + 1)],
                    in_=scd[512 * ob:512 * (ob + 1), 128 * i:128 * (i + 1)],
                    transpose=True), inc=ring_sems[slot], n=16)
                P.waitall("t", ring_sems[slot])
                for oc in range(4):
                    m1 = P.emit("t", lambda t, slot=slot, oc=oc, i=i: t.matmul(
                        pb[4 + oc][0:128, 0:W],
                        wring[:, 512 * slot + 128 * oc:512 * slot + 128 * (oc + 1)],
                        H[:, W * i:W * (i + 1)],
                        start=(i == 0), stop=(i == NH - 1)), inc=cm.mm)
                ring_cons[slot] = m1
            oproj_mm = m1
            for oc in range(4):
                chn = ob * 4 + oc
                P.wait("v", cm.mm, oproj_mm)
                ee = pln.add_chunk(chn, pb[4 + oc][0:128, 0:W])
            P.wait("t", cm.dve, ee)
        pln.finalize(dst_dram_ap=out_d[:, W * s_:W * (s_ + 1)])
        res_prev = ee
    P.wait("y", cm.st, cm.st.n)


def build_l2_ffn():
    nc = bass.Bass("TRN2", target_bir_lowering=False)
    io = {
        "xin": _ei(nc, "xin", [DM, TOK]),
        "wg": _ei(nc, "wg", [6464, 2048]), "wu": _ei(nc, "wu", [6464, 2048]),
        "wd": _ei(nc, "wd", [2048, 6464]),
        "out": _eo(nc, "out", [DM, TOK]),
    }
    return _single(nc, body_l2_ffn, io)



def body_l3a(P, cm, io):
    xin, wq_d, wk_d, wv_d, bq_d, bk_d, bv_d, cc_d, q_out, kv_out, ks_out = (
        io["xin"], io["wq"], io["wk"], io["wv"], io["bq"], io["bk"], io["bv"],
        io["cc"], io["q_out"], io["kv_out"], io["ks_out"])
    pb = [P.ps(f"pb{i}", [128, 512], F32) for i in range(8)]
    cm.pbR1, cm.pbR2, cm.pbB, cm.pbB2 = pb[0], pb[1], pb[2], pb[3]
    cm.pb_misc = pb[2]
    ident_dve = cm.dve.n

    sml = cm.newsem("y")
    ccr = P.sb("cc_r", [1, 8], F32)
    bqr = P.sb("bq_r", [1, 2048], F32)
    bkr = P.sb("bk_r", [1, 2048], F32)
    bvr = P.sb("bv_r", [1, 2048], F32)
    P.emit("y", lambda y: y.dma_start(out=ccr[:, :], in_=cc_d[:, :]), inc=sml, n=16)
    P.emit("y", lambda y: y.dma_start(out=bqr[:, :], in_=bq_d[:, :]), inc=sml, n=16)
    P.emit("y", lambda y: y.dma_start(out=bkr[:, :], in_=bk_d[:, :]), inc=sml, n=16)
    P.emit("y", lambda y: y.dma_start(out=bvr[:, :], in_=bv_d[:, :]), inc=sml, n=16)
    bqc = P.sb("bqc", [128, 16], F32)
    cc_cols = P.sb("cc_cols", [128, 8], F32)
    P.waitall("t", sml)
    P.wait("t", cm.dve, ident_dve)
    t0 = P.emit("t", lambda t: t.matmul(cm.pbB[0:128, 0:8], cm.ones1_128[:, :],
                                        ccr[0:1, 0:8], start=True, stop=True), inc=cm.mm)
    P.wait("s", cm.mm, t0)
    acc0 = P.emit("s", lambda s: s.copy(cc_cols[:, :], cm.pbB[0:128, 0:8]), inc=cm.act)
    P.wait("t", cm.act, acc0)
    for c in range(16):
        tq = P.emit("t", lambda t, c=c: t.transpose(
            cm.pbB[0:128, c:c + 1], bqr[0:1, 128 * c:128 * (c + 1)],
            cm.identf[0:1, 0:1]), inc=cm.mm)
    P.wait("s", cm.mm, tq)
    abq = P.emit("s", lambda s: s.copy(bqc[:, :], cm.pbB[0:128, 0:16]), inc=cm.act)
    decB = P.sb("decB", [128, 2048], F32)
    dec_cols = P.sb("dec_cols", [128, 16], F32)
    dtmp = P.sb("dtmp3", [128, 1], F32)
    iotaW = P.sb("iotaW3", [1, 2048], F32)
    P.emit("g", lambda g: g.iota(iotaW[:, :], pattern=[[1, 2048]], base=0,
                                 channel_multiplier=0,
                                 allow_small_or_imprecise_dtypes=True), inc=cm.pool)
    gp = cm.pool.n
    drow = P.sb("drow", [1, 2048], F32)
    P.wait("s", cm.pool, gp)
    P.waitall("s", sml)
    ad = P.emit("s", lambda s: s.activation(
        out=drow[:, :], in_=iotaW[:, :], func=AF.Exp, scale=-1.0 / 4096.0,
        bias=ccr[0:1, 0:1]), inc=cm.act)
    P.wait("t", cm.act, ad)
    ab = 0
    for k in range(4):
        _, ab = bcast_row(P, cm, drow[0:1, 512 * k:512 * (k + 1)], cm.pbB,
                          decB[:, 512 * k:512 * (k + 1)], 512)
    P.wait("v", cm.act, max(ab, acc0))
    a_prev = 0
    for tt in range(16):
        if a_prev:
            P.wait("v", cm.act, a_prev)
        e1 = P.emit("v", lambda v, tt=tt: v.tensor_scalar(
            dtmp[:, :], cc_cols[0:128, 0:1], float(-128 * tt / 4096.0), None,
            op0=ALU.add), inc=cm.dve)
        P.wait("s", cm.dve, e1)
        a_prev = P.emit("s", lambda s, tt=tt: s.activation(
            out=dec_cols[:, tt:tt + 1], in_=cm.iotaP[:, :], func=AF.Exp,
            scale=-1.0 / 4096.0, bias=dtmp[0:128, 0:1]), inc=cm.act)
    dec_done = cm.act.n
    bkB = P.sb("bkB", [128, 2048], F32)
    bvB = P.sb("bvB", [128, 2048], F32)
    for k in range(4):
        _, ab1 = bcast_row(P, cm, bkr[0:1, 512 * k:512 * (k + 1)], cm.pbB,
                           bkB[:, 512 * k:512 * (k + 1)], 512)
        _, ab2 = bcast_row(P, cm, bvr[0:1, 512 * k:512 * (k + 1)], cm.pbB,
                           bvB[:, 512 * k:512 * (k + 1)], 512)
    bias_done = ab2

    QsT = P.sb("QsT", [128, 16 * SL], BF16)
    wscb = QsT[:, 0:4096]
    sc_q, ss_q = w_to_scratch(P, cm, "q", wq_d, 2048, 2048, buf=wscb)
    sc_k, ss_k = w_to_scratch(P, cm, "k", wk_d, 2048, 2048, buf=wscb, prev=ss_q)
    sc_v, ss_v = w_to_scratch(P, cm, "v", wv_d, 2048, 2048, buf=wscb, prev=ss_k)

    wT = QsT
    xq = P.sb("xq", [128, 16 * 512], BF16)
    qd_t = P.sb("qd_t", [128, 512], F32)
    qb16 = P.sb("qb16", [128, 512], BF16)
    s_xq = cm.newsem("g")
    s_wt = cm.newsem("y")
    s_st = cm.newsem("y")
    s_stq = cm.newsem("y")

    P.waitall("y", ss_v)
    P.waitall("v", ss_v)
    P.wait("v", cm.act, max(abq, bias_done, dec_done))
    P.wait("s", cm.act, max(abq, bias_done, dec_done))
    wt_free = 0
    xq_free = 0
    qb_free = 0
    for tsl in range(4):
        if xq_free:
            P.wait("g", cm.mm, xq_free)
        for i in range(16):
            P.emit("g", lambda g, i=i, tsl=tsl: g.dma_start(
                out=xq[:, 512 * i:512 * (i + 1)],
                in_=xin[128 * i:128 * (i + 1), 512 * tsl:512 * (tsl + 1)]),
                inc=s_xq, n=16, dma=True)
        P.waitall("t", s_xq)
        for oblk in range(4):
            if wt_free:
                P.wait("y", cm.mm, wt_free)
            for i in range(16):
                P.emit("y", lambda y, i=i, oblk=oblk: y.dma_start(
                    out=wT[:, 512 * i:512 * (i + 1)],
                    in_=sc_q[512 * oblk:512 * (oblk + 1), 128 * i:128 * (i + 1)],
                    transpose=True), inc=s_wt, n=16)
            P.waitall("t", s_wt)
            if qb_free:
                P.wait("t", cm.dve, qb_free)
            for i in range(16):
                for oc in range(4):
                    m1 = P.emit("t", lambda t, i=i, oc=oc: t.matmul(
                        pb[4 + oc][0:128, 0:512],
                        wT[:, 512 * i + 128 * oc:512 * i + 128 * (oc + 1)],
                        xq[:, 512 * i:512 * (i + 1)],
                        start=(i == 0), stop=(i == 15)), inc=cm.mm)
            wt_free = m1
            for oc in range(4):
                ch = oblk * 4 + oc
                P.wait("s", cm.mm, m1)
                aq = P.emit("s", lambda s, oc=oc, ch=ch: s.activation(
                    out=qd_t[:, :], in_=pb[4 + oc][0:128, 0:512], func=AF.Relu,
                    bias=bqc[0:128, ch:ch + 1]), inc=cm.act)
                P.wait("v", cm.act, aq)
                ed = P.emit("v", lambda v, tsl=tsl: v.tensor_tensor(
                    qd_t[:, :], qd_t[:, :], decB[:, 512 * tsl:512 * (tsl + 1)],
                    ALU.mult), inc=cm.dve)
                P.wait("s", cm.dve, ed)
                aq2 = P.emit("s", lambda s: s.copy(qb16[:, :], qd_t[:, :]), inc=cm.act)
                P.wait("y", cm.act, aq2)
                st = P.emit("y", lambda y, ch=ch, tsl=tsl: y.dma_start(
                    out=q_out[128 * ch:128 * (ch + 1), 512 * tsl:512 * (tsl + 1)],
                    in_=qb16[:, :]), inc=s_stq, n=16)
                P.wait("s", s_stq, s_stq.n)
                qb_free = ed
        xq_free = cm.mm.n

    G4 = 4
    KD = P.sb("KD", [128, G4 * 2048], BF16)
    VD = P.sb("VD", [128, G4 * 2048], BF16)
    KVacc = P.sb("KVacc", [128, 16 * 128], F32)
    KSacc = P.sb("KSacc", [128, 16], F32)
    ez = P.emit("v", lambda v: v.memset(KVacc[:, :], 0.0), inc=cm.dve)
    ez2 = P.emit("v", lambda v: v.memset(KSacc[:, :], 0.0), inc=cm.dve)
    XG = xq
    s_xg = cm.newsem("g")
    s_kw = cm.newsem("y")
    s_vw = cm.newsem("y")
    kwT = P.sb("kwT", [128, 16 * 512], BF16)
    P.wait("t", cm.dve, ez2)
    xg_cons = cm.mm.n
    kw_cons = 0
    evac_prev = 0
    for grp in range(4):
        tok0 = 512 * grp
        P.wait("g", cm.mm, max(xg_cons, kw_cons))
        for i in range(16):
            P.emit("g", lambda g, i=i, tok0=tok0: g.dma_start(
                out=XG[:, 512 * i:512 * (i + 1)],
                in_=xin[128 * i:128 * (i + 1), tok0:tok0 + 512]),
                inc=s_xg, n=16, dma=True)
        P.waitall("t", s_xg)
        for src, ssem_, wsem, dst, isK in ((sc_k, ss_k, s_kw, KD, True),
                                           (sc_v, ss_v, s_vw, VD, False)):
            for eb in range(4):
                if grp == 0 and eb == 0:
                    P.waitall("y", ssem_)
                if kw_cons:
                    P.wait("y", cm.mm, kw_cons)
                for i in range(16):
                    P.emit("y", lambda y, i=i, eb=eb, src=src: y.dma_start(
                        out=kwT[:, 512 * i:512 * (i + 1)],
                        in_=src[512 * eb:512 * (eb + 1), 128 * i:128 * (i + 1)],
                        transpose=True), inc=wsem, n=16)
                P.waitall("t", wsem)
                if evac_prev:
                    P.wait("t", cm.dve, evac_prev)
                for i in range(16):
                    for t4 in range(G4):
                        m1 = P.emit("t", lambda t, i=i, t4=t4: t.matmul(
                            pb[4 + t4][0:128, 0:512],
                            XG[:, 512 * i + 128 * t4:512 * i + 128 * (t4 + 1)],
                            kwT[:, 512 * i:512 * (i + 1)],
                            start=(i == 0), stop=(i == 15)), inc=cm.mm)
                kw_cons = m1
                P.wait("v", cm.mm, m1)
                P.wait("v", cm.act, cm.act.n)
                for t4 in range(G4):
                    tt = G4 * grp + t4
                    if isK:
                        e1 = P.emit("v", lambda v, t4=t4, eb=eb: v.tensor_tensor(
                            qd_t[:, :], pb[4 + t4][0:128, 0:512],
                            bkB[:, 512 * eb:512 * (eb + 1)], ALU.add), inc=cm.dve)
                        e2 = P.emit("v", lambda v, t4=t4, eb=eb, tt=tt: v.tensor_scalar(
                            KD[:, 2048 * t4 + 512 * eb:2048 * t4 + 512 * (eb + 1)],
                            qd_t[:, :], 0.0,
                            dec_cols[0:128, tt:tt + 1], op0=ALU.max, op1=ALU.mult),
                            inc=cm.dve)
                    else:
                        e2 = P.emit("v", lambda v, t4=t4, eb=eb: v.tensor_tensor(
                            VD[:, 2048 * t4 + 512 * eb:2048 * t4 + 512 * (eb + 1)],
                            pb[4 + t4][0:128, 0:512],
                            bvB[:, 512 * eb:512 * (eb + 1)], ALU.add), inc=cm.dve)
                evac_prev = e2
        P.wait("t", cm.dve, evac_prev)
        for t4 in range(G4):
            for h in range(16):
                m1 = P.emit("t", lambda t, t4=t4, h=h: t.matmul(
                    pb[2][0:128, 0:128],
                    KD[:, 2048 * t4 + 128 * h:2048 * t4 + 128 * (h + 1)],
                    VD[:, 2048 * t4 + 128 * h:2048 * t4 + 128 * (h + 1)],
                    start=True, stop=True), inc=cm.mm)
                m2 = P.emit("t", lambda t, t4=t4, h=h: t.matmul(
                    pb[3][0:128, 0:1],
                    KD[:, 2048 * t4 + 128 * h:2048 * t4 + 128 * (h + 1)],
                    cm.ones128_1b[:, :], start=True, stop=True), inc=cm.mm)
                P.wait("v", cm.mm, m2)
                ekv = P.emit("v", lambda v, h=h: v.tensor_tensor(
                    KVacc[:, 128 * h:128 * (h + 1)], KVacc[:, 128 * h:128 * (h + 1)],
                    pb[2][0:128, 0:128], ALU.add), inc=cm.dve)
                eks = P.emit("v", lambda v, h=h: v.tensor_tensor(
                    KSacc[:, h:h + 1], KSacc[:, h:h + 1], pb[3][0:128, 0:1], ALU.add),
                    inc=cm.dve)
                P.wait("t", cm.dve, eks)
        xg_cons = cm.mm.n
    P.wait("y", cm.dve, cm.dve.n)
    for h in range(16):
        P.emit("y", lambda y, h=h: y.dma_start(
            out=kv_out[128 * h:128 * (h + 1), :], in_=KVacc[:, 128 * h:128 * (h + 1)]),
            inc=s_st, n=16)
    krow = P.sb("krow", [1, 2048], F32)
    P.wait("t", cm.dve, cm.dve.n)
    P.wait("t", cm.act, cm.act.n)
    ak2 = 0
    for hb in range(4):
        if ak2:
            P.wait("t", cm.act, ak2)
        for hh in range(4):
            th = P.emit("t", lambda t, hb=hb, hh=hh: t.transpose(
                cm.pbB[0:1, 128 * hh:128 * (hh + 1)], KSacc[:, 4 * hb + hh:4 * hb + hh + 1],
                cm.identf[0:128, 0:128]), inc=cm.mm)
        P.wait("s", cm.mm, th)
        ak2 = P.emit("s", lambda s, hb=hb: s.copy(
            krow[0:1, 512 * hb:512 * (hb + 1)], cm.pbB[0:1, 0:512]), inc=cm.act)
    P.wait("y", cm.act, ak2)
    P.emit("y", lambda y: y.dma_start(out=ks_out[:, :], in_=krow[:, :]), inc=s_st, n=16)
    P.waitall("y", s_st)
    P.waitall("y", s_stq)


def build_l3a():
    nc = bass.Bass("TRN2", target_bir_lowering=False)
    io = {
        "xin": _ei(nc, "xin", [DM, TOK]),
        "wq": _ei(nc, "wq", [2048, 2048]), "wk": _ei(nc, "wk", [2048, 2048]),
        "wv": _ei(nc, "wv", [2048, 2048]),
        "bq": _ei(nc, "bq", [1, 2048]), "bk": _ei(nc, "bk", [1, 2048]),
        "bv": _ei(nc, "bv", [1, 2048]), "cc": _ei(nc, "cc", [1, 8]),
        "q_out": _eo(nc, "q_out", [DM, TOK], BF16),
        "kv_out": _eo(nc, "kv_out", [16 * 128, 128]),
        "ks_out": _eo(nc, "ks_out", [1, 2048]),
    }
    return _single(nc, body_l3a, io)



def body_l3b(P, cm, io):
    xin, q_in, kva_d, kvb_d, ksa_d, ksb_d, wo_d, bo_d, out_d = (
        io["xin"], io["q_in"], io["kva"], io["kvb"], io["ksa"], io["ksb"],
        io["wo"], io["bo"], io["out"])
    pb = [P.ps(f"pb{i}", [128, 512], F32) for i in range(7)]
    pbT = P.ps("pbT", [128, 1024], BF16)
    cm.pbR1, cm.pbR2, cm.pbB = pb[0], pb[1], pb[2]
    cm.pb_misc = pb[2]
    ident_dve = cm.dve.n

    sml = cm.newsem("y")
    bor = P.sb("bo_r", [1, 2048], F32)
    ksa = P.sb("ksa_r", [1, 2048], F32)
    ksb = P.sb("ksb_r", [1, 2048], F32)
    P.emit("y", lambda y: y.dma_start(out=bor[:, :], in_=bo_d[:, :]), inc=sml, n=16)
    P.emit("y", lambda y: y.dma_start(out=ksa[:, :], in_=ksa_d[:, :]), inc=sml, n=16)
    P.emit("y", lambda y: y.dma_start(out=ksb[:, :], in_=ksb_d[:, :]), inc=sml, n=16)
    boc = P.sb("boc", [128, 16], F32)
    ksc = P.sb("ksc", [128, 16], F32)
    kscb = P.sb("kscb", [128, 16], BF16)
    P.waitall("v", sml)
    eks = P.emit("v", lambda v: v.tensor_tensor(ksa[:, :], ksa[:, :], ksb[:, :],
                                                ALU.add), inc=cm.dve)
    P.wait("t", cm.dve, eks)
    P.wait("t", cm.dve, ident_dve)
    P.waitall("t", sml)
    ak2 = 0
    for hb in range(4):
        if ak2:
            P.wait("t", cm.act, ak2)
        for hh in range(4):
            tq = P.emit("t", lambda t, hb=hb, hh=hh: t.transpose(
                cm.pbB[0:128, 32 * hh:32 * hh + 1] if False else cm.pbB[0:128, hh:hh + 1],
                ksa[0:1, 512 * hb + 128 * hh:512 * hb + 128 * (hh + 1)],
                cm.identf[0:1, 0:1]), inc=cm.mm)
            tb = P.emit("t", lambda t, hb=hb, hh=hh: t.transpose(
                cm.pbB[0:128, 4 + hh:5 + hh],
                bor[0:1, 512 * hb + 128 * hh:512 * hb + 128 * (hh + 1)],
                cm.identf[0:1, 0:1]), inc=cm.mm)
        P.wait("s", cm.mm, tb)
        ak2 = P.emit("s", lambda s, hb=hb: s.copy(ksc[:, 4 * hb:4 * (hb + 1)],
                                                  cm.pbB[0:128, 0:4]), inc=cm.act)
        ak2 = P.emit("s", lambda s, hb=hb: s.copy(boc[:, 4 * hb:4 * (hb + 1)],
                                                  cm.pbB[0:128, 4:8]), inc=cm.act)
    P.wait("v", cm.act, ak2)
    ekc = P.emit("v", lambda v: v.tensor_copy(kscb[:, :], ksc[:, :]), inc=cm.dve)

    KVt = P.sb("KVt", [128, 16 * 128], BF16)
    kvtmp = P.sb("kvtmp", [128, 2 * 128], F32)
    s_kv = cm.newsem("y")
    for h in range(16):
        P.emit("y", lambda y, h=h: y.dma_start(
            out=kvtmp[:, 0:128], in_=kva_d[128 * h:128 * (h + 1), :]), inc=s_kv, n=16)
        P.emit("y", lambda y, h=h: y.dma_start(
            out=kvtmp[:, 128:256], in_=kvb_d[128 * h:128 * (h + 1), :]), inc=s_kv, n=16)
        P.waitall("v", s_kv)
        ekv = P.emit("v", lambda v, h=h: v.tensor_tensor(
            kvtmp[:, 0:128], kvtmp[:, 0:128], kvtmp[:, 128:256], ALU.add), inc=cm.dve)
        ekv2 = P.emit("v", lambda v, h=h: v.tensor_copy(
            KVt[:, 128 * h:128 * (h + 1)], kvtmp[:, 0:128]), inc=cm.dve)
        P.wait("y", cm.dve, ekv2)
    kv_done = cm.dve.n

    QTB = P.sb("QTB", [128, 16 * 2048], BF16)
    s_q = cm.newsem("y")
    for ch in range(16):
        P.emit("y", lambda y, ch=ch: y.dma_start(
            out=QTB[:, 2048 * ch:2048 * (ch + 1)], in_=q_in[128 * ch:128 * (ch + 1), :]),
            inc=s_q, n=16)
    P.waitall("t", s_q)
    P.wait("t", cm.dve, kv_done)

    OT = P.sb("OT", [128, 2048], BF16)
    YT = P.sb("YT", [128, 16 * 512], BF16)
    zc = P.sb("zc", [128, 2], F32)
    wring = P.sb("wring", [128, 8 * 512], BF16)
    tmp32 = P.sb("tmp32l3", [128, 512], F32)
    ring_sems = [cm.newsem("y") for _ in range(8)]
    ring_cons = [0] * 8
    s_res = cm.newsem("y")
    sc_o, ss_o = w_to_scratch(P, cm, "o", wo_d, 2048, 2048, buf=YT[:, 0:4096])
    pln = PostLN(P, cm, "l3b", W=SL)
    res_prev = 0
    ot_free = 0
    yt_cons = 0
    s_wt = cm.newsem("y")
    P.waitall("y", ss_o)
    P.wait("s", ss_o, ss_o.n)
    P.wait("t", ss_o, ss_o.n)
    for s_ in range(4):
        P.wait("t", cm.dve, cm.dve.n)
        for tt4 in range(4):
            tt = 4 * s_ + tt4
            for h in range(16):
                m1 = P.emit("t", lambda t, h=h, tt=tt: t.matmul(
                    pb[3 + (h % 2)][0:128, 0:128],
                    QTB[:, 2048 * h + 128 * tt:2048 * h + 128 * (tt + 1)],
                    KVt[:, 128 * h:128 * (h + 1)], start=True, stop=True), inc=cm.mm)
                m2 = P.emit("t", lambda t, h=h, tt=tt: t.matmul(
                    pb[5][0:128, 0:1],
                    QTB[:, 2048 * h + 128 * tt:2048 * h + 128 * (tt + 1)],
                    kscb[:, h:h + 1], start=True, stop=True), inc=cm.mm)
                P.wait("v", cm.mm, m2)
                e1 = P.emit("v", lambda v: v.tensor_scalar(
                    zc[:, 0:1], pb[5][0:128, 0:1], 1e-6, None, op0=ALU.add), inc=cm.dve)
                e2 = P.emit("v", lambda v: v.reciprocal(zc[:, 0:1], zc[:, 0:1]),
                            inc=cm.dve)
                eo = P.emit("v", lambda v, h=h: v.tensor_scalar(
                    OT[:, 128 * h:128 * (h + 1)], pb[3 + (h % 2)][0:128, 0:128],
                    zc[0:128, 0:1], None, op0=ALU.mult), inc=cm.dve)
                P.wait("t", cm.dve, eo)
            for ch in range(16):
                tr = P.emit("t", lambda t, ch=ch: t.transpose(
                    pbT[0:128, 0:128], OT[:, 128 * ch:128 * (ch + 1)], cm.identb[:, :]),
                    inc=cm.mm)
                P.wait("s", cm.mm, tr)
                aty = P.emit("s", lambda s, ch=ch, tt4=tt4: s.copy(
                    YT[:, 512 * ch + 128 * tt4:512 * ch + 128 * (tt4 + 1)],
                    pbT[0:128, 0:128]), inc=cm.act)
                P.wait("t", cm.act, aty)
        yt_done = cm.act.n
        pln.begin()
        P.wait("t", cm.act, yt_done)
        colq = slice(512 * s_, 512 * (s_ + 1))
        for eb in range(4):
            for i in range(16):
                slot = i % 8
                if ring_cons[slot]:
                    P.wait("y", cm.mm, ring_cons[slot])
                P.emit("y", lambda y, i=i, eb=eb, slot=slot: y.dma_start(
                    out=wring[:, 512 * slot:512 * (slot + 1)],
                    in_=sc_o[512 * eb:512 * (eb + 1), 128 * i:128 * (i + 1)],
                    transpose=True), inc=ring_sems[slot], n=16)
                P.waitall("t", ring_sems[slot])
                for ec in range(4):
                    m1 = P.emit("t", lambda t, i=i, ec=ec, slot=slot: t.matmul(
                        pb[3 + ec][0:128, 0:512],
                        wring[:, 512 * slot + 128 * ec:512 * slot + 128 * (ec + 1)],
                        YT[:, 512 * i:512 * (i + 1)],
                        start=(i == 0), stop=(i == 15)), inc=cm.mm)
                ring_cons[slot] = m1
            oproj_mm = m1
            for ec in range(4):
                oc = eb * 4 + ec
                P.wait("s", cm.mm, oproj_mm)
                if res_prev:
                    P.wait("s", cm.dve, res_prev)
                atmp = P.emit("s", lambda s, ec=ec, oc=oc: s.activation(
                    out=tmp32[:, :], in_=pb[3 + ec][0:128, 0:512], func=AF.Identity,
                    bias=boc[0:128, oc:oc + 1]), inc=cm.act)
                if res_prev:
                    P.wait("y", cm.dve, res_prev)
                P.emit("y", lambda y, oc=oc, colq=colq: y.dma_start(
                    out=pln.Rs[:, SL * oc:SL * (oc + 1)],
                    in_=xin[128 * oc:128 * (oc + 1), colq]), inc=s_res, n=16)
                P.wait("v", cm.mm, oproj_mm)
                P.wait("v", cm.act, atmp)
                P.waitall("v", s_res)
                ee = pln.add_chunk(oc, tmp32[:, :])
                res_prev = ee
            P.wait("t", cm.dve, ee)
        pln.finalize(dst_dram_ap=out_d[:, SL * s_:SL * (s_ + 1)])
    P.wait("y", cm.st, cm.st.n)


def build_l3b():
    nc = bass.Bass("TRN2", target_bir_lowering=False)
    io = {
        "xin": _ei(nc, "xin", [DM, TOK]),
        "q_in": _ei(nc, "q_in", [DM, TOK], BF16),
        "kva": _ei(nc, "kva", [16 * 128, 128]),
        "kvb": _ei(nc, "kvb", [16 * 128, 128]),
        "ksa": _ei(nc, "ksa", [1, 2048]), "ksb": _ei(nc, "ksb", [1, 2048]),
        "wo": _ei(nc, "wo", [2048, 2048]), "bo": _ei(nc, "bo", [1, 2048]),
        "out": _eo(nc, "out", [DM, TOK]),
    }
    return _single(nc, body_l3b, io)



def body_l3_ffn(P, cm, io):
    xin, wf_d, wp_d, out_d = io["xin"], io["wf"], io["wp"], io["out"]
    pb = [P.ps(f"pb{i}", [128, 512], F32) for i in range(8)]
    cm.pbR1, cm.pbR2, cm.pbB, cm.pbB2 = pb[0], pb[1], pb[2], pb[3]
    cm.pb_misc = pb[2]

    W = 512
    NSLF = TOK // W
    QB = 128.0
    EPS3 = 1e-6
    H = P.sb("Hf", [128, 64 * W], BF16)
    pln = PostLN(P, cm, "l3f", W=W)
    buf = H[:, 0:2 * 2048]
    f32buf = pln.Rs[:, 0:2 * 2048]
    sc1, ss1, oscale1, rdy1 = w_to_scratch_ternary(
        P, cm, "wf", wf_d, 8192, 2048, EPS3, "l3", buf, f32buf, cm.pbR1, cm.pbB2)
    sc2, ss2, oscale2, rdy2 = w_to_scratch_ternary(
        P, cm, "wp", wp_d, 2048, 8192, EPS3, "l3", buf, f32buf, cm.pbR1, cm.pbB2,
        prev_ssem=ss1)
    XN = P.sb("XN", [128, 16 * W], BF16)
    wring = P.sb("wring", [128, 8 * 512], BF16)
    acc = P.sb("acc", [128, W], F32)
    acc2 = P.sb("acc2", [128, W], F32)
    sB = P.sb("sB", [128, W], F32)
    invsB = P.sb("invsB", [128, W], F32)
    aB = P.sb("aB3", [128, W], F32)
    bB = P.sb("bB3", [128, W], F32)
    srow = P.sb("srow", [1, 2 * W], F32)
    rws = P.sb("rws", [1, 4 * W], F32)
    cols = P.sb("colsq", [128, 8], F32)
    rb16 = P.sb("rb16", [128, W], BF16)
    sq16 = P.sb("sq16", [128, W], BF16)
    ring_sems = [cm.newsem("y") for _ in range(8)]
    ring_cons = [0] * 8

    def inner_ln(nchunks, src_fn, dst_fn, eps, scale_chunks):
        first = True
        a2 = 0
        t2 = 0
        for ch in range(nchunks):
            src = src_fn(ch)
            if a2:
                P.wait("s", cm.mm, t2)
            a1 = P.emit("s", lambda s, src=src: s.copy(rb16[:, :], src), inc=cm.act)
            a2 = P.emit("s", lambda s, src=src: s.activation(
                out=sq16[:, :], in_=src, func=AF.Square), inc=cm.act)
            P.wait("t", cm.act, a2)
            fst = bool(first)
            t1 = P.emit("t", lambda t, fst=fst, ch=ch: t.matmul(
                cm.pbR1[0:1, 0:W], cm.ones128_1b[:, :], rb16[:, :],
                start=fst, stop=(ch == nchunks - 1)), inc=cm.mm)
            t2 = P.emit("t", lambda t, fst=fst, ch=ch: t.matmul(
                cm.pbR2[0:1, 0:W], cm.ones128_1b[:, :], sq16[:, :],
                start=fst, stop=(ch == nchunks - 1)), inc=cm.mm)
            first = False
        P.wait("s", cm.mm, t2)
        n = nchunks * 128
        a1 = P.emit("s", lambda s: s.copy(rws[0:1, 0:W], cm.pbR1[0:1, 0:W]), inc=cm.act)
        a2 = P.emit("s", lambda s: s.copy(rws[0:1, W:2 * W], cm.pbR2[0:1, 0:W]),
                    inc=cm.act)
        P.wait("v", cm.act, a2)
        P.emit("v", lambda v: v.tensor_scalar(rws[0:1, 0:W], rws[0:1, 0:W], 1.0 / n,
                                              None, op0=ALU.mult), inc=cm.dve)
        P.emit("v", lambda v: v.tensor_scalar(rws[0:1, W:2 * W], rws[0:1, W:2 * W],
                                              1.0 / n, None, op0=ALU.mult), inc=cm.dve)
        P.emit("v", lambda v: v.tensor_tensor(rws[0:1, 2 * W:3 * W], rws[0:1, 0:W],
                                              rws[0:1, 0:W], ALU.mult), inc=cm.dve)
        P.emit("v", lambda v: v.tensor_tensor(rws[0:1, W:2 * W], rws[0:1, W:2 * W],
                                              rws[0:1, 2 * W:3 * W], ALU.subtract),
               inc=cm.dve)
        e5 = P.emit("v", lambda v: v.tensor_scalar(rws[0:1, W:2 * W], rws[0:1, W:2 * W],
                                                   float(eps), None, op0=ALU.add),
                    inc=cm.dve)
        P.wait("s", cm.dve, e5)
        a3 = P.emit("s", lambda s: s.activation(out=rws[0:1, W:2 * W],
                                                in_=rws[0:1, W:2 * W], func=AF.Sqrt),
                    inc=cm.act)
        P.wait("v", cm.act, a3)
        P.emit("v", lambda v: v.reciprocal(rws[0:1, W:2 * W], rws[0:1, W:2 * W]),
               inc=cm.dve)
        e7 = P.emit("v", lambda v: v.scalar_tensor_tensor(
            rws[0:1, 0:W], rws[0:1, 0:W], -1.0, rws[0:1, W:2 * W],
            op0=ALU.mult, op1=ALU.mult), inc=cm.dve)
        P.wait("t", cm.dve, e7)
        bcast_row(P, cm, rws[0:1, W:2 * W], cm.pbB, aB, W)
        _, abx = bcast_row(P, cm, rws[0:1, 0:W], cm.pbB, bB, W)
        P.wait("v", cm.act, abx)
        el = 0
        for ch in range(nchunks):
            src = src_fn(ch)
            dst = dst_fn(ch)
            P.emit("v", lambda v, src=src, dst=dst: v.tensor_tensor(
                dst, src, aB[:, :], ALU.mult), inc=cm.dve)
            el = P.emit("v", lambda v, dst=dst: v.tensor_tensor(
                dst, dst, bB[:, :], ALU.add), inc=cm.dve)
        return el

    P.waitall("y", ss2)
    P.waitall("g", ss2)
    P.wait("v", cm.act, max(rdy1, rdy2))
    P.wait("s", cm.act, max(rdy1, rdy2))
    res_prev = 0
    s_res = cm.newsem("y")
    for s_ in range(NSLF):
        colt = slice(W * s_, W * (s_ + 1))
        pln.begin()
        for i in range(16):
            if res_prev:
                P.wait("y", cm.dve, res_prev)
                res_prev = 0
            rt = P.emit("y", lambda y, i=i, colt=colt: y.dma_start(
                out=pln.Rs[:, W * i:W * (i + 1)], in_=xin[128 * i:128 * (i + 1), colt]),
                inc=s_res, n=16)
            P.waitall("v", s_res)
            res_prev = P.emit("v", lambda v, i=i: v.engine_nop(), inc=cm.dve)
        P.wait("s", cm.dve, cm.dve.n)
        eln = inner_ln(16, lambda ch: pln.Rs[:, W * ch:W * (ch + 1)],
                       lambda ch: XN[:, W * ch:W * (ch + 1)], EPS3, None)
        for i in range(16):
            if i == 0:
                e = amax_start(P, cm, acc[:, :], acc2[:, :], XN[:, 0:W])
            else:
                e = amax_step(P, cm, acc[:, :], acc2[:, :],
                              XN[:, W * i:W * (i + 1)])
        amax_finish(P, cm, acc[:, :], acc2[:, :])
        nch = W // 128
        elast = part_reduce_max(P, cm, acc, W, cols[:, 0:nch], cm.pbB2, j0=0)
        P.emit("v", lambda v: v.tensor_scalar(
            cols[:, nch:2 * nch], cols[:, 0:nch], 1.0 / QB, None, op0=ALU.mult),
            inc=cm.dve)
        P.emit("v", lambda v: v.reciprocal(cols[:, 0:nch], cols[:, 0:nch]), inc=cm.dve)
        e4 = P.emit("v", lambda v: v.tensor_scalar(
            cols[:, 0:nch], cols[:, 0:nch], QB, None, op0=ALU.mult), inc=cm.dve)
        P.wait("t", cm.dve, e4)
        P.wait("t", cm.act, cm.act.n)
        tl = None
        for j in range(nch):
            tl = P.emit("t", lambda t, j=j: t.transpose(
                cm.pbB2[0:1, 128 * j:128 * (j + 1)], cols[:, j:j + 1],
                cm.identf[0:128, 0:128]), inc=cm.mm)
            tl = P.emit("t", lambda t, j=j: t.transpose(
                cm.pbB[0:1, 128 * j:128 * (j + 1)],
                cols[:, nch + j:nch + j + 1], cm.identf[0:128, 0:128]), inc=cm.mm)
        P.wait("s", cm.mm, tl)
        a1 = P.emit("s", lambda s: s.copy(srow[0:1, 0:W], cm.pbB2[0:1, 0:W]),
                    inc=cm.act)
        a1 = P.emit("s", lambda s: s.copy(srow[0:1, W:2 * W], cm.pbB[0:1, 0:W]),
                    inc=cm.act)
        P.wait("t", cm.act, a1)
        bcast_row(P, cm, srow[0:1, 0:W], cm.pbB, sB, W)
        _, ab = bcast_row(P, cm, srow[0:1, W:2 * W], cm.pbB, invsB, W)
        P.wait("v", cm.act, ab)
        CL = float(QB - EPS3)
        for i in range(16):
            P.emit("v", lambda v, i=i: v.tensor_tensor(
                XN[:, W * i:W * (i + 1)], XN[:, W * i:W * (i + 1)], sB[:, :],
                ALU.mult), inc=cm.dve)
            P.emit("v", lambda v, i=i: v.tensor_scalar(
                XN[:, W * i:W * (i + 1)], XN[:, W * i:W * (i + 1)], CL, -CL,
                op0=ALU.min, op1=ALU.max), inc=cm.dve)
            e3 = P.emit("v", lambda v, i=i: v.tensor_tensor(
                XN[:, W * i:W * (i + 1)], XN[:, W * i:W * (i + 1)], invsB[:, :],
                ALU.mult), inc=cm.dve)
        xq_done = e3
        P.wait("t", cm.dve, xq_done)
        P.wait("t", cm.act, cm.act.n)
        bank_free_dve = 0
        for hb in range(16):
            for i in range(16):
                slot = (hb * 16 + i) % 8
                if ring_cons[slot]:
                    P.wait("y", cm.mm, ring_cons[slot])
                P.emit("y", lambda y, hb=hb, i=i, slot=slot: y.dma_start(
                    out=wring[:, 512 * slot:512 * (slot + 1)],
                    in_=sc1[512 * hb:512 * (hb + 1), 128 * i:128 * (i + 1)],
                    transpose=True), inc=ring_sems[slot], n=16)
                P.waitall("t", ring_sems[slot])
                if i == 0 and bank_free_dve:
                    P.wait("t", cm.dve, bank_free_dve)
                    P.wait("t", cm.act, cm.act.n)
                for oc in range(4):
                    m1 = P.emit("t", lambda t, slot=slot, oc=oc, i=i: t.matmul(
                        pb[4 + oc][0:128, 0:W],
                        wring[:, 512 * slot + 128 * oc:512 * slot + 128 * (oc + 1)],
                        XN[:, W * i:W * (i + 1)],
                        start=(i == 0), stop=(i == 15)), inc=cm.mm)
                ring_cons[slot] = m1
            P.wait("s", cm.mm, m1)
            for oc in range(4):
                ch = hb * 4 + oc
                a1 = P.emit("s", lambda s, oc=oc, ch=ch: s.activation(
                    out=H[:, W * ch:W * (ch + 1)], in_=pb[4 + oc][0:128, 0:W],
                    func=AF.Gelu, scale=oscale1), inc=cm.act)
                P.wait("v", cm.act, a1)
                bank_free_dve = P.emit("v", lambda v: v.engine_nop(), inc=cm.dve)
        P.wait("s", cm.dve, cm.dve.n)
        eln2 = inner_ln(64, lambda ch: H[:, W * ch:W * (ch + 1)],
                        lambda ch: H[:, W * ch:W * (ch + 1)], EPS3, None)
        for i in range(64):
            if i == 0:
                e = amax_start(P, cm, acc[:, :], acc2[:, :], H[:, 0:W])
            else:
                e = amax_step(P, cm, acc[:, :], acc2[:, :],
                              H[:, W * i:W * (i + 1)])
        amax_finish(P, cm, acc[:, :], acc2[:, :])
        elast = part_reduce_max(P, cm, acc, W, cols[:, 0:nch], cm.pbB2, j0=0)
        P.emit("v", lambda v: v.tensor_scalar(
            cols[:, nch:2 * nch], cols[:, 0:nch], 1.0 / QB, None, op0=ALU.mult),
            inc=cm.dve)
        P.emit("v", lambda v: v.reciprocal(cols[:, 0:nch], cols[:, 0:nch]), inc=cm.dve)
        e4 = P.emit("v", lambda v: v.tensor_scalar(
            cols[:, 0:nch], cols[:, 0:nch], QB, None, op0=ALU.mult), inc=cm.dve)
        P.wait("t", cm.dve, e4)
        P.wait("t", cm.act, cm.act.n)
        for j in range(nch):
            tl = P.emit("t", lambda t, j=j: t.transpose(
                cm.pbB2[0:1, 128 * j:128 * (j + 1)], cols[:, j:j + 1],
                cm.identf[0:128, 0:128]), inc=cm.mm)
            tl = P.emit("t", lambda t, j=j: t.transpose(
                cm.pbB[0:1, 128 * j:128 * (j + 1)],
                cols[:, nch + j:nch + j + 1], cm.identf[0:128, 0:128]), inc=cm.mm)
        P.wait("s", cm.mm, tl)
        a1 = P.emit("s", lambda s: s.copy(srow[0:1, 0:W], cm.pbB2[0:1, 0:W]),
                    inc=cm.act)
        a1 = P.emit("s", lambda s: s.copy(srow[0:1, W:2 * W], cm.pbB[0:1, 0:W]),
                    inc=cm.act)
        P.wait("t", cm.act, a1)
        bcast_row(P, cm, srow[0:1, 0:W], cm.pbB, sB, W)
        _, ab = bcast_row(P, cm, srow[0:1, W:2 * W], cm.pbB, invsB, W)
        P.wait("v", cm.act, ab)
        for ch in range(64):
            P.emit("v", lambda v, ch=ch: v.tensor_tensor(
                H[:, W * ch:W * (ch + 1)], H[:, W * ch:W * (ch + 1)], sB[:, :],
                ALU.mult), inc=cm.dve)
            P.emit("v", lambda v, ch=ch: v.tensor_scalar(
                H[:, W * ch:W * (ch + 1)], H[:, W * ch:W * (ch + 1)], CL, -CL,
                op0=ALU.min, op1=ALU.max), inc=cm.dve)
            e3 = P.emit("v", lambda v, ch=ch: v.tensor_tensor(
                H[:, W * ch:W * (ch + 1)], H[:, W * ch:W * (ch + 1)], invsB[:, :],
                ALU.mult), inc=cm.dve)
        hq_done = e3
        P.wait("t", cm.dve, hq_done)
        pln.scale_col = oscale2
        for ob in range(4):
            for i in range(64):
                slot = i % 8
                if ring_cons[slot]:
                    P.wait("y", cm.mm, ring_cons[slot])
                P.emit("y", lambda y, ob=ob, i=i, slot=slot: y.dma_start(
                    out=wring[:, 512 * slot:512 * (slot + 1)],
                    in_=sc2[512 * ob:512 * (ob + 1), 128 * i:128 * (i + 1)],
                    transpose=True), inc=ring_sems[slot], n=16)
                P.waitall("t", ring_sems[slot])
                for oc in range(4):
                    m1 = P.emit("t", lambda t, slot=slot, oc=oc, i=i: t.matmul(
                        pb[4 + oc][0:128, 0:W],
                        wring[:, 512 * slot + 128 * oc:512 * slot + 128 * (oc + 1)],
                        H[:, W * i:W * (i + 1)],
                        start=(i == 0), stop=(i == 63)), inc=cm.mm)
                ring_cons[slot] = m1
            oproj_mm = m1
            for oc in range(4):
                chn = ob * 4 + oc
                P.wait("v", cm.mm, oproj_mm)
                ee = pln.add_chunk(chn, pb[4 + oc][0:128, 0:W])
            P.wait("t", cm.dve, ee)
        pln.finalize(dst_dram_ap=out_d[:, W * s_:W * (s_ + 1)])
        res_prev = ee
    P.wait("y", cm.st, cm.st.n)


def build_l3_ffn():
    nc = bass.Bass("TRN2", target_bir_lowering=False)
    io = {
        "xin": _ei(nc, "xin", [DM, TOK]),
        "wf": _ei(nc, "wf", [8192, 2048]), "wp": _ei(nc, "wp", [2048, 8192]),
        "out": _eo(nc, "out", [DM, TOK]),
    }
    return _single(nc, body_l3_ffn, io)



def build_g1():
    nc = bass.Bass("TRN2", target_bir_lowering=False)
    x0i = nc.dram_tensor("x0i", [DM, TOK], F32, kind="Internal").ap()
    io_a = {
        "xctx": _ei(nc, "xctx", [DM, 4096]),
        "qw": _ei(nc, "qw", [2048, 2048]), "kw": _ei(nc, "kw", [256, 2048]),
        "vw": _ei(nc, "vw", [256, 2048]), "ow": _ei(nc, "ow", [2048, 2048]),
        "qn": _ei(nc, "qn", [1, 256]), "kn": _ei(nc, "kn", [1, 256]),
        "cc": _ei(nc, "cc", [1, 8]),
        "out": x0i,
    }
    io_b = {
        "xin": x0i,
        "w1": _ei(nc, "w1", [8192, 2048]), "w2": _ei(nc, "w2", [2048, 8192]),
        "out": _eo(nc, "out", [DM, TOK]),
    }
    return _multi(nc, [(body_l0_attn, io_a, {}), (body_l0_ffn, io_b, {})])


def build_g2():
    nc = bass.Bass("TRN2", target_bir_lowering=False)
    x2i = nc.dram_tensor("x2i", [DM, TOK], F32, kind="Internal").ap()
    io_a = {
        "xbuf": _ei(nc, "xbuf", [DM, 3072]),
        "qw": _ei(nc, "qw", [2048, 2048]), "kw": _ei(nc, "kw", [512, 2048]),
        "vw": _ei(nc, "vw", [512, 2048]), "ow": _ei(nc, "ow", [2048, 2048]),
        "cc": _ei(nc, "cc", [1, 8]),
        "out": x2i,
    }
    io_b = {
        "xin": x2i,
        "w1": _ei(nc, "w1", [8192, 2048]), "w2": _ei(nc, "w2", [2048, 8192]),
        "w3": _ei(nc, "w3", [8192, 2048]),
        "out": _eo(nc, "xout", [DM, TOK]),
    }
    io_c = {
        "xin": io_b["out"],
        "wq": _ei(nc, "wq", [2048, 2048]), "wk": _ei(nc, "wk", [2048, 2048]),
        "wv": _ei(nc, "wv", [2048, 2048]), "ld": _ei(nc, "ld", [1, 16]),
        "o_out": _eo(nc, "o_out", [TOK, DM], BF16),
        "q_out": _eo(nc, "q_out", [DM, TOK], BF16),
        "s_out": _eo(nc, "s_out", [16 * 128, 128]),
    }
    return _multi(nc, [(body_l1_attn, io_a, {}), (body_l1_ffn, io_b, {}),
                       (body_l2a, io_c, {})])


def build_g3():
    nc = bass.Bass("TRN2", target_bir_lowering=False)
    x4i = nc.dram_tensor("x4i", [DM, TOK], F32, kind="Internal").ap()
    io_a = {
        "xin": _ei(nc, "xin", [DM, TOK]),
        "o_in": _ei(nc, "o_in", [TOK, DM], BF16),
        "q_in": _ei(nc, "q_in", [DM, TOK], BF16),
        "s0": _ei(nc, "s0", [16 * 128, 128]), "ld": _ei(nc, "ld", [1, 16]),
        "wg1": _ei(nc, "wg1", [2048, 128]), "wg2": _ei(nc, "wg2", [128, 2048]),
        "nw": _ei(nc, "nw", [1, 2048]), "wo": _ei(nc, "wo", [2048, 2048]),
        "out": x4i,
    }
    io_b = {
        "xin": x4i,
        "wg": _ei(nc, "wg", [6464, 2048]), "wu": _ei(nc, "wu", [6464, 2048]),
        "wd": _ei(nc, "wd", [2048, 6464]),
        "out": _eo(nc, "xout", [DM, TOK]),
    }
    io_c = {
        "xin": io_b["out"],
        "wq": _ei(nc, "wq3", [2048, 2048]), "wk": _ei(nc, "wk3", [2048, 2048]),
        "wv": _ei(nc, "wv3", [2048, 2048]),
        "bq": _ei(nc, "bq", [1, 2048]), "bk": _ei(nc, "bk", [1, 2048]),
        "bv": _ei(nc, "bv", [1, 2048]), "cc": _ei(nc, "cc", [1, 8]),
        "q_out": _eo(nc, "q_out", [DM, TOK], BF16),
        "kv_out": _eo(nc, "kv_out", [16 * 128, 128]),
        "ks_out": _eo(nc, "ks_out", [1, 2048]),
    }
    return _multi(nc, [(body_l2b, io_a, {}), (body_l2_ffn, io_b, {}),
                       (body_l3a, io_c, {})])


def build_g4():
    nc = bass.Bass("TRN2", target_bir_lowering=False)
    x6i = nc.dram_tensor("x6i", [DM, TOK], F32, kind="Internal").ap()
    io_a = {
        "xin": _ei(nc, "xin", [DM, TOK]),
        "q_in": _ei(nc, "q_in", [DM, TOK], BF16),
        "kva": _ei(nc, "kva", [16 * 128, 128]),
        "kvb": _ei(nc, "kvb", [16 * 128, 128]),
        "ksa": _ei(nc, "ksa", [1, 2048]), "ksb": _ei(nc, "ksb", [1, 2048]),
        "wo": _ei(nc, "wo", [2048, 2048]), "bo": _ei(nc, "bo", [1, 2048]),
        "out": x6i,
    }
    io_b = {
        "xin": x6i,
        "wf": _ei(nc, "wf", [8192, 2048]), "wp": _ei(nc, "wp", [2048, 8192]),
        "out": _eo(nc, "out", [DM, TOK]),
    }
    return _multi(nc, [(body_l3b, io_a, {}), (body_l3_ffn, io_b, {})])


_NC_CACHE = {}


def _prog(name, builder):
    if name not in _NC_CACHE:
        _NC_CACHE[name] = builder()
    return _NC_CACHE[name]



_PERMQ = (np.arange(16)[:, None] * 128 +
          np.concatenate([np.arange(0, 128, 2),
                          np.arange(1, 128, 2)])[None, :]).reshape(-1)
_PERMK = (np.arange(4)[:, None] * 128 +
          np.concatenate([np.arange(0, 128, 2),
                          np.arange(1, 128, 2)])[None, :]).reshape(-1)


def _cf(a):
    return np.ascontiguousarray(np.asarray(a, dtype=np.float32))


def kernel(**inputs):
    cores = list(range(8))
    BH = [(c // 2, 2048 * (c % 2)) for c in cores]

    x = _cf(inputs["L0_attn_x"])

    qw = _cf(inputs["L0_attn_q_proj"]); kw = _cf(inputs["L0_attn_k_proj"])
    vw = _cf(inputs["L0_attn_v_proj"]); ow = _cf(inputs["L0_attn_o_proj"])
    qn = _cf(inputs["L0_attn_q_norm"]).reshape(1, 256)
    kn = _cf(inputs["L0_attn_k_norm"]).reshape(1, 256)
    w1 = _cf(inputs["L0_ffn_w1"]); w2 = _cf(inputs["L0_ffn_w2"])
    nc = _prog("g1", build_g1)
    maps = []
    for c in cores:
        b, B0 = BH[c]
        c2 = 2048 - B0
        xc = np.zeros((2048, 4096), np.float32)
        xc[:, c2:] = x[b, 0:4096 - c2, :].T
        maps.append({"xctx": xc, "qw": qw, "kw": kw, "vw": vw, "ow": ow,
                     "qn": qn, "kn": kn,
                     "cc": np.array([[c2, B0, 0, 0, 0, 0, 0, 0]], np.float32),
                     "w1": w1, "w2": w2})
    res = run_bass_kernel_spmd(nc, maps, core_ids=cores)
    s = [np.asarray(res.results[c]["out"]) for c in cores]

    qp = _cf(inputs["L1_attn_wq"])[_PERMQ]
    kp = _cf(inputs["L1_attn_wk"])[_PERMK]
    vw1 = _cf(inputs["L1_attn_wv"]); ow1 = _cf(inputs["L1_attn_wo"])
    W1 = _cf(inputs["L1_ffn_W1"]); W2 = _cf(inputs["L1_ffn_W2"])
    W3 = _cf(inputs["L1_ffn_W3"])
    wq2 = _cf(inputs["L2_attn_Wq"]); wk2 = _cf(inputs["L2_attn_Wk"])
    wv2 = _cf(inputs["L2_attn_Wv"])
    ld2 = _cf(inputs["L2_attn_log_decay"]).reshape(1, 16)
    nc = _prog("g2", build_g2)
    maps = []
    for c in cores:
        b, B0 = BH[c]
        c2 = 1024 - B0
        halo = (s[c - 1][:, 1024:2048] if B0 >= 1024
                else np.zeros((2048, 1024), np.float32))
        xbuf = np.ascontiguousarray(np.concatenate([halo, s[c]], axis=1))
        maps.append({"xbuf": xbuf, "qw": qp, "kw": kp, "vw": vw1, "ow": ow1,
                     "cc": np.array([[c2, c2, 0, 0, 0, 0, 0, 0]], np.float32),
                     "w1": W1, "w2": W2, "w3": W3,
                     "wq": wq2, "wk": wk2, "wv": wv2, "ld": ld2})
    res = run_bass_kernel_spmd(nc, maps, core_ids=cores)
    s = [np.asarray(res.results[c]["xout"]) for c in cores]
    l2 = [{k: np.asarray(res.results[c][k]) for k in ("o_out", "q_out", "s_out")}
          for c in cores]

    wg1 = _cf(inputs["L2_attn_Wg1"]); wg2 = _cf(inputs["L2_attn_Wg2"])
    nw2 = _cf(inputs["L2_attn_norm_w"]).reshape(1, 2048)
    wo2 = _cf(inputs["L2_attn_Wo"])
    wg = _cf(inputs["L2_ffn_w_gate"]); wu = _cf(inputs["L2_ffn_w_up"])
    wd = _cf(inputs["L2_ffn_w_down"])
    wq3 = _cf(inputs["L3_attn_Wq"]); wk3 = _cf(inputs["L3_attn_Wk"])
    wv3 = _cf(inputs["L3_attn_Wv"])
    bq3 = _cf(inputs["L3_attn_bq"]).reshape(1, 2048)
    bk3 = _cf(inputs["L3_attn_bk"]).reshape(1, 2048)
    bv3 = _cf(inputs["L3_attn_bv"]).reshape(1, 2048)
    nc = _prog("g3", build_g3)
    maps = []
    for c in cores:
        b, B0 = BH[c]
        s0 = (l2[c - 1]["s_out"] if c % 2 == 1
              else np.zeros((2048, 128), np.float32))
        maps.append({"xin": s[c], "o_in": l2[c]["o_out"], "q_in": l2[c]["q_out"],
                     "s0": s0, "ld": ld2, "wg1": wg1, "wg2": wg2, "nw": nw2,
                     "wo": wo2, "wg": wg, "wu": wu, "wd": wd,
                     "wq3": wq3, "wk3": wk3, "wv3": wv3,
                     "bq": bq3, "bk": bk3, "bv": bv3,
                     "cc": np.array([[-B0 / 4096.0, 0, 0, 0, 0, 0, 0, 0]],
                                    np.float32)})
    res = run_bass_kernel_spmd(nc, maps, core_ids=cores)
    s = [np.asarray(res.results[c]["xout"]) for c in cores]
    l3 = [{k: np.asarray(res.results[c][k]) for k in ("q_out", "kv_out", "ks_out")}
          for c in cores]

    wo3 = _cf(inputs["L3_attn_Wo"])
    bo3 = _cf(inputs["L3_attn_bo"]).reshape(1, 2048)
    wf = _cf(inputs["L3_ffn_w_fc"]); wp = _cf(inputs["L3_ffn_w_proj"])
    nc = _prog("g4", build_g4)
    maps = []
    for c in cores:
        p = c ^ 1
        maps.append({"xin": s[c], "q_in": l3[c]["q_out"],
                     "kva": l3[c]["kv_out"], "kvb": l3[p]["kv_out"],
                     "ksa": l3[c]["ks_out"], "ksb": l3[p]["ks_out"],
                     "wo": wo3, "bo": bo3, "wf": wf, "wp": wp})
    res = run_bass_kernel_spmd(nc, maps, core_ids=cores)
    s = [np.asarray(res.results[c]["out"]) for c in cores]

    out = np.empty((4, 4096, 2048), np.float32)
    for c in cores:
        b, B0 = BH[c]
        out[b, B0:B0 + 2048, :] = s[c].T
    return out


# version 07
# speedup vs baseline: 1.0090x; 1.0090x over previous
import math
from contextlib import ExitStack

import numpy as np

import concourse.bass as bass
import concourse.mybir as mybir
from concourse.bass_utils import run_bass_kernel_spmd

F32 = mybir.dt.float32
BF16 = mybir.dt.bfloat16
AF = mybir.ActivationFunctionType
ALU = mybir.AluOpType
AX = mybir.AxisListType.X

TOK = 2048
DM = 2048
NSL = 4
SL = 512
RNE = 12582912.0
NEG = -30000.0


class Sem:
    __slots__ = ("h", "n")

    def __init__(self, h):
        self.h = h
        self.n = 0


class Prog:

    def __init__(self, nc):
        self.nc = nc
        self.th = {k: [] for k in "tvsgy"}
        self.ctx = ExitStack()
        self.cur_ctx = self.ctx
        self.prefix = ""
        self._i = 0
        self.autosem = {}
        self.last = {}

    def begin_phase(self, tag=""):
        assert self.cur_ctx is self.ctx
        self.cur_ctx = ExitStack()
        self.prefix = tag

    def end_phase(self):
        self.cur_ctx.close()
        self.cur_ctx = self.ctx
        self.prefix = ""

    def barrier(self, cm):
        sems = [cm.ld, cm.gld, cm.mm, cm.act, cm.dve, cm.pool, cm.st]
        sems += [s for s in cm.aux if s.n > 0]
        for eng in "tvsgy":
            for s in sems:
                if s.n > 0:
                    self.wait(eng, s, s.n)

    def sem(self, name=None):
        self._i += 1
        return Sem(self.ctx.enter_context(self.nc.semaphore(name or f"sm{self._i}")))

    def sb(self, name, shape, dt=F32):
        return self.cur_ctx.enter_context(
            self.nc.sbuf_tensor(self.prefix + name, shape, dt))

    def ps(self, name, shape, dt=F32):
        return self.cur_ctx.enter_context(
            self.nc.psum_tensor(self.prefix + name, shape, dt))

    def dram(self, name, shape, dt=BF16):
        return self.nc.dram_tensor(self.prefix + name, shape, dt,
                                   kind="Internal").ap()

    def emit(self, eng, fn, inc=None, n=1, dma=False):
        auto = self.autosem.get(eng) if not dma else None
        if auto is not None:
            prev = self.last.get(eng)
            if prev is not None:
                self.th[eng].append(lambda e, h=prev[0], v=prev[1]: e.wait_ge(h, v))
            if inc is None:
                inc = auto
        if inc is not None:
            inc.n += n
            tgt = inc.n
            self.th[eng].append(lambda e, fn=fn, h=inc.h, n=n: fn(e).then_inc(h, n))
            if auto is not None:
                self.last[eng] = (inc.h, tgt)
            return tgt
        self.th[eng].append(fn)
        return None

    def wait(self, eng, sem, val=None):
        v = sem.n if val is None else val
        self.th[eng].append(lambda e, h=sem.h, v=v: e.wait_ge(h, v))

    def waitall(self, eng, sem):
        self.th[eng].append(lambda e, h=sem.h, v=sem.n: e.wait_ge(h, v))

    def finish(self):
        with self.nc.Block() as block:
            block.tensor(lambda e: [f(e) for f in self.th["t"]])
            block.vector(lambda e: [f(e) for f in self.th["v"]])
            block.scalar(lambda e: [f(e) for f in self.th["s"]])
            block.gpsimd(lambda e: [f(e) for f in self.th["g"]])
            block.sync(lambda e: [f(e) for f in self.th["y"]])



class Common:

    def __init__(self, P: Prog, n_dram_out=1):
        self.P = P
        self.ld = P.sem("ld")
        self.gld = P.sem("gld")
        self.mm = P.sem("mm")
        self.act = P.sem("act")
        self.dve = P.sem("dve")
        self.pool = P.sem("pool")
        self.st = P.sem("st")
        self.aux = [P.sem(f"aux{i}") for i in range(88)]
        self._free = list(range(88))
        self._pools = {"y": [], "g": [], "c": []}
        self._cursor = {"y": 0, "g": 0, "c": 0}
        P.autosem = {"v": self.dve, "s": self.act, "g": self.pool}

    def newsem(self, kind="y"):
        cur = self._cursor[kind]
        self._cursor[kind] = cur + 1
        pool = self._pools[kind]
        if cur < len(pool):
            return pool[cur]
        s = self.aux[self._free.pop(0)]
        pool.append(s)
        return s

    def reset_aux(self):
        self._cursor = {"y": 0, "g": 0, "c": 0}

    def build_basics(self, P):
        self.iotaPF = P.sb("iotaPF", [128, SL], F32)
        self.iotaP = P.sb("iotaP", [128, 1], F32)
        self.iotaF = P.sb("iotaF", [1, SL], F32)
        P.emit("g", lambda g: g.iota(self.iotaPF[:, :], pattern=[[-1, SL]], base=0,
                                     channel_multiplier=1,
                                     allow_small_or_imprecise_dtypes=True), inc=self.pool)
        P.emit("g", lambda g: g.iota(self.iotaP[:, :], pattern=[[0, 1]], base=0,
                                     channel_multiplier=1,
                                     allow_small_or_imprecise_dtypes=True), inc=self.pool)
        P.emit("g", lambda g: g.iota(self.iotaF[:, :], pattern=[[1, SL]], base=0,
                                     channel_multiplier=0,
                                     allow_small_or_imprecise_dtypes=True), inc=self.pool)
        self.pool_basics = self.pool.n
        self.ones1_128 = P.sb("ones1_128", [1, 128], F32)
        self.ones128_1b = P.sb("ones128_1b", [128, 1], BF16)
        self.ones1_128b = P.sb("ones1_128b", [1, 128], BF16)
        P.emit("v", lambda v: v.memset(self.ones1_128[:, :], 1.0), inc=self.dve)
        P.emit("v", lambda v: v.memset(self.ones128_1b[:, :], 1.0), inc=self.dve)
        P.emit("v", lambda v: v.memset(self.ones1_128b[:, :], 1.0), inc=self.dve)
        self.cst = P.sb("cst", [128, 32], F32)
        self.n_cst = 0
        self.dve_basics = self.dve.n

    def const_col(self, val):
        P = self.P
        j = self.n_cst
        self.n_cst += 1
        assert j < 32
        ap = self.cst[:, j:j + 1]
        P.emit("v", lambda v, ap=ap, val=float(val): v.memset(ap, val), inc=self.dve)
        return ap


def bcast_row(P, cm, row_ap, out_psum, out_sbuf, width, sdep=()):
    P.wait("t", cm.act, cm.act.n)
    t1 = P.emit("t", lambda t: t.matmul(out_psum[0:128, 0:width], cm.ones1_128[:, :],
                                        row_ap, start=True, stop=True), inc=cm.mm)
    P.wait("s", cm.mm, t1)
    t2 = P.emit("s", lambda s: s.copy(out_sbuf[0:128, 0:width], out_psum[0:128, 0:width]),
                inc=cm.act)
    return t1, t2


def part_reduce_max(P, cm, acc, width, cols_out, psum_t, j0=0, absolute=False):
    nch = width // 128
    P.wait("t", cm.dve, cm.dve.n)
    tlast = None
    for j in range(nch):
        tlast = P.emit("t", lambda t, j=j: t.transpose(
            psum_t[0:128, j0 + 128 * j:j0 + 128 * (j + 1)],
            acc[0:128, 128 * j:128 * (j + 1)], cm.identf[0:128, 0:128]), inc=cm.mm)
    P.wait("v", cm.mm, tlast)
    elast = None
    for j in range(nch):
        elast = P.emit("v", lambda v, j=j: v.reduce_max(
            cols_out[0:128, j:j + 1], psum_t[0:128, j0 + 128 * j:j0 + 128 * (j + 1)],
            axis=AX, apply_absolute_value=absolute), inc=cm.dve)
    return elast


def amax_start(P, cm, mx, mn, src):
    P.emit("v", lambda v: v.tensor_copy(mx, src), inc=cm.dve)
    return P.emit("v", lambda v: v.tensor_copy(mn, src), inc=cm.dve)


def amax_step(P, cm, mx, mn, src):
    P.emit("v", lambda v: v.tensor_tensor(mx, mx, src, ALU.max), inc=cm.dve)
    return P.emit("v", lambda v: v.tensor_tensor(mn, mn, src, ALU.min), inc=cm.dve)


def amax_finish(P, cm, mx, mn):
    P.emit("v", lambda v: v.tensor_scalar(mn, mn, -1.0, None, op0=ALU.mult),
           inc=cm.dve)
    return P.emit("v", lambda v: v.tensor_tensor(mx, mx, mn, ALU.max), inc=cm.dve)


def transpose_col_to_row(P, cm, col_ap, psum_ap, row_out, n=128):
    t1 = P.emit("t", lambda t: t.transpose(psum_ap[0:1, 0:n], col_ap, cm.identf[0:n, 0:n]),
                inc=cm.mm)
    P.wait("s", cm.mm, t1)
    t2 = P.emit("s", lambda s: s.copy(row_out[0:1, 0:n], psum_ap[0:1, 0:n]), inc=cm.act)
    return t2


def build_idents(P, cm):
    cm.identf = P.sb("identf", [128, 128], F32)
    cm.identb = P.sb("identb", [128, 128], BF16)
    cm.ident32f = cm.identf
    P.wait("v", cm.pool, cm.pool_basics)
    e1 = P.emit("v", lambda v: v.tensor_scalar(
        out=cm.identf[:, :], in0=cm.iotaPF[:, 0:128], scalar1=0.0, scalar2=None,
        op0=ALU.is_equal), inc=cm.dve)
    e2 = P.emit("v", lambda v: v.tensor_scalar(
        out=cm.identb[:, :], in0=cm.iotaPF[:, 0:128], scalar1=0.0, scalar2=None,
        op0=ALU.is_equal), inc=cm.dve)
    return e2



def w_to_scratch(P, cm, name, w_ap, O, I, buf=None, prev=None):
    sc = P.dram(f"sc_{name}", [O, I], BF16)
    if buf is None:
        buf = P.sb(f"wsc_{name}", [128, 2 * I], BF16)
    if prev is not None:
        P.wait("g", prev, prev.n)
    ssem = cm.newsem("y")
    lsem = [cm.newsem("g"), cm.newsem("g")]
    CW = min(I, 2048)
    tiles = [(o0, c0) for o0 in range(0, O, 128) for c0 in range(0, I, CW)]
    for j, (o0, c0) in enumerate(tiles):
        half = (j % 2) * CW
        b = buf[:, half:half + CW]
        cw = min(CW, I - c0)
        if j >= 2:
            P.waitall("g", ssem)
        tgt = P.emit("g", lambda g, b=b, cw=cw, s=w_ap[o0:o0 + 128, c0:c0 + CW]:
                     g.dma_start(out=b[:, 0:cw] if cw < CW else b, in_=s),
                     inc=lsem[j % 2], n=16, dma=True)
        P.wait("y", lsem[j % 2], tgt)
        P.emit("y", lambda y, b=b, cw=cw, d=sc[o0:o0 + 128, c0:c0 + CW]:
               y.dma_start(out=d, in_=b[:, 0:cw] if cw < CW else b), inc=ssem, n=16)
    return sc, ssem


def w_to_scratch_ternary(P, cm, name, w_ap, O, I, eps, mode, buf, f32buf, psA, psB,
                         prev_ssem=None, Opad=None, Ipad=None):
    Opad = Opad or O
    Ipad = Ipad or I
    sc = P.dram(f"sc_{name}", [Opad, Ipad], BF16)
    if prev_ssem is not None:
        P.waitall("g", prev_ssem)
        P.waitall("y", prev_ssem)
    CW = min(I, 2048)
    tiles = [(o0, c0) for o0 in range(0, O, 128) for c0 in range(0, I, CW)]
    row = P.sb(f"wqr_{name}", [1, 520], F32)
    gcol = P.sb(f"wqg_{name}", [128, 4], F32)
    lsem = [cm.newsem("g"), cm.newsem("g")]
    first = True
    mm_abs = 0
    P.wait("t", cm.dve, cm.dve.n)
    nslice = CW // 512
    for j, (o0, c0) in enumerate(tiles):
        half = (j % 2) * CW
        cw = min(CW, I - c0)
        ow = min(128, O - o0)
        b = buf[0:ow, half:half + cw]
        if j >= 2:
            P.wait("g", cm.mm, mm_abs)
        tgt = P.emit("g", lambda g, b=b, s=w_ap[o0:o0 + ow, c0:c0 + cw]:
                     g.dma_start(out=b, in_=s), inc=lsem[j % 2], n=16, dma=True)
        P.wait("s", lsem[j % 2], tgt)
        ev = P.emit("s", lambda s, b=b: s.activation(out=b, in_=b, func=AF.Abs),
                    inc=cm.act)
        P.wait("t", cm.act, ev)
        nsl_ = (cw + 511) // 512
        for k in range(nsl_):
            kw_ = min(512, cw - 512 * k)
            stp = (k == nsl_ - 1 and j == len(tiles) - 1)
            mm_abs = P.emit("t", lambda t, b=b, k=k, kw_=kw_, ow=ow, first=first, stp=stp: t.matmul(
                psA[0:1, 0:kw_], cm.ones128_1b[0:ow, 0:1], b[0:ow, 512 * k:512 * k + kw_],
                start=first, stop=stp), inc=cm.mm)
            first = False
    P.wait("s", cm.mm, mm_abs)
    e1 = P.emit("s", lambda s: s.copy(row[0:1, 0:512], psA[0:1, 0:512]), inc=cm.act)
    P.wait("v", cm.act, e1)
    P.emit("v", lambda v: v.tensor_reduce(row[0:1, 512:513], row[0:1, 0:512],
                                          axis=AX, op=ALU.add), inc=cm.dve)
    scale = 1.0 / (O * I)
    if mode == "wq":
        P.emit("v", lambda v: v.tensor_scalar(
            row[0:1, 513:514], row[0:1, 512:513], scale, float(eps),
            op0=ALU.mult, op1=ALU.max), inc=cm.dve)
        e4 = P.emit("v", lambda v: v.tensor_scalar(
            row[0:1, 514:515], row[0:1, 513:514], 0.5, None, op0=ALU.mult), inc=cm.dve)
    else:
        P.emit("v", lambda v: v.tensor_scalar(
            row[0:1, 513:514], row[0:1, 512:513], scale, None, op0=ALU.mult),
            inc=cm.dve)
        e4 = P.emit("v", lambda v: v.tensor_scalar(
            row[0:1, 514:515], row[0:1, 513:514], float(eps), 0.5,
            op0=ALU.add, op1=ALU.mult), inc=cm.dve)
    e5 = P.emit("v", lambda v: v.tensor_scalar(
        row[0:1, 515:516], row[0:1, 513:514], 0.5, None, op0=ALU.mult), inc=cm.dve)
    P.wait("t", cm.dve, e5)
    P.wait("t", cm.act, cm.act.n)
    t1 = P.emit("t", lambda t: t.matmul(psB[0:128, 0:2], cm.ones1_128[:, :],
                                        row[0:1, 514:516], start=True, stop=True),
                inc=cm.mm)
    P.wait("s", cm.mm, t1)
    P.emit("s", lambda s: s.copy(gcol[0:128, 0:1], psB[0:128, 0:1]), inc=cm.act)
    P.emit("s", lambda s: s.mul(gcol[0:128, 1:2], psB[0:128, 0:1], -1.0), inc=cm.act)
    rdy = P.emit("s", lambda s: s.copy(gcol[0:128, 2:3], psB[0:128, 1:2]), inc=cm.act)
    sgn2 = P.sb(f"wq4_{name}", [128, CW], BF16)
    ssem = cm.newsem("g")
    p2s = [cm.newsem("y"), cm.newsem("y")]
    eb_hist = {}
    ec_prev = 0
    P.wait("s", cm.act, rdy)
    for j, (o0, c0) in enumerate(tiles):
        half = (j % 2) * CW
        cw = min(CW, I - c0)
        ow = min(128, O - o0)
        b = f32buf[0:ow, half:half + cw]
        sg_a = buf[0:ow, half:half + cw]
        if j >= 2:
            P.wait("y", cm.act, eb_hist[j - 2])
        tgt = P.emit("y", lambda y, b=b, s=w_ap[o0:o0 + ow, c0:c0 + cw]:
                     y.dma_start(out=b, in_=s), inc=p2s[j % 2], n=16)
        P.wait("s", p2s[j % 2], tgt)
        if j >= 2:
            P.waitall("s", ssem)
        ea = P.emit("s", lambda s, b=b, o=sg_a, ow=ow: s.activation(
            out=o, in_=b, func=AF.Sign, bias=gcol[0:ow, 1:2], scale=1.0), inc=cm.act)
        if ec_prev:
            P.wait("s", cm.dve, ec_prev)
        eb = P.emit("s", lambda s, b=b, ow=ow, cw=cw: s.activation(
            out=sgn2[0:ow, 0:cw], in_=b, func=AF.Sign, bias=gcol[0:ow, 0:1], scale=1.0),
            inc=cm.act)
        eb_hist[j] = eb
        P.wait("v", cm.act, eb)
        ec_prev = P.emit("v", lambda v, a=sg_a, ow=ow, cw=cw: v.tensor_tensor(
            a, a, sgn2[0:ow, 0:cw], ALU.add), inc=cm.dve)
        P.wait("g", cm.dve, ec_prev)
        P.emit("g", lambda g, a=sg_a, d=sc[o0:o0 + ow, c0:c0 + cw]:
               g.dma_start(out=d, in_=a), inc=ssem, n=16, dma=True)
    if Opad > O or Ipad > I:
        P.waitall("v", ssem)
        ez = P.emit("v", lambda v: v.memset(buf[:, 0:2048], 0.0), inc=cm.dve)
        P.wait("g", cm.dve, ez)
        if Opad > O:
            for c0 in range(0, Ipad, 2048):
                cw = min(2048, Ipad - c0)
                P.emit("g", lambda g, c0=c0, cw=cw, dO=Opad - O: g.dma_start(
                    out=sc[O:Opad, c0:c0 + cw], in_=buf[0:Opad - O, 0:cw]),
                    inc=ssem, n=16, dma=True)
        if Ipad > I:
            for o0 in range(0, O, 128):
                P.emit("g", lambda g, o0=o0: g.dma_start(
                    out=sc[o0:min(o0 + 128, O), I:Ipad],
                    in_=buf[0:min(128, O - o0), 0:Ipad - I]), inc=ssem, n=16, dma=True)
    return sc, ssem, gcol[0:128, 2:3], rdy


def load_wT(P, cm, sc, i0, o0, oW, dst, dep_sem, dep_val, sem=None):
    P.wait("y", dep_sem, dep_val)
    return P.emit("y", lambda y: y.dma_start(
        out=dst, in_=sc[o0:o0 + oW, i0:i0 + 128], transpose=True),
        inc=(sem or cm.ld), n=16)



def build_rope_tables(P, cm, theta, hd, width, c2_scalar_ap, posmul=1.0,
                      posB_ap=None, frq_ap=None):
    half = hd // 2
    iotaW = (posB_ap[0:1, :] if posB_ap is not None
             else P.sb(f"iotaW{theta}", [1, width], F32))
    P.emit("g", lambda g: g.iota(iotaW[:, :], pattern=[[1, width]], base=0,
                                 channel_multiplier=0,
                                 allow_small_or_imprecise_dtypes=True), inc=cm.pool)
    gtgt = cm.pool.n
    posr = (frq_ap[0:1, :] if frq_ap is not None
            else P.sb(f"posr{theta}", [1, width], F32))
    P.wait("v", cm.pool, gtgt)
    if posmul != 1.0:
        e = P.emit("v", lambda v: v.tensor_scalar(
            posr[:, :], iotaW[:, :], c2_scalar_ap, float(posmul),
            op0=ALU.subtract, op1=ALU.mult), inc=cm.dve)
    else:
        e = P.emit("v", lambda v: v.tensor_scalar(
            posr[:, :], iotaW[:, :], c2_scalar_ap, None,
            op0=ALU.subtract), inc=cm.dve)
    invf = P.sb(f"invf{theta}", [128, 1], F32)
    P.wait("s", cm.pool, cm.pool_basics)
    a1 = P.emit("s", lambda s: s.activation(
        out=invf[:, :], in_=cm.iotaP[:, :], func=AF.Exp,
        scale=-2.0 * math.log(theta) / hd), inc=cm.act)
    posB = posB_ap if posB_ap is not None else P.sb(f"posB{theta}", [128, width], F32)
    pb = cm.pb_misc
    P.wait("t", cm.dve, e)
    last = None
    for k in range(width // 512):
        if last is not None:
            P.wait("t", cm.act, last)
        t1 = P.emit("t", lambda t, k=k: t.matmul(
            pb[0:128, 0:512], cm.ones1_128[:, :], posr[0:1, 512 * k:512 * (k + 1)],
            start=True, stop=True), inc=cm.mm)
        P.wait("s", cm.mm, t1)
        last = P.emit("s", lambda s, k=k: s.copy(
            posB[0:128, 512 * k:512 * (k + 1)], pb[0:128, 0:512]), inc=cm.act)
    frq = frq_ap if frq_ap is not None else P.sb(f"frq{theta}", [128, width], F32)
    P.wait("v", cm.act, last)
    e1 = P.emit("v", lambda v: v.tensor_scalar(
        frq[0:half, :], posB[0:half, :], invf[0:half, 0:1], None, op0=ALU.mult),
        inc=cm.dve)
    TWO_PI = 2.0 * math.pi
    m = posB
    e2 = P.emit("v", lambda v: v.tensor_scalar(
        m[0:half, :], frq[0:half, :], 1.0 / TWO_PI, None, op0=ALU.mult), inc=cm.dve)
    e3 = P.emit("v", lambda v: v.tensor_scalar(
        m[0:half, :], m[0:half, :], RNE, RNE, op0=ALU.add, op1=ALU.subtract), inc=cm.dve)
    e4 = P.emit("v", lambda v: v.scalar_tensor_tensor(
        m[0:half, :], m[0:half, :], -TWO_PI, frq[0:half, :], op0=ALU.mult, op1=ALU.add),
        inc=cm.dve)
    e5 = P.emit("v", lambda v: v.tensor_scalar(
        m[0:half, :], m[0:half, :], math.pi, -math.pi, op0=ALU.min, op1=ALU.max),
        inc=cm.dve)
    sin_sb = P.sb(f"sin{theta}", [128, width], BF16)
    cos_sb = P.sb(f"cos{theta}", [128, width], BF16)
    P.wait("s", cm.dve, e5)
    a2 = P.emit("s", lambda s: s.activation(out=sin_sb[0:half, :], in_=m[0:half, :],
                                            func=AF.Sin), inc=cm.act)
    e6 = P.emit("v", lambda v: v.tensor_scalar(
        frq[0:half, :], m[0:half, :], math.pi / 2.0, None, op0=ALU.add), inc=cm.dve)
    P.wait("v", cm.act, a2)
    e7 = P.emit("v", lambda v: v.tensor_scalar(
        m[0:half, :], frq[0:half, :], math.pi, None, op0=ALU.is_gt), inc=cm.dve)
    e8 = P.emit("v", lambda v: v.scalar_tensor_tensor(
        m[0:half, :], m[0:half, :], -TWO_PI, frq[0:half, :], op0=ALU.mult, op1=ALU.add),
        inc=cm.dve)
    e9 = P.emit("v", lambda v: v.tensor_scalar(
        m[0:half, :], m[0:half, :], math.pi, -math.pi, op0=ALU.min, op1=ALU.max),
        inc=cm.dve)
    P.wait("s", cm.dve, e9)
    a3 = P.emit("s", lambda s: s.activation(out=cos_sb[0:half, :], in_=m[0:half, :],
                                            func=AF.Sin), inc=cm.act)
    return cos_sb, sin_sb, a3


def rope_apply(P, cm, src0, src1, cosS, sinS, dst0, dst1, tmp, w, half=128,
               tmp2=None, ps_shift=None):
    h = half
    if tmp2 is not None:
        psB = ps_shift
        P.wait("t", cm.dve, cm.dve.n)
        P.wait("t", cm.act, cm.act.n)
        mm1 = P.emit("t", lambda t, src1=src1: t.matmul(
            psB[0:h, 0:w], cm.identf[64:128, 64:128], src1,
            start=True, stop=True), inc=cm.mm)
        P.wait("s", cm.mm, mm1)
        a0 = P.emit("s", lambda s: s.copy(tmp2[0:h, 0:w], psB[0:h, 0:w]),
                    inc=cm.act)
        P.wait("v", cm.act, a0)
        src1 = tmp2[0:h, 0:w]
    e1 = P.emit("v", lambda v: v.tensor_tensor(tmp[0:h, 0:w], src1, sinS, ALU.mult),
                inc=cm.dve)
    e2 = P.emit("v", lambda v: v.tensor_tensor(dst0, src0, cosS, ALU.mult), inc=cm.dve)
    e3 = P.emit("v", lambda v: v.tensor_tensor(dst0, dst0, tmp[0:h, 0:w], ALU.subtract),
                inc=cm.dve)
    e4 = P.emit("v", lambda v: v.tensor_tensor(tmp[0:h, 0:w], src0, sinS, ALU.mult),
                inc=cm.dve)
    if tmp2 is not None:
        e5 = P.emit("v", lambda v: v.tensor_tensor(tmp2[0:h, 0:w], tmp2[0:h, 0:w],
                                                   cosS, ALU.mult), inc=cm.dve)
        e6 = P.emit("v", lambda v: v.tensor_tensor(tmp2[0:h, 0:w], tmp2[0:h, 0:w],
                                                   tmp[0:h, 0:w], ALU.add), inc=cm.dve)
        P.wait("t", cm.dve, e6)
        mm2 = P.emit("t", lambda t: t.matmul(
            psB[h:2 * h, 0:w], cm.identf[0:h, 0:h], tmp2[0:h, 0:w],
            start=True, stop=True), inc=cm.mm)
        P.wait("s", cm.mm, mm2)
        a1 = P.emit("s", lambda s: s.copy(dst1, psB[h:2 * h, 0:w]), inc=cm.act)
        P.wait("v", cm.act, a1)
        e7 = P.emit("v", lambda v: v.engine_nop(), inc=cm.dve)
        return e7
    e5 = P.emit("v", lambda v: v.tensor_tensor(dst1, src1, cosS, ALU.mult), inc=cm.dve)
    e6 = P.emit("v", lambda v: v.tensor_tensor(dst1, dst1, tmp[0:h, 0:w], ALU.add),
                inc=cm.dve)
    return e6



class PostLN:

    def __init__(self, P, cm, name, eps=1e-5, Rs_ap=None, W=SL):
        self.P, self.cm = P, cm
        self.eps = eps
        self.W = W
        self.Rs = Rs_ap if Rs_ap is not None else P.sb(f"{name}_Rs", [128, 16 * W], F32)
        W = W
        ali = getattr(cm, "pln_alias", None)
        if ali is None:
            self.rb = P.sb(f"{name}_rb", [128, W], BF16)
            self.sq = P.sb(f"{name}_sq", [128, W], BF16)
            self.aB = P.sb(f"{name}_aB", [128, W], F32)
            self.bB = P.sb(f"{name}_bB", [128, W], F32)
        else:
            self.rb, self.sq, self.aB, self.bB = ali
        self.rows = P.sb(f"{name}_rows", [1, 8], F32)
        self.mu2 = P.sb(f"{name}_mu2", [1, W], F32)
        self.arow = self.bB[0:1, 0:W]
        self.brow = self.aB[0:1, 0:W]

    def begin(self):
        self.n_mm = 0
        self.first = True
        self.P.wait("v", self.cm.st, self.cm.st.n)
        self.P.wait("y", self.cm.st, self.cm.st.n)

    def add_chunk(self, oc, psum_ap, res_ap=None):
        P, cm = self.P, self.cm
        R = self.Rs[:, self.W * oc:self.W * (oc + 1)]
        if getattr(self, "scale_col", None) is not None:
            sc_ = self.scale_col
            e1 = P.emit("v", lambda v: v.scalar_tensor_tensor(
                R, psum_ap, sc_, R, op0=ALU.mult, op1=ALU.add), inc=cm.dve)
        else:
            e1 = P.emit("v", lambda v: v.tensor_tensor(R, R, psum_ap, ALU.add),
                        inc=cm.dve)
        P.wait("s", cm.dve, e1)
        if not self.first:
            P.wait("s", cm.mm, self.last_mm)
        a1 = P.emit("s", lambda s: s.copy(self.rb[:, :], R), inc=cm.act)
        a2 = P.emit("s", lambda s: s.activation(out=self.sq[:, :], in_=R,
                                                func=AF.Square), inc=cm.act)
        P.wait("t", cm.act, a2)
        fst = bool(self.first)
        t1 = P.emit("t", lambda t, fst=fst, oc=oc: t.matmul(
            cm.pbR1[0:1, 0:self.W], cm.ones128_1b[:, :], self.rb[:, :],
            start=fst, stop=(oc == 15)), inc=cm.mm)
        t2 = P.emit("t", lambda t, fst=fst, oc=oc: t.matmul(
            cm.pbR2[0:1, 0:self.W], cm.ones128_1b[:, :], self.sq[:, :],
            start=fst, stop=(oc == 15)), inc=cm.mm)
        self.first = False
        self.last_mm = t2
        return e1

    def finalize(self, dst_dram_ap=None, dst_sbuf=None, dst_bf16=None):
        P, cm = self.P, self.cm
        P.wait("s", cm.mm, self.last_mm)
        a1 = P.emit("s", lambda s: s.copy(self.arow[0:1, 0:self.W], cm.pbR1[0:1, 0:self.W]),
                    inc=cm.act)
        a2 = P.emit("s", lambda s: s.copy(self.brow[0:1, 0:self.W], cm.pbR2[0:1, 0:self.W]),
                    inc=cm.act)
        P.wait("v", cm.act, a2)
        e1 = P.emit("v", lambda v: v.tensor_scalar(
            self.arow[0:1, 0:self.W], self.arow[0:1, 0:self.W], 1.0 / DM, None, op0=ALU.mult),
            inc=cm.dve)
        e2 = P.emit("v", lambda v: v.tensor_scalar(
            self.brow[0:1, 0:self.W], self.brow[0:1, 0:self.W], 1.0 / DM, None, op0=ALU.mult),
            inc=cm.dve)
        tmp = self.sq
        e3 = P.emit("v", lambda v: v.tensor_tensor(
            self.mu2[0:1, 0:self.W], self.arow[0:1, 0:self.W], self.arow[0:1, 0:self.W], ALU.mult),
            inc=cm.dve)
        e4 = P.emit("v", lambda v: v.tensor_tensor(
            self.brow[0:1, 0:self.W], self.brow[0:1, 0:self.W], self.mu2[0:1, 0:self.W],
            ALU.subtract), inc=cm.dve)
        e5 = P.emit("v", lambda v: v.tensor_scalar(
            self.brow[0:1, 0:self.W], self.brow[0:1, 0:self.W], float(self.eps), None,
            op0=ALU.add), inc=cm.dve)
        P.wait("s", cm.dve, e5)
        a3 = P.emit("s", lambda s: s.activation(out=self.brow[0:1, 0:self.W],
                                                in_=self.brow[0:1, 0:self.W], func=AF.Sqrt),
                    inc=cm.act)
        P.wait("v", cm.act, a3)
        e6 = P.emit("v", lambda v: v.reciprocal(self.brow[0:1, 0:self.W],
                                                self.brow[0:1, 0:self.W]), inc=cm.dve)
        e7 = P.emit("v", lambda v: v.scalar_tensor_tensor(
            self.arow[0:1, 0:self.W], self.arow[0:1, 0:self.W], -1.0, self.brow[0:1, 0:self.W],
            op0=ALU.mult, op1=ALU.mult), inc=cm.dve)
        P.wait("t", cm.dve, e7)
        P.wait("s", cm.dve, e7)
        bcast_row(P, cm, self.brow[0:1, 0:self.W], cm.pbB, self.aB, self.W)
        _, ab = bcast_row(P, cm, self.arow[0:1, 0:self.W], cm.pbB, self.bB, self.W)
        P.wait("v", cm.act, ab)
        stt = None
        outs = []
        for oc in range(16):
            R = self.Rs[:, self.W * oc:self.W * (oc + 1)]
            e8 = P.emit("v", lambda v, R=R: v.tensor_tensor(
                R, R, self.aB[0:128, 0:self.W], ALU.mult), inc=cm.dve)
            e9 = P.emit("v", lambda v, R=R: v.tensor_tensor(
                R, R, self.bB[0:128, 0:self.W], ALU.add), inc=cm.dve)
            if dst_bf16 is not None:
                P.wait("s", cm.dve, e9)
                ac = P.emit("s", lambda s, R=R, oc=oc: s.copy(
                    dst_bf16[:, self.W * oc:self.W * (oc + 1)], R), inc=cm.act)
                outs.append(("act", ac))
            if dst_dram_ap is not None:
                P.wait("y", cm.dve, e9)
                stt = P.emit("y", lambda y, R=R, oc=oc: y.dma_start(
                    out=dst_dram_ap[128 * oc:128 * (oc + 1), :], in_=R),
                    inc=cm.st, n=16)
                outs.append(("st", stt))
        return outs


def _single(nc, body, io, **kw):
    return _multi(nc, [(body, io, kw)])


def _multi(nc, phases):
    P = Prog(nc)
    cm = Common(P)
    cm.build_basics(P)
    build_idents(P, cm)
    for i, (body, io, kw) in enumerate(phases):
        if i:
            P.barrier(cm)
            cm.reset_aux()
            cm.pln_alias = None
        P.begin_phase(f"ph{i}_")
        body(P, cm, io, **kw)
        P.end_phase()
    P.finish()
    return nc


def _ei(nc, name, shape, dt=F32):
    return nc.dram_tensor(name, shape, dt, kind="ExternalInput").ap()


def _eo(nc, name, shape, dt=F32):
    return nc.dram_tensor(name, shape, dt, kind="ExternalOutput").ap()



def body_l0_attn(P, cm, io, dbg=False):
    xctx, qw_d, kw_d, vw_d, ow_d, qn_d, kn_d, cc_d, out_d = (
        io["xctx"], io["qw"], io["kw"], io["vw"], io["ow"], io["qn"],
        io["kn"], io["cc"], io["out"])
    if dbg:
        d_rope, d_kt, d_vt, d_qs, d_ctx, d_misc = (
            io["d_rope"], io["d_kt"], io["d_vt"], io["d_qs"], io["d_ctx"],
            io["d_misc"])

    pb = [P.ps(f"pb{i}", [128, 512], F32) for i in range(8)]
    cm.pbR1, cm.pbR2, cm.pbB, cm.pbB2 = pb[0], pb[1], pb[2], pb[3]
    cm.pb_misc = pb[2]
    pbS = [pb[4], pb[5]]
    pbO = [pb[6], pb[7]]
    ident_dve = cm.dve.n

    qn = P.sb("qn_r", [1, 256], F32)
    kn = P.sb("kn_r", [1, 256], F32)
    ccr = P.sb("cc_r", [1, 8], F32)
    sml = cm.newsem("y")
    P.emit("y", lambda y: y.dma_start(out=qn[:, :], in_=qn_d[:, :]), inc=sml, n=16)
    P.emit("y", lambda y: y.dma_start(out=kn[:, :], in_=kn_d[:, :]), inc=sml, n=16)
    P.emit("y", lambda y: y.dma_start(out=ccr[:, :], in_=cc_d[:, :]), inc=sml, n=16)
    c2_ap = ccr[0:1, 0:1]

    cc_cols = P.sb("cc_cols", [128, 8], F32)
    P.waitall("t", sml)
    P.wait("t", cm.dve, ident_dve)
    t1 = P.emit("t", lambda t: t.matmul(cm.pbB[0:128, 0:8], cm.ones1_128[:, :],
                                        ccr[0:1, 0:8], start=True, stop=True), inc=cm.mm)
    P.wait("s", cm.mm, t1)
    acc0 = P.emit("s", lambda s: s.copy(cc_cols[:, :], cm.pbB[0:128, 0:8]), inc=cm.act)

    qn_cols = P.sb("qn_cols", [128, 2], F32)
    kn_cols = P.sb("kn_cols", [128, 2], F32)
    P.wait("t", cm.act, acc0)
    for c in range(2):
        P.emit("t", lambda t, c=c: t.transpose(
            cm.pbB[0:128, c:c + 1], qn[0:1, 128 * c:128 * (c + 1)],
            cm.identf[0:1, 0:1]), inc=cm.mm)
    P.wait("s", cm.mm, cm.mm.n)
    aq = P.emit("s", lambda s: s.copy(qn_cols[:, :], cm.pbB[0:128, 0:2]), inc=cm.act)
    P.wait("t", cm.act, aq)
    for c in range(2):
        P.emit("t", lambda t, c=c: t.transpose(
            cm.pbB[0:128, c:c + 1], kn[0:1, 128 * c:128 * (c + 1)],
            cm.identf[0:1, 0:1]), inc=cm.mm)
    P.wait("s", cm.mm, cm.mm.n)
    ak = P.emit("s", lambda s: s.copy(kn_cols[:, :], cm.pbB[0:128, 0:2]), inc=cm.act)

    mrow = P.sb("mrow", [1, 4], F32)
    negB = P.sb("negB", [128, 1], F32)
    P.waitall("v", sml)
    P.emit("v", lambda v: v.reduce_max(mrow[0:1, 0:1], qn[0:1, :], axis=AX,
                                       apply_absolute_value=True), inc=cm.dve)
    P.emit("v", lambda v: v.reduce_max(mrow[0:1, 1:2], kn[0:1, :], axis=AX,
                                       apply_absolute_value=True), inc=cm.dve)
    P.emit("v", lambda v: v.tensor_tensor(mrow[0:1, 2:3], mrow[0:1, 0:1],
                                          mrow[0:1, 1:2], ALU.mult), inc=cm.dve)
    e4 = P.emit("v", lambda v: v.tensor_scalar(mrow[0:1, 2:3], mrow[0:1, 2:3],
                                               -16.0, None, op0=ALU.mult), inc=cm.dve)
    P.wait("t", cm.dve, e4)
    P.wait("t", cm.act, ak)
    t1 = P.emit("t", lambda t: t.matmul(cm.pbB[0:128, 0:1], cm.ones1_128[:, :],
                                        mrow[0:1, 2:3], start=True, stop=True), inc=cm.mm)
    P.wait("s", cm.mm, t1)
    anb = P.emit("s", lambda s: s.copy(negB[:, :], cm.pbB[0:128, 0:1]), inc=cm.act)

    big1 = P.sb("big1", [128, 8192], F32)
    P.wait("t", cm.act, anb)
    cosT, sinT, rope_rdy = build_rope_tables(
        P, cm, 1000000.0, 256, 4096, c2_ap,
        posB_ap=big1[:, 0:4096], frq_ap=big1[:, 4096:8192])

    QsT = P.sb("QsT", [128, 16 * SL], BF16)
    wscb = QsT[:, 0:4096]
    sc_q, ss_q = w_to_scratch(P, cm, "q", qw_d, 2048, 2048, buf=wscb)
    sc_k, ss_k = w_to_scratch(P, cm, "k", kw_d, 256, 2048, buf=wscb, prev=ss_q)
    sc_v, ss_v = w_to_scratch(P, cm, "v", vw_d, 256, 2048, buf=wscb, prev=ss_k)
    sc_o, ss_o = w_to_scratch(P, cm, "o", ow_d, 2048, 2048, buf=wscb, prev=ss_v)

    s_kw = cm.newsem("y")
    s_vw = cm.newsem("y")
    kwT = P.sb("kwT", [128, 16 * 256], BF16)
    for i in range(16):
        load_wT(P, cm, sc_k, 128 * i, 0, 256, kwT[:, 256 * i:256 * (i + 1)],
                ss_k, ss_k.n, sem=s_kw)
    vwT = P.sb("vwT", [128, 16 * 256], BF16)
    for i in range(16):
        P.wait("y", ss_v, ss_v.n)
        P.emit("y", lambda y, i=i: y.dma_start(
            out=vwT[:, 256 * i:256 * (i + 1)], in_=sc_v[0:256, 128 * i:128 * (i + 1)],
            transpose=True), inc=s_vw, n=16)

    Kt = P.sb("Kt", [128, 2 * 4096], BF16)
    Vt = P.sb("Vt", [128, 32 * 256], BF16)
    xr = P.sb("xr", [128, 4 * 512], BF16)
    xr_cons = [0, 0, 0, 0]
    s_xr = [cm.newsem("g") for _ in range(4)]
    s_xv = [cm.newsem("g") for _ in range(4)]
    rstdB = P.sb("rstdB", [128, SL], F32)
    msrow = rstdB[0:1, 0:SL]
    k0n = P.sb("k0n", [128, SL], F32)
    k1n = P.sb("k1n", [128, SL], F32)
    ktmp = P.sb("ktmp", [128, SL], F32)
    sqb = P.sb("sqb", [128, 2 * SL], BF16)
    cm.pln_alias = (sqb[:, 0:SL], sqb[:, SL:2 * SL], k0n[:, 0:SL], k1n[:, 0:SL])

    P.wait("v", cm.act, rope_rdy)
    pbS_free = 0
    for ts in range(8):
        col = slice(512 * ts, 512 * (ts + 1))
        for i in range(16):
            slot = i % 4
            b = xr[:, 512 * slot:512 * (slot + 1)]
            if xr_cons[slot]:
                P.wait("g", cm.mm, xr_cons[slot])
            tgt = P.emit("g", lambda g, b=b, s=xctx[128 * i:128 * (i + 1), col]:
                         g.dma_start(out=b, in_=s), inc=s_xr[slot], n=16, dma=True)
            P.wait("t", s_xr[slot], tgt)
            if i == 0:
                P.waitall("t", s_kw)
                if pbS_free:
                    P.wait("t", cm.dve, pbS_free)
            m1 = P.emit("t", lambda t, b=b, i=i: t.matmul(
                pbS[0][0:128, 0:512], kwT[:, 256 * i:256 * i + 128], b,
                start=(i == 0), stop=(i == 15)), inc=cm.mm)
            m2 = P.emit("t", lambda t, b=b, i=i: t.matmul(
                pbS[1][0:128, 0:512], kwT[:, 256 * i + 128:256 * i + 256], b,
                start=(i == 0), stop=(i == 15)), inc=cm.mm)
            xr_cons[slot] = m2
        P.wait("s", cm.mm, m2)
        a1 = P.emit("s", lambda s: s.activation(out=sqb[:, 0:SL], in_=pbS[0][0:128, 0:512],
                                                func=AF.Square), inc=cm.act)
        a2 = P.emit("s", lambda s: s.activation(out=sqb[:, SL:2 * SL],
                                                in_=pbS[1][0:128, 0:512],
                                                func=AF.Square), inc=cm.act)
        P.wait("t", cm.act, a2)
        s1 = P.emit("t", lambda t: t.matmul(cm.pbR1[0:1, 0:SL], cm.ones128_1b[:, :],
                                            sqb[:, 0:SL], start=True, stop=False),
                    inc=cm.mm)
        s2 = P.emit("t", lambda t: t.matmul(cm.pbR1[0:1, 0:SL], cm.ones128_1b[:, :],
                                            sqb[:, SL:2 * SL], start=False, stop=True),
                    inc=cm.mm)
        P.wait("s", cm.mm, s2)
        if pbS_free:
            P.wait("s", cm.dve, pbS_free)
        a3 = P.emit("s", lambda s: s.copy(msrow[0:1, :], cm.pbR1[0:1, 0:SL]), inc=cm.act)
        P.wait("v", cm.act, a3)
        e1 = P.emit("v", lambda v: v.tensor_scalar(
            msrow[0:1, :], msrow[0:1, :], 1.0 / 256.0, 1e-6, op0=ALU.mult, op1=ALU.add),
            inc=cm.dve)
        P.wait("s", cm.dve, e1)
        a4 = P.emit("s", lambda s: s.activation(out=msrow[0:1, :], in_=msrow[0:1, :],
                                                func=AF.Sqrt), inc=cm.act)
        P.wait("v", cm.act, a4)
        e2 = P.emit("v", lambda v: v.reciprocal(msrow[0:1, :], msrow[0:1, :]), inc=cm.dve)
        P.wait("t", cm.dve, e2)
        _, ab = bcast_row(P, cm, msrow[0:1, :], cm.pbB, rstdB, SL)
        P.wait("v", cm.act, ab)
        e3 = P.emit("v", lambda v: v.scalar_tensor_tensor(
            k0n[:, :], pbS[0][0:128, 0:512], kn_cols[0:128, 0:1], rstdB[:, :],
            op0=ALU.mult, op1=ALU.mult), inc=cm.dve)
        e4k = P.emit("v", lambda v: v.scalar_tensor_tensor(
            k1n[:, :], pbS[1][0:128, 0:512], kn_cols[0:128, 1:2], rstdB[:, :],
            op0=ALU.mult, op1=ALU.mult), inc=cm.dve)
        pbS_free = e4k
        rope_apply(P, cm, k0n[:, :], k1n[:, :], cosT[:, col], sinT[:, col],
                   Kt[:, 512 * ts:512 * (ts + 1)],
                   Kt[:, 4096 + 512 * ts:4096 + 512 * (ts + 1)], ktmp, SL)
    k_done_dve = cm.dve.n
    if dbg:
        P.wait("y", cm.dve, k_done_dve)
        P.emit("y", lambda y: y.dma_start(out=d_kt[:, :], in_=Kt[:, 0:8192]), inc=cm.st, n=16)
        P.wait("y", cm.act, rope_rdy)
        P.emit("y", lambda y: y.dma_start(out=d_rope[:, 0:4096], in_=cosT[:, :]), inc=cm.st, n=16)
        P.emit("y", lambda y: y.dma_start(out=d_rope[:, 4096:8192], in_=sinT[:, :]), inc=cm.st, n=16)

    xv = P.sb("xv", [128, 4 * 128], BF16)
    xv_cons = [0, 0, 0, 0]
    va_prev = {}
    for kt in range(32):
        bank = pbO[kt % 2]
        for i in range(16):
            slot = i % 4
            b = xv[:, 128 * slot:128 * (slot + 1)]
            if xv_cons[slot]:
                P.wait("g", cm.mm, xv_cons[slot])
            tgt = P.emit("g", lambda g, b=b,
                         s=xctx[128 * i:128 * (i + 1), 128 * kt:128 * (kt + 1)]:
                         g.dma_start(out=b, in_=s), inc=s_xv[slot], n=16, dma=True)
            P.wait("t", s_xv[slot], tgt)
            if kt == 0 and i == 0:
                P.waitall("t", s_vw)
            if i == 0 and kt >= 2:
                P.wait("t", cm.act, va_prev[kt % 2])
            m1 = P.emit("t", lambda t, b=b, i=i, bank=bank: t.matmul(
                bank[0:128, 0:256], b, vwT[:, 256 * i:256 * (i + 1)],
                start=(i == 0), stop=(i == 15)), inc=cm.mm)
            xv_cons[slot] = m1
        P.wait("s", cm.mm, m1)
        va = P.emit("s", lambda s, kt=kt, bank=bank: s.copy(
            Vt[:, 256 * kt:256 * (kt + 1)], bank[0:128, 0:256]), inc=cm.act)
        va_prev[kt % 2] = va
    v_done = va
    if dbg:
        P.wait("y", cm.act, v_done)
        P.emit("y", lambda y: y.dma_start(out=d_vt[:, :], in_=Vt[:, 0:8192]), inc=cm.st, n=16)

    Ed = P.sb("Ed", [128, 4 * SL], BF16)
    for d in range(4):
        P.emit("v", lambda v, d=d: v.tensor_scalar(
            Ed[:, SL * d:SL * (d + 1)], cm.iotaPF[:, :], float(128 * d), 0.0,
            op0=ALU.add, op1=ALU.is_le), inc=cm.dve)
    ed_done = cm.dve.n

    qwT = P.sb("qwT", [128, 16 * 512], BF16)
    xq = P.sb("xq", [128, 16 * 512], BF16)
    Qs = QsT
    Ctx = xq
    Pt = P.sb("Pt", [128, 2 * SL], BF16)
    bvs = P.sb("bvs", [128, 36], F32)
    rbS = rstdB
    sumr = rstdB[0:1, 0:SL]
    owT = qwT
    pln = PostLN(P, cm, "l0a", Rs_ap=big1[:, 0:8192])

    q0n, q1n = k0n, k1n
    exp_t = [0, 0]
    pv_t = [0, 0]
    pmask_t = [0, 0]
    sum_evac = 0
    qwT_free_mm = 0
    oproj_last_mm = 0
    for s_ in range(NSL):
        ts4 = 16 + 4 * s_
        T_s = ts4 + 4
        colq = slice(2048 + SL * s_, 2048 + SL * (s_ + 1))
        P.wait("g", cm.mm, cm.mm.n)
        if s_ == 0:
            s_xq = cm.newsem("g")
            s_qw = cm.newsem("y")
            s_res = cm.newsem("y")
            P.wait("y", ss_q, ss_q.n)
            P.wait("v", ss_o, ss_o.n)
            P.wait("s", ss_o, ss_o.n)
        for i in range(16):
            P.emit("g", lambda g, i=i, s=xctx[128 * i:128 * (i + 1), colq]:
                   g.dma_start(out=xq[:, 512 * i:512 * (i + 1)], in_=s), inc=s_xq, n=16, dma=True)
        q_bank_free = k_done_dve if s_ == 0 else 0
        for oblk in range(4):
            if s_ == 0 and oblk == 0:
                P.wait("t", cm.act, v_done)
            if qwT_free_mm:
                P.wait("y", cm.mm, qwT_free_mm)
            for i in range(16):
                P.emit("y", lambda y, i=i, oblk=oblk: y.dma_start(
                    out=qwT[:, 512 * i:512 * (i + 1)],
                    in_=sc_q[512 * oblk:512 * (oblk + 1), 128 * i:128 * (i + 1)],
                    transpose=True), inc=s_qw, n=16)
            P.waitall("t", s_qw)
            if oblk == 0:
                P.waitall("t", s_xq)
            if q_bank_free:
                P.wait("t", cm.dve, q_bank_free)
            for i in range(16):
                for oc in range(4):
                    bank = pb[4 + oc]
                    m1 = P.emit("t", lambda t, i=i, oc=oc, bank=bank: t.matmul(
                        bank[0:128, 0:512], qwT[:, 512 * i + 128 * oc:512 * i + 128 * (oc + 1)],
                        xq[:, 512 * i:512 * (i + 1)],
                        start=(i == 0), stop=(i == 15)), inc=cm.mm)
            qwT_free_mm = m1
            for hh in range(2):
                head = oblk * 2 + hh
                c0 = pb[4 + 2 * hh]
                c1 = pb[4 + 2 * hh + 1]
                P.wait("s", cm.mm, m1)
                a1 = P.emit("s", lambda s, c0=c0: s.activation(
                    out=sqb[:, 0:SL], in_=c0[0:128, 0:512], func=AF.Square), inc=cm.act)
                a2 = P.emit("s", lambda s, c1=c1: s.activation(
                    out=sqb[:, SL:2 * SL], in_=c1[0:128, 0:512], func=AF.Square),
                    inc=cm.act)
                P.wait("t", cm.act, a2)
                s1 = P.emit("t", lambda t: t.matmul(
                    cm.pbR1[0:1, 0:SL], cm.ones128_1b[:, :], sqb[:, 0:SL],
                    start=True, stop=False), inc=cm.mm)
                s2 = P.emit("t", lambda t: t.matmul(
                    cm.pbR1[0:1, 0:SL], cm.ones128_1b[:, :], sqb[:, SL:2 * SL],
                    start=False, stop=True), inc=cm.mm)
                P.wait("s", cm.mm, s2)
                P.wait("s", cm.dve, cm.dve.n)
                a3 = P.emit("s", lambda s: s.copy(msrow[0:1, :], cm.pbR1[0:1, 0:SL]),
                            inc=cm.act)
                P.wait("v", cm.act, a3)
                e1 = P.emit("v", lambda v: v.tensor_scalar(
                    msrow[0:1, :], msrow[0:1, :], 1.0 / 256.0, 1e-6,
                    op0=ALU.mult, op1=ALU.add), inc=cm.dve)
                P.wait("s", cm.dve, e1)
                a4 = P.emit("s", lambda s: s.activation(
                    out=msrow[0:1, :], in_=msrow[0:1, :], func=AF.Sqrt), inc=cm.act)
                P.wait("v", cm.act, a4)
                e2 = P.emit("v", lambda v: v.reciprocal(msrow[0:1, :], msrow[0:1, :]),
                            inc=cm.dve)
                P.wait("t", cm.dve, e2)
                _, ab = bcast_row(P, cm, msrow[0:1, :], cm.pbB, rstdB, SL)
                P.wait("v", cm.act, ab)
                e3 = P.emit("v", lambda v, c0=c0: v.scalar_tensor_tensor(
                    q0n[:, :], c0[0:128, 0:512], qn_cols[0:128, 0:1],
                    rstdB[:, :], op0=ALU.mult, op1=ALU.mult), inc=cm.dve)
                e4q = P.emit("v", lambda v, c1=c1: v.scalar_tensor_tensor(
                    q1n[:, :], c1[0:128, 0:512], qn_cols[0:128, 1:2],
                    rstdB[:, :], op0=ALU.mult, op1=ALU.mult), inc=cm.dve)
                rope_apply(P, cm, q0n[:, :], q1n[:, :], cosT[:, colq], sinT[:, colq],
                           Qs[:, SL * (2 * head):SL * (2 * head + 1)],
                           Qs[:, SL * (2 * head + 1):SL * (2 * head + 2)], ktmp, SL)
                q_bank_free = e4q
        q_rope_done = cm.dve.n

        P.wait("v", cm.act, max(anb, acc0))
        for kt in range(T_s):
            P.emit("v", lambda v, kt=kt: v.tensor_scalar(
                bvs[:, kt:kt + 1], cm.iotaP[:, :], float(128 * kt), cc_cols[0:128, 0:1],
                op0=ALU.add, op1=ALU.is_lt), inc=cm.dve)
            P.emit("v", lambda v, kt=kt: v.scalar_tensor_tensor(
                bvs[:, kt:kt + 1], bvs[:, kt:kt + 1], NEG, negB[:, :],
                op0=ALU.mult, op1=ALU.add), inc=cm.dve)
        bv_done = cm.dve.n

        if dbg and s_ == 0:
            P.wait("y", cm.dve, q_rope_done)
            P.emit("y", lambda y: y.dma_start(out=d_qs[:, :], in_=Qs[:, 0:8192]), inc=cm.st, n=16)
            P.wait("y", cm.dve, bv_done)
            P.emit("y", lambda y: y.dma_start(out=d_misc[:, 0:20], in_=bvs[:, 0:20]), inc=cm.st, n=16)
            P.emit("y", lambda y: y.dma_start(out=d_misc[:, 36:37], in_=negB[:, :], allow_slow_non_contiguous=True), inc=cm.st, n=16)
        P.wait("t", cm.dve, max(k_done_dve, ed_done, q_rope_done))
        P.wait("t", cm.act, v_done)
        for h in range(8):
            if sum_evac:
                P.wait("t", cm.act, sum_evac)
            for kt in range(T_s):
                par = kt % 2
                bankS = pbS[par]
                if exp_t[par]:
                    P.wait("t", cm.act, exp_t[par])
                if pv_t[par]:
                    P.wait("t", cm.mm, pv_t[par])
                m1 = P.emit("t", lambda t, h=h, kt=kt, bankS=bankS: t.matmul(
                    bankS[0:128, 0:512], Kt[:, 128 * kt:128 * (kt + 1)],
                    Qs[:, SL * 2 * h:SL * (2 * h + 1)], start=True, stop=False),
                    inc=cm.mm)
                m2 = P.emit("t", lambda t, h=h, kt=kt, bankS=bankS: t.matmul(
                    bankS[0:128, 0:512], Kt[:, 4096 + 128 * kt:4096 + 128 * (kt + 1)],
                    Qs[:, SL * (2 * h + 1):SL * (2 * h + 2)], start=False, stop=True),
                    inc=cm.mm)
                P.wait("s", cm.mm, m2)
                P.wait("s", cm.dve, bv_done)
                ptile = Pt[:, SL * par:SL * (par + 1)]
                if pmask_t[par]:
                    P.wait("s", cm.dve, pmask_t[par])
                ex = P.emit("s", lambda s, bankS=bankS, ptile=ptile, kt=kt: s.activation(
                    out=ptile, in_=bankS[0:128, 0:512], func=AF.Exp,
                    bias=bvs[0:128, kt:kt + 1], scale=1.0 / 16.0), inc=cm.act)
                exp_t[par] = ex
                dlt = kt - ts4
                if dlt >= 0:
                    P.wait("v", cm.act, ex)
                    pm = P.emit("v", lambda v, ptile=ptile, dlt=dlt: v.tensor_tensor(
                        ptile, ptile, Ed[:, SL * dlt:SL * (dlt + 1)], ALU.mult),
                        inc=cm.dve)
                    pmask_t[par] = pm
                    P.wait("t", cm.dve, pm)
                else:
                    P.wait("t", cm.act, ex)
                st_ = (kt == T_s - 1)
                p1 = P.emit("t", lambda t, kt=kt, ptile=ptile, st_=st_: t.matmul(
                    pbO[0][0:128, 0:512], Vt[:, 256 * kt:256 * kt + 128], ptile,
                    start=(kt == 0), stop=st_), inc=cm.mm)
                p2 = P.emit("t", lambda t, kt=kt, ptile=ptile, st_=st_: t.matmul(
                    pbO[1][0:128, 0:512], Vt[:, 256 * kt + 128:256 * kt + 256], ptile,
                    start=(kt == 0), stop=st_), inc=cm.mm)
                p3 = P.emit("t", lambda t, kt=kt, ptile=ptile, st_=st_: t.matmul(
                    cm.pbB2[0:1, 0:512], cm.ones128_1b[:, :], ptile,
                    start=(kt == 0), stop=st_), inc=cm.mm)
                pv_t[par] = p3
            P.wait("s", cm.mm, p3)
            sum_evac = P.emit("s", lambda s: s.copy(sumr[0:1, :], cm.pbB2[0:1, 0:512]),
                              inc=cm.act)
            P.wait("v", cm.act, sum_evac)
            e1 = P.emit("v", lambda v: v.reciprocal(sumr[0:1, :], sumr[0:1, :]),
                        inc=cm.dve)
            P.wait("t", cm.dve, e1)
            _, ab = bcast_row(P, cm, sumr[0:1, :], cm.pbB, rbS, SL)
            P.wait("v", cm.act, ab)
            ec0 = P.emit("v", lambda v, h=h: v.tensor_tensor(
                Ctx[:, SL * 2 * h:SL * (2 * h + 1)], pbO[0][0:128, 0:512], rbS[:, :],
                ALU.mult), inc=cm.dve)
            ec1 = P.emit("v", lambda v, h=h: v.tensor_tensor(
                Ctx[:, SL * (2 * h + 1):SL * (2 * h + 2)], pbO[1][0:128, 0:512],
                rbS[:, :], ALU.mult), inc=cm.dve)
            P.wait("t", cm.dve, ec1)

        if dbg and s_ == 0:
            P.wait("y", cm.dve, cm.dve.n)
            P.emit("y", lambda y: y.dma_start(out=d_ctx[:, :], in_=Ctx[:, 0:8192]), inc=cm.st, n=16)
        P.wait("v", cm.act, rope_rdy)
        pln.begin()
        res_cons = [0, 0]
        for oblk in range(4):
            if s_ == 0 and oblk == 0:
                P.wait("y", ss_o, ss_o.n)
            P.wait("y", cm.mm, max(oproj_last_mm, qwT_free_mm))
            for i in range(16):
                P.emit("y", lambda y, i=i, oblk=oblk: y.dma_start(
                    out=owT[:, 512 * i:512 * (i + 1)],
                    in_=sc_o[512 * oblk:512 * (oblk + 1), 128 * i:128 * (i + 1)],
                    transpose=True), inc=s_qw, n=16)
            P.waitall("t", s_qw)
            for i in range(16):
                for j in range(4):
                    bank = pb[4 + j]
                    if i == 0:
                        pass
                    m1 = P.emit("t", lambda t, i=i, j=j, oblk=oblk, bank=bank: t.matmul(
                        bank[0:128, 0:512],
                        owT[:, 512 * i + 128 * j:512 * i + 128 * (j + 1)],
                        Ctx[:, 512 * i:512 * (i + 1)],
                        start=(i == 0), stop=(i == 15)), inc=cm.mm)
            oproj_last_mm = m1
            qwT_free_mm = m1
            for j in range(4):
                oc = oblk * 4 + j
                bank = pb[4 + j]
                if res_cons[0]:
                    P.wait("y", cm.dve, res_cons[0])
                rtgt = P.emit("y", lambda y, oc=oc, colq=colq: y.dma_start(
                    out=pln.Rs[:, SL * oc:SL * (oc + 1)],
                    in_=xctx[128 * oc:128 * (oc + 1), colq]), inc=s_res, n=16)
                P.wait("v", cm.mm, oproj_last_mm)
                P.waitall("v", s_res)
                ee = pln.add_chunk(oc, bank[0:128, 0:512])
                res_cons[0] = ee
            P.wait("t", cm.dve, ee)
        pln.finalize(dst_dram_ap=out_d[:, SL * s_:SL * (s_ + 1)])

    P.wait("y", cm.st, cm.st.n)


def build_l0_attn(dbg=False):
    nc = bass.Bass("TRN2", target_bir_lowering=False)
    io = {
        "xctx": _ei(nc, "xctx", [DM, 4096]),
        "qw": _ei(nc, "qw", [2048, 2048]), "kw": _ei(nc, "kw", [256, 2048]),
        "vw": _ei(nc, "vw", [256, 2048]), "ow": _ei(nc, "ow", [2048, 2048]),
        "qn": _ei(nc, "qn", [1, 256]), "kn": _ei(nc, "kn", [1, 256]),
        "cc": _ei(nc, "cc", [1, 8]),
        "out": _eo(nc, "out", [DM, TOK]),
    }
    if dbg:
        for nm in ("d_rope", "d_kt", "d_vt", "d_qs", "d_ctx"):
            io[nm] = _eo(nc, nm, [128, 8192], BF16)
        io["d_misc"] = _eo(nc, "d_misc", [128, 64])
    return _single(nc, body_l0_attn, io, dbg=dbg)


def quant_scales(P, cm, acc, W, sB, invsB, srow, cols, eps=1e-5):
    nch = W // 128
    elast = part_reduce_max(P, cm, acc, W, cols[:, 0:nch], cm.pbB2, j0=0)
    e1 = P.emit("v", lambda v: v.tensor_scalar(
        cols[:, 0:nch], cols[:, 0:nch], float(eps), None, op0=ALU.max), inc=cm.dve)
    e2 = P.emit("v", lambda v: v.tensor_scalar(
        cols[:, nch:2 * nch], cols[:, 0:nch], 1.0 / 127.0, None, op0=ALU.mult),
        inc=cm.dve)
    e3 = P.emit("v", lambda v: v.reciprocal(cols[:, 0:nch], cols[:, 0:nch]), inc=cm.dve)
    e4 = P.emit("v", lambda v: v.tensor_scalar(
        cols[:, 0:nch], cols[:, 0:nch], 127.0, None, op0=ALU.mult), inc=cm.dve)
    P.wait("t", cm.dve, e4)
    P.wait("t", cm.act, cm.act.n)
    tl = None
    for j in range(nch):
        tl = P.emit("t", lambda t, j=j: t.transpose(
            cm.pbB2[0:1, 128 * j:128 * (j + 1)],
            cols[:, j:j + 1], cm.identf[0:128, 0:128]), inc=cm.mm)
        tl = P.emit("t", lambda t, j=j: t.transpose(
            cm.pbB[0:1, 128 * j:128 * (j + 1)],
            cols[:, nch + j:nch + j + 1], cm.identf[0:128, 0:128]), inc=cm.mm)
    P.wait("s", cm.mm, tl)
    a1 = P.emit("s", lambda s: s.copy(srow[0:1, 0:W], cm.pbB2[0:1, 0:W]),
                inc=cm.act)
    a1 = P.emit("s", lambda s: s.copy(srow[0:1, W:2 * W], cm.pbB[0:1, 0:W]),
                inc=cm.act)
    P.wait("t", cm.act, a1)
    bcast_row(P, cm, srow[0:1, 0:W], cm.pbB, sB, W)
    _, ab = bcast_row(P, cm, srow[0:1, W:2 * W], cm.pbB, invsB, W)
    return ab



def body_l0_ffn(P, cm, io):
    xin, w1_d, w2_d, out_d = io["xin"], io["w1"], io["w2"], io["out"]
    pb = [P.ps(f"pb{i}", [128, 512], F32) for i in range(8)]
    cm.pbR1, cm.pbR2, cm.pbB, cm.pbB2 = pb[0], pb[1], pb[2], pb[3]
    cm.pb_misc = pb[2]

    W = 512
    NSLF = TOK // W

    buf = P.sb("tern_b", [128, 2 * 2048], BF16)
    f32buf = P.sb("tern_f", [128, 2 * 2048], F32)
    sc1, ss1, oscale1, rdy1 = w_to_scratch_ternary(
        P, cm, "w1", w1_d, 8192, 2048, 1e-5, "wq", buf, f32buf, cm.pbR1, cm.pbB2)
    sc2, ss2, oscale2, rdy2 = w_to_scratch_ternary(
        P, cm, "w2", w2_d, 2048, 8192, 1e-5, "wq", buf, f32buf, cm.pbR1, cm.pbB2,
        prev_ssem=ss1)

    H = P.sb("Hf", [128, 64 * W], BF16)
    tq = P.sb("tq", [128, W], F32)
    XQ = P.sb("XQ", [128, 16 * W], BF16)
    wring = P.sb("wring", [128, 8 * 512], BF16)
    acc = P.sb("acc", [128, W], F32)
    acc2 = P.sb("acc2", [128, W], F32)
    accH = P.sb("accH", [128, W], F32)
    accH2 = P.sb("accH2", [128, W], F32)
    sB = P.sb("sB", [128, W], F32)
    invsB = P.sb("invsB", [128, W], F32)
    srow = P.sb("srow", [1, 2 * W], F32)
    cols = P.sb("colsq", [128, 8], F32)
    pln = PostLN(P, cm, "l0f", W=W)
    ring_sems = [cm.newsem("y") for _ in range(8)]
    ring_cons = [0] * 8

    P.waitall("y", ss1)
    P.waitall("y", ss2)
    P.wait("v", cm.act, max(rdy1, rdy2))
    P.wait("s", cm.act, max(rdy1, rdy2))
    res_prev = 0
    s_res = cm.newsem("y")
    for s_ in range(NSLF):
        colt = slice(W * s_, W * (s_ + 1))
        pln.begin()
        for i in range(16):
            if res_prev:
                P.wait("y", cm.dve, res_prev)
            rt = P.emit("y", lambda y, i=i, colt=colt: y.dma_start(
                out=pln.Rs[:, W * i:W * (i + 1)], in_=xin[128 * i:128 * (i + 1), colt]),
                inc=s_res, n=16)
            P.waitall("v", s_res)
            if i == 0:
                e = amax_start(P, cm, acc[:, :], acc2[:, :], pln.Rs[:, 0:W])
            else:
                e = amax_step(P, cm, acc[:, :], acc2[:, :],
                              pln.Rs[:, W * i:W * (i + 1)])
            res_prev = e
        amax_finish(P, cm, acc[:, :], acc2[:, :])
        qs = quant_scales(P, cm, acc, W, sB, invsB, srow, cols)
        P.wait("v", cm.act, qs)
        for i in range(16):
            e1 = P.emit("v", lambda v, i=i: v.tensor_tensor(
                XQ[:, W * i:W * (i + 1)], pln.Rs[:, W * i:W * (i + 1)], sB[:, :],
                ALU.mult), inc=cm.dve)
            e2 = P.emit("v", lambda v, i=i: v.tensor_scalar(
                XQ[:, W * i:W * (i + 1)], XQ[:, W * i:W * (i + 1)], RNE, RNE,
                op0=ALU.add, op1=ALU.subtract), inc=cm.dve)
            e3 = P.emit("v", lambda v, i=i: v.tensor_tensor(
                XQ[:, W * i:W * (i + 1)], XQ[:, W * i:W * (i + 1)], invsB[:, :],
                ALU.mult), inc=cm.dve)
        xq_done = e3
        P.wait("t", cm.dve, xq_done)
        hacc_first = True
        bank_free_dve = 0
        bank_free_act = 0
        for hb in range(16):
            for i in range(16):
                slot = (hb * 16 + i) % 8
                if ring_cons[slot]:
                    P.wait("y", cm.mm, ring_cons[slot])
                lt = P.emit("y", lambda y, hb=hb, i=i, slot=slot: y.dma_start(
                    out=wring[:, 512 * slot:512 * (slot + 1)],
                    in_=sc1[512 * hb:512 * (hb + 1), 128 * i:128 * (i + 1)],
                    transpose=True), inc=ring_sems[slot], n=16)
                P.wait("t", ring_sems[slot], lt)
                if i == 0 and bank_free_act:
                    P.wait("t", cm.act, bank_free_act)
                    P.wait("t", cm.dve, bank_free_dve)
                for oc in range(4):
                    m1 = P.emit("t", lambda t, slot=slot, oc=oc, i=i: t.matmul(
                        pb[4 + oc][0:128, 0:W],
                        wring[:, 512 * slot + 128 * oc:512 * slot + 128 * (oc + 1)],
                        XQ[:, W * i:W * (i + 1)],
                        start=(i == 0), stop=(i == 15)), inc=cm.mm)
                ring_cons[slot] = m1
            P.wait("s", cm.mm, m1)
            for oc in range(4):
                ch = hb * 4 + oc
                a1 = P.emit("s", lambda s, oc=oc, ch=ch: s.activation(
                    out=H[:, W * ch:W * (ch + 1)], in_=pb[4 + oc][0:128, 0:W],
                    func=AF.Gelu_apprx_tanh, scale=oscale1), inc=cm.act)
                P.wait("v", cm.act, a1)
                if hacc_first:
                    e = amax_start(P, cm, accH[:, :], accH2[:, :],
                                   H[:, W * ch:W * (ch + 1)])
                    hacc_first = False
                else:
                    e = amax_step(P, cm, accH[:, :], accH2[:, :],
                                  H[:, W * ch:W * (ch + 1)])
            bank_free_act = a1
            bank_free_dve = e
        amax_finish(P, cm, accH[:, :], accH2[:, :])
        qs2 = quant_scales(P, cm, accH, W, sB, invsB, srow, cols)
        P.wait("v", cm.act, qs2)
        for ch in range(64):
            e1 = P.emit("v", lambda v, ch=ch: v.tensor_tensor(
                tq[:, :], H[:, W * ch:W * (ch + 1)], sB[:, :],
                ALU.mult), inc=cm.dve)
            e2 = P.emit("v", lambda v, ch=ch: v.tensor_scalar(
                tq[:, :], tq[:, :], RNE, RNE,
                op0=ALU.add, op1=ALU.subtract), inc=cm.dve)
            e3 = P.emit("v", lambda v, ch=ch: v.tensor_tensor(
                H[:, W * ch:W * (ch + 1)], tq[:, :], invsB[:, :],
                ALU.mult), inc=cm.dve)
        hq_done = e3
        P.wait("t", cm.dve, hq_done)
        pln.scale_col = oscale2
        oproj_mm = 0
        for ob in range(4):
            for i in range(64):
                slot = i % 8
                if ring_cons[slot]:
                    P.wait("y", cm.mm, ring_cons[slot])
                lt = P.emit("y", lambda y, ob=ob, i=i, slot=slot: y.dma_start(
                    out=wring[:, 512 * slot:512 * (slot + 1)],
                    in_=sc2[512 * ob:512 * (ob + 1), 128 * i:128 * (i + 1)],
                    transpose=True), inc=ring_sems[slot], n=16)
                P.wait("t", ring_sems[slot], lt)
                if i == 0 and ob == 0:
                    pass
                for oc in range(4):
                    m1 = P.emit("t", lambda t, slot=slot, oc=oc, i=i: t.matmul(
                        pb[4 + oc][0:128, 0:W],
                        wring[:, 512 * slot + 128 * oc:512 * slot + 128 * (oc + 1)],
                        H[:, W * i:W * (i + 1)],
                        start=(i == 0), stop=(i == 63)), inc=cm.mm)
                ring_cons[slot] = m1
            oproj_mm = m1
            for oc in range(4):
                ch = ob * 4 + oc
                P.wait("v", cm.mm, oproj_mm)
                ee = pln.add_chunk(ch, pb[4 + oc][0:128, 0:W])
            P.wait("t", cm.dve, ee)
        pln.finalize(dst_dram_ap=out_d[:, W * s_:W * (s_ + 1)])
        res_prev = ee
    P.wait("y", cm.st, cm.st.n)


def build_l0_ffn():
    nc = bass.Bass("TRN2", target_bir_lowering=False)
    io = {
        "xin": _ei(nc, "xin", [DM, TOK]),
        "w1": _ei(nc, "w1", [8192, 2048]), "w2": _ei(nc, "w2", [2048, 8192]),
        "out": _eo(nc, "out", [DM, TOK]),
    }
    return _single(nc, body_l0_ffn, io)



def body_l1_attn(P, cm, io):
    xbuf, qw_d, kw_d, vw_d, ow_d, cc_d, out_d = (
        io["xbuf"], io["qw"], io["kw"], io["vw"], io["ow"], io["cc"], io["out"])
    pb = [P.ps(f"pb{i}", [128, 512], F32) for i in range(8)]
    cm.pbR1, cm.pbR2, cm.pbB, cm.pbB2 = pb[0], pb[1], pb[2], pb[3]
    cm.pb_misc = pb[2]
    pbS = [pb[4], pb[5]]
    pbO = [pb[6], pb[7]]
    ident_dve = cm.dve.n

    ccr = P.sb("cc_r", [1, 8], F32)
    sml = cm.newsem("y")
    P.emit("y", lambda y: y.dma_start(out=ccr[:, :], in_=cc_d[:, :]), inc=sml, n=16)
    cc_cols = P.sb("cc_cols", [128, 8], F32)
    P.waitall("t", sml)
    P.wait("t", cm.dve, ident_dve)
    t1 = P.emit("t", lambda t: t.matmul(cm.pbB[0:128, 0:8], cm.ones1_128[:, :],
                                        ccr[0:1, 0:8], start=True, stop=True), inc=cm.mm)
    P.wait("s", cm.mm, t1)
    acc0 = P.emit("s", lambda s: s.copy(cc_cols[:, :], cm.pbB[0:128, 0:8]), inc=cm.act)
    negB = cm.const_col(-math.sqrt(128.0))

    big1 = P.sb("big1", [128, 8192], F32)
    P.wait("t", cm.act, acc0)
    P.waitall("v", sml)
    cosT, sinT, rope_rdy = build_rope_tables(
        P, cm, 10000.0, 128, 3072, ccr[0:1, 1:2],
        posB_ap=big1[:, 0:3072], frq_ap=big1[:, 3072:6144])

    QsT = P.sb("QsT", [128, 16 * SL], BF16)
    wscb = QsT[:, 0:4096]
    sc_q, ss_q = w_to_scratch(P, cm, "q", qw_d, 2048, 2048, buf=wscb)
    sc_k, ss_k = w_to_scratch(P, cm, "k", kw_d, 512, 2048, buf=wscb, prev=ss_q)
    sc_v, ss_v = w_to_scratch(P, cm, "v", vw_d, 512, 2048, buf=wscb, prev=ss_k)
    sc_o, ss_o = w_to_scratch(P, cm, "o", ow_d, 2048, 2048, buf=wscb, prev=ss_v)

    s_kw = cm.newsem("y")
    s_vw = cm.newsem("y")
    kwT = P.sb("kwT", [128, 16 * 512], BF16)
    for i in range(16):
        load_wT(P, cm, sc_k, 128 * i, 0, 512, kwT[:, 512 * i:512 * (i + 1)],
                ss_k, ss_k.n, sem=s_kw)
    vwT = P.sb("vwT", [128, 16 * 512], BF16)
    for i in range(16):
        P.wait("y", ss_v, ss_v.n)
        P.emit("y", lambda y, i=i: y.dma_start(
            out=vwT[:, 512 * i:512 * (i + 1)], in_=sc_v[0:512, 128 * i:128 * (i + 1)],
            transpose=True), inc=s_vw, n=16)

    NKT = 24
    Kt = P.sb("Kt", [128, 4 * 3072], BF16)
    Vt = P.sb("Vt", [128, NKT * 512], BF16)
    xr = P.sb("xr", [128, 4 * 512], BF16)
    xr_cons = [0, 0, 0, 0]
    s_xr = [cm.newsem("g") for _ in range(4)]
    s_xv = [cm.newsem("g") for _ in range(4)]
    rstdB = P.sb("rstdB", [128, SL], F32)
    msrow = rstdB[0:1, 0:SL]
    k0n = P.sb("k0n", [128, SL], F32)
    k1n = P.sb("k1n", [128, SL], F32)
    ktmp = P.sb("ktmp", [128, SL], F32)
    ktmp2 = P.sb("ktmp2", [64, SL], F32)
    sqb = P.sb("sqb", [128, 2 * SL], BF16)
    cm.pln_alias = (sqb[:, 0:SL], sqb[:, SL:2 * SL], k0n[:, 0:SL], k1n[:, 0:SL])

    P.wait("v", cm.act, rope_rdy)
    pbS_free = 0
    for ts in range(6):
        col = slice(512 * ts, 512 * (ts + 1))
        for g in range(2):
            for i in range(16):
                slot = i % 4
                b = xr[:, 512 * slot:512 * (slot + 1)]
                if xr_cons[slot]:
                    P.wait("g", cm.mm, xr_cons[slot])
                tgt = P.emit("g", lambda g_, b=b, s=xbuf[128 * i:128 * (i + 1), col]:
                             g_.dma_start(out=b, in_=s), inc=s_xr[slot], n=16, dma=True)
                P.wait("t", s_xr[slot], tgt)
                if i == 0:
                    P.waitall("t", s_kw)
                    if pbS_free:
                        P.wait("t", cm.dve, pbS_free)
                m1 = P.emit("t", lambda t, b=b, i=i, g=g: t.matmul(
                    pbS[0][0:128, 0:512], kwT[:, 512 * i + 256 * g:512 * i + 256 * g + 128],
                    b, start=(i == 0), stop=(i == 15)), inc=cm.mm)
                m2 = P.emit("t", lambda t, b=b, i=i, g=g: t.matmul(
                    pbS[1][0:128, 0:512],
                    kwT[:, 512 * i + 256 * g + 128:512 * i + 256 * (g + 1)],
                    b, start=(i == 0), stop=(i == 15)), inc=cm.mm)
                xr_cons[slot] = m2
            for hh in range(2):
                kvh = 2 * g + hh
                bank = pbS[hh]
                P.wait("s", cm.mm, m2)
                if pbS_free:
                    P.wait("s", cm.dve, pbS_free)
                a1 = P.emit("s", lambda s, bank=bank: s.activation(
                    out=sqb[:, 0:SL], in_=bank[0:128, 0:512], func=AF.Square),
                    inc=cm.act)
                P.wait("t", cm.act, a1)
                s1 = P.emit("t", lambda t: t.matmul(
                    cm.pbR1[0:1, 0:SL], cm.ones128_1b[:, :], sqb[:, 0:SL],
                    start=True, stop=True), inc=cm.mm)
                P.wait("s", cm.mm, s1)
                a3 = P.emit("s", lambda s: s.copy(msrow[0:1, :], cm.pbR1[0:1, 0:SL]),
                            inc=cm.act)
                P.wait("v", cm.act, a3)
                e1 = P.emit("v", lambda v: v.tensor_scalar(
                    msrow[0:1, :], msrow[0:1, :], 1.0 / 128.0, 1e-6,
                    op0=ALU.mult, op1=ALU.add), inc=cm.dve)
                P.wait("s", cm.dve, e1)
                a4 = P.emit("s", lambda s: s.activation(
                    out=msrow[0:1, :], in_=msrow[0:1, :], func=AF.Sqrt), inc=cm.act)
                P.wait("v", cm.act, a4)
                e2 = P.emit("v", lambda v: v.reciprocal(msrow[0:1, :], msrow[0:1, :]),
                            inc=cm.dve)
                P.wait("t", cm.dve, e2)
                _, ab = bcast_row(P, cm, msrow[0:1, :], cm.pbB, rstdB, SL)
                P.wait("v", cm.act, ab)
                e3 = P.emit("v", lambda v, bank=bank: v.tensor_tensor(
                    k0n[:, :], bank[0:128, 0:512], rstdB[:, :], ALU.mult), inc=cm.dve)
                pbS_free = e3
                rope_apply(P, cm, k0n[0:64, :], k0n[64:128, :],
                           cosT[0:64, col], sinT[0:64, col],
                           Kt[0:64, 3072 * kvh + 512 * ts:3072 * kvh + 512 * (ts + 1)],
                           Kt[64:128, 3072 * kvh + 512 * ts:3072 * kvh + 512 * (ts + 1)],
                           ktmp, SL, half=64, tmp2=ktmp2, ps_shift=cm.pbB)
                pbS_free = cm.dve.n
    k_done_dve = cm.dve.n

    xv = P.sb("xv", [128, 4 * 128], BF16)
    xv_cons = [0, 0, 0, 0]
    va_prev = {}
    for kt in range(NKT):
        bank = pbO[kt % 2]
        for i in range(16):
            slot = i % 4
            b = xv[:, 128 * slot:128 * (slot + 1)]
            if xv_cons[slot]:
                P.wait("g", cm.mm, xv_cons[slot])
            tgt = P.emit("g", lambda g_, b=b,
                         s=xbuf[128 * i:128 * (i + 1), 128 * kt:128 * (kt + 1)]:
                         g_.dma_start(out=b, in_=s), inc=s_xv[slot], n=16, dma=True)
            P.wait("t", s_xv[slot], tgt)
            if kt == 0 and i == 0:
                P.waitall("t", s_vw)
            if i == 0 and kt >= 2:
                P.wait("t", cm.act, va_prev[kt % 2])
            m1 = P.emit("t", lambda t, b=b, i=i, bank=bank: t.matmul(
                bank[0:128, 0:512], b, vwT[:, 512 * i:512 * (i + 1)],
                start=(i == 0), stop=(i == 15)), inc=cm.mm)
            xv_cons[slot] = m1
        P.wait("s", cm.mm, m1)
        va = P.emit("s", lambda s, kt=kt, bank=bank: s.copy(
            Vt[:, 512 * kt:512 * (kt + 1)], bank[0:128, 0:512]), inc=cm.act)
        va_prev[kt % 2] = va
    v_done = va

    Ed = P.sb("Ed", [128, 4 * SL], BF16)
    for d in range(4):
        P.emit("v", lambda v, d=d: v.tensor_scalar(
            Ed[:, SL * d:SL * (d + 1)], cm.iotaPF[:, :], float(128 * d), 0.0,
            op0=ALU.add, op1=ALU.is_le), inc=cm.dve)
    ed_done = cm.dve.n

    qwT = P.sb("qwT", [128, 16 * 512], BF16)
    xq = P.sb("xq", [128, 16 * 512], BF16)
    Qs = QsT
    Ctx = xq
    Pt = P.sb("Pt", [128, 2 * SL], BF16)
    bvs = P.sb("bvs", [128, 16], F32)
    rbS = rstdB
    sumr = rstdB[0:1, 0:SL]
    owT = qwT
    pln = PostLN(P, cm, "l1a", Rs_ap=big1[:, 0:8192])
    q0n, q1n = k0n, k1n

    exp_t = [0, 0]
    pv_t = [0, 0]
    pmask_t = [0, 0]
    sum_evac = 0
    qwT_free_mm = 0
    oproj_last_mm = 0
    for s_ in range(NSL):
        colq = slice(1024 + SL * s_, 1024 + SL * (s_ + 1))
        P.wait("g", cm.mm, cm.mm.n)
        if s_ == 0:
            s_xq = cm.newsem("g")
            s_qw = cm.newsem("y")
            s_res = cm.newsem("y")
            P.wait("y", ss_q, ss_q.n)
            P.wait("v", ss_o, ss_o.n)
            P.wait("s", ss_o, ss_o.n)
        for i in range(16):
            P.emit("g", lambda g_, i=i, s=xbuf[128 * i:128 * (i + 1), colq]:
                   g_.dma_start(out=xq[:, 512 * i:512 * (i + 1)], in_=s),
                   inc=s_xq, n=16, dma=True)
        q_bank_free = k_done_dve if s_ == 0 else 0
        for oblk in range(4):
            if s_ == 0 and oblk == 0:
                P.wait("t", cm.act, v_done)
            P.wait("y", cm.mm, max(qwT_free_mm, oproj_last_mm))
            for i in range(16):
                P.emit("y", lambda y, i=i, oblk=oblk: y.dma_start(
                    out=qwT[:, 512 * i:512 * (i + 1)],
                    in_=sc_q[512 * oblk:512 * (oblk + 1), 128 * i:128 * (i + 1)],
                    transpose=True), inc=s_qw, n=16)
            P.waitall("t", s_qw)
            if oblk == 0:
                P.waitall("t", s_xq)
            if q_bank_free:
                P.wait("t", cm.dve, q_bank_free)
            for i in range(16):
                for oc in range(4):
                    m1 = P.emit("t", lambda t, i=i, oc=oc: t.matmul(
                        pb[4 + oc][0:128, 0:512],
                        qwT[:, 512 * i + 128 * oc:512 * i + 128 * (oc + 1)],
                        xq[:, 512 * i:512 * (i + 1)],
                        start=(i == 0), stop=(i == 15)), inc=cm.mm)
            qwT_free_mm = m1
            for hh in range(4):
                head = oblk * 4 + hh
                bank = pb[4 + hh]
                P.wait("s", cm.mm, m1)
                P.wait("s", cm.dve, cm.dve.n)
                a1 = P.emit("s", lambda s, bank=bank: s.activation(
                    out=sqb[:, 0:SL], in_=bank[0:128, 0:512], func=AF.Square),
                    inc=cm.act)
                P.wait("t", cm.act, a1)
                s1 = P.emit("t", lambda t: t.matmul(
                    cm.pbR1[0:1, 0:SL], cm.ones128_1b[:, :], sqb[:, 0:SL],
                    start=True, stop=True), inc=cm.mm)
                P.wait("s", cm.mm, s1)
                a3 = P.emit("s", lambda s: s.copy(msrow[0:1, :], cm.pbR1[0:1, 0:SL]),
                            inc=cm.act)
                P.wait("v", cm.act, a3)
                e1 = P.emit("v", lambda v: v.tensor_scalar(
                    msrow[0:1, :], msrow[0:1, :], 1.0 / 128.0, 1e-6,
                    op0=ALU.mult, op1=ALU.add), inc=cm.dve)
                P.wait("s", cm.dve, e1)
                a4 = P.emit("s", lambda s: s.activation(
                    out=msrow[0:1, :], in_=msrow[0:1, :], func=AF.Sqrt), inc=cm.act)
                P.wait("v", cm.act, a4)
                e2 = P.emit("v", lambda v: v.reciprocal(msrow[0:1, :], msrow[0:1, :]),
                            inc=cm.dve)
                P.wait("t", cm.dve, e2)
                _, ab = bcast_row(P, cm, msrow[0:1, :], cm.pbB, rstdB, SL)
                P.wait("v", cm.act, ab)
                e3 = P.emit("v", lambda v, bank=bank: v.tensor_tensor(
                    q0n[:, :], bank[0:128, 0:512], rstdB[:, :], ALU.mult), inc=cm.dve)
                rope_apply(P, cm, q0n[0:64, :], q0n[64:128, :],
                           cosT[0:64, colq], sinT[0:64, colq],
                           Qs[0:64, SL * head:SL * (head + 1)],
                           Qs[64:128, SL * head:SL * (head + 1)], ktmp, SL,
                           half=64, tmp2=ktmp2, ps_shift=cm.pbB)
                q_bank_free = cm.dve.n
        q_rope_done = cm.dve.n

        P.wait("v", cm.act, acc0)
        if exp_t[0] or exp_t[1]:
            P.wait("v", cm.act, max(exp_t))
        for j in range(12):
            kt = 4 * s_ + j
            P.emit("v", lambda v, j=j, kt=kt: v.tensor_scalar(
                bvs[:, j:j + 1], cm.iotaP[:, :], float(128 * kt), cc_cols[0:128, 0:1],
                op0=ALU.add, op1=ALU.is_lt), inc=cm.dve)
            P.emit("v", lambda v, j=j, kt=kt, s_=s_: v.tensor_scalar(
                bvs[:, 12:13], cm.iotaP[:, :], float(128 * kt - (512 * s_ + 1)), 0.0,
                op0=ALU.add, op1=ALU.is_lt), inc=cm.dve)
            P.emit("v", lambda v, j=j: v.tensor_tensor(
                bvs[:, j:j + 1], bvs[:, j:j + 1], bvs[:, 12:13], ALU.add), inc=cm.dve)
            P.emit("v", lambda v, j=j: v.scalar_tensor_tensor(
                bvs[:, j:j + 1], bvs[:, j:j + 1], NEG, negB[:, :],
                op0=ALU.mult, op1=ALU.add), inc=cm.dve)
        bv_done = cm.dve.n

        P.wait("t", cm.dve, max(k_done_dve, ed_done, q_rope_done, bv_done))
        P.wait("t", cm.act, v_done)
        for h in range(16):
            kvh = h // 4
            if sum_evac:
                P.wait("t", cm.act, sum_evac)
            for j in range(12):
                kt = 4 * s_ + j
                par = j % 2
                bankS = pbS[par]
                if exp_t[par]:
                    P.wait("t", cm.act, exp_t[par])
                if pv_t[par]:
                    P.wait("t", cm.mm, pv_t[par])
                m1 = P.emit("t", lambda t, h=h, kt=kt, kvh=kvh, bankS=bankS: t.matmul(
                    bankS[0:128, 0:512], Kt[:, 3072 * kvh + 128 * kt:3072 * kvh + 128 * (kt + 1)],
                    Qs[:, SL * h:SL * (h + 1)], start=True, stop=True), inc=cm.mm)
                P.wait("s", cm.mm, m1)
                P.wait("s", cm.dve, bv_done)
                ptile = Pt[:, SL * par:SL * (par + 1)]
                if pmask_t[par]:
                    P.wait("s", cm.dve, pmask_t[par])
                ex = P.emit("s", lambda s, bankS=bankS, ptile=ptile, j=j: s.activation(
                    out=ptile, in_=bankS[0:128, 0:512], func=AF.Exp,
                    bias=bvs[0:128, j:j + 1], scale=1.0 / math.sqrt(128.0)), inc=cm.act)
                exp_t[par] = ex
                dep = ("act", ex)
                if j >= 8:
                    P.wait("v", cm.act, ex)
                    pm = P.emit("v", lambda v, ptile=ptile, j=j: v.tensor_tensor(
                        ptile, ptile, Ed[:, SL * (j - 8):SL * (j - 7)], ALU.mult),
                        inc=cm.dve)
                    pmask_t[par] = pm
                    dep = ("dve", pm)
                if dep[0] == "act":
                    P.wait("t", cm.act, dep[1])
                else:
                    P.wait("t", cm.dve, dep[1])
                p1 = P.emit("t", lambda t, kt=kt, kvh=kvh, ptile=ptile, j=j: t.matmul(
                    pbO[0][0:128, 0:512], Vt[:, 512 * kt + 128 * kvh:512 * kt + 128 * (kvh + 1)],
                    ptile, start=(j == 0), stop=(j == 11)), inc=cm.mm)
                p3 = P.emit("t", lambda t, ptile=ptile, j=j: t.matmul(
                    cm.pbB2[0:1, 0:512], cm.ones128_1b[:, :], ptile,
                    start=(j == 0), stop=(j == 11)), inc=cm.mm)
                pv_t[par] = p3
            P.wait("s", cm.mm, p3)
            sum_evac = P.emit("s", lambda s: s.copy(sumr[0:1, :], cm.pbB2[0:1, 0:512]),
                              inc=cm.act)
            P.wait("v", cm.act, sum_evac)
            e1 = P.emit("v", lambda v: v.reciprocal(sumr[0:1, :], sumr[0:1, :]),
                        inc=cm.dve)
            P.wait("t", cm.dve, e1)
            _, ab = bcast_row(P, cm, sumr[0:1, :], cm.pbB, rbS, SL)
            P.wait("v", cm.act, ab)
            ec0 = P.emit("v", lambda v, h=h: v.tensor_tensor(
                Ctx[:, SL * h:SL * (h + 1)], pbO[0][0:128, 0:512], rbS[:, :],
                ALU.mult), inc=cm.dve)
            P.wait("t", cm.dve, ec0)

        P.wait("v", cm.act, rope_rdy)
        pln.begin()
        res_cons = [0, 0]
        for oblk in range(4):
            if s_ == 0 and oblk == 0:
                P.wait("y", ss_o, ss_o.n)
            P.wait("y", cm.mm, max(oproj_last_mm, qwT_free_mm))
            for i in range(16):
                P.emit("y", lambda y, i=i, oblk=oblk: y.dma_start(
                    out=owT[:, 512 * i:512 * (i + 1)],
                    in_=sc_o[512 * oblk:512 * (oblk + 1), 128 * i:128 * (i + 1)],
                    transpose=True), inc=s_qw, n=16)
            P.waitall("t", s_qw)
            for i in range(16):
                for j in range(4):
                    m1 = P.emit("t", lambda t, i=i, j=j: t.matmul(
                        pb[4 + j][0:128, 0:512],
                        owT[:, 512 * i + 128 * j:512 * i + 128 * (j + 1)],
                        Ctx[:, 512 * i:512 * (i + 1)],
                        start=(i == 0), stop=(i == 15)), inc=cm.mm)
            oproj_last_mm = m1
            qwT_free_mm = m1
            for j in range(4):
                oc = oblk * 4 + j
                if res_cons[0]:
                    P.wait("y", cm.dve, res_cons[0])
                P.emit("y", lambda y, oc=oc, colq=colq: y.dma_start(
                    out=pln.Rs[:, SL * oc:SL * (oc + 1)],
                    in_=xbuf[128 * oc:128 * (oc + 1), colq]), inc=s_res, n=16)
                P.wait("v", cm.mm, oproj_last_mm)
                P.waitall("v", s_res)
                ee = pln.add_chunk(oc, pb[4 + j][0:128, 0:512])
                res_cons[0] = ee
            P.wait("t", cm.dve, ee)
        pln.finalize(dst_dram_ap=out_d[:, SL * s_:SL * (s_ + 1)])
    P.wait("y", cm.st, cm.st.n)


def build_l1_attn():
    nc = bass.Bass("TRN2", target_bir_lowering=False)
    io = {
        "xbuf": _ei(nc, "xbuf", [DM, 3072]),
        "qw": _ei(nc, "qw", [2048, 2048]), "kw": _ei(nc, "kw", [512, 2048]),
        "vw": _ei(nc, "vw", [512, 2048]), "ow": _ei(nc, "ow", [2048, 2048]),
        "cc": _ei(nc, "cc", [1, 8]),
        "out": _eo(nc, "out", [DM, TOK]),
    }
    return _single(nc, body_l1_attn, io)


I32 = mybir.dt.int32


def build_hadamard(P, cm, name, n, scale):
    ha = P.sb(f"had_{name}", [128, n], BF16)
    ai = P.sb(f"hadi_{name}", [128, 2 * n], I32)
    A = ai[0:n, 0:n]
    B = ai[0:n, n:2 * n]
    P.emit("g", lambda g: g.iota(A, pattern=[[0, n]], base=0, channel_multiplier=1),
           inc=cm.pool)
    gp = P.emit("g", lambda g: g.iota(B, pattern=[[1, n]], base=0, channel_multiplier=0),
                inc=cm.pool)
    P.wait("v", cm.pool, gp)
    P.emit("v", lambda v: v.tensor_tensor(A, A, B, ALU.bitwise_and), inc=cm.dve)
    for sh in (1, 2, 4, 8, 16, 32, 64):
        if sh < n:
            P.emit("v", lambda v, sh=sh: v.tensor_scalar(
                B, A, sh, None, op0=ALU.logical_shift_right), inc=cm.dve)
            P.emit("v", lambda v: v.tensor_tensor(A, A, B, ALU.bitwise_xor), inc=cm.dve)
    P.emit("v", lambda v: v.tensor_scalar(A, A, 1, None, op0=ALU.bitwise_and),
           inc=cm.dve)
    e1 = P.emit("v", lambda v: v.tensor_copy(ha[0:n, 0:n], A), inc=cm.dve)
    e2 = P.emit("v", lambda v: v.tensor_scalar(
        ha[0:n, 0:n], ha[0:n, 0:n], -2.0 * scale, float(scale),
        op0=ALU.mult, op1=ALU.add), inc=cm.dve)
    return ha, e2



def body_l1_ffn(P, cm, io):
    xin, w1_d, w2_d, w3_d, out_d = (io["xin"], io["w1"], io["w2"], io["w3"],
                                    io["out"])
    pb = [P.ps(f"pb{i}", [128, 512], F32) for i in range(8)]
    cm.pbR1, cm.pbR2, cm.pbB, cm.pbB2 = pb[0], pb[1], pb[2], pb[3]
    cm.pb_misc = pb[2]

    W = 512
    NSLF = TOK // W
    h128, hrdy1 = build_hadamard(P, cm, "h128", 128, 2.0 ** -7)
    h64, hrdy2 = build_hadamard(P, cm, "h64", 64, 1.0)

    QsT = P.sb("QsT", [128, 64 * W], BF16)
    HG = QsT
    wscb = QsT[:, 0:4096]
    sc1, ss1 = w_to_scratch(P, cm, "w1", w1_d, 8192, 2048, buf=wscb)
    sc3, ss3 = w_to_scratch(P, cm, "w3", w3_d, 8192, 2048, buf=wscb, prev=ss1)
    sc2, ss2 = w_to_scratch(P, cm, "w2", w2_d, 2048, 8192, buf=wscb, prev=ss3)
    zscr = P.dram("zscr", [8192, W], BF16)
    zview = zscr.rearrange("(k p) t -> k (p t)", k=64)
    XB = P.sb("XB", [128, 16 * W], BF16)
    SG = P.sb("SGb", [128, W], BF16)
    wring = P.sb("wring", [128, 8 * 512], BF16)
    zring = P.sb("zring", [64, 4 * 512], BF16)
    pln = PostLN(P, cm, "l1f", W=W)
    ring_sems = [cm.newsem("y") for _ in range(8)]
    ring_cons = [0] * 8
    zr_sems = [cm.newsem("y") for _ in range(4)]
    zr_cons = [0] * 4
    s_xb = cm.newsem("g")
    s_res = cm.newsem("y")
    s_zst = cm.newsem("y")
    s_yst = cm.newsem("y")
    s_hld = cm.newsem("y")

    P.waitall("y", ss2)
    P.waitall("g", ss2)
    P.wait("t", cm.dve, max(hrdy1, hrdy2))
    res_prev = 0
    xb_cons = 0
    hg_evac_prev = 0
    hload_prev = 0
    for s_ in range(NSLF):
        colt = slice(W * s_, W * (s_ + 1))
        pln.begin()
        for i in range(16):
            if res_prev:
                P.wait("y", cm.dve, res_prev)
            P.emit("y", lambda y, i=i, colt=colt: y.dma_start(
                out=pln.Rs[:, W * i:W * (i + 1)], in_=xin[128 * i:128 * (i + 1), colt]),
                inc=s_res, n=16)
            res_prev = 0 if i < 15 else res_prev
        if xb_cons:
            P.wait("g", cm.mm, xb_cons)
        for i in range(16):
            P.emit("g", lambda g, i=i, colt=colt: g.dma_start(
                out=XB[:, W * i:W * (i + 1)], in_=xin[128 * i:128 * (i + 1), colt]),
                inc=s_xb, n=16, dma=True)
        P.waitall("t", s_xb)
        P.wait("t", cm.act, cm.act.n)
        for hb in range(16):
            banks_done = 0
            for src, bs in ((sc1, 4), (sc3, 0)):
                for i in range(16):
                    slot = (hb * 32 + (0 if bs else 16) + i) % 8
                    if ring_cons[slot]:
                        P.wait("y", cm.mm, ring_cons[slot])
                    lt = P.emit("y", lambda y, hb=hb, i=i, slot=slot, src=src: y.dma_start(
                        out=wring[:, 512 * slot:512 * (slot + 1)],
                        in_=src[512 * hb:512 * (hb + 1), 128 * i:128 * (i + 1)],
                        transpose=True), inc=ring_sems[slot], n=16)
                    P.wait("t", ring_sems[slot], lt)
                    if hb == 0 and i == 0 and bs == 4 and s_ == 0:
                        pass
                    if i == 0 and hg_evac_prev and bs == 4:
                        P.wait("t", cm.dve, hg_evac_prev)
                    for oc in range(4):
                        m1 = P.emit("t", lambda t, slot=slot, oc=oc, i=i, bs=bs: t.matmul(
                            pb[bs + oc][0:128, 0:W],
                            wring[:, 512 * slot + 128 * oc:512 * slot + 128 * (oc + 1)],
                            XB[:, W * i:W * (i + 1)],
                            start=(i == 0), stop=(i == 15)), inc=cm.mm)
                    ring_cons[slot] = m1
            P.wait("s", cm.mm, m1)
            for oc in range(4):
                ch = hb * 4 + oc
                if hg_evac_prev:
                    P.wait("s", cm.dve, hg_evac_prev)
                a1 = P.emit("s", lambda s, oc=oc: s.activation(
                    out=SG[:, :], in_=pb[4 + oc][0:128, 0:W], func=AF.Silu), inc=cm.act)
                P.wait("v", cm.act, a1)
                e1 = P.emit("v", lambda v, oc=oc, ch=ch: v.tensor_tensor(
                    HG[:, W * ch:W * (ch + 1)], SG[:, :], pb[oc][0:128, 0:W], ALU.mult),
                    inc=cm.dve)
                hg_evac_prev = e1
        av_par = {0: 0, 1: 0}
        for ch in range(64):
            P.wait("t", cm.dve, hg_evac_prev)
            if av_par[ch % 2]:
                P.wait("t", cm.act, av_par[ch % 2])
            m1 = P.emit("t", lambda t, ch=ch: t.matmul(
                pb[4 + (ch % 2)][0:128, 0:W], h128[0:128, 0:128],
                HG[:, W * ch:W * (ch + 1)], start=True, stop=True), inc=cm.mm)
            P.wait("s", cm.mm, m1)
            av = P.emit("s", lambda s, ch=ch: s.copy(
                HG[:, W * ch:W * (ch + 1)], pb[4 + (ch % 2)][0:128, 0:W]), inc=cm.act)
            av_par[ch % 2] = av
            P.wait("y", cm.act, av)
            P.emit("y", lambda y, ch=ch: y.dma_start(
                out=zscr[128 * ch:128 * (ch + 1), :], in_=HG[:, W * ch:W * (ch + 1)]),
                inc=s_zst, n=16)
        P.waitall("y", s_zst)
        nz = (128 * W) // 512
        for j in range(nz):
            slot = j % 4
            if zr_cons[slot]:
                P.wait("y", cm.mm, zr_cons[slot])
            lt = P.emit("y", lambda y, j=j, slot=slot: y.dma_start(
                out=zring[:, 512 * slot:512 * (slot + 1)],
                in_=zview[:, 512 * j:512 * (j + 1)]), inc=zr_sems[slot], n=16)
            P.wait("t", zr_sems[slot], lt)
            m1 = P.emit("t", lambda t, slot=slot, j=j: t.matmul(
                pb[4 + (j % 2)][0:64, 0:512], h64[0:64, 0:64],
                zring[0:64, 512 * slot:512 * (slot + 1)], start=True, stop=True),
                inc=cm.mm)
            zr_cons[slot] = m1
            P.wait("s", cm.mm, m1)
            av = P.emit("s", lambda s, slot=slot, j=j: s.activation(
                out=zring[0:64, 512 * slot:512 * (slot + 1)],
                in_=pb[4 + (j % 2)][0:64, 0:512], func=AF.Copy,
                scale=float(math.sqrt(2.0))), inc=cm.act)
            P.wait("y", cm.act, av)
            zr_cons[slot] = cm.mm.n
            P.emit("y", lambda y, j=j, slot=slot: y.dma_start(
                out=zview[:, 512 * j:512 * (j + 1)],
                in_=zring[0:64, 512 * slot:512 * (slot + 1)]), inc=s_yst, n=16)
            zr_cons[slot] = 0
            P.waitall("y", s_yst)
        P.waitall("y", s_yst)
        if hload_prev:
            P.wait("y", cm.mm, hload_prev)
        for ch in range(64):
            P.emit("y", lambda y, ch=ch: y.dma_start(
                out=HG[:, W * ch:W * (ch + 1)], in_=zscr[128 * ch:128 * (ch + 1), :]),
                inc=s_hld, n=16)
        P.waitall("t", s_hld)
        oproj_mm = 0
        for ob in range(4):
            for i in range(64):
                slot = i % 8
                if ring_cons[slot]:
                    P.wait("y", cm.mm, ring_cons[slot])
                lt = P.emit("y", lambda y, ob=ob, i=i, slot=slot: y.dma_start(
                    out=wring[:, 512 * slot:512 * (slot + 1)],
                    in_=sc2[512 * ob:512 * (ob + 1), 128 * i:128 * (i + 1)],
                    transpose=True), inc=ring_sems[slot], n=16)
                P.wait("t", ring_sems[slot], lt)
                for oc in range(4):
                    m1 = P.emit("t", lambda t, slot=slot, oc=oc, i=i: t.matmul(
                        pb[4 + oc][0:128, 0:W],
                        wring[:, 512 * slot + 128 * oc:512 * slot + 128 * (oc + 1)],
                        HG[:, W * i:W * (i + 1)],
                        start=(i == 0), stop=(i == 63)), inc=cm.mm)
                ring_cons[slot] = m1
            oproj_mm = m1
            for oc in range(4):
                chn = ob * 4 + oc
                P.wait("v", cm.mm, oproj_mm)
                P.waitall("v", s_res)
                ee = pln.add_chunk(chn, pb[4 + oc][0:128, 0:W])
            P.wait("t", cm.dve, ee)
        pln.finalize(dst_dram_ap=out_d[:, W * s_:W * (s_ + 1)])
        res_prev = ee
        xb_cons = cm.mm.n
        hload_prev = cm.mm.n
    P.wait("y", cm.st, cm.st.n)


def build_l1_ffn():
    nc = bass.Bass("TRN2", target_bir_lowering=False)
    io = {
        "xin": _ei(nc, "xin", [DM, TOK]),
        "w1": _ei(nc, "w1", [8192, 2048]), "w2": _ei(nc, "w2", [2048, 8192]),
        "w3": _ei(nc, "w3", [8192, 2048]),
        "out": _eo(nc, "out", [DM, TOK]),
    }
    return _single(nc, body_l1_ffn, io)



def body_l2a(P, cm, io, dbg=False):
    xin, wq_d, wk_d, wv_d, ld_d, o_out, q_out, s_out = (
        io["xin"], io["wq"], io["wk"], io["wv"], io["ld"], io["o_out"],
        io["q_out"], io["s_out"])
    if dbg:
        d_dbg = io["d_dbg"]
    pb = [P.ps(f"pb{i}", [128, 512], F32) for i in range(7)]
    pbT = P.ps("pbT", [128, 1024], BF16)
    cm.pbB = pb[2]
    cm.pb_misc = pb[2]
    ident_dve = cm.dve.n

    ldr = P.sb("ldr", [1, 16], F32)
    sml = cm.newsem("y")
    P.emit("y", lambda y: y.dma_start(out=ldr[:, :], in_=ld_d[:, :]), inc=sml, n=16)
    ldc = P.sb("ldc", [128, 64], F32)
    P.waitall("t", sml)
    P.wait("t", cm.dve, ident_dve)
    t1 = P.emit("t", lambda t: t.matmul(pb[2][0:128, 0:16], cm.ones1_128[:, :],
                                        ldr[0:1, 0:16], start=True, stop=True), inc=cm.mm)
    P.wait("s", cm.mm, t1)
    a1 = P.emit("s", lambda s: s.copy(ldc[:, 0:16], pb[2][0:128, 0:16]), inc=cm.act)
    P.wait("v", cm.act, a1)
    P.emit("v", lambda v: v.tensor_scalar(ldc[:, 16:32], ldc[:, 0:16], -1.0, None,
                                          op0=ALU.mult), inc=cm.dve)
    el = P.emit("v", lambda v: v.tensor_scalar(ldc[:, 32:48], ldc[:, 0:16], 127.0, None,
                                               op0=ALU.mult), inc=cm.dve)
    Dt = P.sb("Dt", [128, 16 * 128], BF16)
    qdec = P.sb("qdec", [128, 16], F32)
    kdec = P.sb("kdec", [128, 16], F32)
    cdec = P.sb("cdec", [128, 16], F32)
    Ed0 = P.sb("Ed0", [128, 128], BF16)
    P.emit("v", lambda v: v.tensor_scalar(
        Ed0[:, :], cm.iotaPF[:, 0:128], 0.0, None, op0=ALU.is_le), inc=cm.dve)
    dtmp = P.sb("dtmp", [128, 128], F32)
    P.wait("s", cm.dve, el)
    dtmp2 = P.sb("dtmp2", [128, 128], F32)
    for h in range(16):
        ea = P.emit("v", lambda v, h=h: v.tensor_scalar(
            dtmp2[:, :], cm.iotaPF[:, 0:128], ldc[0:128, 16 + h:17 + h], 0.0,
            op0=ALU.mult, op1=ALU.min), inc=cm.dve)
        P.wait("s", cm.dve, ea)
        a2 = P.emit("s", lambda s, h=h: s.activation(
            out=dtmp[:, :], in_=dtmp2[:, :], func=AF.Exp), inc=cm.act)
        P.wait("v", cm.act, a2)
        ed = P.emit("v", lambda v, h=h: v.tensor_tensor(
            Dt[:, 128 * h:128 * (h + 1)], dtmp[:, :], Ed0[:, :], ALU.mult), inc=cm.dve)
        P.wait("s", cm.dve, ed)
        P.emit("s", lambda s, h=h: s.activation(
            out=qdec[:, h:h + 1], in_=cm.iotaP[:, :], func=AF.Exp,
            scale=ldc[0:128, h:h + 1], bias=ldc[0:128, h:h + 1]), inc=cm.act)
        P.emit("s", lambda s, h=h: s.activation(
            out=kdec[:, h:h + 1], in_=cm.iotaP[:, :], func=AF.Exp,
            scale=ldc[0:128, 16 + h:17 + h], bias=ldc[0:128, 32 + h:33 + h]), inc=cm.act)
        P.emit("s", lambda s, h=h: s.activation(
            out=cdec[:, h:h + 1], in_=ldc[0:128, h:h + 1], func=AF.Exp,
            scale=128.0), inc=cm.act)
    dec_done_act = cm.act.n
    dec_done_dve = cm.dve.n

    QG = P.sb("QG", [128, 4 * 2048], BF16)
    KG = P.sb("KG", [128, 16 * 512], BF16)
    VG = P.sb("VG", [128, 16 * 512], BF16)
    OG = P.sb("OG", [128, 16 * 512], BF16)
    wring = P.sb("wring", [128, 8 * 512], BF16)
    wring2 = P.sb("wring2", [128, 8 * 512], BF16)
    xr = P.sb("xr", [128, 4 * 512], BF16)
    xv = P.sb("xv", [128, 4 * 128], BF16)
    ring_sems = [cm.newsem("g") for _ in range(8)]
    ring2_sems = [cm.newsem("g") for _ in range(8)]
    xr_sems = [cm.newsem("g") for _ in range(4)]
    xv_sems = [cm.newsem("g") for _ in range(4)]
    ring_cons = [0] * 8
    ring2_cons = [0] * 8
    xr_cons = [0] * 4
    xv_cons = [0] * 4
    ktb = P.sb("ktb", [128, 128], BF16)
    kd_b = P.sb("kd_b", [128, 128], BF16)
    oi_b = P.sb("oi_b", [128, 128], BF16)
    S32 = P.sb("S32", [128, 2 * 128], F32)
    Sb = P.sb("Sb", [128, 128], BF16)
    ktb2 = P.sb("ktb2", [128, 128], BF16)
    kd_b2 = P.sb("kd_b2", [128, 128], BF16)
    oi_b2 = P.sb("oi_b2", [128, 128], BF16)
    S32b = P.sb("S32b", [128, 2 * 128], F32)
    Sbb = P.sb("Sbb", [128, 128], BF16)
    s_st = cm.newsem("y")

    P.wait("t", cm.dve, dec_done_dve)
    P.wait("t", cm.act, dec_done_act)
    XG = P.sb("XG2", [128, 16 * 512], BF16)
    kwT2 = P.sb("kwT2", [128, 16 * 512], BF16)
    s_xg = cm.newsem("g")
    s_kw2 = cm.newsem("g")
    xg_cons = 0
    kw_cons = 0
    kv_evac_prev = 0
    for g in range(4):
        for tsl in range(4):
            for i in range(16):
                xslot = i % 4
                if xr_cons[xslot]:
                    P.wait("g", cm.mm, xr_cons[xslot])
                xt = P.emit("g", lambda g_, i=i, tsl=tsl, xslot=xslot: g_.dma_start(
                    out=xr[:, 512 * xslot:512 * (xslot + 1)],
                    in_=xin[128 * i:128 * (i + 1), 512 * tsl:512 * (tsl + 1)]),
                    inc=xr_sems[xslot], n=16, dma=True)
                slot = i % 8
                if ring_cons[slot]:
                    P.wait("g", cm.mm, ring_cons[slot])
                lt = P.emit("g", lambda g_, i=i, g=g, slot=slot: g_.dma_start(
                    out=wring[:, 512 * slot:512 * (slot + 1)],
                    in_=wq_d[128 * i:128 * (i + 1), 512 * g:512 * (g + 1)]),
                    inc=ring_sems[slot], n=16, dma=True)
                P.wait("t", ring_sems[slot], lt)
                P.wait("t", xr_sems[xslot], xt)
                for cc in range(4):
                    m1 = P.emit("t", lambda t, slot=slot, xslot=xslot, cc=cc, i=i: t.matmul(
                        pb[3 + cc][0:128, 0:512],
                        wring[:, 512 * slot + 128 * cc:512 * slot + 128 * (cc + 1)],
                        xr[:, 512 * xslot:512 * (xslot + 1)],
                        start=(i == 0), stop=(i == 15)), inc=cm.mm)
                ring_cons[slot] = m1
                xr_cons[xslot] = m1
            P.wait("s", cm.mm, m1)
            for cc in range(4):
                av = P.emit("s", lambda s, cc=cc, tsl=tsl: s.activation(
                    out=QG[:, 2048 * cc + 512 * tsl:2048 * cc + 512 * (tsl + 1)],
                    in_=pb[3 + cc][0:128, 0:512], func=AF.Silu), inc=cm.act)
            P.wait("t", cm.act, av)
        q_done = cm.act.n
        P.wait("y", cm.act, q_done)
        for cc in range(4):
            P.emit("y", lambda y, g=g, cc=cc: y.dma_start(
                out=q_out[512 * g + 128 * cc:512 * g + 128 * (cc + 1), :],
                in_=QG[:, 2048 * cc:2048 * (cc + 1)]), inc=s_st, n=16)
        for grp in range(4):
            tok0 = 512 * grp
            P.wait("g", cm.mm, max(xg_cons, kw_cons) if (xg_cons or kw_cons)
                   else 0)
            for i in range(16):
                P.emit("g", lambda g_, i=i, tok0=tok0: g_.dma_start(
                    out=XG[:, 512 * i:512 * (i + 1)],
                    in_=xin[128 * i:128 * (i + 1), tok0:tok0 + 512]),
                    inc=s_xg, n=16, dma=True)
            P.waitall("t", s_xg)
            for src, dstG in ((wk_d, KG), (wv_d, VG)):
                if kw_cons:
                    P.wait("g", cm.mm, kw_cons)
                for i in range(16):
                    P.emit("g", lambda g_, i=i, g=g, src=src: g_.dma_start(
                        out=kwT2[:, 512 * i:512 * (i + 1)],
                        in_=src[128 * i:128 * (i + 1), 512 * g:512 * (g + 1)]),
                        inc=s_kw2, n=16, dma=True)
                P.waitall("t", s_kw2)
                if kv_evac_prev:
                    P.wait("t", cm.act, kv_evac_prev)
                for i in range(16):
                    for t4 in range(4):
                        m1 = P.emit("t", lambda t, i=i, t4=t4: t.matmul(
                            pb[3 + t4][0:128, 0:512],
                            XG[:, 512 * i + 128 * t4:512 * i + 128 * (t4 + 1)],
                            kwT2[:, 512 * i:512 * (i + 1)],
                            start=(i == 0), stop=(i == 15)), inc=cm.mm)
                kw_cons = m1
                P.wait("s", cm.mm, m1)
                for t4 in range(4):
                    tt = 4 * grp + t4
                    av = P.emit("s", lambda s, t4=t4, tt=tt, dstG=dstG: s.activation(
                        out=dstG[:, 512 * tt:512 * (tt + 1)],
                        in_=pb[3 + t4][0:128, 0:512], func=AF.Silu), inc=cm.act)
                kv_evac_prev = av
            xg_cons = cm.mm.n
        P.wait("t", cm.act, kv_evac_prev)
        kv_done = cm.act.n
        P.wait("t", cm.act, kv_done)
        for hp in range(2):
            sts = []
            for j in range(2):
                hh = 2 * hp + j
                sts.append({
                    "hh": hh, "hgl": 4 * g + hh, "spar": 0,
                    "S32": S32 if j == 0 else S32b,
                    "Sb": Sb if j == 0 else Sbb,
                    "ktb": ktb if j == 0 else ktb2,
                    "kd_b": kd_b if j == 0 else kd_b2,
                    "oi_b": oi_b if j == 0 else oi_b2,
                    "bk": (4, 5, 6) if j == 0 else (3, 0, 1),
                    "ptc": 128 * j,
                })
            P.waitall("v", s_st)
            for st in sts:
                P.emit("v", lambda v, S32_=st["S32"]: v.memset(S32_[:, :], 0.0),
                       inc=cm.dve)
                e2 = P.emit("v", lambda v, Sb_=st["Sb"]: v.memset(Sb_[:, :], 0.0),
                            inc=cm.dve)
                st["ecp"] = e2
            for c in range(16):
                for st in sts:
                    hh, hgl = st["hh"], st["hgl"]
                    bkA, bkB_, bkC = st["bk"]
                    ktb_, kd_b_, oi_b_ = st["ktb"], st["kd_b"], st["oi_b"]
                    S32_, Sb_ = st["S32"], st["Sb"]
                    ptc = st["ptc"]
                    spar = st["spar"]
                    P.wait("t", cm.dve, st["ecp"])
                    P.wait("s", cm.dve, st["ecp"])
                    tr = P.emit("t", lambda t, c=c, hh=hh, ptc=ptc: t.transpose(
                        pbT[0:128, ptc:ptc + 128],
                        KG[:, 512 * c + 128 * hh:512 * c + 128 * (hh + 1)],
                        cm.identb[:, :]), inc=cm.mm)
                    P.wait("s", cm.mm, tr)
                    ak = P.emit("s", lambda s, ktb_=ktb_, ptc=ptc: s.copy(
                        ktb_[:, :], pbT[0:128, ptc:ptc + 128]), inc=cm.act)
                    P.wait("t", cm.act, ak)
                    m1 = P.emit("t", lambda t, c=c, hh=hh, ktb_=ktb_, bkA=bkA:
                                t.matmul(
                        pb[bkA][0:128, 0:128], ktb_[:, :],
                        QG[:, 2048 * hh + 128 * c:2048 * hh + 128 * (c + 1)],
                        start=True, stop=True), inc=cm.mm)
                    P.wait("v", cm.mm, m1)
                    P.wait("v", cm.act, ak)
                    ep = P.emit("v", lambda v, hgl=hgl, ktb_=ktb_, bkA=bkA:
                                v.tensor_tensor(
                        ktb_[:, :], pb[bkA][0:128, 0:128],
                        Dt[:, 128 * hgl:128 * (hgl + 1)], ALU.mult), inc=cm.dve)
                    P.wait("t", cm.dve, ep)
                    m2 = P.emit("t", lambda t, c=c, hh=hh, ktb_=ktb_, bkB_=bkB_:
                                t.matmul(
                        pb[bkB_][0:128, 0:128], ktb_[:, :],
                        VG[:, 512 * c + 128 * hh:512 * c + 128 * (hh + 1)],
                        start=True, stop=True), inc=cm.mm)
                    P.wait("s", cm.mm, m2)
                    ao = P.emit("s", lambda s, oi_b_=oi_b_, bkB_=bkB_: s.copy(
                        oi_b_[:, :], pb[bkB_][0:128, 0:128]), inc=cm.act)
                    m3 = P.emit("t", lambda t, c=c, hh=hh, Sb_=Sb_, bkC=bkC:
                                t.matmul(
                        pb[bkC][0:128, 0:128],
                        QG[:, 2048 * hh + 128 * c:2048 * hh + 128 * (c + 1)],
                        Sb_[:, :], start=True, stop=True), inc=cm.mm)
                    P.wait("v", cm.mm, m3)
                    P.wait("v", cm.act, ao)
                    eo = P.emit("v", lambda v, hgl=hgl, c=c, hh=hh, oi_b_=oi_b_,
                                bkC=bkC: v.scalar_tensor_tensor(
                        OG[:, 512 * c + 128 * hh:512 * c + 128 * (hh + 1)],
                        pb[bkC][0:128, 0:128], qdec[0:128, hgl:hgl + 1],
                        oi_b_[:, :], op0=ALU.mult, op1=ALU.add), inc=cm.dve)
                    ekd = P.emit("v", lambda v, c=c, hh=hh, hgl=hgl, kd_b_=kd_b_:
                                 v.tensor_scalar(
                        kd_b_[:, :],
                        KG[:, 512 * c + 128 * hh:512 * c + 128 * (hh + 1)],
                        kdec[0:128, hgl:hgl + 1], None, op0=ALU.mult), inc=cm.dve)
                    P.wait("t", cm.dve, ekd)
                    m4 = P.emit("t", lambda t, c=c, hh=hh, kd_b_=kd_b_, bkA=bkA:
                                t.matmul(
                        pb[bkA][0:128, 0:128], kd_b_[:, :],
                        VG[:, 512 * c + 128 * hh:512 * c + 128 * (hh + 1)],
                        start=True, stop=True), inc=cm.mm)
                    P.wait("v", cm.mm, m4)
                    src = S32_[:, 128 * spar:128 * (spar + 1)]
                    dst_ = S32_[:, 128 * (1 - spar):128 * (2 - spar)]
                    P.emit("v", lambda v, src=src, dst_=dst_, hgl=hgl, bkA=bkA:
                           v.scalar_tensor_tensor(
                        dst_, src, cdec[0:128, hgl:hgl + 1], pb[bkA][0:128, 0:128],
                        op0=ALU.mult, op1=ALU.add), inc=cm.dve)
                    ecp = P.emit("v", lambda v, Sb_=Sb_, dst_=dst_: v.tensor_copy(
                        Sb_[:, :], dst_), inc=cm.dve)
                    st["ecp"] = ecp
                    st["spar"] = 1 - spar
            P.wait("y", cm.dve, cm.dve.n)
            for st in sts:
                P.emit("y", lambda y, hgl=st["hgl"], S32_=st["S32"],
                       spar=st["spar"]: y.dma_start(
                    out=s_out[128 * hgl:128 * (hgl + 1), :],
                    in_=S32_[:, 128 * spar:128 * (spar + 1)]), inc=s_st, n=16)
        P.wait("y", cm.dve, cm.dve.n)
        for tt in range(16):
            P.emit("y", lambda y, g=g, tt=tt: y.dma_start(
                out=o_out[128 * tt:128 * (tt + 1), 512 * g:512 * (g + 1)],
                in_=OG[:, 512 * tt:512 * (tt + 1)]), inc=s_st, n=16)
        P.waitall("s", s_st)
        P.waitall("t", s_st)
        P.waitall("g", s_st)
    P.waitall("y", s_st)


def build_l2a(dbg=False):
    nc = bass.Bass("TRN2", target_bir_lowering=False)
    io = {
        "xin": _ei(nc, "xin", [DM, TOK]),
        "wq": _ei(nc, "wq", [2048, 2048]), "wk": _ei(nc, "wk", [2048, 2048]),
        "wv": _ei(nc, "wv", [2048, 2048]), "ld": _ei(nc, "ld", [1, 16]),
        "o_out": _eo(nc, "o_out", [TOK, DM], BF16),
        "q_out": _eo(nc, "q_out", [DM, TOK], BF16),
        "s_out": _eo(nc, "s_out", [16 * 128, 128]),
    }
    if dbg:
        io["d_dbg"] = _eo(nc, "d_dbg", [128, 1024])
    return _single(nc, body_l2a, io, dbg=dbg)



def body_l2b(P, cm, io):
    xin, o_in, q_in, s0_d, ld_d, wg1_d, wg2_d, nw_d, wo_d, out_d = (
        io["xin"], io["o_in"], io["q_in"], io["s0"], io["ld"], io["wg1"],
        io["wg2"], io["nw"], io["wo"], io["out"])
    pb = [P.ps(f"pb{i}", [128, 512], F32) for i in range(7)]
    pbT = P.ps("pbT", [128, 1024], BF16)
    cm.pbR1, cm.pbR2, cm.pbB = pb[0], pb[1], pb[2]
    cm.pb_misc = pb[2]
    ident_dve = cm.dve.n

    ldr = P.sb("ldr", [1, 16], F32)
    nwr = P.sb("nwr", [1, 2048], F32)
    sml = cm.newsem("y")
    P.emit("y", lambda y: y.dma_start(out=ldr[:, :], in_=ld_d[:, :]), inc=sml, n=16)
    P.emit("y", lambda y: y.dma_start(out=nwr[:, :], in_=nw_d[:, :]), inc=sml, n=16)
    ldc = P.sb("ldc", [128, 4], F32)
    P.waitall("t", sml)
    P.wait("t", cm.dve, ident_dve)
    t1 = P.emit("t", lambda t: t.matmul(pb[2][0:128, 0:16], cm.ones1_128[:, :],
                                        ldr[0:1, 0:16], start=True, stop=True), inc=cm.mm)
    P.wait("s", cm.mm, t1)
    ldc16 = P.sb("ldc16", [128, 16], F32)
    a1 = P.emit("s", lambda s: s.copy(ldc16[:, :], pb[2][0:128, 0:16]), inc=cm.act)
    nwB = P.sb("nwB", [128, 2048], F32)
    P.wait("t", cm.act, a1)
    ab = 0
    for k in range(4):
        _, ab = bcast_row(P, cm, nwr[0:1, 512 * k:512 * (k + 1)], cm.pbB,
                          nwB[:, 512 * k:512 * (k + 1)], 512)
    nw_done = ab

    QT = P.sb("QT", [128, 16 * 2048], BF16)
    S0B = P.sb("S0B", [128, 16 * 128], F32)
    S0b = P.sb("S0b", [128, 16 * 128], BF16)
    wg1b = P.sb("wg1b", [128, 16 * 128], BF16)
    wg2b = P.sb("wg2b", [128, 2048], BF16)
    g1T = P.sb("g1T", [128, 2048], BF16)
    s_ld = cm.newsem("y")
    for ch in range(16):
        P.emit("y", lambda y, ch=ch: y.dma_start(
            out=QT[:, 2048 * ch:2048 * (ch + 1)], in_=q_in[128 * ch:128 * (ch + 1), :]),
            inc=s_ld, n=16)
    for h in range(16):
        P.emit("y", lambda y, h=h: y.dma_start(
            out=S0B[:, 128 * h:128 * (h + 1)], in_=s0_d[128 * h:128 * (h + 1), :]),
            inc=s_ld, n=16)
    s_gld2 = cm.newsem("g")
    for c_ in range(16):
        P.emit("g", lambda g, c_=c_: g.dma_start(
            out=wg1b[:, 128 * c_:128 * (c_ + 1)],
            in_=wg1_d[128 * c_:128 * (c_ + 1), :]), inc=s_gld2, n=16, dma=True)
    P.emit("g", lambda g: g.dma_start(out=wg2b[:, :], in_=wg2_d[:, :]),
           inc=s_gld2, n=16, dma=True)
    P.waitall("v", s_ld)
    P.waitall("v", s_gld2)
    for h in range(16):
        ecast = P.emit("v", lambda v, h=h: v.tensor_copy(
            S0b[:, 128 * h:128 * (h + 1)], S0B[:, 128 * h:128 * (h + 1)]), inc=cm.dve)
    s0_done = ecast

    xr = P.sb("xr", [128, 4 * 512], BF16)
    xr_sems = [cm.newsem("g") for _ in range(4)]
    xr_cons = [0] * 4
    P.waitall("t", s_ld)
    P.waitall("t", s_gld2)
    for tsl in range(4):
        for i in range(16):
            xslot = i % 4
            if xr_cons[xslot]:
                P.wait("g", cm.mm, xr_cons[xslot])
            xt = P.emit("g", lambda g_, i=i, tsl=tsl, xslot=xslot: g_.dma_start(
                out=xr[:, 512 * xslot:512 * (xslot + 1)],
                in_=xin[128 * i:128 * (i + 1), 512 * tsl:512 * (tsl + 1)]),
                inc=xr_sems[xslot], n=16, dma=True)
            P.wait("t", xr_sems[xslot], xt)
            m1 = P.emit("t", lambda t, i=i, xslot=xslot: t.matmul(
                pb[3][0:128, 0:512], wg1b[:, 128 * i:128 * (i + 1)],
                xr[:, 512 * xslot:512 * (xslot + 1)],
                start=(i == 0), stop=(i == 15)), inc=cm.mm)
            xr_cons[xslot] = m1
        P.wait("s", cm.mm, m1)
        ag = P.emit("s", lambda s, tsl=tsl: s.activation(
            out=g1T[:, 512 * tsl:512 * (tsl + 1)], in_=pb[3][0:128, 0:512],
            func=AF.Copy), inc=cm.act)
        P.wait("t", cm.act, ag)
    g1_done = cm.act.n

    OB = P.sb("OB", [128, 2048], BF16)
    SGB = P.sb("SGB", [128, 2048], BF16)
    YT = P.sb("YT", [128, 16 * 512], BF16)
    qd_col = P.sb("qd_col", [128, 2], F32)
    rst_col = P.sb("rst_col", [128, 2], F32)
    sq_s = P.sb("sq_s", [128, 128], BF16)
    wring = P.sb("wring", [128, 8 * 512], BF16)
    ring_sems = [cm.newsem("g") for _ in range(8)]
    ring_cons = [0] * 8
    s_ob = cm.newsem("y")
    s_res = cm.newsem("y")
    pln = PostLN(P, cm, "l2b", W=SL)
    tmp32 = P.sb("tmp32", [128, 512], F32)

    P.wait("v", cm.act, max(g1_done, nw_done))
    ob_cons = 0
    yt_cons = 0
    res_prev = 0
    oproj_mm = 0
    for s_ in range(4):
        P.wait("t", cm.dve, cm.dve.n)
        for tt4 in range(4):
            tt = 4 * s_ + tt4
            if ob_cons:
                P.wait("y", cm.mm, ob_cons)
            P.emit("y", lambda y, tt=tt: y.dma_start(
                out=OB[:, :], in_=o_in[128 * tt:128 * (tt + 1), :]), inc=s_ob, n=16)
            P.waitall("v", s_ob)
            P.wait("v", cm.dve, s0_done)
            for h in range(16):
                e1 = P.emit("v", lambda v, h=h, tt=tt: v.tensor_scalar(
                    qd_col[:, 0:1], ldc16[:, h:h + 1], float(128 * tt + 1), None,
                    op0=ALU.mult), inc=cm.dve)
                P.wait("s", cm.dve, e1)
                a2 = P.emit("s", lambda s, h=h: s.activation(
                    out=qd_col[:, 1:2], in_=cm.iotaP[:, :], func=AF.Exp,
                    scale=ldc16[0:128, h:h + 1], bias=qd_col[0:128, 0:1]), inc=cm.act)
                m1 = P.emit("t", lambda t, h=h, tt=tt: t.matmul(
                    pb[3 + (h % 2)][0:128, 0:128],
                    QT[:, 2048 * h + 128 * tt:2048 * h + 128 * (tt + 1)],
                    S0b[:, 128 * h:128 * (h + 1)], start=True, stop=True), inc=cm.mm)
                P.wait("v", cm.mm, m1)
                P.wait("v", cm.act, a2)
                eo = P.emit("v", lambda v, h=h: v.scalar_tensor_tensor(
                    OB[:, 128 * h:128 * (h + 1)], pb[3 + (h % 2)][0:128, 0:128],
                    qd_col[0:128, 1:2], OB[:, 128 * h:128 * (h + 1)],
                    op0=ALU.mult, op1=ALU.add), inc=cm.dve)
                P.wait("t", cm.dve, eo)
            for eb in range(4):
                m1 = P.emit("t", lambda t, tt=tt, eb=eb: t.matmul(
                    pb[5][0:128, 0:512], g1T[:, 128 * tt:128 * (tt + 1)],
                    wg2b[:, 512 * eb:512 * (eb + 1)], start=True, stop=True), inc=cm.mm)
                P.wait("s", cm.mm, m1)
                asg = P.emit("s", lambda s, eb=eb: s.activation(
                    out=SGB[:, 512 * eb:512 * (eb + 1)], in_=pb[5][0:128, 0:512],
                    func=AF.Sigmoid), inc=cm.act)
                P.wait("t", cm.act, asg)
            P.wait("v", cm.act, cm.act.n)
            eog = P.emit("v", lambda v: v.tensor_tensor(OB[:, :], OB[:, :], SGB[:, :],
                                                        ALU.mult), inc=cm.dve)
            P.wait("s", cm.dve, eog)
            for h in range(16):
                a3 = P.emit("s", lambda s, h=h: s.activation(
                    out=sq_s[:, :], in_=OB[:, 128 * h:128 * (h + 1)], func=AF.Square,
                    accum_out=rst_col[:, 0:1]), inc=cm.act)
                P.wait("v", cm.act, a3)
                e2 = P.emit("v", lambda v: v.tensor_scalar(
                    rst_col[:, 0:1], rst_col[:, 0:1], 1.0 / 128.0, 1e-5,
                    op0=ALU.mult, op1=ALU.add), inc=cm.dve)
                P.wait("s", cm.dve, e2)
                a4 = P.emit("s", lambda s: s.activation(
                    out=rst_col[:, 1:2], in_=rst_col[:, 0:1], func=AF.Sqrt), inc=cm.act)
                P.wait("v", cm.act, a4)
                e3 = P.emit("v", lambda v: v.reciprocal(rst_col[:, 1:2],
                                                        rst_col[:, 1:2]), inc=cm.dve)
                e4 = P.emit("v", lambda v, h=h: v.tensor_scalar(
                    OB[:, 128 * h:128 * (h + 1)], OB[:, 128 * h:128 * (h + 1)],
                    rst_col[0:128, 1:2], None, op0=ALU.mult), inc=cm.dve)
                P.wait("s", cm.dve, e4)
            enw = P.emit("v", lambda v: v.tensor_tensor(OB[:, :], OB[:, :], nwB[:, :],
                                                        ALU.mult), inc=cm.dve)
            ob_cons = enw
            P.wait("t", cm.dve, enw)
            if yt_cons:
                P.wait("t", cm.act, 0) if False else None
            for ch in range(16):
                tr = P.emit("t", lambda t, ch=ch: t.transpose(
                    pbT[0:128, 0:128], OB[:, 128 * ch:128 * (ch + 1)], cm.identb[:, :]),
                    inc=cm.mm)
                P.wait("s", cm.mm, tr)
                aty = P.emit("s", lambda s, ch=ch, tt4=tt4: s.copy(
                    YT[:, 512 * ch + 128 * tt4:512 * ch + 128 * (tt4 + 1)],
                    pbT[0:128, 0:128]), inc=cm.act)
                P.wait("t", cm.act, aty)
            ob_cons = cm.mm.n
        yt_done = cm.act.n
        pln.begin()
        P.wait("t", cm.act, yt_done)
        colq = slice(512 * s_, 512 * (s_ + 1))
        for eb in range(4):
            for i in range(16):
                slot = i % 8
                if ring_cons[slot]:
                    P.wait("g", cm.mm, ring_cons[slot])
                lt = P.emit("g", lambda g_, i=i, eb=eb, slot=slot: g_.dma_start(
                    out=wring[:, 512 * slot:512 * (slot + 1)],
                    in_=wo_d[128 * i:128 * (i + 1), 512 * eb:512 * (eb + 1)]),
                    inc=ring_sems[slot], n=16, dma=True)
                P.wait("t", ring_sems[slot], lt)
                for ec in range(4):
                    m1 = P.emit("t", lambda t, slot=slot, ec=ec, i=i: t.matmul(
                        pb[3 + ec][0:128, 0:512],
                        wring[:, 512 * slot + 128 * ec:512 * slot + 128 * (ec + 1)],
                        YT[:, 512 * i:512 * (i + 1)],
                        start=(i == 0), stop=(i == 15)), inc=cm.mm)
                ring_cons[slot] = m1
            oproj_mm = m1
            for ec in range(4):
                oc = eb * 4 + ec
                if res_prev:
                    P.wait("y", cm.dve, res_prev)
                P.emit("y", lambda y, oc=oc, colq=colq: y.dma_start(
                    out=pln.Rs[:, SL * oc:SL * (oc + 1)],
                    in_=xin[128 * oc:128 * (oc + 1), colq]), inc=s_res, n=16)
                P.wait("v", cm.mm, oproj_mm)
                P.waitall("v", s_res)
                ee = pln.add_chunk(oc, pb[3 + ec][0:128, 0:512])
                res_prev = ee
            P.wait("t", cm.dve, ee)
        pln.finalize(dst_dram_ap=out_d[:, SL * s_:SL * (s_ + 1)])
        res_prev = ee
        yt_cons = cm.mm.n
    P.wait("y", cm.st, cm.st.n)


def build_l2b():
    nc = bass.Bass("TRN2", target_bir_lowering=False)
    io = {
        "xin": _ei(nc, "xin", [DM, TOK]),
        "o_in": _ei(nc, "o_in", [TOK, DM], BF16),
        "q_in": _ei(nc, "q_in", [DM, TOK], BF16),
        "s0": _ei(nc, "s0", [16 * 128, 128]), "ld": _ei(nc, "ld", [1, 16]),
        "wg1": _ei(nc, "wg1", [2048, 128]), "wg2": _ei(nc, "wg2", [128, 2048]),
        "nw": _ei(nc, "nw", [1, 2048]), "wo": _ei(nc, "wo", [2048, 2048]),
        "out": _eo(nc, "out", [DM, TOK]),
    }
    return _single(nc, body_l2b, io)



def body_l2_ffn(P, cm, io):
    xin, wg_d, wu_d, wd_d, out_d = (io["xin"], io["wg"], io["wu"], io["wd"],
                                    io["out"])
    pb = [P.ps(f"pb{i}", [128, 512], F32) for i in range(8)]
    cm.pbR1, cm.pbR2, cm.pbB, cm.pbB2 = pb[0], pb[1], pb[2], pb[3]
    cm.pb_misc = pb[2]

    W = 512
    NSLF = TOK // W
    NH = 51
    buf = P.sb("tern_b", [128, 2 * 2048], BF16)
    f32buf = P.sb("tern_f", [128, 2 * 2048], F32)
    scg, ssg, oscg, rdyg = w_to_scratch_ternary(
        P, cm, "wg", wg_d, 6464, 2048, 1e-5, "wq", buf, f32buf, cm.pbR1, cm.pbB2,
        Opad=6528)
    scu, ssu, oscu, rdyu = w_to_scratch_ternary(
        P, cm, "wu", wu_d, 6464, 2048, 1e-5, "wq", buf, f32buf, cm.pbR1, cm.pbB2,
        prev_ssem=ssg, Opad=6528)
    scd, ssd, oscd, rdyd = w_to_scratch_ternary(
        P, cm, "wd", wd_d, 2048, 6464, 1e-5, "wq", buf, f32buf, cm.pbR1, cm.pbB2,
        prev_ssem=ssu, Ipad=6528)

    H = P.sb("Hf", [128, NH * W], BF16)
    tq = P.sb("tq", [128, W], F32)
    XQ = P.sb("XQ", [128, 16 * W], BF16)
    UP = P.sb("UPf", [128, W], F32)
    wring = P.sb("wring", [128, 8 * 512], BF16)
    wring2 = P.sb("wring2", [128, 8 * 512], BF16)
    acc = P.sb("acc", [128, W], F32)
    acc2 = P.sb("acc2", [128, W], F32)
    sB = P.sb("sB", [128, W], F32)
    invsB = P.sb("invsB", [128, W], F32)
    srow = P.sb("srow", [1, 2 * W], F32)
    cols = P.sb("colsq", [128, 8], F32)
    pln = PostLN(P, cm, "l2f", W=W)
    ring_sems = [cm.newsem("y") for _ in range(8)]
    ring2_sems = [cm.newsem("y") for _ in range(8)]
    ring_cons = [0] * 8
    ring2_cons = [0] * 8

    P.waitall("y", ssd)
    P.waitall("g", ssd)
    P.wait("v", cm.act, max(rdyg, rdyu, rdyd))
    P.wait("s", cm.act, max(rdyg, rdyu, rdyd))
    res_prev = 0
    s_res = cm.newsem("y")
    hblocks = [(b_ * 512, 512) for b_ in range(12)] + [(6144, 384)]
    for s_ in range(NSLF):
        colt = slice(W * s_, W * (s_ + 1))
        pln.begin()
        for i in range(16):
            if res_prev:
                P.wait("y", cm.dve, res_prev)
                res_prev = 0
            P.emit("y", lambda y, i=i, colt=colt: y.dma_start(
                out=pln.Rs[:, W * i:W * (i + 1)], in_=xin[128 * i:128 * (i + 1), colt]),
                inc=s_res, n=16)
            P.waitall("v", s_res)
            if i == 0:
                e = amax_start(P, cm, acc[:, :], acc2[:, :], pln.Rs[:, 0:W])
            else:
                e = amax_step(P, cm, acc[:, :], acc2[:, :],
                              pln.Rs[:, W * i:W * (i + 1)])
            res_prev = e
        amax_finish(P, cm, acc[:, :], acc2[:, :])
        qs = quant_scales(P, cm, acc, W, sB, invsB, srow, cols)
        P.wait("v", cm.act, qs)
        for i in range(16):
            P.emit("v", lambda v, i=i: v.tensor_tensor(
                XQ[:, W * i:W * (i + 1)], pln.Rs[:, W * i:W * (i + 1)], sB[:, :],
                ALU.mult), inc=cm.dve)
            P.emit("v", lambda v, i=i: v.tensor_scalar(
                XQ[:, W * i:W * (i + 1)], XQ[:, W * i:W * (i + 1)], RNE, RNE,
                op0=ALU.add, op1=ALU.subtract), inc=cm.dve)
            e3 = P.emit("v", lambda v, i=i: v.tensor_tensor(
                XQ[:, W * i:W * (i + 1)], XQ[:, W * i:W * (i + 1)], invsB[:, :],
                ALU.mult), inc=cm.dve)
        xq_done = e3
        P.wait("t", cm.dve, xq_done)
        P.wait("t", cm.act, cm.act.n)
        hacc_first = True
        bank_free_dve = 0
        for bi, (h0, hw) in enumerate(hblocks):
            ncw = hw // 128 if hw % 128 == 0 else hw // 128 + 1
            for i in range(16):
                slot = i % 8
                if ring_cons[slot]:
                    P.wait("y", cm.mm, ring_cons[slot])
                P.emit("y", lambda y, h0=h0, hw=hw, i=i, slot=slot: y.dma_start(
                    out=wring[:, 512 * slot:512 * slot + hw],
                    in_=scg[h0:h0 + hw, 128 * i:128 * (i + 1)],
                    transpose=True), inc=ring_sems[slot], n=16)
                if ring2_cons[slot]:
                    P.wait("y", cm.mm, ring2_cons[slot])
                lt2 = P.emit("y", lambda y, h0=h0, hw=hw, i=i, slot=slot: y.dma_start(
                    out=wring2[:, 512 * slot:512 * slot + hw],
                    in_=scu[h0:h0 + hw, 128 * i:128 * (i + 1)],
                    transpose=True), inc=ring2_sems[slot], n=16)
                P.waitall("t", ring_sems[slot])
                P.waitall("t", ring2_sems[slot])
                if i == 0 and bank_free_dve:
                    P.wait("t", cm.dve, bank_free_dve)
                for oc in range((hw + 127) // 128):
                    ow = min(128, hw - 128 * oc)
                    m1 = P.emit("t", lambda t, slot=slot, oc=oc, i=i, ow=ow: t.matmul(
                        pb[4 + oc][0:ow, 0:W],
                        wring[:, 512 * slot + 128 * oc:512 * slot + 128 * oc + ow],
                        XQ[:, W * i:W * (i + 1)],
                        start=(i == 0), stop=(i == 15)), inc=cm.mm)
                    m2 = P.emit("t", lambda t, slot=slot, oc=oc, i=i, ow=ow: t.matmul(
                        pb[0 + oc][0:ow, 0:W],
                        wring2[:, 512 * slot + 128 * oc:512 * slot + 128 * oc + ow],
                        XQ[:, W * i:W * (i + 1)],
                        start=(i == 0), stop=(i == 15)), inc=cm.mm)
                ring_cons[slot] = m2
                ring2_cons[slot] = m2
            P.wait("s", cm.mm, m2)
            for oc in range((hw + 127) // 128):
                ch = h0 // 128 + oc
                ow = min(128, hw - 128 * oc)
                if not hacc_first:
                    P.wait("s", cm.dve, bank_free_dve)
                a0 = P.emit("s", lambda s, oc=oc, ow=ow: s.activation(
                    out=UP[0:ow, :], in_=pb[0 + oc][0:ow, 0:W], func=AF.Copy,
                    scale=oscu[0:ow, 0:1]), inc=cm.act)
                a1 = P.emit("s", lambda s, oc=oc, ch=ch, ow=ow: s.activation(
                    out=H[0:ow, W * ch:W * (ch + 1)], in_=pb[4 + oc][0:ow, 0:W],
                    func=AF.Silu, scale=oscg[0:ow, 0:1]), inc=cm.act)
                P.wait("v", cm.act, a1)
                em = P.emit("v", lambda v, ch=ch, ow=ow: v.tensor_tensor(
                    H[0:ow, W * ch:W * (ch + 1)], H[0:ow, W * ch:W * (ch + 1)],
                    UP[0:ow, :], ALU.mult), inc=cm.dve)
                if hacc_first:
                    e = amax_start(P, cm, acc[:, :], acc2[:, :],
                                   H[:, W * ch:W * (ch + 1)])
                    hacc_first = False
                else:
                    e = amax_step(P, cm, acc[0:ow, :], acc2[0:ow, :],
                                  H[0:ow, W * ch:W * (ch + 1)])
                bank_free_dve = e
        amax_finish(P, cm, acc[:, :], acc2[:, :])
        qs2 = quant_scales(P, cm, acc, W, sB, invsB, srow, cols)
        P.wait("v", cm.act, qs2)
        if s_ == 0:
            P.emit("v", lambda v: v.memset(H[64:128, W * 50:W * 51], 0.0), inc=cm.dve)
        for ch in range(NH):
            ow = 64 if ch == 50 else 128
            P.emit("v", lambda v, ch=ch, ow=ow: v.tensor_tensor(
                tq[0:ow, :], H[0:ow, W * ch:W * (ch + 1)], sB[0:ow, :],
                ALU.mult), inc=cm.dve)
            P.emit("v", lambda v, ch=ch, ow=ow: v.tensor_scalar(
                tq[0:ow, :], tq[0:ow, :], RNE, RNE,
                op0=ALU.add, op1=ALU.subtract), inc=cm.dve)
            e3 = P.emit("v", lambda v, ch=ch, ow=ow: v.tensor_tensor(
                H[0:ow, W * ch:W * (ch + 1)], tq[0:ow, :],
                invsB[0:ow, :], ALU.mult), inc=cm.dve)
        hq_done = e3
        P.wait("t", cm.dve, hq_done)
        pln.scale_col = oscd
        for ob in range(4):
            for i in range(NH):
                slot = i % 8
                if ring_cons[slot]:
                    P.wait("y", cm.mm, ring_cons[slot])
                P.emit("y", lambda y, ob=ob, i=i, slot=slot: y.dma_start(
                    out=wring[:, 512 * slot:512 * (slot + 1)],
                    in_=scd[512 * ob:512 * (ob + 1), 128 * i:128 * (i + 1)],
                    transpose=True), inc=ring_sems[slot], n=16)
                P.waitall("t", ring_sems[slot])
                for oc in range(4):
                    m1 = P.emit("t", lambda t, slot=slot, oc=oc, i=i: t.matmul(
                        pb[4 + oc][0:128, 0:W],
                        wring[:, 512 * slot + 128 * oc:512 * slot + 128 * (oc + 1)],
                        H[:, W * i:W * (i + 1)],
                        start=(i == 0), stop=(i == NH - 1)), inc=cm.mm)
                ring_cons[slot] = m1
            oproj_mm = m1
            for oc in range(4):
                chn = ob * 4 + oc
                P.wait("v", cm.mm, oproj_mm)
                ee = pln.add_chunk(chn, pb[4 + oc][0:128, 0:W])
            P.wait("t", cm.dve, ee)
        pln.finalize(dst_dram_ap=out_d[:, W * s_:W * (s_ + 1)])
        res_prev = ee
    P.wait("y", cm.st, cm.st.n)


def build_l2_ffn():
    nc = bass.Bass("TRN2", target_bir_lowering=False)
    io = {
        "xin": _ei(nc, "xin", [DM, TOK]),
        "wg": _ei(nc, "wg", [6464, 2048]), "wu": _ei(nc, "wu", [6464, 2048]),
        "wd": _ei(nc, "wd", [2048, 6464]),
        "out": _eo(nc, "out", [DM, TOK]),
    }
    return _single(nc, body_l2_ffn, io)



def body_l3a(P, cm, io):
    xin, wq_d, wk_d, wv_d, bq_d, bk_d, bv_d, cc_d, q_out, kv_out, ks_out = (
        io["xin"], io["wq"], io["wk"], io["wv"], io["bq"], io["bk"], io["bv"],
        io["cc"], io["q_out"], io["kv_out"], io["ks_out"])
    pb = [P.ps(f"pb{i}", [128, 512], F32) for i in range(8)]
    cm.pbR1, cm.pbR2, cm.pbB, cm.pbB2 = pb[0], pb[1], pb[2], pb[3]
    cm.pb_misc = pb[2]
    ident_dve = cm.dve.n

    sml = cm.newsem("y")
    ccr = P.sb("cc_r", [1, 8], F32)
    bqr = P.sb("bq_r", [1, 2048], F32)
    bkr = P.sb("bk_r", [1, 2048], F32)
    bvr = P.sb("bv_r", [1, 2048], F32)
    P.emit("y", lambda y: y.dma_start(out=ccr[:, :], in_=cc_d[:, :]), inc=sml, n=16)
    P.emit("y", lambda y: y.dma_start(out=bqr[:, :], in_=bq_d[:, :]), inc=sml, n=16)
    P.emit("y", lambda y: y.dma_start(out=bkr[:, :], in_=bk_d[:, :]), inc=sml, n=16)
    P.emit("y", lambda y: y.dma_start(out=bvr[:, :], in_=bv_d[:, :]), inc=sml, n=16)
    bqc = P.sb("bqc", [128, 16], F32)
    cc_cols = P.sb("cc_cols", [128, 8], F32)
    P.waitall("t", sml)
    P.wait("t", cm.dve, ident_dve)
    t0 = P.emit("t", lambda t: t.matmul(cm.pbB[0:128, 0:8], cm.ones1_128[:, :],
                                        ccr[0:1, 0:8], start=True, stop=True), inc=cm.mm)
    P.wait("s", cm.mm, t0)
    acc0 = P.emit("s", lambda s: s.copy(cc_cols[:, :], cm.pbB[0:128, 0:8]), inc=cm.act)
    P.wait("t", cm.act, acc0)
    for c in range(16):
        tq = P.emit("t", lambda t, c=c: t.transpose(
            cm.pbB[0:128, c:c + 1], bqr[0:1, 128 * c:128 * (c + 1)],
            cm.identf[0:1, 0:1]), inc=cm.mm)
    P.wait("s", cm.mm, tq)
    abq = P.emit("s", lambda s: s.copy(bqc[:, :], cm.pbB[0:128, 0:16]), inc=cm.act)
    decB = P.sb("decB", [128, 2048], F32)
    dec_cols = P.sb("dec_cols", [128, 16], F32)
    dtmp = P.sb("dtmp3", [128, 1], F32)
    iotaW = P.sb("iotaW3", [1, 2048], F32)
    P.emit("g", lambda g: g.iota(iotaW[:, :], pattern=[[1, 2048]], base=0,
                                 channel_multiplier=0,
                                 allow_small_or_imprecise_dtypes=True), inc=cm.pool)
    gp = cm.pool.n
    drow = P.sb("drow", [1, 2048], F32)
    P.wait("s", cm.pool, gp)
    P.waitall("s", sml)
    ad = P.emit("s", lambda s: s.activation(
        out=drow[:, :], in_=iotaW[:, :], func=AF.Exp, scale=-1.0 / 4096.0,
        bias=ccr[0:1, 0:1]), inc=cm.act)
    P.wait("t", cm.act, ad)
    ab = 0
    for k in range(4):
        _, ab = bcast_row(P, cm, drow[0:1, 512 * k:512 * (k + 1)], cm.pbB,
                          decB[:, 512 * k:512 * (k + 1)], 512)
    P.wait("v", cm.act, max(ab, acc0))
    a_prev = 0
    for tt in range(16):
        if a_prev:
            P.wait("v", cm.act, a_prev)
        e1 = P.emit("v", lambda v, tt=tt: v.tensor_scalar(
            dtmp[:, :], cc_cols[0:128, 0:1], float(-128 * tt / 4096.0), None,
            op0=ALU.add), inc=cm.dve)
        P.wait("s", cm.dve, e1)
        a_prev = P.emit("s", lambda s, tt=tt: s.activation(
            out=dec_cols[:, tt:tt + 1], in_=cm.iotaP[:, :], func=AF.Exp,
            scale=-1.0 / 4096.0, bias=dtmp[0:128, 0:1]), inc=cm.act)
    dec_done = cm.act.n
    bkB = P.sb("bkB", [128, 2048], F32)
    bvB = P.sb("bvB", [128, 2048], F32)
    for k in range(4):
        _, ab1 = bcast_row(P, cm, bkr[0:1, 512 * k:512 * (k + 1)], cm.pbB,
                           bkB[:, 512 * k:512 * (k + 1)], 512)
        _, ab2 = bcast_row(P, cm, bvr[0:1, 512 * k:512 * (k + 1)], cm.pbB,
                           bvB[:, 512 * k:512 * (k + 1)], 512)
    bias_done = ab2

    QsT = P.sb("QsT", [128, 16 * SL], BF16)
    wscb = QsT[:, 0:4096]
    sc_q, ss_q = w_to_scratch(P, cm, "q", wq_d, 2048, 2048, buf=wscb)
    sc_k, ss_k = w_to_scratch(P, cm, "k", wk_d, 2048, 2048, buf=wscb, prev=ss_q)
    sc_v, ss_v = w_to_scratch(P, cm, "v", wv_d, 2048, 2048, buf=wscb, prev=ss_k)

    wT = QsT
    xq = P.sb("xq", [128, 16 * 512], BF16)
    qd_t = P.sb("qd_t", [128, 512], F32)
    qb16 = P.sb("qb16", [128, 512], BF16)
    s_xq = cm.newsem("g")
    s_wt = cm.newsem("y")
    s_st = cm.newsem("y")
    s_stq = cm.newsem("y")

    P.waitall("y", ss_v)
    P.waitall("v", ss_v)
    P.wait("v", cm.act, max(abq, bias_done, dec_done))
    P.wait("s", cm.act, max(abq, bias_done, dec_done))
    wt_free = 0
    xq_free = 0
    qb_free = 0
    for tsl in range(4):
        if xq_free:
            P.wait("g", cm.mm, xq_free)
        for i in range(16):
            P.emit("g", lambda g, i=i, tsl=tsl: g.dma_start(
                out=xq[:, 512 * i:512 * (i + 1)],
                in_=xin[128 * i:128 * (i + 1), 512 * tsl:512 * (tsl + 1)]),
                inc=s_xq, n=16, dma=True)
        P.waitall("t", s_xq)
        for oblk in range(4):
            if wt_free:
                P.wait("y", cm.mm, wt_free)
            for i in range(16):
                P.emit("y", lambda y, i=i, oblk=oblk: y.dma_start(
                    out=wT[:, 512 * i:512 * (i + 1)],
                    in_=sc_q[512 * oblk:512 * (oblk + 1), 128 * i:128 * (i + 1)],
                    transpose=True), inc=s_wt, n=16)
            P.waitall("t", s_wt)
            if qb_free:
                P.wait("t", cm.dve, qb_free)
            for i in range(16):
                for oc in range(4):
                    m1 = P.emit("t", lambda t, i=i, oc=oc: t.matmul(
                        pb[4 + oc][0:128, 0:512],
                        wT[:, 512 * i + 128 * oc:512 * i + 128 * (oc + 1)],
                        xq[:, 512 * i:512 * (i + 1)],
                        start=(i == 0), stop=(i == 15)), inc=cm.mm)
            wt_free = m1
            for oc in range(4):
                ch = oblk * 4 + oc
                P.wait("s", cm.mm, m1)
                aq = P.emit("s", lambda s, oc=oc, ch=ch: s.activation(
                    out=qd_t[:, :], in_=pb[4 + oc][0:128, 0:512], func=AF.Relu,
                    bias=bqc[0:128, ch:ch + 1]), inc=cm.act)
                P.wait("v", cm.act, aq)
                ed = P.emit("v", lambda v, tsl=tsl: v.tensor_tensor(
                    qd_t[:, :], qd_t[:, :], decB[:, 512 * tsl:512 * (tsl + 1)],
                    ALU.mult), inc=cm.dve)
                P.wait("s", cm.dve, ed)
                aq2 = P.emit("s", lambda s: s.copy(qb16[:, :], qd_t[:, :]), inc=cm.act)
                P.wait("y", cm.act, aq2)
                st = P.emit("y", lambda y, ch=ch, tsl=tsl: y.dma_start(
                    out=q_out[128 * ch:128 * (ch + 1), 512 * tsl:512 * (tsl + 1)],
                    in_=qb16[:, :]), inc=s_stq, n=16)
                P.wait("s", s_stq, s_stq.n)
                qb_free = ed
        xq_free = cm.mm.n

    G4 = 4
    KD = P.sb("KD", [128, G4 * 2048], BF16)
    VD = P.sb("VD", [128, G4 * 2048], BF16)
    KVacc = P.sb("KVacc", [128, 16 * 128], F32)
    KSacc = P.sb("KSacc", [128, 16], F32)
    ez = P.emit("v", lambda v: v.memset(KVacc[:, :], 0.0), inc=cm.dve)
    ez2 = P.emit("v", lambda v: v.memset(KSacc[:, :], 0.0), inc=cm.dve)
    XG = xq
    s_xg = cm.newsem("g")
    s_kw = cm.newsem("y")
    s_vw = cm.newsem("y")
    kwT = P.sb("kwT", [128, 16 * 512], BF16)
    P.wait("t", cm.dve, ez2)
    xg_cons = cm.mm.n
    kw_cons = 0
    evac_prev = 0
    for grp in range(4):
        tok0 = 512 * grp
        P.wait("g", cm.mm, max(xg_cons, kw_cons))
        for i in range(16):
            P.emit("g", lambda g, i=i, tok0=tok0: g.dma_start(
                out=XG[:, 512 * i:512 * (i + 1)],
                in_=xin[128 * i:128 * (i + 1), tok0:tok0 + 512]),
                inc=s_xg, n=16, dma=True)
        P.waitall("t", s_xg)
        for src, ssem_, wsem, dst, isK in ((sc_k, ss_k, s_kw, KD, True),
                                           (sc_v, ss_v, s_vw, VD, False)):
            for eb in range(4):
                if grp == 0 and eb == 0:
                    P.waitall("y", ssem_)
                if kw_cons:
                    P.wait("y", cm.mm, kw_cons)
                for i in range(16):
                    P.emit("y", lambda y, i=i, eb=eb, src=src: y.dma_start(
                        out=kwT[:, 512 * i:512 * (i + 1)],
                        in_=src[512 * eb:512 * (eb + 1), 128 * i:128 * (i + 1)],
                        transpose=True), inc=wsem, n=16)
                P.waitall("t", wsem)
                if evac_prev:
                    P.wait("t", cm.dve, evac_prev)
                for i in range(16):
                    for t4 in range(G4):
                        m1 = P.emit("t", lambda t, i=i, t4=t4: t.matmul(
                            pb[4 + t4][0:128, 0:512],
                            XG[:, 512 * i + 128 * t4:512 * i + 128 * (t4 + 1)],
                            kwT[:, 512 * i:512 * (i + 1)],
                            start=(i == 0), stop=(i == 15)), inc=cm.mm)
                kw_cons = m1
                P.wait("v", cm.mm, m1)
                P.wait("v", cm.act, cm.act.n)
                for t4 in range(G4):
                    tt = G4 * grp + t4
                    if isK:
                        e1 = P.emit("v", lambda v, t4=t4, eb=eb: v.tensor_tensor(
                            qd_t[:, :], pb[4 + t4][0:128, 0:512],
                            bkB[:, 512 * eb:512 * (eb + 1)], ALU.add), inc=cm.dve)
                        e2 = P.emit("v", lambda v, t4=t4, eb=eb, tt=tt: v.tensor_scalar(
                            KD[:, 2048 * t4 + 512 * eb:2048 * t4 + 512 * (eb + 1)],
                            qd_t[:, :], 0.0,
                            dec_cols[0:128, tt:tt + 1], op0=ALU.max, op1=ALU.mult),
                            inc=cm.dve)
                    else:
                        e2 = P.emit("v", lambda v, t4=t4, eb=eb: v.tensor_tensor(
                            VD[:, 2048 * t4 + 512 * eb:2048 * t4 + 512 * (eb + 1)],
                            pb[4 + t4][0:128, 0:512],
                            bvB[:, 512 * eb:512 * (eb + 1)], ALU.add), inc=cm.dve)
                evac_prev = e2
        P.wait("t", cm.dve, evac_prev)
        for t4 in range(G4):
            for h in range(16):
                m1 = P.emit("t", lambda t, t4=t4, h=h: t.matmul(
                    pb[2][0:128, 0:128],
                    KD[:, 2048 * t4 + 128 * h:2048 * t4 + 128 * (h + 1)],
                    VD[:, 2048 * t4 + 128 * h:2048 * t4 + 128 * (h + 1)],
                    start=True, stop=True), inc=cm.mm)
                m2 = P.emit("t", lambda t, t4=t4, h=h: t.matmul(
                    pb[3][0:128, 0:1],
                    KD[:, 2048 * t4 + 128 * h:2048 * t4 + 128 * (h + 1)],
                    cm.ones128_1b[:, :], start=True, stop=True), inc=cm.mm)
                P.wait("v", cm.mm, m2)
                ekv = P.emit("v", lambda v, h=h: v.tensor_tensor(
                    KVacc[:, 128 * h:128 * (h + 1)], KVacc[:, 128 * h:128 * (h + 1)],
                    pb[2][0:128, 0:128], ALU.add), inc=cm.dve)
                eks = P.emit("v", lambda v, h=h: v.tensor_tensor(
                    KSacc[:, h:h + 1], KSacc[:, h:h + 1], pb[3][0:128, 0:1], ALU.add),
                    inc=cm.dve)
                P.wait("t", cm.dve, eks)
        xg_cons = cm.mm.n
    P.wait("y", cm.dve, cm.dve.n)
    for h in range(16):
        P.emit("y", lambda y, h=h: y.dma_start(
            out=kv_out[128 * h:128 * (h + 1), :], in_=KVacc[:, 128 * h:128 * (h + 1)]),
            inc=s_st, n=16)
    krow = P.sb("krow", [1, 2048], F32)
    P.wait("t", cm.dve, cm.dve.n)
    P.wait("t", cm.act, cm.act.n)
    ak2 = 0
    for hb in range(4):
        if ak2:
            P.wait("t", cm.act, ak2)
        for hh in range(4):
            th = P.emit("t", lambda t, hb=hb, hh=hh: t.transpose(
                cm.pbB[0:1, 128 * hh:128 * (hh + 1)], KSacc[:, 4 * hb + hh:4 * hb + hh + 1],
                cm.identf[0:128, 0:128]), inc=cm.mm)
        P.wait("s", cm.mm, th)
        ak2 = P.emit("s", lambda s, hb=hb: s.copy(
            krow[0:1, 512 * hb:512 * (hb + 1)], cm.pbB[0:1, 0:512]), inc=cm.act)
    P.wait("y", cm.act, ak2)
    P.emit("y", lambda y: y.dma_start(out=ks_out[:, :], in_=krow[:, :]), inc=s_st, n=16)
    P.waitall("y", s_st)
    P.waitall("y", s_stq)


def build_l3a():
    nc = bass.Bass("TRN2", target_bir_lowering=False)
    io = {
        "xin": _ei(nc, "xin", [DM, TOK]),
        "wq": _ei(nc, "wq", [2048, 2048]), "wk": _ei(nc, "wk", [2048, 2048]),
        "wv": _ei(nc, "wv", [2048, 2048]),
        "bq": _ei(nc, "bq", [1, 2048]), "bk": _ei(nc, "bk", [1, 2048]),
        "bv": _ei(nc, "bv", [1, 2048]), "cc": _ei(nc, "cc", [1, 8]),
        "q_out": _eo(nc, "q_out", [DM, TOK], BF16),
        "kv_out": _eo(nc, "kv_out", [16 * 128, 128]),
        "ks_out": _eo(nc, "ks_out", [1, 2048]),
    }
    return _single(nc, body_l3a, io)



def body_l3b(P, cm, io):
    xin, q_in, kva_d, kvb_d, ksa_d, ksb_d, wo_d, bo_d, out_d = (
        io["xin"], io["q_in"], io["kva"], io["kvb"], io["ksa"], io["ksb"],
        io["wo"], io["bo"], io["out"])
    pb = [P.ps(f"pb{i}", [128, 512], F32) for i in range(7)]
    pbT = P.ps("pbT", [128, 1024], BF16)
    cm.pbR1, cm.pbR2, cm.pbB = pb[0], pb[1], pb[2]
    cm.pb_misc = pb[2]
    ident_dve = cm.dve.n

    sml = cm.newsem("y")
    bor = P.sb("bo_r", [1, 2048], F32)
    ksa = P.sb("ksa_r", [1, 2048], F32)
    ksb = P.sb("ksb_r", [1, 2048], F32)
    P.emit("y", lambda y: y.dma_start(out=bor[:, :], in_=bo_d[:, :]), inc=sml, n=16)
    P.emit("y", lambda y: y.dma_start(out=ksa[:, :], in_=ksa_d[:, :]), inc=sml, n=16)
    P.emit("y", lambda y: y.dma_start(out=ksb[:, :], in_=ksb_d[:, :]), inc=sml, n=16)
    boc = P.sb("boc", [128, 16], F32)
    ksc = P.sb("ksc", [128, 16], F32)
    kscb = P.sb("kscb", [128, 16], BF16)
    P.waitall("v", sml)
    eks = P.emit("v", lambda v: v.tensor_tensor(ksa[:, :], ksa[:, :], ksb[:, :],
                                                ALU.add), inc=cm.dve)
    P.wait("t", cm.dve, eks)
    P.wait("t", cm.dve, ident_dve)
    P.waitall("t", sml)
    ak2 = 0
    for hb in range(4):
        if ak2:
            P.wait("t", cm.act, ak2)
        for hh in range(4):
            tq = P.emit("t", lambda t, hb=hb, hh=hh: t.transpose(
                cm.pbB[0:128, 32 * hh:32 * hh + 1] if False else cm.pbB[0:128, hh:hh + 1],
                ksa[0:1, 512 * hb + 128 * hh:512 * hb + 128 * (hh + 1)],
                cm.identf[0:1, 0:1]), inc=cm.mm)
            tb = P.emit("t", lambda t, hb=hb, hh=hh: t.transpose(
                cm.pbB[0:128, 4 + hh:5 + hh],
                bor[0:1, 512 * hb + 128 * hh:512 * hb + 128 * (hh + 1)],
                cm.identf[0:1, 0:1]), inc=cm.mm)
        P.wait("s", cm.mm, tb)
        ak2 = P.emit("s", lambda s, hb=hb: s.copy(ksc[:, 4 * hb:4 * (hb + 1)],
                                                  cm.pbB[0:128, 0:4]), inc=cm.act)
        ak2 = P.emit("s", lambda s, hb=hb: s.copy(boc[:, 4 * hb:4 * (hb + 1)],
                                                  cm.pbB[0:128, 4:8]), inc=cm.act)
    P.wait("v", cm.act, ak2)
    ekc = P.emit("v", lambda v: v.tensor_copy(kscb[:, :], ksc[:, :]), inc=cm.dve)

    KVt = P.sb("KVt", [128, 16 * 128], BF16)
    kvtmp = P.sb("kvtmp", [128, 2 * 128], F32)
    s_kv = cm.newsem("y")
    for h in range(16):
        P.emit("y", lambda y, h=h: y.dma_start(
            out=kvtmp[:, 0:128], in_=kva_d[128 * h:128 * (h + 1), :]), inc=s_kv, n=16)
        P.emit("y", lambda y, h=h: y.dma_start(
            out=kvtmp[:, 128:256], in_=kvb_d[128 * h:128 * (h + 1), :]), inc=s_kv, n=16)
        P.waitall("v", s_kv)
        ekv = P.emit("v", lambda v, h=h: v.tensor_tensor(
            kvtmp[:, 0:128], kvtmp[:, 0:128], kvtmp[:, 128:256], ALU.add), inc=cm.dve)
        ekv2 = P.emit("v", lambda v, h=h: v.tensor_copy(
            KVt[:, 128 * h:128 * (h + 1)], kvtmp[:, 0:128]), inc=cm.dve)
        P.wait("y", cm.dve, ekv2)
    kv_done = cm.dve.n

    QTB = P.sb("QTB", [128, 16 * 2048], BF16)
    s_q = cm.newsem("y")
    for ch in range(16):
        P.emit("y", lambda y, ch=ch: y.dma_start(
            out=QTB[:, 2048 * ch:2048 * (ch + 1)], in_=q_in[128 * ch:128 * (ch + 1), :]),
            inc=s_q, n=16)
    P.waitall("t", s_q)
    P.wait("t", cm.dve, kv_done)

    OT = P.sb("OT", [128, 2048], BF16)
    YT = P.sb("YT", [128, 16 * 512], BF16)
    zc = P.sb("zc", [128, 2], F32)
    wring = P.sb("wring", [128, 8 * 512], BF16)
    tmp32 = P.sb("tmp32l3", [128, 512], F32)
    ring_sems = [cm.newsem("y") for _ in range(8)]
    ring_cons = [0] * 8
    s_res = cm.newsem("y")
    sc_o, ss_o = w_to_scratch(P, cm, "o", wo_d, 2048, 2048, buf=YT[:, 0:4096])
    pln = PostLN(P, cm, "l3b", W=SL)
    res_prev = 0
    ot_free = 0
    yt_cons = 0
    s_wt = cm.newsem("y")
    P.waitall("y", ss_o)
    P.wait("s", ss_o, ss_o.n)
    P.wait("t", ss_o, ss_o.n)
    for s_ in range(4):
        P.wait("t", cm.dve, cm.dve.n)
        for tt4 in range(4):
            tt = 4 * s_ + tt4
            eo_hist = {}
            prev = None
            for h in range(17):
                if h < 16:
                    if h >= 2:
                        P.wait("t", cm.dve, eo_hist[h - 2])
                    m1 = P.emit("t", lambda t, h=h, tt=tt: t.matmul(
                        pb[3 + (h % 2)][0:128, 0:128],
                        QTB[:, 2048 * h + 128 * tt:2048 * h + 128 * (tt + 1)],
                        KVt[:, 128 * h:128 * (h + 1)], start=True, stop=True),
                        inc=cm.mm)
                    m2 = P.emit("t", lambda t, h=h, tt=tt: t.matmul(
                        pb[5 + (h % 2)][0:128, 0:1],
                        QTB[:, 2048 * h + 128 * tt:2048 * h + 128 * (tt + 1)],
                        kscb[:, h:h + 1], start=True, stop=True), inc=cm.mm)
                if prev is not None:
                    ph, pm2 = prev
                    P.wait("v", cm.mm, pm2)
                    e1 = P.emit("v", lambda v, ph=ph: v.tensor_scalar(
                        zc[:, ph % 2:ph % 2 + 1], pb[5 + (ph % 2)][0:128, 0:1],
                        1e-6, None, op0=ALU.add), inc=cm.dve)
                    e2 = P.emit("v", lambda v, ph=ph: v.reciprocal(
                        zc[:, ph % 2:ph % 2 + 1], zc[:, ph % 2:ph % 2 + 1]),
                        inc=cm.dve)
                    eo = P.emit("v", lambda v, ph=ph: v.tensor_scalar(
                        OT[:, 128 * ph:128 * (ph + 1)], pb[3 + (ph % 2)][0:128, 0:128],
                        zc[0:128, ph % 2:ph % 2 + 1], None, op0=ALU.mult),
                        inc=cm.dve)
                    eo_hist[ph] = eo
                prev = (h, m2) if h < 16 else None
            P.wait("t", cm.dve, eo_hist[15])
            for ch in range(16):
                tr = P.emit("t", lambda t, ch=ch: t.transpose(
                    pbT[0:128, 0:128], OT[:, 128 * ch:128 * (ch + 1)], cm.identb[:, :]),
                    inc=cm.mm)
                P.wait("s", cm.mm, tr)
                aty = P.emit("s", lambda s, ch=ch, tt4=tt4: s.copy(
                    YT[:, 512 * ch + 128 * tt4:512 * ch + 128 * (tt4 + 1)],
                    pbT[0:128, 0:128]), inc=cm.act)
                P.wait("t", cm.act, aty)
        yt_done = cm.act.n
        pln.begin()
        P.wait("t", cm.act, yt_done)
        colq = slice(512 * s_, 512 * (s_ + 1))
        for eb in range(4):
            for i in range(16):
                slot = i % 8
                if ring_cons[slot]:
                    P.wait("y", cm.mm, ring_cons[slot])
                P.emit("y", lambda y, i=i, eb=eb, slot=slot: y.dma_start(
                    out=wring[:, 512 * slot:512 * (slot + 1)],
                    in_=sc_o[512 * eb:512 * (eb + 1), 128 * i:128 * (i + 1)],
                    transpose=True), inc=ring_sems[slot], n=16)
                P.waitall("t", ring_sems[slot])
                for ec in range(4):
                    m1 = P.emit("t", lambda t, i=i, ec=ec, slot=slot: t.matmul(
                        pb[3 + ec][0:128, 0:512],
                        wring[:, 512 * slot + 128 * ec:512 * slot + 128 * (ec + 1)],
                        YT[:, 512 * i:512 * (i + 1)],
                        start=(i == 0), stop=(i == 15)), inc=cm.mm)
                ring_cons[slot] = m1
            oproj_mm = m1
            for ec in range(4):
                oc = eb * 4 + ec
                P.wait("s", cm.mm, oproj_mm)
                if res_prev:
                    P.wait("s", cm.dve, res_prev)
                atmp = P.emit("s", lambda s, ec=ec, oc=oc: s.activation(
                    out=tmp32[:, :], in_=pb[3 + ec][0:128, 0:512], func=AF.Identity,
                    bias=boc[0:128, oc:oc + 1]), inc=cm.act)
                if res_prev:
                    P.wait("y", cm.dve, res_prev)
                P.emit("y", lambda y, oc=oc, colq=colq: y.dma_start(
                    out=pln.Rs[:, SL * oc:SL * (oc + 1)],
                    in_=xin[128 * oc:128 * (oc + 1), colq]), inc=s_res, n=16)
                P.wait("v", cm.mm, oproj_mm)
                P.wait("v", cm.act, atmp)
                P.waitall("v", s_res)
                ee = pln.add_chunk(oc, tmp32[:, :])
                res_prev = ee
            P.wait("t", cm.dve, ee)
        pln.finalize(dst_dram_ap=out_d[:, SL * s_:SL * (s_ + 1)])
    P.wait("y", cm.st, cm.st.n)


def build_l3b():
    nc = bass.Bass("TRN2", target_bir_lowering=False)
    io = {
        "xin": _ei(nc, "xin", [DM, TOK]),
        "q_in": _ei(nc, "q_in", [DM, TOK], BF16),
        "kva": _ei(nc, "kva", [16 * 128, 128]),
        "kvb": _ei(nc, "kvb", [16 * 128, 128]),
        "ksa": _ei(nc, "ksa", [1, 2048]), "ksb": _ei(nc, "ksb", [1, 2048]),
        "wo": _ei(nc, "wo", [2048, 2048]), "bo": _ei(nc, "bo", [1, 2048]),
        "out": _eo(nc, "out", [DM, TOK]),
    }
    return _single(nc, body_l3b, io)



def body_l3_ffn(P, cm, io):
    xin, wf_d, wp_d, out_d = io["xin"], io["wf"], io["wp"], io["out"]
    pb = [P.ps(f"pb{i}", [128, 512], F32) for i in range(8)]
    cm.pbR1, cm.pbR2, cm.pbB, cm.pbB2 = pb[0], pb[1], pb[2], pb[3]
    cm.pb_misc = pb[2]

    W = 512
    NSLF = TOK // W
    QB = 128.0
    EPS3 = 1e-6
    H = P.sb("Hf", [128, 64 * W], BF16)
    pln = PostLN(P, cm, "l3f", W=W)
    buf = H[:, 0:2 * 2048]
    f32buf = pln.Rs[:, 0:2 * 2048]
    sc1, ss1, oscale1, rdy1 = w_to_scratch_ternary(
        P, cm, "wf", wf_d, 8192, 2048, EPS3, "l3", buf, f32buf, cm.pbR1, cm.pbB2)
    sc2, ss2, oscale2, rdy2 = w_to_scratch_ternary(
        P, cm, "wp", wp_d, 2048, 8192, EPS3, "l3", buf, f32buf, cm.pbR1, cm.pbB2,
        prev_ssem=ss1)
    XN = P.sb("XN", [128, 16 * W], BF16)
    wring = P.sb("wring", [128, 8 * 512], BF16)
    acc = P.sb("acc", [128, W], F32)
    acc2 = P.sb("acc2", [128, W], F32)
    sB = P.sb("sB", [128, W], F32)
    invsB = P.sb("invsB", [128, W], F32)
    aB = P.sb("aB3", [128, W], F32)
    bB = P.sb("bB3", [128, W], F32)
    srow = P.sb("srow", [1, 2 * W], F32)
    rws = P.sb("rws", [1, 4 * W], F32)
    cols = P.sb("colsq", [128, 8], F32)
    rb16 = P.sb("rb16", [128, W], BF16)
    sq16 = P.sb("sq16", [128, W], BF16)
    ring_sems = [cm.newsem("y") for _ in range(8)]
    ring_cons = [0] * 8

    def inner_ln(nchunks, src_fn, dst_fn, eps, scale_chunks):
        first = True
        a2 = 0
        t2 = 0
        for ch in range(nchunks):
            src = src_fn(ch)
            if a2:
                P.wait("s", cm.mm, t2)
            a1 = P.emit("s", lambda s, src=src: s.copy(rb16[:, :], src), inc=cm.act)
            a2 = P.emit("s", lambda s, src=src: s.activation(
                out=sq16[:, :], in_=src, func=AF.Square), inc=cm.act)
            P.wait("t", cm.act, a2)
            fst = bool(first)
            t1 = P.emit("t", lambda t, fst=fst, ch=ch: t.matmul(
                cm.pbR1[0:1, 0:W], cm.ones128_1b[:, :], rb16[:, :],
                start=fst, stop=(ch == nchunks - 1)), inc=cm.mm)
            t2 = P.emit("t", lambda t, fst=fst, ch=ch: t.matmul(
                cm.pbR2[0:1, 0:W], cm.ones128_1b[:, :], sq16[:, :],
                start=fst, stop=(ch == nchunks - 1)), inc=cm.mm)
            first = False
        P.wait("s", cm.mm, t2)
        n = nchunks * 128
        a1 = P.emit("s", lambda s: s.copy(rws[0:1, 0:W], cm.pbR1[0:1, 0:W]), inc=cm.act)
        a2 = P.emit("s", lambda s: s.copy(rws[0:1, W:2 * W], cm.pbR2[0:1, 0:W]),
                    inc=cm.act)
        P.wait("v", cm.act, a2)
        P.emit("v", lambda v: v.tensor_scalar(rws[0:1, 0:W], rws[0:1, 0:W], 1.0 / n,
                                              None, op0=ALU.mult), inc=cm.dve)
        P.emit("v", lambda v: v.tensor_scalar(rws[0:1, W:2 * W], rws[0:1, W:2 * W],
                                              1.0 / n, None, op0=ALU.mult), inc=cm.dve)
        P.emit("v", lambda v: v.tensor_tensor(rws[0:1, 2 * W:3 * W], rws[0:1, 0:W],
                                              rws[0:1, 0:W], ALU.mult), inc=cm.dve)
        P.emit("v", lambda v: v.tensor_tensor(rws[0:1, W:2 * W], rws[0:1, W:2 * W],
                                              rws[0:1, 2 * W:3 * W], ALU.subtract),
               inc=cm.dve)
        e5 = P.emit("v", lambda v: v.tensor_scalar(rws[0:1, W:2 * W], rws[0:1, W:2 * W],
                                                   float(eps), None, op0=ALU.add),
                    inc=cm.dve)
        P.wait("s", cm.dve, e5)
        a3 = P.emit("s", lambda s: s.activation(out=rws[0:1, W:2 * W],
                                                in_=rws[0:1, W:2 * W], func=AF.Sqrt),
                    inc=cm.act)
        P.wait("v", cm.act, a3)
        P.emit("v", lambda v: v.reciprocal(rws[0:1, W:2 * W], rws[0:1, W:2 * W]),
               inc=cm.dve)
        e7 = P.emit("v", lambda v: v.scalar_tensor_tensor(
            rws[0:1, 0:W], rws[0:1, 0:W], -1.0, rws[0:1, W:2 * W],
            op0=ALU.mult, op1=ALU.mult), inc=cm.dve)
        P.wait("t", cm.dve, e7)
        bcast_row(P, cm, rws[0:1, W:2 * W], cm.pbB, aB, W)
        _, abx = bcast_row(P, cm, rws[0:1, 0:W], cm.pbB, bB, W)
        P.wait("v", cm.act, abx)
        el = 0
        for ch in range(nchunks):
            src = src_fn(ch)
            dst = dst_fn(ch)
            P.emit("v", lambda v, src=src, dst=dst: v.tensor_tensor(
                dst, src, aB[:, :], ALU.mult), inc=cm.dve)
            el = P.emit("v", lambda v, dst=dst: v.tensor_tensor(
                dst, dst, bB[:, :], ALU.add), inc=cm.dve)
        return el

    P.waitall("y", ss2)
    P.waitall("g", ss2)
    P.wait("v", cm.act, max(rdy1, rdy2))
    P.wait("s", cm.act, max(rdy1, rdy2))
    res_prev = 0
    s_res = cm.newsem("y")
    for s_ in range(NSLF):
        colt = slice(W * s_, W * (s_ + 1))
        pln.begin()
        for i in range(16):
            if res_prev:
                P.wait("y", cm.dve, res_prev)
                res_prev = 0
            rt = P.emit("y", lambda y, i=i, colt=colt: y.dma_start(
                out=pln.Rs[:, W * i:W * (i + 1)], in_=xin[128 * i:128 * (i + 1), colt]),
                inc=s_res, n=16)
            P.waitall("v", s_res)
            res_prev = P.emit("v", lambda v, i=i: v.engine_nop(), inc=cm.dve)
        P.wait("s", cm.dve, cm.dve.n)
        eln = inner_ln(16, lambda ch: pln.Rs[:, W * ch:W * (ch + 1)],
                       lambda ch: XN[:, W * ch:W * (ch + 1)], EPS3, None)
        for i in range(16):
            if i == 0:
                e = amax_start(P, cm, acc[:, :], acc2[:, :], XN[:, 0:W])
            else:
                e = amax_step(P, cm, acc[:, :], acc2[:, :],
                              XN[:, W * i:W * (i + 1)])
        amax_finish(P, cm, acc[:, :], acc2[:, :])
        nch = W // 128
        elast = part_reduce_max(P, cm, acc, W, cols[:, 0:nch], cm.pbB2, j0=0)
        P.emit("v", lambda v: v.tensor_scalar(
            cols[:, nch:2 * nch], cols[:, 0:nch], 1.0 / QB, None, op0=ALU.mult),
            inc=cm.dve)
        P.emit("v", lambda v: v.reciprocal(cols[:, 0:nch], cols[:, 0:nch]), inc=cm.dve)
        e4 = P.emit("v", lambda v: v.tensor_scalar(
            cols[:, 0:nch], cols[:, 0:nch], QB, None, op0=ALU.mult), inc=cm.dve)
        P.wait("t", cm.dve, e4)
        P.wait("t", cm.act, cm.act.n)
        tl = None
        for j in range(nch):
            tl = P.emit("t", lambda t, j=j: t.transpose(
                cm.pbB2[0:1, 128 * j:128 * (j + 1)], cols[:, j:j + 1],
                cm.identf[0:128, 0:128]), inc=cm.mm)
            tl = P.emit("t", lambda t, j=j: t.transpose(
                cm.pbB[0:1, 128 * j:128 * (j + 1)],
                cols[:, nch + j:nch + j + 1], cm.identf[0:128, 0:128]), inc=cm.mm)
        P.wait("s", cm.mm, tl)
        a1 = P.emit("s", lambda s: s.copy(srow[0:1, 0:W], cm.pbB2[0:1, 0:W]),
                    inc=cm.act)
        a1 = P.emit("s", lambda s: s.copy(srow[0:1, W:2 * W], cm.pbB[0:1, 0:W]),
                    inc=cm.act)
        P.wait("t", cm.act, a1)
        bcast_row(P, cm, srow[0:1, 0:W], cm.pbB, sB, W)
        _, ab = bcast_row(P, cm, srow[0:1, W:2 * W], cm.pbB, invsB, W)
        P.wait("v", cm.act, ab)
        CL = float(QB - EPS3)
        for i in range(16):
            P.emit("v", lambda v, i=i: v.tensor_tensor(
                XN[:, W * i:W * (i + 1)], XN[:, W * i:W * (i + 1)], sB[:, :],
                ALU.mult), inc=cm.dve)
            P.emit("v", lambda v, i=i: v.tensor_scalar(
                XN[:, W * i:W * (i + 1)], XN[:, W * i:W * (i + 1)], CL, -CL,
                op0=ALU.min, op1=ALU.max), inc=cm.dve)
            e3 = P.emit("v", lambda v, i=i: v.tensor_tensor(
                XN[:, W * i:W * (i + 1)], XN[:, W * i:W * (i + 1)], invsB[:, :],
                ALU.mult), inc=cm.dve)
        xq_done = e3
        P.wait("t", cm.dve, xq_done)
        P.wait("t", cm.act, cm.act.n)
        bank_free_dve = 0
        for hb in range(16):
            for i in range(16):
                slot = (hb * 16 + i) % 8
                if ring_cons[slot]:
                    P.wait("y", cm.mm, ring_cons[slot])
                P.emit("y", lambda y, hb=hb, i=i, slot=slot: y.dma_start(
                    out=wring[:, 512 * slot:512 * (slot + 1)],
                    in_=sc1[512 * hb:512 * (hb + 1), 128 * i:128 * (i + 1)],
                    transpose=True), inc=ring_sems[slot], n=16)
                P.waitall("t", ring_sems[slot])
                if i == 0 and bank_free_dve:
                    P.wait("t", cm.dve, bank_free_dve)
                    P.wait("t", cm.act, cm.act.n)
                for oc in range(4):
                    m1 = P.emit("t", lambda t, slot=slot, oc=oc, i=i: t.matmul(
                        pb[4 + oc][0:128, 0:W],
                        wring[:, 512 * slot + 128 * oc:512 * slot + 128 * (oc + 1)],
                        XN[:, W * i:W * (i + 1)],
                        start=(i == 0), stop=(i == 15)), inc=cm.mm)
                ring_cons[slot] = m1
            P.wait("s", cm.mm, m1)
            for oc in range(4):
                ch = hb * 4 + oc
                a1 = P.emit("s", lambda s, oc=oc, ch=ch: s.activation(
                    out=H[:, W * ch:W * (ch + 1)], in_=pb[4 + oc][0:128, 0:W],
                    func=AF.Gelu, scale=oscale1), inc=cm.act)
                P.wait("v", cm.act, a1)
                bank_free_dve = P.emit("v", lambda v: v.engine_nop(), inc=cm.dve)
        P.wait("s", cm.dve, cm.dve.n)
        eln2 = inner_ln(64, lambda ch: H[:, W * ch:W * (ch + 1)],
                        lambda ch: H[:, W * ch:W * (ch + 1)], EPS3, None)
        for i in range(64):
            if i == 0:
                e = amax_start(P, cm, acc[:, :], acc2[:, :], H[:, 0:W])
            else:
                e = amax_step(P, cm, acc[:, :], acc2[:, :],
                              H[:, W * i:W * (i + 1)])
        amax_finish(P, cm, acc[:, :], acc2[:, :])
        elast = part_reduce_max(P, cm, acc, W, cols[:, 0:nch], cm.pbB2, j0=0)
        P.emit("v", lambda v: v.tensor_scalar(
            cols[:, nch:2 * nch], cols[:, 0:nch], 1.0 / QB, None, op0=ALU.mult),
            inc=cm.dve)
        P.emit("v", lambda v: v.reciprocal(cols[:, 0:nch], cols[:, 0:nch]), inc=cm.dve)
        e4 = P.emit("v", lambda v: v.tensor_scalar(
            cols[:, 0:nch], cols[:, 0:nch], QB, None, op0=ALU.mult), inc=cm.dve)
        P.wait("t", cm.dve, e4)
        P.wait("t", cm.act, cm.act.n)
        for j in range(nch):
            tl = P.emit("t", lambda t, j=j: t.transpose(
                cm.pbB2[0:1, 128 * j:128 * (j + 1)], cols[:, j:j + 1],
                cm.identf[0:128, 0:128]), inc=cm.mm)
            tl = P.emit("t", lambda t, j=j: t.transpose(
                cm.pbB[0:1, 128 * j:128 * (j + 1)],
                cols[:, nch + j:nch + j + 1], cm.identf[0:128, 0:128]), inc=cm.mm)
        P.wait("s", cm.mm, tl)
        a1 = P.emit("s", lambda s: s.copy(srow[0:1, 0:W], cm.pbB2[0:1, 0:W]),
                    inc=cm.act)
        a1 = P.emit("s", lambda s: s.copy(srow[0:1, W:2 * W], cm.pbB[0:1, 0:W]),
                    inc=cm.act)
        P.wait("t", cm.act, a1)
        bcast_row(P, cm, srow[0:1, 0:W], cm.pbB, sB, W)
        _, ab = bcast_row(P, cm, srow[0:1, W:2 * W], cm.pbB, invsB, W)
        P.wait("v", cm.act, ab)
        for ch in range(64):
            P.emit("v", lambda v, ch=ch: v.tensor_tensor(
                H[:, W * ch:W * (ch + 1)], H[:, W * ch:W * (ch + 1)], sB[:, :],
                ALU.mult), inc=cm.dve)
            P.emit("v", lambda v, ch=ch: v.tensor_scalar(
                H[:, W * ch:W * (ch + 1)], H[:, W * ch:W * (ch + 1)], CL, -CL,
                op0=ALU.min, op1=ALU.max), inc=cm.dve)
            e3 = P.emit("v", lambda v, ch=ch: v.tensor_tensor(
                H[:, W * ch:W * (ch + 1)], H[:, W * ch:W * (ch + 1)], invsB[:, :],
                ALU.mult), inc=cm.dve)
        hq_done = e3
        P.wait("t", cm.dve, hq_done)
        pln.scale_col = oscale2
        for ob in range(4):
            for i in range(64):
                slot = i % 8
                if ring_cons[slot]:
                    P.wait("y", cm.mm, ring_cons[slot])
                P.emit("y", lambda y, ob=ob, i=i, slot=slot: y.dma_start(
                    out=wring[:, 512 * slot:512 * (slot + 1)],
                    in_=sc2[512 * ob:512 * (ob + 1), 128 * i:128 * (i + 1)],
                    transpose=True), inc=ring_sems[slot], n=16)
                P.waitall("t", ring_sems[slot])
                for oc in range(4):
                    m1 = P.emit("t", lambda t, slot=slot, oc=oc, i=i: t.matmul(
                        pb[4 + oc][0:128, 0:W],
                        wring[:, 512 * slot + 128 * oc:512 * slot + 128 * (oc + 1)],
                        H[:, W * i:W * (i + 1)],
                        start=(i == 0), stop=(i == 63)), inc=cm.mm)
                ring_cons[slot] = m1
            oproj_mm = m1
            for oc in range(4):
                chn = ob * 4 + oc
                P.wait("v", cm.mm, oproj_mm)
                ee = pln.add_chunk(chn, pb[4 + oc][0:128, 0:W])
            P.wait("t", cm.dve, ee)
        pln.finalize(dst_dram_ap=out_d[:, W * s_:W * (s_ + 1)])
        res_prev = ee
    P.wait("y", cm.st, cm.st.n)


def build_l3_ffn():
    nc = bass.Bass("TRN2", target_bir_lowering=False)
    io = {
        "xin": _ei(nc, "xin", [DM, TOK]),
        "wf": _ei(nc, "wf", [8192, 2048]), "wp": _ei(nc, "wp", [2048, 8192]),
        "out": _eo(nc, "out", [DM, TOK]),
    }
    return _single(nc, body_l3_ffn, io)



def build_g1():
    nc = bass.Bass("TRN2", target_bir_lowering=False)
    x0i = nc.dram_tensor("x0i", [DM, TOK], F32, kind="Internal").ap()
    io_a = {
        "xctx": _ei(nc, "xctx", [DM, 4096]),
        "qw": _ei(nc, "qw", [2048, 2048]), "kw": _ei(nc, "kw", [256, 2048]),
        "vw": _ei(nc, "vw", [256, 2048]), "ow": _ei(nc, "ow", [2048, 2048]),
        "qn": _ei(nc, "qn", [1, 256]), "kn": _ei(nc, "kn", [1, 256]),
        "cc": _ei(nc, "cc", [1, 8]),
        "out": x0i,
    }
    io_b = {
        "xin": x0i,
        "w1": _ei(nc, "w1", [8192, 2048]), "w2": _ei(nc, "w2", [2048, 8192]),
        "out": _eo(nc, "out", [DM, TOK]),
    }
    return _multi(nc, [(body_l0_attn, io_a, {}), (body_l0_ffn, io_b, {})])


def build_g2():
    nc = bass.Bass("TRN2", target_bir_lowering=False)
    x2i = nc.dram_tensor("x2i", [DM, TOK], F32, kind="Internal").ap()
    io_a = {
        "xbuf": _ei(nc, "xbuf", [DM, 3072]),
        "qw": _ei(nc, "qw", [2048, 2048]), "kw": _ei(nc, "kw", [512, 2048]),
        "vw": _ei(nc, "vw", [512, 2048]), "ow": _ei(nc, "ow", [2048, 2048]),
        "cc": _ei(nc, "cc", [1, 8]),
        "out": x2i,
    }
    io_b = {
        "xin": x2i,
        "w1": _ei(nc, "w1", [8192, 2048]), "w2": _ei(nc, "w2", [2048, 8192]),
        "w3": _ei(nc, "w3", [8192, 2048]),
        "out": _eo(nc, "xout", [DM, TOK]),
    }
    io_c = {
        "xin": io_b["out"],
        "wq": _ei(nc, "wq", [2048, 2048]), "wk": _ei(nc, "wk", [2048, 2048]),
        "wv": _ei(nc, "wv", [2048, 2048]), "ld": _ei(nc, "ld", [1, 16]),
        "o_out": _eo(nc, "o_out", [TOK, DM], BF16),
        "q_out": _eo(nc, "q_out", [DM, TOK], BF16),
        "s_out": _eo(nc, "s_out", [16 * 128, 128]),
    }
    return _multi(nc, [(body_l1_attn, io_a, {}), (body_l1_ffn, io_b, {}),
                       (body_l2a, io_c, {})])


def build_g3():
    nc = bass.Bass("TRN2", target_bir_lowering=False)
    x4i = nc.dram_tensor("x4i", [DM, TOK], F32, kind="Internal").ap()
    io_a = {
        "xin": _ei(nc, "xin", [DM, TOK]),
        "o_in": _ei(nc, "o_in", [TOK, DM], BF16),
        "q_in": _ei(nc, "q_in", [DM, TOK], BF16),
        "s0": _ei(nc, "s0", [16 * 128, 128]), "ld": _ei(nc, "ld", [1, 16]),
        "wg1": _ei(nc, "wg1", [2048, 128]), "wg2": _ei(nc, "wg2", [128, 2048]),
        "nw": _ei(nc, "nw", [1, 2048]), "wo": _ei(nc, "wo", [2048, 2048]),
        "out": x4i,
    }
    io_b = {
        "xin": x4i,
        "wg": _ei(nc, "wg", [6464, 2048]), "wu": _ei(nc, "wu", [6464, 2048]),
        "wd": _ei(nc, "wd", [2048, 6464]),
        "out": _eo(nc, "xout", [DM, TOK]),
    }
    io_c = {
        "xin": io_b["out"],
        "wq": _ei(nc, "wq3", [2048, 2048]), "wk": _ei(nc, "wk3", [2048, 2048]),
        "wv": _ei(nc, "wv3", [2048, 2048]),
        "bq": _ei(nc, "bq", [1, 2048]), "bk": _ei(nc, "bk", [1, 2048]),
        "bv": _ei(nc, "bv", [1, 2048]), "cc": _ei(nc, "cc", [1, 8]),
        "q_out": _eo(nc, "q_out", [DM, TOK], BF16),
        "kv_out": _eo(nc, "kv_out", [16 * 128, 128]),
        "ks_out": _eo(nc, "ks_out", [1, 2048]),
    }
    return _multi(nc, [(body_l2b, io_a, {}), (body_l2_ffn, io_b, {}),
                       (body_l3a, io_c, {})])


def build_g4():
    nc = bass.Bass("TRN2", target_bir_lowering=False)
    x6i = nc.dram_tensor("x6i", [DM, TOK], F32, kind="Internal").ap()
    io_a = {
        "xin": _ei(nc, "xin", [DM, TOK]),
        "q_in": _ei(nc, "q_in", [DM, TOK], BF16),
        "kva": _ei(nc, "kva", [16 * 128, 128]),
        "kvb": _ei(nc, "kvb", [16 * 128, 128]),
        "ksa": _ei(nc, "ksa", [1, 2048]), "ksb": _ei(nc, "ksb", [1, 2048]),
        "wo": _ei(nc, "wo", [2048, 2048]), "bo": _ei(nc, "bo", [1, 2048]),
        "out": x6i,
    }
    io_b = {
        "xin": x6i,
        "wf": _ei(nc, "wf", [8192, 2048]), "wp": _ei(nc, "wp", [2048, 8192]),
        "out": _eo(nc, "out", [DM, TOK]),
    }
    return _multi(nc, [(body_l3b, io_a, {}), (body_l3_ffn, io_b, {})])


_NC_CACHE = {}


def _prog(name, builder):
    if name not in _NC_CACHE:
        _NC_CACHE[name] = builder()
    return _NC_CACHE[name]



_PERMQ = (np.arange(16)[:, None] * 128 +
          np.concatenate([np.arange(0, 128, 2),
                          np.arange(1, 128, 2)])[None, :]).reshape(-1)
_PERMK = (np.arange(4)[:, None] * 128 +
          np.concatenate([np.arange(0, 128, 2),
                          np.arange(1, 128, 2)])[None, :]).reshape(-1)


def _cf(a):
    return np.ascontiguousarray(np.asarray(a, dtype=np.float32))


def kernel(**inputs):
    cores = list(range(8))
    BH = [(c // 2, 2048 * (c % 2)) for c in cores]

    x = _cf(inputs["L0_attn_x"])

    qw = _cf(inputs["L0_attn_q_proj"]); kw = _cf(inputs["L0_attn_k_proj"])
    vw = _cf(inputs["L0_attn_v_proj"]); ow = _cf(inputs["L0_attn_o_proj"])
    qn = _cf(inputs["L0_attn_q_norm"]).reshape(1, 256)
    kn = _cf(inputs["L0_attn_k_norm"]).reshape(1, 256)
    w1 = _cf(inputs["L0_ffn_w1"]); w2 = _cf(inputs["L0_ffn_w2"])
    nc = _prog("g1", build_g1)
    maps = []
    for c in cores:
        b, B0 = BH[c]
        c2 = 2048 - B0
        xc = np.zeros((2048, 4096), np.float32)
        xc[:, c2:] = x[b, 0:4096 - c2, :].T
        maps.append({"xctx": xc, "qw": qw, "kw": kw, "vw": vw, "ow": ow,
                     "qn": qn, "kn": kn,
                     "cc": np.array([[c2, B0, 0, 0, 0, 0, 0, 0]], np.float32),
                     "w1": w1, "w2": w2})
    res = run_bass_kernel_spmd(nc, maps, core_ids=cores)
    s = [np.asarray(res.results[c]["out"]) for c in cores]

    qp = _cf(inputs["L1_attn_wq"])[_PERMQ]
    kp = _cf(inputs["L1_attn_wk"])[_PERMK]
    vw1 = _cf(inputs["L1_attn_wv"]); ow1 = _cf(inputs["L1_attn_wo"])
    W1 = _cf(inputs["L1_ffn_W1"]); W2 = _cf(inputs["L1_ffn_W2"])
    W3 = _cf(inputs["L1_ffn_W3"])
    wq2 = _cf(inputs["L2_attn_Wq"]); wk2 = _cf(inputs["L2_attn_Wk"])
    wv2 = _cf(inputs["L2_attn_Wv"])
    ld2 = _cf(inputs["L2_attn_log_decay"]).reshape(1, 16)
    nc = _prog("g2", build_g2)
    maps = []
    for c in cores:
        b, B0 = BH[c]
        c2 = 1024 - B0
        halo = (s[c - 1][:, 1024:2048] if B0 >= 1024
                else np.zeros((2048, 1024), np.float32))
        xbuf = np.ascontiguousarray(np.concatenate([halo, s[c]], axis=1))
        maps.append({"xbuf": xbuf, "qw": qp, "kw": kp, "vw": vw1, "ow": ow1,
                     "cc": np.array([[c2, c2, 0, 0, 0, 0, 0, 0]], np.float32),
                     "w1": W1, "w2": W2, "w3": W3,
                     "wq": wq2, "wk": wk2, "wv": wv2, "ld": ld2})
    res = run_bass_kernel_spmd(nc, maps, core_ids=cores)
    s = [np.asarray(res.results[c]["xout"]) for c in cores]
    l2 = [{k: np.asarray(res.results[c][k]) for k in ("o_out", "q_out", "s_out")}
          for c in cores]

    wg1 = _cf(inputs["L2_attn_Wg1"]); wg2 = _cf(inputs["L2_attn_Wg2"])
    nw2 = _cf(inputs["L2_attn_norm_w"]).reshape(1, 2048)
    wo2 = _cf(inputs["L2_attn_Wo"])
    wg = _cf(inputs["L2_ffn_w_gate"]); wu = _cf(inputs["L2_ffn_w_up"])
    wd = _cf(inputs["L2_ffn_w_down"])
    wq3 = _cf(inputs["L3_attn_Wq"]); wk3 = _cf(inputs["L3_attn_Wk"])
    wv3 = _cf(inputs["L3_attn_Wv"])
    bq3 = _cf(inputs["L3_attn_bq"]).reshape(1, 2048)
    bk3 = _cf(inputs["L3_attn_bk"]).reshape(1, 2048)
    bv3 = _cf(inputs["L3_attn_bv"]).reshape(1, 2048)
    nc = _prog("g3", build_g3)
    maps = []
    for c in cores:
        b, B0 = BH[c]
        s0 = (l2[c - 1]["s_out"] if c % 2 == 1
              else np.zeros((2048, 128), np.float32))
        maps.append({"xin": s[c], "o_in": l2[c]["o_out"], "q_in": l2[c]["q_out"],
                     "s0": s0, "ld": ld2, "wg1": wg1, "wg2": wg2, "nw": nw2,
                     "wo": wo2, "wg": wg, "wu": wu, "wd": wd,
                     "wq3": wq3, "wk3": wk3, "wv3": wv3,
                     "bq": bq3, "bk": bk3, "bv": bv3,
                     "cc": np.array([[-B0 / 4096.0, 0, 0, 0, 0, 0, 0, 0]],
                                    np.float32)})
    res = run_bass_kernel_spmd(nc, maps, core_ids=cores)
    s = [np.asarray(res.results[c]["xout"]) for c in cores]
    l3 = [{k: np.asarray(res.results[c][k]) for k in ("q_out", "kv_out", "ks_out")}
          for c in cores]

    wo3 = _cf(inputs["L3_attn_Wo"])
    bo3 = _cf(inputs["L3_attn_bo"]).reshape(1, 2048)
    wf = _cf(inputs["L3_ffn_w_fc"]); wp = _cf(inputs["L3_ffn_w_proj"])
    nc = _prog("g4", build_g4)
    maps = []
    for c in cores:
        p = c ^ 1
        maps.append({"xin": s[c], "q_in": l3[c]["q_out"],
                     "kva": l3[c]["kv_out"], "kvb": l3[p]["kv_out"],
                     "ksa": l3[c]["ks_out"], "ksb": l3[p]["ks_out"],
                     "wo": wo3, "bo": bo3, "wf": wf, "wp": wp})
    res = run_bass_kernel_spmd(nc, maps, core_ids=cores)
    s = [np.asarray(res.results[c]["out"]) for c in cores]

    out = np.empty((4, 4096, 2048), np.float32)
    for c in cores:
        b, B0 = BH[c]
        out[b, B0:B0 + 2048, :] = s[c].T
    return out
